# version 01
# baseline (speedup 1.0000x reference)
import contextlib

import numpy as np

import concourse.bass as bass
import concourse.mybir as mybir
from concourse.bass_core_rust import IndirectOffsetOnAxis
from concourse.bass_utils import run_bass_kernel_spmd

F32 = mybir.dt.float32
BF16 = mybir.dt.bfloat16
I32 = mybir.dt.int32
U8 = mybir.dt.uint8
AF = mybir.ActivationFunctionType
ALU = mybir.AluOpType
AX = mybir.AxisListType

D = 1024
T = 4096
NC = 8
NTT = 8
TTW = 512
NH = 16
HD = 64
NE = 8
CAP = 1024
DFF = 4096
RANK = 16
TOPK = 2
WIN = 256
LN_EPS = 1e-5
RMS_EPS = 1e-8
SIG_EPS = 1e-8
NEG_BIG = -1e30


class KB:

    def __init__(self):
        self.nc = bass.Bass("TRN2", target_bir_lowering=False)
        self.es = contextlib.ExitStack()
        self.sems = {}
        self.cnt = {}
        self.consts = {}

    def sem(self, name):
        if name not in self.sems:
            self.sems[name] = self.es.enter_context(self.nc.semaphore(name))
            self.cnt[name] = 0
        return self.sems[name]

    def inc(self, inst, name, k=1):
        s = self.sem(name)
        self.cnt[name] += k
        inst.then_inc(s, k)
        return self.cnt[name]

    def dinc(self, inst, name):
        return self.inc(inst, name, 16)

    def wait(self, eng, name, val=None):
        if val is None:
            val = self.cnt[name]
        if val <= 0:
            return
        eng.wait_ge(self.sem(name), val)

    def sb(self, name, shape, dt):
        return self.es.enter_context(self.nc.sbuf_tensor(name, shape, dt))

    def ps(self, name, shape, dt=F32):
        return self.es.enter_context(self.nc.psum_tensor(name, shape, dt))

    def din(self, name, shape, dt=F32):
        return self.nc.dram_tensor(name, shape, dt, kind="ExternalInput").ap()

    def dout(self, name, shape, dt=F32):
        return self.nc.dram_tensor(name, shape, dt, kind="ExternalOutput").ap()

    def dtmp(self, name, shape, dt):
        return self.nc.dram_tensor(name, shape, dt, kind="Internal").ap()

    def const(self, name, arr):
        arr = np.ascontiguousarray(arr)
        self.consts[name] = arr
        dt = {np.dtype(np.float32): F32, np.dtype(np.int32): I32,
              np.dtype(np.uint16): BF16}[arr.dtype]
        return self.nc.dram_tensor(name, list(arr.shape), dt, kind="ExternalInput").ap()

    def cbf16(self, name, arr_f32):
        import ml_dtypes
        a = np.ascontiguousarray(np.asarray(arr_f32, np.float32).astype(ml_dtypes.bfloat16)).view(np.uint16)
        return self.const(name, a)

    def start(self):
        self.block = self.es.enter_context(self.nc.Block())

    def finish(self):
        self.es.close()
        return self


class Bank:

    def __init__(self, ap):
        self.ap = ap
        self.sem = None
        self.tok = 0

    def wait_reuse(self, kb, eng):
        if self.sem is not None:
            kb.wait(eng, self.sem, self.tok)

    def consumed(self, sem, tok):
        self.sem = sem
        self.tok = tok


def host_consts(kb, extra=()):
    spec = {
        "ones_col_bf": (np.ones((128, 1), np.float32), BF16),
        "ones_col_f32": (np.ones((128, 1), np.float32), F32),
        "ones_row_f32": (np.ones((1, 128), np.float32), F32),
        "ones_row_bf": (np.ones((1, 128), np.float32), BF16),
        "id_bf": (np.eye(128, dtype=np.float32), BF16),
        "id_f32": (np.eye(128, dtype=np.float32), F32),
        "iota_col_f32": (np.arange(128, dtype=np.float32).reshape(128, 1), F32),
    }
    for name, arr, dt in extra:
        spec[name] = (arr, dt)
    c = {}
    kb._const_pairs = []
    for name, (arr, dt) in spec.items():
        if dt == BF16:
            dap = kb.cbf16("c_" + name, arr)
        elif dt == I32:
            dap = kb.const("c_" + name, np.asarray(arr, np.int32))
        else:
            dap = kb.const("c_" + name, np.asarray(arr, np.float32))
        tile = kb.sb(name, list(arr.shape), dt)
        kb._const_pairs.append((tile, dap))
        c[name] = tile[:]
    return c


def load_consts(kb):
    nc = kb.nc
    t = 0
    for tile, dap in kb._const_pairs:
        t = kb.dinc(nc.sync.dma_start(out=tile[:], in_=dap), "ldq")
    for eng in (nc.tensor, nc.vector, nc.scalar, nc.gpsimd):
        kb.wait(eng, "ldq", t)


def emit_ln(kb, C, x_dram, w, xln, pb, pbf=None, xrms=None, xtm_dram=None, per_tt_cb=None, f32_stats=False):
    nc = kb.nc
    xf, xb, sq, stat = w["xf"], w["xb"], w["sq"], w["stat"]
    s0 = stat[:, 0 * TTW:1 * TTW]
    s1 = stat[:, 1 * TTW:2 * TTW]
    s2 = stat[:, 2 * TTW:3 * TTW]
    x_src = x_dram.rearrange("(c p) t -> p c t", p=128)
    t_store = 0
    t_ap = 0
    t_pe_prev = 0
    for tt in range(NTT):
        ts = tt * TTW
        kb.wait(nc.sync, "dve", t_ap)
        t_ld = kb.dinc(nc.sync.dma_start(
            out=xf[:].rearrange("p (c t) -> p c t", c=NC),
            in_=x_src[:, :, ts:ts + TTW]), "ldq")
        kb.wait(nc.vector, "ldq", t_ld)
        kb.wait(nc.vector, "pe", t_pe_prev)
        t_cast = kb.inc(nc.vector.tensor_copy(xb[:], xf[:]), "dve")
        for bk in (pb[0], pb[1]):
            bk.wait_reuse(kb, nc.tensor)
        if f32_stats:
            sqb = [w["sqf"][:, 0:TTW], w["sqf"][:, TTW:2 * TTW]]
            kb.wait(nc.tensor, "ldq", t_ld)
            for cc in range(NC):
                nc.tensor.matmul(pb[0].ap[0:1, :], C["ones_col_f32"],
                                 xf[:, cc * TTW:(cc + 1) * TTW],
                                 start=(cc == 0), stop=(cc == NC - 1))
            t_sqa = [0, 0]
            kb.wait(nc.scalar, "ldq", t_ld)
            for cc in range(NC):
                buf = cc % 2
                if t_sqa[buf]:
                    kb.wait(nc.scalar, "pe", t_sqa[buf])
                t_sq = kb.inc(nc.scalar.activation(sqb[buf], xf[:, cc * TTW:(cc + 1) * TTW],
                                                   AF.Square), "act")
                kb.wait(nc.tensor, "act", t_sq)
                i = nc.tensor.matmul(pb[1].ap[0:1, :], C["ones_col_f32"], sqb[buf],
                                     start=(cc == 0), stop=(cc == NC - 1))
                t_sqa[buf] = t_mm = kb.inc(i, "pe")
        else:
            kb.wait(nc.scalar, "dve", t_cast)
            t_sq = kb.inc(nc.scalar.activation(sq[:], xb[:], AF.Square), "act")
            kb.wait(nc.tensor, "dve", t_cast)
            kb.wait(nc.tensor, "act", t_sq)
            for cc in range(NC):
                nc.tensor.matmul(pb[0].ap[0:1, :], C["ones_col_bf"], xb[:, cc * TTW:(cc + 1) * TTW],
                                 start=(cc == 0), stop=(cc == NC - 1))
            for cc in range(NC):
                i = nc.tensor.matmul(pb[1].ap[0:1, :], C["ones_col_bf"], sq[:, cc * TTW:(cc + 1) * TTW],
                                     start=(cc == 0), stop=(cc == NC - 1))
            t_mm = kb.inc(i, "pe")
        kb.wait(nc.vector, "pe", t_mm)
        nc.vector.tensor_scalar(s0, pb[0].ap[0:1, :], 1.0 / D, None, op0=ALU.mult)
        nc.vector.tensor_scalar(s1, pb[1].ap[0:1, :], 1.0 / D, None, op0=ALU.mult)
        nc.vector.tensor_tensor(s2, s0, s0, op=ALU.mult)
        nc.vector.tensor_tensor(s1, s1, s2, op=ALU.subtract)
        t_v = kb.inc(nc.vector.tensor_scalar(s1, s1, LN_EPS, None, op0=ALU.add), "dve")
        pb[0].consumed("dve", t_v)
        pb[1].consumed("dve", t_v)
        kb.wait(nc.scalar, "dve", t_v)
        t_sd = kb.inc(nc.scalar.activation(s2, s1, AF.Sqrt), "act")
        kb.wait(nc.vector, "act", t_sd)
        t_rc = kb.inc(nc.vector.reciprocal(s1, s2), "dve")
        kb.wait(nc.vector, "dve", t_rc)
        t_mr = kb.inc(nc.vector.tensor_tensor(s2, s0, s1, op=ALU.mult), "dve")
        for bk in (pb[2], pb[3]):
            bk.wait_reuse(kb, nc.tensor)
        kb.wait(nc.tensor, "dve", t_mr)
        nc.tensor.matmul(pb[2].ap[:, :], C["ones_row_f32"], s1, start=True, stop=True)
        t_bc = kb.inc(nc.tensor.matmul(pb[3].ap[:, :], C["ones_row_f32"], s2,
                                       start=True, stop=True), "pe")
        kb.wait(nc.vector, "pe", t_bc)
        if per_tt_cb is None:
            for cc in range(NC):
                nc.vector.tensor_tensor(xf[:, cc * TTW:(cc + 1) * TTW],
                                        xf[:, cc * TTW:(cc + 1) * TTW], pb[2].ap[:, :], op=ALU.mult)
                i = nc.vector.tensor_tensor(xln[cc][:, ts:ts + TTW],
                                            xf[:, cc * TTW:(cc + 1) * TTW], pb[3].ap[:, :],
                                            op=ALU.subtract)
            t_ap = kb.inc(i, "dve")
        else:
            for cc in range(NC):
                nc.vector.tensor_tensor(xf[:, cc * TTW:(cc + 1) * TTW],
                                        xf[:, cc * TTW:(cc + 1) * TTW], pb[2].ap[:, :], op=ALU.mult)
                nc.vector.tensor_tensor(xf[:, cc * TTW:(cc + 1) * TTW],
                                        xf[:, cc * TTW:(cc + 1) * TTW], pb[3].ap[:, :],
                                        op=ALU.subtract)
                i = nc.vector.tensor_copy(xln[cc][:, ts:ts + TTW],
                                          xf[:, cc * TTW:(cc + 1) * TTW])
            t_ap = kb.inc(i, "dve")
            per_tt_cb(tt, ts, t_ap)
        pb[2].consumed("dve", t_ap)
        pb[3].consumed("dve", t_ap)

        if xrms is not None:
            kb.wait(nc.scalar, "dve", t_ap)
            last = None
            for cc in range(NC):
                last = nc.scalar.activation(sq[:, cc * TTW:(cc + 1) * TTW],
                                            xln[cc][:, ts:ts + TTW], AF.Square)
            t_sq2 = kb.inc(last, "act")
            pb[0].wait_reuse(kb, nc.tensor)
            kb.wait(nc.tensor, "act", t_sq2)
            for cc in range(NC):
                i = nc.tensor.matmul(pb[0].ap[0:1, :], C["ones_col_bf"],
                                     sq[:, cc * TTW:(cc + 1) * TTW],
                                     start=(cc == 0), stop=(cc == NC - 1))
            t_mm2 = kb.inc(i, "pe")
            kb.wait(nc.vector, "pe", t_mm2)
            t_v2 = kb.inc(nc.vector.tensor_scalar(s1, pb[0].ap[0:1, :], 1.0 / D, RMS_EPS,
                                                  op0=ALU.mult, op1=ALU.add), "dve")
            pb[0].consumed("dve", t_v2)
            kb.wait(nc.scalar, "dve", t_v2)
            t_sd2 = kb.inc(nc.scalar.activation(s2, s1, AF.Sqrt), "act")
            kb.wait(nc.vector, "act", t_sd2)
            t_r2 = kb.inc(nc.vector.reciprocal(s1, s2), "dve")
            pb[2].wait_reuse(kb, nc.tensor)
            kb.wait(nc.tensor, "dve", t_r2)
            t_bc2 = kb.inc(nc.tensor.matmul(pb[2].ap[:, :], C["ones_row_f32"], s1,
                                            start=True, stop=True), "pe")
            kb.wait(nc.vector, "pe", t_bc2)
            nc.vector.tensor_copy(xb[:, 0:TTW], pb[2].ap[:, :])
            for cc in range(NC):
                i = nc.vector.tensor_tensor(xrms[cc][:, ts:ts + TTW],
                                            xln[cc][:, ts:ts + TTW], xb[:, 0:TTW],
                                            op=ALU.mult)
            t_ap = kb.inc(i, "dve")
            pb[2].consumed("dve", t_ap)

        t_pe_prev = kb.cnt["pe"]
        if xtm_dram is not None:
            xtm_sb = w["xtm_sb"]
            for j in range(4):
                js = ts + j * 128
                kb.wait(nc.vector, "ldq", t_store)
                t_done = 0
                for cc in range(NC):
                    slot = pbf[cc % 2]
                    slot.wait_reuse(kb, nc.tensor)
                    kb.wait(nc.tensor, "dve", t_ap)
                    t_t = kb.inc(nc.tensor.transpose(slot.ap[:, 0:128],
                                                     xln[cc][:, js:js + 128],
                                                     C["id_bf"]), "pe")
                    kb.wait(nc.vector, "pe", t_t)
                    t_done = kb.inc(nc.vector.tensor_copy(
                        xtm_sb[:, cc * 128:(cc + 1) * 128], slot.ap[:, 0:128]), "dve")
                    slot.consumed("dve", t_done)
                kb.wait(nc.sync, "dve", t_done)
                t_store = kb.dinc(nc.sync.dma_start(out=xtm_dram[js:js + 128, :],
                                                    in_=xtm_sb[:]), "ldq")
        t_pe_prev = kb.cnt["pe"]


def ln_work_tiles(kb):
    scratch = kb.sb("ln_scr", [128, 3 * TTW], F32)
    return {
        "xf": kb.sb("ln_xf", [128, NC * TTW], F32),
        "xb": kb.sb("ln_xb", [128, NC * TTW], BF16),
        "sq": kb.sb("ln_sq", [128, NC * TTW], BF16),
        "stat": scratch[0:1, :],
        "scr": scratch,
    }


DEBUG_L1FFN = False


def build_l1ffn():
    kb = KB()
    nc = kb.nc
    C = host_consts(kb)
    x_in = kb.din("x_in", [D, T], F32)
    wiT = kb.din("wiT", [D, DFF], F32)
    woT = kb.din("woT", [DFF, D], F32)
    AsT = kb.din("AsT", [NE * D, RANK], F32)
    BsT = kb.din("BsT", [NE * RANK, DFF], F32)
    tki = kb.din("tki", [1, TOPK], I32)
    tkv = kb.din("tkv", [1, TOPK], F32)
    x_out = kb.dout("x_out", [D, T], F32)
    if DEBUG_L1FFN:
        dbg_xln = kb.dout("dbg_xln", [128, T], BF16)
        dbg_asel = kb.dout("dbg_asel", [128, NC * RANK], BF16)
        dbg_bsel = kb.dout("dbg_bsel", [16, DFF], BF16)
        dbg_t1 = kb.dout("dbg_t1", [16, TTW], BF16)
        dbg_int = kb.dout("dbg_int", [128, TTW], BF16)
        dbg_h0 = kb.dout("dbg_h0", [128, TTW], BF16)
        dbg_idxf = kb.dout("dbg_idxf", [128, 1], F32)
        dbg_idxi = kb.dout("dbg_idxi", [128, 1], I32)

    a_xln = kb.sb("a_xln", [128, NC * T], BF16)
    xln = [a_xln[:, c * T:(c + 1) * T] for c in range(NC)]
    a_int = kb.sb("a_int", [128, 32 * TTW], BF16)
    a_h0 = kb.sb("a_h0", [128, 32 * TTW], BF16)
    w = ln_work_tiles(kb)
    wi_sb = [kb.sb(f"wi{i}", [128, NC * 128], BF16) for i in range(2)]
    wo_sb = [kb.sb(f"wo{i}", [128, 32 * 128], BF16) for i in range(2)]
    a_f32 = kb.sb("a_f32", [128, RANK], F32)
    b_f32 = w["xf"][0:16, :]
    a_sel = [kb.sb(f"a_sel{k}", [128, NC * RANK], BF16) for k in range(TOPK)]
    b_sel = [kb.sb(f"b_sel{k}", [16, DFF], BF16) for k in range(TOPK)]
    t1_sb = [kb.sb(f"t1_{k}", [16, TTW], BF16) for k in range(TOPK)]
    tki_i = kb.sb("tki_i", [1, TOPK], I32)
    tki_f = kb.sb("tki_f", [1, TOPK], F32)
    tkv_f = kb.sb("tkv_f", [1, TOPK], F32)
    tkv_col = [kb.sb(f"tkv_col{k}", [128, 1], F32) for k in range(TOPK)]
    idx_f = kb.sb("idx_f", [128, 1], F32)
    idx_i = kb.sb("idx_i", [128, 1], I32)
    x_sb = [w["xf"][:, 0:TTW], w["xf"][:, TTW:2 * TTW]]

    pb = [Bank(kb.ps(f"pb{i}", [128, TTW])) for i in range(6)]
    pmisc = kb.ps("pmisc", [128, 128])

    kb.start()
    load_consts(kb)

    t_ld = kb.dinc(nc.sync.dma_start(out=tki_i[:], in_=tki[:, :]), "ldq")
    t_ld2 = kb.dinc(nc.sync.dma_start(out=tkv_f[:], in_=tkv[:, :]), "ldq")
    kb.wait(nc.vector, "ldq", t_ld2)
    t_cf = kb.inc(nc.vector.tensor_copy(tki_f[:], tki_i[:]), "dve")
    t_half = kb.inc(nc.vector.tensor_scalar(tkv_f[:], tkv_f[:], 0.5, None, op0=ALU.mult), "dve")
    for k in range(TOPK):
        kb.wait(nc.tensor, "dve")
        t_bc = kb.inc(nc.tensor.matmul(pmisc[:, 0:1], C["ones_row_f32"], tki_f[0:1, k:k + 1],
                                       start=True, stop=True), "pe")
        t_bv = kb.inc(nc.tensor.matmul(pmisc[:, 1:2], C["ones_row_f32"], tkv_f[0:1, k:k + 1],
                                       start=True, stop=True), "pe")
        kb.wait(nc.vector, "pe", t_bv)
        kb.inc(nc.vector.tensor_copy(tkv_col[k][:, :], pmisc[:, 1:2]), "dve")
        for cc in range(NC):
            t_a = kb.inc(nc.vector.tensor_scalar(idx_f[:], pmisc[:, 0:1], float(D), float(128 * cc),
                                                 op0=ALU.mult, op1=ALU.add), "dve")
            kb.wait(nc.vector, "dve", t_a)
            t_b = kb.inc(nc.vector.tensor_tensor(idx_f[:], idx_f[:], C_iota(kb, C), op=ALU.add), "dve")
            kb.wait(nc.vector, "dve", t_b)
            t_ix = kb.inc(nc.vector.tensor_copy(idx_i[:], idx_f[:]), "dve")
            kb.wait(nc.gpsimd, "dve", t_ix)
            t_g = kb.dinc(nc.gpsimd.indirect_dma_start(
                out=a_f32[:, :], out_offset=None, in_=AsT[:, :],
                in_offset=IndirectOffsetOnAxis(ap=idx_i[:, :], axis=0)), "gq")
            kb.wait(nc.vector, "gq", t_g)
            kb.inc(nc.vector.tensor_copy(
                a_sel[k][:, cc * RANK:(cc + 1) * RANK], a_f32[:, :]), "dve")
        t_a = kb.inc(nc.vector.tensor_scalar(idx_f[:], pmisc[:, 0:1], float(RANK), None, op0=ALU.mult), "dve")
        kb.wait(nc.vector, "dve", t_a)
        t_b = kb.inc(nc.vector.tensor_tensor(idx_f[:], idx_f[:], C_iota(kb, C), op=ALU.add), "dve")
        kb.wait(nc.vector, "dve", t_b)
        t_ix = kb.inc(nc.vector.tensor_copy(idx_i[:], idx_f[:]), "dve")
        kb.wait(nc.gpsimd, "dve", t_ix)
        t_g = kb.dinc(nc.gpsimd.indirect_dma_start(
            out=b_f32, out_offset=None, in_=BsT[:, :],
            in_offset=IndirectOffsetOnAxis(ap=idx_i[0:16, :], axis=0)), "gq")
        kb.wait(nc.vector, "gq", t_g)
        kb.inc(nc.vector.tensor_copy(b_sel[k][:, :], b_f32), "dve")
        if DEBUG_L1FFN and k == 1:
            kb.wait(nc.sync, "dve", t_ix)
            kb.dinc(nc.sync.dma_start(out=dbg_idxf[:, :], in_=idx_f[:]), "stq")
            kb.dinc(nc.sync.dma_start(out=dbg_idxi[:, :], in_=idx_i[:]), "stq")
    t_prep = kb.cnt["dve"]

    kb.wait(nc.sync, "dve", t_prep)
    emit_ln(kb, C, x_in, w, xln, pb)
    t_ln = kb.cnt["dve"]
    if DEBUG_L1FFN:
        kb.wait(nc.sync, "dve", max(t_prep, t_ln))
        kb.dinc(nc.sync.dma_start(out=dbg_xln[:, :], in_=a_xln[:, 0:T]), "stq")
        kb.dinc(nc.sync.dma_start(out=dbg_asel[:, :], in_=a_sel[0][:]), "stq")
        kb.dinc(nc.sync.dma_start(out=dbg_bsel[:, :], in_=b_sel[0][:]), "stq")

    wi_src = wiT.rearrange("(c p) h -> p c h", p=128)
    wo_src = woT.rearrange("(h p) c -> p h c", p=128)
    wi_tok = [0, 0]
    wo_tok = [0, 0]
    x_tok = [t_ln, t_ln]
    for tt in range(NTT):
        ts = tt * TTW
        kb.wait(nc.tensor, "dve", max(t_prep, t_ln))
        for k in range(TOPK):
            pb[4].wait_reuse(kb, nc.tensor)
            for cc in range(NC):
                i = nc.tensor.matmul(pb[4].ap[0:16, :],
                                     a_sel[k][:, cc * RANK:(cc + 1) * RANK],
                                     xln[cc][:, ts:ts + TTW],
                                     start=(cc == 0), stop=(cc == NC - 1))
            t_t1 = kb.inc(i, "pe")
            kb.wait(nc.vector, "pe", t_t1)
            t_ev = kb.inc(nc.vector.tensor_copy(t1_sb[k][:, :], pb[4].ap[0:16, :]), "dve")
            pb[4].consumed("dve", t_ev)
            if DEBUG_L1FFN and tt == 0 and k == 0:
                kb.wait(nc.sync, "dve", t_ev)
                kb.dinc(nc.sync.dma_start(out=dbg_t1[:, :], in_=t1_sb[k][:]), "stq")
        t_t1all = t_ev
        for hc in range(32):
            buf = hc % 2
            kb.wait(nc.gpsimd, "pe", wi_tok[buf])
            t_w = kb.dinc(nc.gpsimd.dma_start(
                out=wi_sb[buf][:].rearrange("p (c h) -> p c h", c=NC),
                in_=wi_src[:, :, hc * 128:(hc + 1) * 128]), "wq")
            bank = pb[4 + (hc % 2)]
            bank.wait_reuse(kb, nc.tensor)
            kb.wait(nc.tensor, "wq", t_w)
            for cc in range(NC):
                i = nc.tensor.matmul(bank.ap[:, :],
                                     wi_sb[buf][:, cc * 128:(cc + 1) * 128],
                                     xln[cc][:, ts:ts + TTW],
                                     start=(cc == 0), stop=(cc == NC - 1))
            wi_tok[buf] = t_mm = kb.inc(i, "pe")
            kb.wait(nc.scalar, "pe", t_mm)
            t_e = kb.inc(nc.scalar.activation(
                a_int[:, hc * TTW:(hc + 1) * TTW], bank.ap[:, :], AF.Copy), "act")
            bank.consumed("act", t_e)
            if DEBUG_L1FFN and tt == 0 and hc == 0:
                kb.wait(nc.sync, "act", t_e)
                kb.dinc(nc.sync.dma_start(out=dbg_int[:, :], in_=a_int[:, 0:TTW]), "stq")
        for k in range(TOPK):
            dst = a_h0 if k == 0 else a_int
            for hc in range(32):
                bank = pb[4 + (hc % 2)]
                bank.wait_reuse(kb, nc.tensor)
                kb.wait(nc.tensor, "dve", t_t1all)
                nc.tensor.matmul(bank.ap[:, :], b_sel[k][:, hc * 128:(hc + 1) * 128],
                                 t1_sb[k][:, :], start=True, stop=False)
                i = nc.tensor.matmul(bank.ap[:, :], C["id_bf"],
                                     a_int[:, hc * TTW:(hc + 1) * TTW],
                                     start=False, stop=True)
                t_mm = kb.inc(i, "pe")
                kb.wait(nc.scalar, "pe", t_mm)
                t_e = kb.inc(nc.scalar.activation(
                    dst[:, hc * TTW:(hc + 1) * TTW], bank.ap[:, :], AF.Relu), "act")
                bank.consumed("act", t_e)
                if DEBUG_L1FFN and tt == 0 and hc == 0 and k == 0:
                    kb.wait(nc.sync, "act", t_e)
                    kb.dinc(nc.sync.dma_start(out=dbg_h0[:, :], in_=a_h0[:, 0:TTW]), "stq")
        t_relu = kb.cnt["act"]
        for grp in ((0, 1, 2), (3, 4, 5), (6, 7)):
            for cc in grp:
                buf = cc % 2
                kb.wait(nc.gpsimd, "pe", wo_tok[buf])
                t_w = kb.dinc(nc.gpsimd.dma_start(
                    out=wo_sb[buf][:].rearrange("p (h c) -> p h c", h=32),
                    in_=wo_src[:, :, cc * 128:(cc + 1) * 128]), "wq")
                b0 = pb[2 * (cc - grp[0])]
                b1 = pb[2 * (cc - grp[0]) + 1]
                b0.wait_reuse(kb, nc.tensor)
                b1.wait_reuse(kb, nc.tensor)
                kb.wait(nc.tensor, "wq", t_w)
                kb.wait(nc.tensor, "act", t_relu)
                for hc in range(32):
                    nc.tensor.matmul(b0.ap[:, :], wo_sb[buf][:, hc * 128:(hc + 1) * 128],
                                     a_h0[:, hc * TTW:(hc + 1) * TTW],
                                     start=(hc == 0), stop=(hc == 31))
                for hc in range(32):
                    i = nc.tensor.matmul(b1.ap[:, :], wo_sb[buf][:, hc * 128:(hc + 1) * 128],
                                         a_int[:, hc * TTW:(hc + 1) * TTW],
                                         start=(hc == 0), stop=(hc == 31))
                wo_tok[buf] = t_mm = kb.inc(i, "pe")
                xbuf = cc % 2
                kb.wait(nc.sync, "dve", x_tok[xbuf])
                t_x = kb.dinc(nc.sync.dma_start(
                    out=x_sb[xbuf][:],
                    in_=x_in[cc * 128:(cc + 1) * 128, ts:ts + TTW]), "ldq")
                kb.wait(nc.vector, "ldq", t_x)
                kb.wait(nc.vector, "pe", t_mm)
                nc.vector.scalar_tensor_tensor(out=x_sb[xbuf][:], in0=b0.ap[:, :],
                                               scalar=tkv_col[0][:, :], in1=x_sb[xbuf][:],
                                               op0=ALU.mult, op1=ALU.add)
                i = nc.vector.scalar_tensor_tensor(out=x_sb[xbuf][:], in0=b1.ap[:, :],
                                                   scalar=tkv_col[1][:, :], in1=x_sb[xbuf][:],
                                                   op0=ALU.mult, op1=ALU.add)
                x_tok[xbuf] = t_u = kb.inc(i, "dve")
                b0.consumed("dve", t_u)
                b1.consumed("dve", t_u)
                kb.wait(nc.sync, "dve", t_u)
                kb.dinc(nc.sync.dma_start(
                    out=x_out[cc * 128:(cc + 1) * 128, ts:ts + TTW],
                    in_=x_sb[xbuf][:]), "stq")

    kb.wait(nc.sync, "stq")
    return kb.finish()


def C_iota(kb, C):
    return C["iota_col_f32"]




def _rope_tables():
    inv = 1.0 / (10000.0 ** (np.arange(0, HD, 2, dtype=np.float32) / HD))
    t = np.arange(T, dtype=np.float32)
    freqs = t[:, None] * inv[None, :]
    emb = np.concatenate([freqs, freqs], axis=-1)
    cos = np.cos(emb)
    sin = np.sin(emb)
    return np.tile(cos.T, (2, 1)), np.tile(sin.T, (2, 1))


def _p_rot():
    p = np.zeros((128, 128), np.float32)
    for b in (0, 64):
        for j in range(32):
            p[b + 2 * j + 1, b + 2 * j] = -1.0
            p[b + 2 * j, b + 2 * j + 1] = 1.0
    return p


def build_l0attn():
    kb = KB()
    nc = kb.nc
    cos_t, sin_t = _rope_tables()
    C = host_consts(kb, extra=(
        ("ones_row512_bf", np.ones((1, TTW), np.float32), BF16),
    ))
    cos_d = kb.cbf16("c_cos_fm", cos_t)
    sin_d = kb.cbf16("c_sin_fm", sin_t)
    p_rot_d = kb.cbf16("c_p_rot", _p_rot())
    p_rot = kb.sb("p_rot", [128, 128], BF16)
    kb._const_pairs.append((p_rot, p_rot_d))
    x_in = kb.din("x_in", [D, T], F32)
    qwT = kb.din("qwT", [D, D], F32)
    kwT = kb.din("kwT", [D, D], F32)
    vwT = kb.din("vwT", [D, D], F32)
    owT = kb.din("owT", [D, D], F32)
    qb_d = kb.din("qb", [128, NC], F32)
    kbias_d = kb.din("kbias", [128, NC], F32)
    obr_d = kb.din("ob_row", [1, D], F32)
    vbr_d = kb.din("vb_row", [1, D], F32)
    qn_d = kb.din("qn", [128, NC], F32)
    kn_d = kb.din("kn", [128, NC], F32)
    x_out = kb.dout("x_out", [D, T], F32)
    q_dram = kb.dtmp("q_dram", [D, T], BF16)
    k_dram = kb.dtmp("k_dram", [D, T], BF16)

    a_xln = kb.sb("a_xln", [128, NC * T], BF16)
    xln = [a_xln[:, c * T:(c + 1) * T] for c in range(NC)]
    VA_W = NH * 65
    v_attn = kb.sb("v_attn", [128, 32 * VA_W], BF16)
    w = ln_work_tiles(kb)
    warena = kb.sb("warena", [128, NC * D], BF16)
    wA = warena[:, 0:NC * D]
    qb_sb = kb.sb("qb_sb", [128, NC], F32)
    kb_sb = kb.sb("kb_sb", [128, NC], F32)
    qn_sb = kb.sb("qn_sb", [128, NC], F32)
    kn_sb = kb.sb("kn_sb", [128, NC], F32)
    obr_sb = kb.sb("obr_sb", [1, D], BF16)
    vbr_sb = kb.sb("vbr_sb", [1, D], BF16)
    qsb = kb.sb("qsb", [128, TTW], BF16)
    rtmp = kb.sb("rtmp", [128, TTW], BF16)
    rtmp2 = kb.sb("rtmp2", [128, TTW], BF16)
    cs_sb = kb.sb("cs_sb", [128, 2 * TTW], BF16)
    k_sb = kb.sb("k_sb", [128, 2048], BF16)
    qq = kb.sb("qq", [128, TTW], BF16)
    w_ring = kb.sb("w_ring", [128, 4 * TTW], BF16)
    pb = [Bank(kb.ps(f"pb{i}", [128, TTW])) for i in range(6)]

    kb.start()
    load_consts(kb)
    out65 = w["scr"][0:65, 0:TTW]
    zr = w["scr"][0:1, TTW:2 * TTW]
    zr2 = w["scr"][0:1, 2 * TTW:3 * TTW]

    t_small = 0
    for dst, src in ((qb_sb, qb_d), (kb_sb, kbias_d), (qn_sb, qn_d), (kn_sb, kn_d)):
        t_small = kb.dinc(nc.sync.dma_start(out=dst[:], in_=src), "ldq")
    t_smallg = kb.dinc(nc.gpsimd.dma_start(out=obr_sb[:], in_=obr_d), "wq")
    t_smallg = kb.dinc(nc.gpsimd.dma_start(out=vbr_sb[:], in_=vbr_d), "wq")
    va = v_attn[:].rearrange("p (c h e) -> p c h e", c=32, h=NH)
    t_ones = kb.inc(nc.vector.memset(va[:, :, :, 64:65], 1.0), "dve")
    t_wv = kb.dinc(nc.gpsimd.dma_start(
        out=wA.rearrange("p (c j) -> p c j", c=NC),
        in_=vwT.rearrange("(c p) j -> p c j", p=128)), "wq")

    emit_ln(kb, C, x_in, w, xln, pb)
    t_ln = kb.cnt["dve"]

    for tc in range(32):
        tcs = tc * 128
        for half in range(2):
            bank = pb[half]
            bank.wait_reuse(kb, nc.tensor)
            if tc == 0 and half == 0:
                kb.wait(nc.tensor, "wq", t_wv)
                kb.wait(nc.tensor, "dve", max(t_ln, t_ones))
            for cc in range(NC):
                nc.tensor.matmul(bank.ap[:, :],
                                 xln[cc][:, tcs:tcs + 128],
                                 wA[:, cc * D + half * TTW:cc * D + half * TTW + TTW],
                                 start=(cc == 0), stop=False)
            i = nc.tensor.matmul(bank.ap[:, :], C["ones_row_bf"][0:1, 0:128],
                                 vbr_sb[0:1, half * TTW:half * TTW + TTW],
                                 start=False, stop=True)
            t_mm = kb.inc(i, "pe")
            kb.wait(nc.vector, "pe", t_mm)
            dstv = va[:, tc, half * NC:half * NC + NC, 0:64]
            i = nc.vector.tensor_copy(
                dstv, bank.ap[:, :].rearrange("p (h e) -> p h e", h=NC))
            t_ev = kb.inc(i, "dve")
            bank.consumed("dve", t_ev)
    t_vdone = kb.cnt["dve"]

    xf, xb, sq, stat = w["xf"], w["xb"], w["sq"], w["stat"]
    s1 = stat[:, TTW:2 * TTW]
    s2 = stat[:, 2 * TTW:3 * TTW]
    for tt in range(NTT):
        ts = tt * TTW
        last = None
        for cc in range(NC):
            last = nc.scalar.activation(sq[:, cc * TTW:(cc + 1) * TTW],
                                        xln[cc][:, ts:ts + TTW], AF.Square)
        t_sq = kb.inc(last, "act")
        pb[0].wait_reuse(kb, nc.tensor)
        kb.wait(nc.tensor, "act", t_sq)
        for cc in range(NC):
            i = nc.tensor.matmul(pb[0].ap[0:1, :], C["ones_col_bf"],
                                 sq[:, cc * TTW:(cc + 1) * TTW],
                                 start=(cc == 0), stop=(cc == NC - 1))
        t_mm = kb.inc(i, "pe")
        kb.wait(nc.vector, "pe", t_mm)
        t_v = kb.inc(nc.vector.tensor_scalar(s1, pb[0].ap[0:1, :], 1.0 / D, RMS_EPS,
                                             op0=ALU.mult, op1=ALU.add), "dve")
        pb[0].consumed("dve", t_v)
        kb.wait(nc.scalar, "dve", t_v)
        t_sd = kb.inc(nc.scalar.activation(s2, s1, AF.Sqrt), "act")
        kb.wait(nc.vector, "act", t_sd)
        t_rc = kb.inc(nc.vector.reciprocal(s1, s2), "dve")
        pb[2].wait_reuse(kb, nc.tensor)
        kb.wait(nc.tensor, "dve", t_rc)
        t_bc = kb.inc(nc.tensor.matmul(pb[2].ap[:, :], C["ones_row_f32"], s1,
                                       start=True, stop=True), "pe")
        kb.wait(nc.vector, "pe", t_bc)
        nc.vector.tensor_copy(xb[:, 0:TTW], pb[2].ap[:, :])
        for cc in range(NC):
            i = nc.vector.tensor_tensor(xln[cc][:, ts:ts + TTW],
                                        xln[cc][:, ts:ts + TTW], xb[:, 0:TTW],
                                        op=ALU.mult)
        t_rms = kb.inc(i, "dve")
        pb[2].consumed("dve", t_rms)

    wsrc = {"q": qwT, "k": kwT}
    wscale = {"q": qn_sb, "k": kn_sb}
    wbias = {"q": qb_sb, "k": kb_sb}
    wdram = {"q": q_dram, "k": k_dram}
    wtmp = [w_ring[:, 0:D], w_ring[:, D:2 * D]]
    t_st_prev = 0
    for name in ("q", "k"):
        src = wsrc[name].rearrange("(c p) j -> p c j", p=128)
        bias, dram = wbias[name], wdram[name]
        wtmp_tok = [0, 0]
        t_prior_pe = kb.cnt["pe"]
        t_wready = 0
        for cc in range(NC):
            buf = cc % 2
            kb.wait(nc.gpsimd, "dve", wtmp_tok[buf])
            t_w = kb.dinc(nc.gpsimd.dma_start(out=wtmp[buf][:, :], in_=src[:, cc, :]), "wq")
            kb.wait(nc.vector, "wq", t_w)
            if cc == 0:
                kb.wait(nc.vector, "pe", t_prior_pe)
            i = nc.vector.tensor_scalar(wA[:, cc * D:(cc + 1) * D],
                                        wtmp[buf][:, :], wscale[name][:, cc:cc + 1], None,
                                        op0=ALU.mult)
            wtmp_tok[buf] = t_wready = kb.inc(i, "dve")
        for tt in range(NTT):
            ts = tt * TTW
            kb.wait(nc.sync, "dve", t_st_prev if False else 0)
            t_cs = kb.dinc(nc.sync.dma_start(out=cs_sb[:, 0:TTW], in_=cos_d[:, ts:ts + TTW]), "ldq")
            t_cs = kb.dinc(nc.sync.dma_start(out=cs_sb[:, TTW:2 * TTW], in_=sin_d[:, ts:ts + TTW]), "ldq")
            for cc in range(NC):
                bank = pb[cc % 2]
                rbank = pb[2 + cc % 2]
                bank.wait_reuse(kb, nc.tensor)
                if tt == 0 and cc == 0:
                    kb.wait(nc.tensor, "dve", t_wready)
                for c2 in range(NC):
                    i = nc.tensor.matmul(bank.ap[:, :],
                                         wA[:, c2 * D + cc * 128:c2 * D + cc * 128 + 128],
                                         xln[c2][:, ts:ts + TTW],
                                         start=(c2 == 0), stop=(c2 == NC - 1))
                t_mm = kb.inc(i, "pe")
                kb.wait(nc.vector, "pe", t_mm)
                t_q = kb.inc(nc.vector.tensor_scalar(qsb[:], bank.ap[:, :],
                                                     bias[:, cc:cc + 1], None,
                                                     op0=ALU.add), "dve")
                bank.consumed("dve", t_q)
                rbank.wait_reuse(kb, nc.tensor)
                kb.wait(nc.tensor, "dve", t_q)
                t_rot = kb.inc(nc.tensor.matmul(rbank.ap[:, :], p_rot[:], qsb[:],
                                                start=True, stop=True), "pe")
                kb.wait(nc.vector, "pe", t_rot)
                kb.wait(nc.vector, "ldq", t_cs)
                kb.wait(nc.vector, "stq", t_st_prev)
                nc.vector.tensor_copy(rtmp[:], rbank.ap[:, :])
                nc.vector.tensor_tensor(rtmp2[:], qsb[:], cs_sb[:, 0:TTW], op=ALU.mult)
                nc.vector.tensor_tensor(rtmp[:], rtmp[:], cs_sb[:, TTW:2 * TTW], op=ALU.mult)
                t_r = kb.inc(nc.vector.tensor_tensor(rtmp2[:], rtmp2[:], rtmp[:],
                                                     op=ALU.add), "dve")
                rbank.consumed("dve", t_r)
                kb.wait(nc.sync, "dve", t_r)
                t_st_prev = kb.dinc(nc.sync.dma_start(
                    out=dram[cc * 128:(cc + 1) * 128, ts:ts + TTW], in_=rtmp2[:]), "stq")
    t_qk = kb.cnt["stq"]

    wslot_tok = [0, 0, 0, 0]
    t_arena_ok = kb.cnt["pe"]
    t_av = 0
    for h in range(NH):
        kb.wait(nc.sync, "stq", t_qk)
        kb.wait(nc.sync, "pe", t_av)
        t_k = kb.dinc(nc.sync.dma_start(out=k_sb[0:64, :],
                                        in_=k_dram[h * 64:h * 64 + 64, 0:2048]), "ldq")
        t_k = kb.dinc(nc.sync.dma_start(out=k_sb[64:128, :],
                                        in_=k_dram[h * 64:h * 64 + 64, 2048:4096]), "ldq")
        for qt in range(NTT):
            qs = qt * TTW
            kb.wait(nc.sync, "pe", t_av)
            t_qq = kb.dinc(nc.sync.dma_start(out=qq[0:64, :],
                                             in_=q_dram[h * 64:h * 64 + 64, qs:qs + TTW]), "ldq")
            t_qq = kb.dinc(nc.sync.dma_start(out=qq[64:128, :],
                                             in_=q_dram[h * 64:h * 64 + 64, qs:qs + TTW]), "ldq")
            kb.wait(nc.tensor, "ldq", t_qq)
            for pc in range(16):
                c0, c1 = pc, pc + 16
                sA = pb[pc % 2]
                sB = pb[2 + pc % 2]
                sA.wait_reuse(kb, nc.tensor)
                sB.wait_reuse(kb, nc.tensor)
                if pc == 0:
                    pb[4].wait_reuse(kb, nc.tensor)
                    pb[5].wait_reuse(kb, nc.tensor)
                nc.tensor.matmul(sA.ap[:, :], k_sb[0:64, c0 * 128:c0 * 128 + 128],
                                 qq[0:64, :], start=True, stop=True, tile_position=(0, 0))
                i = nc.tensor.matmul(sB.ap[:, :], k_sb[64:128, c0 * 128:c0 * 128 + 128],
                                     qq[64:128, :], start=True, stop=True, tile_position=(64, 0))
                t_sc = kb.inc(i, "pe")
                s0, s1_ = (2 * pc) % 4, (2 * pc + 1) % 4
                kb.wait(nc.scalar, "pe", max(t_sc, wslot_tok[s0], wslot_tok[s1_]))
                nc.scalar.activation(w_ring[:, s0 * TTW:(s0 + 1) * TTW], sA.ap[:, :],
                                     AF.Sigmoid, scale=0.125)
                t_sg = kb.inc(nc.scalar.activation(w_ring[:, s1_ * TTW:(s1_ + 1) * TTW],
                                                   sB.ap[:, :], AF.Sigmoid, scale=0.125), "act")
                sA.consumed("act", t_sg)
                sB.consumed("act", t_sg)
                kb.wait(nc.tensor, "act", t_sg)
                for ci, sl in ((c0, s0), (c1, s1_)):
                    vt = v_attn[:, ci * VA_W + h * 65:ci * VA_W + h * 65 + 65]
                    nc.tensor.matmul(pb[4].ap[0:65, :], vt[0:64, :],
                                     w_ring[0:64, sl * TTW:(sl + 1) * TTW],
                                     start=(pc == 0 and ci == c0), stop=(pc == 15 and ci == c1),
                                     tile_position=(0, 0))
                    i = nc.tensor.matmul(pb[5].ap[0:65, :], vt[64:128, :],
                                         w_ring[64:128, sl * TTW:(sl + 1) * TTW],
                                         start=(pc == 0 and ci == c0), stop=(pc == 15 and ci == c1),
                                         tile_position=(64, 0))
                t_av = kb.inc(i, "pe")
                wslot_tok[s0] = wslot_tok[s1_] = t_av
            kb.wait(nc.vector, "pe", t_av)
            nc.vector.tensor_copy(out65, pb[4].ap[0:65, :])
            t_o = kb.inc(nc.vector.tensor_tensor(out65, out65, pb[5].ap[0:65, :],
                                                 op=ALU.add), "dve")
            pb[4].consumed("dve", t_o)
            pb[5].consumed("dve", t_o)
            kb.wait(nc.vector, "dve", t_o)
            t_z = kb.inc(nc.vector.tensor_scalar(zr, out65[64:65, :], SIG_EPS, None,
                                                 op0=ALU.add), "dve")
            kb.wait(nc.vector, "dve", t_z)
            t_rc = kb.inc(nc.vector.reciprocal(zr2, zr), "dve")
            pb[0].wait_reuse(kb, nc.tensor)
            kb.wait(nc.tensor, "dve", t_rc)
            t_bc = kb.inc(nc.tensor.matmul(pb[0].ap[:, :], C["ones_row_f32"], zr2,
                                           start=True, stop=True), "pe")
            kb.wait(nc.vector, "pe", max(t_bc, t_arena_ok))
            dst = a_xln[(h % 2) * 64:(h % 2) * 64 + 64, (h // 2) * T + qs:(h // 2) * T + qs + TTW]
            t_at = kb.inc(nc.vector.tensor_tensor(dst, out65[0:64, 0:TTW], pb[0].ap[0:64, :],
                                                  op=ALU.mult), "dve")
            pb[0].consumed("dve", t_at)
    t_attn = kb.cnt["dve"]

    kb.wait(nc.gpsimd, "pe", kb.cnt["pe"])
    t_wo = kb.dinc(nc.gpsimd.dma_start(
        out=wA.rearrange("p (c j) -> p c j", c=NC),
        in_=owT.rearrange("(c p) j -> p c j", p=128)), "wq")
    x_sb = [xf[:, 0:TTW], xf[:, TTW:2 * TTW]]
    x_tok = [0, 0]
    for cc in range(NC):
        for tt in range(NTT):
            ts = tt * TTW
            bank = pb[tt % 2]
            bank.wait_reuse(kb, nc.tensor)
            if cc == 0 and tt == 0:
                kb.wait(nc.tensor, "wq", t_wo)
                kb.wait(nc.tensor, "dve", t_attn)
                kb.wait(nc.tensor, "wq", t_smallg)
            for c2 in range(NC):
                nc.tensor.matmul(bank.ap[:, :],
                                 wA[:, c2 * D + cc * 128:c2 * D + cc * 128 + 128],
                                 a_xln[:, c2 * T + ts:c2 * T + ts + TTW],
                                 start=(c2 == 0), stop=False)
            i = nc.tensor.matmul(bank.ap[:, :], obr_sb[0:1, cc * 128:cc * 128 + 128],
                                 C["ones_row512_bf"], start=False, stop=True)
            t_mm = kb.inc(i, "pe")
            xb_i = tt % 2
            kb.wait(nc.sync, "dve", x_tok[xb_i])
            t_x = kb.dinc(nc.sync.dma_start(
                out=x_sb[xb_i], in_=x_in[cc * 128:(cc + 1) * 128, ts:ts + TTW]), "ldq")
            kb.wait(nc.vector, "ldq", t_x)
            kb.wait(nc.vector, "pe", t_mm)
            i = nc.vector.tensor_tensor(x_sb[xb_i], x_sb[xb_i], bank.ap[:, :], op=ALU.add)
            x_tok[xb_i] = t_u = kb.inc(i, "dve")
            bank.consumed("dve", t_u)
            kb.wait(nc.sync, "dve", t_u)
            kb.dinc(nc.sync.dma_start(
                out=x_out[cc * 128:(cc + 1) * 128, ts:ts + TTW], in_=x_sb[xb_i]), "stq")

    kb.wait(nc.sync, "stq")
    return kb.finish()


def build_l1attn(phase_limit=None):
    kb = KB()
    nc = kb.nc
    C = host_consts(kb, extra=(
        ("ones_row512_bf", np.ones((1, TTW), np.float32), BF16),
    ))
    x_in = kb.din("x_in", [D, T], F32)
    Wq = kb.din("Wq", [D, D], F32)
    Wk = kb.din("Wk", [D, D], F32)
    Wv = kb.din("Wv", [D, D], F32)
    Wo = kb.din("Wo", [D, D], F32)
    qb_d = kb.din("qb", [128, NC], F32)
    kbias_d = kb.din("kbias", [128, NC], F32)
    obr_d = kb.din("ob_row", [1, D], F32)
    vbr_d = kb.din("vb_row", [1, D], F32)
    x_out = kb.dout("x_out", [D, T], F32)
    q_dram = kb.dtmp("q_dram", [D, T], BF16)
    k_dram = kb.dtmp("k_dram", [D, T], BF16)

    a_xln = kb.sb("a_xln", [128, NC * T], BF16)
    xln = [a_xln[:, c * T:(c + 1) * T] for c in range(NC)]
    VA_W = NH * 65
    v_attn = kb.sb("v_attn", [128, 32 * VA_W], BF16)
    w = ln_work_tiles(kb)
    warena = kb.sb("warena", [128, NC * D], BF16)
    wA = warena[:, 0:NC * D]
    qb_sb = kb.sb("qb_sb", [128, NC], F32)
    kb_sb = kb.sb("kb_sb", [128, NC], F32)
    obr_sb = kb.sb("obr_sb", [1, D], BF16)
    vbr_sb = kb.sb("vbr_sb", [1, D], BF16)
    qsb = kb.sb("qsb", [128, TTW], BF16)
    k_sb = kb.sb("k_sb", [64, 8 * 128], BF16)
    kcol0 = kb.sb("kcol0", [64, 1], BF16)
    qq = kb.sb("qq", [64, TTW], BF16)
    w_ring = kb.sb("w_ring", [128, 4 * TTW], BF16)
    w0_sb = kb.sb("w0_sb", [1, TTW], BF16)
    wq0_sb = kb.sb("wq0_sb", [128, 32], BF16)
    qmax = kb.sb("qmax", [1, NTT], F32)
    kmax = kb.sb("kmax", [1, 1], F32)
    mrow = kb.sb("mrow", [1, NTT], F32)
    mbias = kb.sb("mbias", [128, 1], F32)

    pb = [Bank(kb.ps(f"pb{i}", [128, TTW])) for i in range(6)]

    kb.start()
    load_consts(kb)
    out65 = w["scr"][0:65, 0:TTW]
    zr = w["scr"][0:1, TTW:2 * TTW]
    zr2 = w["scr"][0:1, 2 * TTW:3 * TTW]

    t_small = 0
    for dst, src in ((qb_sb, qb_d), (kb_sb, kbias_d)):
        t_small = kb.dinc(nc.sync.dma_start(out=dst[:], in_=src), "ldq")
    t_smallg = kb.dinc(nc.gpsimd.dma_start(out=obr_sb[:], in_=obr_d), "wq")
    t_smallg = kb.dinc(nc.gpsimd.dma_start(out=vbr_sb[:], in_=vbr_d), "wq")
    va = v_attn[:].rearrange("p (c h e) -> p c h e", c=32, h=NH)
    t_ones = kb.inc(nc.vector.memset(va[:, :, :, 64:65], 1.0), "dve")
    t_wv = kb.dinc(nc.gpsimd.dma_start(
        out=wA.rearrange("p (c j) -> p c j", c=NC),
        in_=Wv.rearrange("(c p) j -> p c j", p=128)), "wq")

    emit_ln(kb, C, x_in, w, xln, pb)
    t_ln = kb.cnt["dve"]
    if phase_limit == "ln":
        kb.wait(nc.sync, "ldq")
        return kb.finish()

    for tc in range(32):
        tcs = tc * 128
        for half in range(2):
            bank = pb[half]
            bank.wait_reuse(kb, nc.tensor)
            if tc == 0 and half == 0:
                kb.wait(nc.tensor, "wq", t_wv)
                kb.wait(nc.tensor, "dve", max(t_ln, t_ones))
            for cc in range(NC):
                nc.tensor.matmul(bank.ap[:, :],
                                 xln[cc][:, tcs:tcs + 128],
                                 wA[:, cc * D + half * TTW:cc * D + half * TTW + TTW],
                                 start=(cc == 0), stop=False)
            i = nc.tensor.matmul(bank.ap[:, :], C["ones_row_bf"][0:1, 0:128],
                                 vbr_sb[0:1, half * TTW:half * TTW + TTW],
                                 start=False, stop=True)
            t_mm = kb.inc(i, "pe")
            kb.wait(nc.vector, "pe", t_mm)
            dstv = va[:, tc, half * NC:half * NC + NC, 0:64]
            i = nc.vector.tensor_copy(
                dstv, bank.ap[:, :].rearrange("p (h e) -> p h e", h=NC))
            t_ev = kb.inc(i, "dve")
            bank.consumed("dve", t_ev)
    if phase_limit == "vproj":
        kb.wait(nc.vector, "dve")
        kb.wait(nc.sync, "ldq")
        return kb.finish()

    wsrc = {"q": Wq, "k": Wk}
    wbias = {"q": qb_sb, "k": kb_sb}
    wdram = {"q": q_dram, "k": k_dram}
    t_st_prev = 0
    for name in ("q", "k"):
        t_prior_pe = kb.cnt["pe"]
        kb.wait(nc.gpsimd, "pe", t_prior_pe)
        t_w = kb.dinc(nc.gpsimd.dma_start(
            out=wA.rearrange("p (c j) -> p c j", c=NC),
            in_=wsrc[name].rearrange("(c p) j -> p c j", p=128)), "wq")
        bias, dram = wbias[name], wdram[name]
        for tt in range(NTT):
            ts = tt * TTW
            for cc in range(NC):
                bank = pb[cc % 2]
                bank.wait_reuse(kb, nc.tensor)
                if tt == 0 and cc == 0:
                    kb.wait(nc.tensor, "wq", t_w)
                for c2 in range(NC):
                    i = nc.tensor.matmul(bank.ap[:, :],
                                         wA[:, c2 * D + cc * 128:c2 * D + cc * 128 + 128],
                                         xln[c2][:, ts:ts + TTW],
                                         start=(c2 == 0), stop=(c2 == NC - 1))
                t_mm = kb.inc(i, "pe")
                kb.wait(nc.vector, "pe", t_mm)
                kb.wait(nc.vector, "stq", t_st_prev)
                t_q = kb.inc(nc.vector.tensor_scalar(qsb[:], bank.ap[:, :],
                                                     bias[:, cc:cc + 1], None,
                                                     op0=ALU.add), "dve")
                bank.consumed("dve", t_q)
                kb.wait(nc.sync, "dve", t_q)
                t_st_prev = kb.dinc(nc.sync.dma_start(
                    out=dram[cc * 128:(cc + 1) * 128, ts:ts + TTW], in_=qsb[:]), "stq")
    t_qk = kb.cnt["stq"]
    if phase_limit == "qkproj":
        kb.wait(nc.sync, "stq")
        return kb.finish()

    wslot_tok = [0, 0, 0, 0]
    t_arena_ok = kb.cnt["pe"]
    t_av = 0
    t_kf_pe = 0
    kfull = warena[0:64, 0:2 * T]
    for h in range(NH):
        hs = h * 64
        kb.wait(nc.sync, "stq", t_qk)
        kb.wait(nc.sync, "pe", max(t_arena_ok, t_kf_pe))
        t_qh = kb.dinc(nc.sync.dma_start(out=kfull[:, 0:T], in_=q_dram[hs:hs + 64, :]), "ldq")
        if phase_limit == "cut1" and h == 0:
            kb.wait(nc.sync, "ldq")
            return kb.finish()
        kb.wait(nc.scalar, "ldq", t_qh)
        t_sqq = kb.inc(nc.scalar.activation(kfull[:, T:2 * T], kfull[:, 0:T], AF.Square), "act")
        if phase_limit == "cut2" and h == 0:
            kb.wait(nc.sync, "act")
            kb.wait(nc.sync, "ldq")
            return kb.finish()
        kb.wait(nc.tensor, "act", t_sqq)
        t_nq = 0
        nrm_row = w["scr"][0:1, 0:TTW]
        for qt in range(NTT):
            pb[0].wait_reuse(kb, nc.tensor)
            i = nc.tensor.matmul(pb[0].ap[0:1, :],
                                 C["ones_col_bf"][0:64, :],
                                 kfull[:, T + qt * TTW:T + (qt + 1) * TTW],
                                 start=True, stop=True)
            t_nq = kb.inc(i, "pe")
            kb.wait(nc.vector, "pe", t_nq)
            t_cp = kb.inc(nc.vector.tensor_copy(nrm_row, pb[0].ap[0:1, :]), "dve")
            pb[0].consumed("dve", t_cp)
            kb.wait(nc.vector, "dve", t_cp)
            t_rm = kb.inc(nc.vector.reduce_max(qmax[0:1, qt:qt + 1], nrm_row,
                                               axis=AX.X), "dve")
        if phase_limit == "cut3" and h == 0:
            kb.wait(nc.sync, "dve")
            kb.wait(nc.sync, "ldq")
            return kb.finish()
        kb.wait(nc.sync, "act", t_sqq)
        t_kh = kb.dinc(nc.sync.dma_start(out=kfull[:, 0:T], in_=k_dram[hs:hs + 64, :]), "ldq")
        kb.wait(nc.scalar, "ldq", t_kh)
        kb.wait(nc.scalar, "pe", t_nq)
        t_sqk = kb.inc(nc.scalar.activation(kfull[:, T:2 * T], kfull[:, 0:T], AF.Square), "act")
        kb.wait(nc.tensor, "act", t_sqk)
        for qt in range(NTT):
            pb[0].wait_reuse(kb, nc.tensor)
            i = nc.tensor.matmul(pb[0].ap[0:1, :], C["ones_col_bf"][0:64, :],
                                 kfull[:, T + qt * TTW:T + (qt + 1) * TTW],
                                 start=True, stop=True)
            t_nk = kb.inc(i, "pe")
            kb.wait(nc.vector, "pe", t_nk)
            t_cp = kb.inc(nc.vector.tensor_copy(nrm_row, pb[0].ap[0:1, :]), "dve")
            pb[0].consumed("dve", t_cp)
            kb.wait(nc.vector, "dve", t_cp)
            t_rm = kb.inc(nc.vector.reduce_max(mrow[0:1, qt:qt + 1], nrm_row,
                                               axis=AX.X), "dve")
        kb.wait(nc.vector, "dve", kb.cnt["dve"])
        t_km = kb.inc(nc.vector.reduce_max(kmax[0:1, :], mrow[0:1, :], axis=AX.X), "dve")
        kb.wait(nc.vector, "dve", t_km)
        t_m1 = kb.inc(nc.vector.tensor_scalar(mrow[0:1, 0:NTT], qmax[0:1, :], kmax[0:1, 0:1],
                                              None, op0=ALU.mult), "dve")
        kb.wait(nc.scalar, "dve", t_m1)
        nc.scalar.activation(mrow[0:1, 0:NTT], mrow[0:1, 0:NTT], AF.Ln)
        t_m2 = kb.inc(nc.scalar.activation(mrow[0:1, 0:NTT], mrow[0:1, 0:NTT], AF.Exp,
                                           scale=0.5), "act")
        kb.wait(nc.vector, "act", t_m2)
        t_m3 = kb.inc(nc.vector.tensor_scalar(mrow[0:1, 0:NTT], mrow[0:1, 0:NTT], -0.125,
                                              None, op0=ALU.mult), "dve")
        if phase_limit == "norms0" and h == 0:
            kb.wait(nc.sync, "stq")
            kb.wait(nc.sync, "dve")
            kb.wait(nc.sync, "pe")
            kb.wait(nc.sync, "act")
            return kb.finish()
        if phase_limit == "norms" and h == 1:
            kb.wait(nc.sync, "stq")
            kb.wait(nc.sync, "dve")
            kb.wait(nc.sync, "pe")
            return kb.finish()
        t_k0 = kb.dinc(nc.sync.dma_start(out=kcol0[:], in_=k_dram[hs:hs + 64, 0:1], allow_slow_non_contiguous=True), "ldq")
        for qt in range(NTT):
            qs = qt * TTW
            lo = max(0, 4 * qt - 2)
            hi = min(31, 4 * qt + 5)
            nblk = hi - lo + 1
            kb.wait(nc.sync, "pe", t_av)
            t_qq = kb.dinc(nc.sync.dma_start(out=qq[:, :],
                                             in_=q_dram[hs:hs + 64, qs:qs + TTW]), "ldq")
            t_kb = kb.dinc(nc.sync.dma_start(out=k_sb[:, 0:nblk * 128],
                                             in_=k_dram[hs:hs + 64, lo * 128:(hi + 1) * 128]), "ldq")
            pb[3].wait_reuse(kb, nc.tensor)
            kb.wait(nc.tensor, "dve", t_m3)
            t_mb = kb.inc(nc.tensor.matmul(pb[3].ap[:, 0:1], C["ones_row_f32"],
                                           mrow[0:1, qt:qt + 1], start=True, stop=True), "pe")
            kb.wait(nc.vector, "pe", t_mb)
            kb.wait(nc.vector, "act")
            t_mb2 = kb.inc(nc.vector.tensor_copy(mbias[:], pb[3].ap[:, 0:1]), "dve")
            pb[3].consumed("dve", t_mb2)
            kb.wait(nc.tensor, "ldq", t_kb)
            n_contrib = nblk + 1 + (30 if qt == 0 else 0)
            ci = 0
            pb[4].wait_reuse(kb, nc.tensor)
            for bi in range(nblk):
                kbk = lo + bi
                sA = pb[bi % 2]
                sA.wait_reuse(kb, nc.tensor)
                i = nc.tensor.matmul(sA.ap[:, :], k_sb[:, bi * 128:bi * 128 + 128],
                                     qq[:, :], start=True, stop=True)
                t_sc = kb.inc(i, "pe")
                sl = bi % 4
                kb.wait(nc.scalar, "pe", max(t_sc, wslot_tok[sl]))
                kb.wait(nc.scalar, "dve", t_mb2)
                t_sg = kb.inc(nc.scalar.activation(w_ring[:, sl * TTW:(sl + 1) * TTW],
                                                   sA.ap[:, :], AF.Exp, bias=mbias[:, :],
                                                   scale=0.125), "act")
                sA.consumed("act", t_sg)
                dmin = qs - (128 * kbk + 127)
                dmax = qs + 511 - 128 * kbk
                wslice = w_ring[:, sl * TTW:(sl + 1) * TTW]
                masks = []
                if dmax > WIN:
                    masks.append(([[-1, TTW]], WIN - qs + 128 * kbk, 1))
                if dmin < -WIN:
                    masks.append(([[1, TTW]], qs - 128 * kbk + WIN, -1))
                if kbk == 0:
                    masks.append(([[0, TTW]], -1, 1))
                t_pool = None
                if masks:
                    kb.wait(nc.gpsimd, "act", t_sg)
                    for pat, base, cm in masks:
                        if t_pool is not None:
                            kb.wait(nc.gpsimd, "pool", t_pool)
                        t_pool = kb.inc(nc.gpsimd.affine_select(
                            wslice, wslice, pat, ALU.is_ge, 0.0,
                            base=base, channel_multiplier=cm), "pool")
                if t_pool is not None:
                    kb.wait(nc.tensor, "pool", t_pool)
                vt = v_attn[:, kbk * VA_W + h * 65:kbk * VA_W + h * 65 + 65]
                i = nc.tensor.matmul(pb[4].ap[0:65, :], vt[:, :],
                                     w_ring[:, sl * TTW:(sl + 1) * TTW],
                                     start=(ci == 0), stop=(ci == n_contrib - 1))
                t_av = kb.inc(i, "pe")
                wslot_tok[sl] = t_av
                ci += 1
            sA = pb[2]
            sA.wait_reuse(kb, nc.tensor)
            kb.wait(nc.tensor, "ldq", t_k0)
            i = nc.tensor.matmul(sA.ap[0:1, :], kcol0[:, :], qq[:, :], start=True, stop=True)
            t_s0 = kb.inc(i, "pe")
            kb.wait(nc.scalar, "pe", max(t_s0, wslot_tok[0]))
            t_w0 = kb.inc(nc.scalar.activation(w0_sb[:], sA.ap[0:1, :], AF.Exp,
                                               bias=mbias[0:1, :], scale=0.125), "act")
            sA.consumed("act", t_w0)
            kb.wait(nc.tensor, "act", t_w0)
            vt = v_attn[:, 0 * VA_W + h * 65:0 * VA_W + h * 65 + 65]
            i = nc.tensor.matmul(pb[4].ap[0:65, :], vt[0:1, :], w0_sb[:, :],
                                 start=False, stop=(ci == n_contrib - 1))
            t_av = kb.inc(i, "pe")
            ci += 1
            if qt == 0:
                sB = pb[5]
                sB.wait_reuse(kb, nc.tensor)
                for kbk in range(2, 32):
                    i = nc.tensor.matmul(sB.ap[:, kbk:kbk + 1],
                                         kfull[:, kbk * 128:kbk * 128 + 128],
                                         qq[:, 0:1], start=True, stop=True)
                t_q0 = kb.inc(i, "pe")
                kb.wait(nc.scalar, "pe", t_q0)
                t_wq0 = kb.inc(nc.scalar.activation(wq0_sb[:, 2:32], sB.ap[:, 2:32], AF.Exp,
                                                    bias=mbias[:, :], scale=0.125), "act")
                sB.consumed("act", t_wq0)
                kb.wait(nc.vector, "act", t_wq0)
                t_wz = kb.inc(nc.vector.memset(wq0_sb[0:1, 2:3], 0.0), "dve")
                kb.wait(nc.tensor, "dve", t_wz)
                for kbk in range(2, 32):
                    vt = v_attn[:, kbk * VA_W + h * 65:kbk * VA_W + h * 65 + 65]
                    i = nc.tensor.matmul(pb[4].ap[0:65, 0:1], vt[:, :],
                                         wq0_sb[:, kbk:kbk + 1],
                                         start=False, stop=(ci == n_contrib - 1))
                    t_av = kb.inc(i, "pe")
                    ci += 1
                t_kf_pe = t_av
            assert ci == n_contrib
            kb.wait(nc.vector, "pe", t_av)
            t_o = kb.inc(nc.vector.tensor_copy(out65, pb[4].ap[0:65, :]), "dve")
            pb[4].consumed("dve", t_o)
            kb.wait(nc.vector, "dve", t_o)
            t_rc = kb.inc(nc.vector.reciprocal(zr2, out65[64:65, 0:TTW]), "dve")
            pb[0].wait_reuse(kb, nc.tensor)
            kb.wait(nc.tensor, "dve", t_rc)
            t_bc = kb.inc(nc.tensor.matmul(pb[0].ap[:, :], C["ones_row_f32"], zr2,
                                           start=True, stop=True), "pe")
            kb.wait(nc.vector, "pe", max(t_bc, t_arena_ok))
            dst = a_xln[(h % 2) * 64:(h % 2) * 64 + 64, (h // 2) * T + qs:(h // 2) * T + qs + TTW]
            t_at = kb.inc(nc.vector.tensor_tensor(dst, out65[0:64, 0:TTW], pb[0].ap[0:64, :],
                                                  op=ALU.mult), "dve")
            pb[0].consumed("dve", t_at)
    t_attn = kb.cnt["dve"]
    if phase_limit == "attn":
        kb.wait(nc.sync, "dve")
        kb.wait(nc.sync, "stq")
        return kb.finish()

    kb.wait(nc.gpsimd, "pe", kb.cnt["pe"])
    t_wo = kb.dinc(nc.gpsimd.dma_start(
        out=wA.rearrange("p (c j) -> p c j", c=NC),
        in_=Wo.rearrange("(c p) j -> p c j", p=128)), "wq")
    xf = w["xf"]
    x_sb = [xf[:, 0:TTW], xf[:, TTW:2 * TTW]]
    x_tok = [0, 0]
    for cc in range(NC):
        for tt in range(NTT):
            ts = tt * TTW
            bank = pb[tt % 2]
            bank.wait_reuse(kb, nc.tensor)
            if cc == 0 and tt == 0:
                kb.wait(nc.tensor, "wq", t_wo)
                kb.wait(nc.tensor, "dve", t_attn)
                kb.wait(nc.tensor, "wq", t_smallg)
            for c2 in range(NC):
                nc.tensor.matmul(bank.ap[:, :],
                                 wA[:, c2 * D + cc * 128:c2 * D + cc * 128 + 128],
                                 a_xln[:, c2 * T + ts:c2 * T + ts + TTW],
                                 start=(c2 == 0), stop=False)
            i = nc.tensor.matmul(bank.ap[:, :], obr_sb[0:1, cc * 128:cc * 128 + 128],
                                 C["ones_row512_bf"], start=False, stop=True)
            t_mm = kb.inc(i, "pe")
            xb_i = tt % 2
            kb.wait(nc.sync, "dve", x_tok[xb_i])
            t_x = kb.dinc(nc.sync.dma_start(
                out=x_sb[xb_i], in_=x_in[cc * 128:(cc + 1) * 128, ts:ts + TTW]), "ldq")
            kb.wait(nc.vector, "ldq", t_x)
            kb.wait(nc.vector, "pe", t_mm)
            i = nc.vector.tensor_tensor(x_sb[xb_i], x_sb[xb_i], bank.ap[:, :], op=ALU.add)
            x_tok[xb_i] = t_u = kb.inc(i, "dve")
            bank.consumed("dve", t_u)
            kb.wait(nc.sync, "dve", t_u)
            kb.dinc(nc.sync.dma_start(
                out=x_out[cc * 128:(cc + 1) * 128, ts:ts + TTW], in_=x_sb[xb_i]), "stq")

    kb.wait(nc.sync, "stq")
    return kb.finish()


def l1attn_weights(inputs):
    return {
        "Wq": np.ascontiguousarray(inputs["L1_attn_Wq"]),
        "Wk": np.ascontiguousarray(inputs["L1_attn_Wk"]),
        "Wv": np.ascontiguousarray(inputs["L1_attn_Wv"]),
        "Wo": np.ascontiguousarray(inputs["L1_attn_Wo"]),
        "qb": np.ascontiguousarray(inputs["L1_attn_bq"].reshape(NC, 128).T),
        "kbias": np.ascontiguousarray(inputs["L1_attn_bk"].reshape(NC, 128).T),
        "ob_row": np.ascontiguousarray(inputs["L1_attn_bo"].reshape(1, D)),
        "vb_row": np.ascontiguousarray(inputs["L1_attn_bv"].reshape(1, D)),
    }


def l0attn_weights(inputs):
    return {
        "qwT": np.ascontiguousarray(inputs["L0_attn_q_w"].T),
        "kwT": np.ascontiguousarray(inputs["L0_attn_k_w"].T),
        "vwT": np.ascontiguousarray(inputs["L0_attn_v_w"].T),
        "owT": np.ascontiguousarray(inputs["L0_attn_o_w"].T),
        "qb": np.ascontiguousarray(inputs["L0_attn_q_b"].reshape(NC, 128).T),
        "kbias": np.ascontiguousarray(inputs["L0_attn_k_b"].reshape(NC, 128).T),
        "ob_row": np.ascontiguousarray(inputs["L0_attn_o_b"].reshape(1, D)),
        "vb_row": np.ascontiguousarray(inputs["L0_attn_v_b"].reshape(1, D)),
        "qn": np.ascontiguousarray(inputs["L0_attn_qn_scale"].reshape(NC, 128).T),
        "kn": np.ascontiguousarray(inputs["L0_attn_kn_scale"].reshape(NC, 128).T),
    }


def _l0ffn_consts():
    iota512 = np.broadcast_to(np.arange(TTW, dtype=np.float32), (128, TTW))
    idxw = np.zeros((128, 32 * 3), np.float32)
    for tb in range(32):
        idxw[:, 3 * tb] = 128.0 * tb
        idxw[:, 3 * tb + 1] = np.arange(128)
        idxw[:, 3 * tb + 2] = 1.0
    grp16 = np.zeros((128, NE), np.float32)
    for p in range(128):
        grp16[p, p // 16] = 1.0
    sel8 = np.zeros((NE, 128), np.float32)
    for p in range(128):
        sel8[p // 16, p] = 1.0
    oh8 = np.zeros((NE, NE * 128), np.float32)
    for e in range(NE):
        oh8[e, e * 128:(e + 1) * 128] = 1.0
    mrg3 = np.array([[1.0], [1.0], [-4096.0]], np.float32)
    return iota512, idxw, grp16, sel8, oh8, mrg3


DEBUG_L0FFN = False


def build_l0ffn():
    kb = KB()
    nc = kb.nc
    iota512, idxw, grp16, sel8, oh8, mrg3 = _l0ffn_consts()
    C = host_consts(kb, extra=(
        ("iota512", iota512, F32),
        ("idxw", idxw, BF16),
        ("grp16", grp16, F32),
        ("sel8", sel8, F32),
        ("oh8", oh8, F32),
        ("mrg3", mrg3, F32),
    ))
    x_in = kb.din("x_in", [D, T], F32)
    gate_w = kb.din("gate_w", [D, NE], F32)
    w1_d = kb.din("w1", [NE, D, DFF], F32)
    w2_d = kb.din("w2", [NE, DFF, D], F32)
    x_out = kb.dout("x_out", [D, T], F32)
    xtm_dram = kb.dtmp("xtm_dram", [T, D], BF16)
    pdram = kb.dtmp("pdram", [NE, T], F32)
    dense_dram = kb.dtmp("dense_dram", [T, D], BF16)
    if DEBUG_L0FFN:
        dbg_gsb = kb.dout("dbg_gsb", [128, NC * NE], F32)
        dbg_sc0 = kb.dout("dbg_sc0", [8, TTW], F32)
        dbg_probs = kb.dout("dbg_probs", [NE, T], F32)
        dbg_mask = kb.dout("dbg_mask", [NE, T], BF16)
        dbg_idx = kb.dout("dbg_idx", [128, NE * 8], I32)
        dbg_lo = kb.dout("dbg_lo", [8, 8], F32)
        dbg_pm = kb.dout("dbg_pm", [128, 256], F32)
        dbg_bis = kb.dout("dbg_bis", [8, 80], F32)
        dbg_bis_sb = kb.sb("dbg_bis_sb", [8, 80], F32)

    a_w2 = kb.sb("a_w2", [128, 32 * D], BF16)
    xln = [a_w2[:, c * T:(c + 1) * T] for c in range(NC)]
    a_h = kb.sb("a_h", [128, 32 * TTW], BF16)
    xb_t = a_h[:, 0:NC * TTW]
    sq_t = a_h[:, NC * TTW:2 * NC * TTW]
    scr = kb.sb("ln_scr", [128, 3 * TTW], F32)
    sqf = kb.sb("ln_sqf", [128, 2 * TTW], F32)
    xf = kb.sb("ln_xf", [128, NC * TTW], F32)
    w = {"xf": xf, "xb": xb_t, "sq": sq_t, "stat": scr[0:1, :], "scr": scr, "sqf": sqf}
    probs_t = kb.sb("probs_t", [8, T], F32)
    probs = probs_t[:]
    pfxB_t = kb.sb("pfxB_t", [8, T], F32)
    pfxA = probs_t[:]
    pfxB = pfxB_t[:]

    gate_sb = kb.sb("gate_sb", [128, NC * NE], F32)
    ones8_f = kb.sb("ones8_f", [128, 1], F32)
    pm = kb.sb("pm", [128, 256], F32)
    tmpc = kb.sb("tmpc", [128, 256], BF16)
    cnt_col = kb.sb("cnt_col", [128, 1], F32)
    bis = kb.sb("bis", [8, 8], F32)
    b_lo = kb.sb("b_lo", [8, 1], F32)
    b_hi = kb.sb("b_hi", [8, 1], F32)
    b_mid = kb.sb("b_mid", [8, 1], F32)
    b_cnt = kb.sb("b_cnt", [8, 1], F32)
    b_mf = kb.sb("b_mf", [8, 1], F32)
    b_tA = kb.sb("b_tA", [8, 1], F32)
    b_tB = kb.sb("b_tB", [8, 1], F32)
    b_cap = kb.sb("b_cap", [8, 1], F32)
    thr128 = kb.sb("thr128", [128, 1], F32)
    mask_bf = a_h[0:8, 0:T]
    pfxT = kb.sb("pfxT", [128, 32 * 8], F32)
    pfxT2 = kb.sb("pfxT2", [128, 32 * 8], F32)
    maskT = kb.sb("maskT", [128, 32 * 8], F32)
    ptile = kb.sb("ptile", [128, TTW], BF16)
    idx3 = scr[0:3, 0:TTW]
    idxrow = scr[0:1, TTW:2 * TTW]
    idx_all = kb.sb("idx_all", [128, NE * 8], I32)
    x_tm = [kb.sb(f"x_tm{i}", [128, D], BF16) for i in range(2)]
    x_fm = kb.sb("x_fm", [128, NC * D], BF16)
    w1_sb = [kb.sb(f"w1_{i}", [128, NC * 128], BF16) for i in range(2)]
    y_arena = kb.sb("y_arena", [128, NC * TTW], BF16)
    y_tm = [kb.sb(f"y_tm{i}", [128, D], BF16) for i in range(2)]
    w["xtm_sb"] = y_tm[1][:]
    p_col = [kb.sb(f"p_col{i}", [128, 1], F32) for i in range(2)]

    pb = [Bank(kb.ps(f"pb{i}", [128, TTW])) for i in range(5)]
    pbf = [Bank(kb.ps(f"pbf{i}", [128, 128], BF16)) for i in range(2)]
    pmisc = Bank(kb.ps("pmisc", [128, 128]))

    kb.start()
    load_consts(kb)
    bnd_reg = nc.gpsimd.to_reg(T - 1)

    t_z = kb.inc(nc.vector.memset(y_tm[0][:], 0.0), "dve")
    kb.wait(nc.sync, "dve", t_z)
    t_zs = 0
    for tc in range(32):
        t_zs = kb.dinc(nc.sync.dma_start(out=dense_dram[tc * 128:(tc + 1) * 128, :],
                                         in_=y_tm[0][:]), "stq")
    t_gw = kb.dinc(nc.sync.dma_start(
        out=gate_sb[:].rearrange("p (c e) -> p c e", c=NC),
        in_=gate_w.rearrange("(c p) e -> p c e", p=128)), "ldq")
    t_o8 = kb.inc(nc.vector.memset(ones8_f[:], 1.0), "dve")

    def gate_cb(tt, ts, t_ap):
        bank = pb[4]
        bank.wait_reuse(kb, nc.tensor)
        if tt == 0:
            kb.wait(nc.tensor, "ldq", t_gw)
            kb.wait(nc.tensor, "dve", t_o8)
        kb.wait(nc.tensor, "dve", t_ap)
        for cc in range(NC):
            i = nc.tensor.matmul(bank.ap[0:8, :], gate_sb[:, cc * NE:(cc + 1) * NE],
                                 xf[:, cc * TTW:(cc + 1) * TTW],
                                 start=(cc == 0), stop=(cc == NC - 1))
        t_mm = kb.inc(i, "pe")
        if DEBUG_L0FFN and tt == 0:
            kb.wait(nc.vector, "pe", t_mm)
            t_d = kb.inc(nc.vector.tensor_copy(scr[0:8, 2 * TTW:3 * TTW], bank.ap[0:8, :]), "dve")
            kb.wait(nc.sync, "dve", t_d)
            kb.dinc(nc.sync.dma_start(out=dbg_sc0[:, :], in_=scr[0:8, 2 * TTW:3 * TTW]), "stq")
            kb.dinc(nc.sync.dma_start(out=dbg_gsb[:, :], in_=gate_sb[:]), "stq")
        kb.wait(nc.vector, "pe", t_mm)
        t_sce = kb.inc(nc.vector.tensor_copy(sqf[0:8, 0:TTW], bank.ap[0:8, :]), "dve")
        bank.consumed("dve", t_sce)
        kb.wait(nc.scalar, "dve", t_sce)
        t_e = kb.inc(nc.scalar.activation(probs[:, ts:ts + TTW], sqf[0:8, 0:TTW],
                                          AF.Exp), "act")
        kb.wait(nc.tensor, "act", t_e)
        t_s = kb.inc(nc.tensor.matmul(bank.ap[0:1, :], ones8_f[0:8, :],
                                      probs[:, ts:ts + TTW], start=True, stop=True), "pe")
        kb.wait(nc.vector, "pe", t_s)
        t_c = kb.inc(nc.vector.tensor_copy(scr[0:1, 0:TTW], bank.ap[0:1, :]), "dve")
        kb.wait(nc.vector, "dve", t_c)
        t_r = kb.inc(nc.vector.reciprocal(scr[0:1, TTW:2 * TTW], scr[0:1, 0:TTW]), "dve")
        kb.wait(nc.tensor, "dve", t_r)
        t_b = kb.inc(nc.tensor.matmul(bank.ap[0:8, :], C["ones_row_f32"][0:1, 0:8],
                                      scr[0:1, TTW:2 * TTW], start=True, stop=True), "pe")
        kb.wait(nc.vector, "pe", t_b)
        t_p = kb.inc(nc.vector.tensor_tensor(probs[:, ts:ts + TTW], probs[:, ts:ts + TTW],
                                             bank.ap[0:8, :], op=ALU.mult), "dve")
        bank.consumed("dve", t_p)

    emit_ln(kb, C, x_in, w, xln, pb, pbf=pbf, xtm_dram=xtm_dram, per_tt_cb=gate_cb, f32_stats=True)
    t_probs = kb.cnt["dve"]
    t_xtm_ldq = kb.cnt["ldq"]

    kb.wait(nc.sync, "dve", t_probs)
    t_pd = kb.dinc(nc.sync.dma_start(out=pdram[:, :], in_=probs), "stq")
    kb.wait(nc.sync, "stq", t_pd)
    t_pm = kb.dinc(nc.sync.dma_start(
        out=pm[:], in_=pdram.rearrange("e (g i) -> (e g) i", g=16)), "ldq")
    lo = b_lo[:]
    hi = b_hi[:]
    mid = b_mid[:]
    cnt8 = b_cnt[:]
    capcol = b_cap[:]
    nc.vector.memset(lo[:, :] if False else lo, 0.0)
    nc.vector.memset(hi, 1.0)
    nc.vector.memset(capcol, float(CAP) - 0.5)
    t_mid = kb.inc(nc.vector.memset(mid, 0.5), "dve")
    for it in range(40):
        pmisc.wait_reuse(kb, nc.tensor)
        kb.wait(nc.tensor, "dve", t_mid)
        t_bc = kb.inc(nc.tensor.matmul(pmisc.ap[:, 0:1], C["sel8"], mid,
                                       start=True, stop=True), "pe")
        kb.wait(nc.vector, "pe", t_bc)
        t_th = kb.inc(nc.vector.tensor_copy(thr128[:], pmisc.ap[:, 0:1]), "dve")
        pmisc.consumed("dve", t_th)
        kb.wait(nc.vector, "dve", t_th)
        kb.wait(nc.vector, "ldq", t_pm)
        t_cm = kb.inc(nc.vector.tensor_scalar(tmpc[:], pm[:], thr128[:, :], None,
                                              op0=ALU.is_gt), "dve")
        kb.wait(nc.vector, "dve", t_cm)
        t_ct = kb.inc(nc.vector.reduce_sum(cnt_col[:], tmpc[:], axis=AX.X), "dve")
        pmisc.wait_reuse(kb, nc.tensor)
        kb.wait(nc.tensor, "dve", t_ct)
        t_c8 = kb.inc(nc.tensor.matmul(pmisc.ap[0:8, 0:1], C["grp16"], cnt_col[:],
                                       start=True, stop=True), "pe")
        kb.wait(nc.vector, "pe", t_c8)
        t_cc = kb.inc(nc.vector.tensor_copy(cnt8, pmisc.ap[0:8, 0:1]), "dve")
        pmisc.consumed("dve", t_cc)
        if DEBUG_L0FFN and it < 13:
            nc.vector.tensor_copy(dbg_bis_sb[:, 6 * it:6 * it + 1], mid)
            nc.vector.tensor_copy(dbg_bis_sb[:, 6 * it + 1:6 * it + 2], cnt8)
        kb.wait(nc.vector, "dve", t_cc)
        mf = b_mf[:]
        tA = b_tA[:]
        tB = b_tB[:]
        t_m8 = kb.inc(nc.vector.tensor_tensor(mf, cnt8, capcol, op=ALU.is_ge), "dve")
        kb.wait(nc.vector, "dve", t_m8)
        nc.vector.tensor_tensor(tA, mid, lo, op=ALU.subtract)
        t_s1 = kb.inc(nc.vector.tensor_tensor(tB, hi, mid, op=ALU.subtract), "dve")
        kb.wait(nc.vector, "dve", t_s1)
        nc.vector.tensor_tensor(tA, tA, mf, op=ALU.mult)
        t_s2 = kb.inc(nc.vector.tensor_tensor(tB, tB, mf, op=ALU.mult), "dve")
        kb.wait(nc.vector, "dve", t_s2)
        nc.vector.tensor_tensor(lo, lo, tA, op=ALU.add)
        t_s3 = kb.inc(nc.vector.tensor_tensor(hi, mid, tB, op=ALU.add), "dve")
        kb.wait(nc.vector, "dve", t_s3)
        if DEBUG_L0FFN and it < 13:
            nc.vector.tensor_copy(dbg_bis_sb[:, 6 * it + 2:6 * it + 3], mf)
            nc.vector.tensor_copy(dbg_bis_sb[:, 6 * it + 3:6 * it + 4], lo)
            nc.vector.tensor_copy(dbg_bis_sb[:, 6 * it + 4:6 * it + 5], hi)
            nc.vector.tensor_copy(dbg_bis_sb[:, 6 * it + 5:6 * it + 6], tA)
        nc.vector.tensor_scalar(tA, lo, 0.5, None, op0=ALU.mult)
        nc.vector.tensor_scalar(tB, hi, 0.5, None, op0=ALU.mult)
        nc.vector.tensor_copy(bis[:, 0:1], mf)
        t_mid = kb.inc(nc.vector.tensor_tensor(mid, tA, tB, op=ALU.add), "dve")
    kb.wait(nc.vector, "dve", t_mid)
    t_msk = kb.inc(nc.vector.tensor_scalar(mask_bf, probs, lo, None, op0=ALU.is_gt), "dve")

    if DEBUG_L0FFN:
        kb.wait(nc.sync, "dve", t_msk)
        kb.dinc(nc.sync.dma_start(out=dbg_probs[:, :], in_=probs), "stq")
        kb.dinc(nc.sync.dma_start(out=dbg_mask[:, :], in_=mask_bf), "stq")
        kb.dinc(nc.sync.dma_start(out=dbg_lo[:, :], in_=bis[:]), "stq")
        kb.dinc(nc.sync.dma_start(out=dbg_pm[:, :], in_=pm[:]), "stq")
        t_dbg = kb.dinc(nc.sync.dma_start(out=dbg_bis[:, :], in_=dbg_bis_sb[:]), "stq")
        kb.wait(nc.vector, "stq", t_dbg)
    kb.wait(nc.vector, "dve", t_msk)
    kb.wait(nc.vector, "stq", t_pd)
    t_pf = kb.inc(nc.vector.tensor_copy(pfxA[:, :], mask_bf), "dve")
    cur, nxt = pfxA, pfxB
    sh = 1
    while sh < T:
        kb.wait(nc.vector, "dve", t_pf)
        nc.vector.tensor_copy(nxt[:, 0:sh], cur[:, 0:sh])
        t_pf = kb.inc(nc.vector.tensor_tensor(nxt[:, sh:T], cur[:, sh:T],
                                              cur[:, 0:T - sh], op=ALU.add), "dve")
        cur, nxt = nxt, cur
        sh *= 2
    kb.wait(nc.vector, "dve", t_pf)
    t_pf = kb.inc(nc.vector.tensor_tensor(nxt[:, :], cur[:, :], mask_bf,
                                          op=ALU.subtract), "dve")
    expfx = nxt
    for tb in range(32):
        for srcv, dstv in ((expfx, pfxT), (mask_bf, maskT)):
            pmisc.wait_reuse(kb, nc.tensor)
            kb.wait(nc.tensor, "dve", t_pf)
            if srcv is mask_bf:
                pbf[0].wait_reuse(kb, nc.tensor)
                t_t = kb.inc(nc.tensor.transpose(pbf[0].ap[:, 0:8],
                                                 srcv[:, tb * 128:(tb + 1) * 128],
                                                 C["id_bf"][0:8, 0:8]), "pe")
                kb.wait(nc.vector, "pe", t_t)
                t_ev = kb.inc(nc.vector.tensor_copy(dstv[:, tb * 8:(tb + 1) * 8],
                                                    pbf[0].ap[:, 0:8]), "dve")
                pbf[0].consumed("dve", t_ev)
            else:
                t_t = kb.inc(nc.tensor.transpose(pmisc.ap[:, 0:8],
                                                 srcv[:, tb * 128:(tb + 1) * 128],
                                                 C["id_f32"][0:8, 0:8]), "pe")
                kb.wait(nc.vector, "pe", t_t)
                t_ev = kb.inc(nc.vector.tensor_copy(dstv[:, tb * 8:(tb + 1) * 8],
                                                    pmisc.ap[:, 0:8]), "dve")
                pmisc.consumed("dve", t_ev)
    t_tr = kb.cnt["dve"]
    kb.wait(nc.vector, "dve", t_tr)
    t_p2 = kb.inc(nc.vector.tensor_scalar(pfxT2[:], pfxT[:], -512.0, None, op0=ALU.add), "dve")

    for e in range(NE):
        for half in range(2):
            bank = pb[4]
            bank.wait_reuse(kb, nc.tensor)
            pfx_src = pfxT if half == 0 else pfxT2
            t_pt_last = 0
            for tb in range(32):
                kb.wait(nc.vector, "dve", max(t_p2, t_tr))
                if tb >= 1:
                    kb.wait(nc.vector, "pe", t_pt_last)
                t_pt = kb.inc(nc.vector.tensor_scalar(
                    ptile[:], C["iota512"], pfx_src[:, tb * 8 + e:tb * 8 + e + 1],
                    maskT[:, tb * 8 + e:tb * 8 + e + 1],
                    op0=ALU.is_equal, op1=ALU.mult), "dve")
                kb.wait(nc.tensor, "dve", t_pt)
                i = nc.tensor.matmul(bank.ap[0:3, :], C["idxw"][:, tb * 3:tb * 3 + 3],
                                     ptile[:], start=(tb == 0), stop=(tb == 31))
                t_pt_last = kb.inc(i, "pe")
            kb.wait(nc.vector, "pe", t_pt_last)
            t_i3 = kb.inc(nc.vector.tensor_copy(idx3, bank.ap[0:3, :]), "dve")
            bank.consumed("dve", t_i3)
            bank2 = pb[3]
            bank2.wait_reuse(kb, nc.tensor)
            kb.wait(nc.tensor, "dve", t_i3)
            t_mg = kb.inc(nc.tensor.matmul(bank2.ap[0:1, :], C["mrg3"], idx3,
                                           start=True, stop=True), "pe")
            kb.wait(nc.vector, "pe", t_mg)
            t_ir = kb.inc(nc.vector.tensor_scalar(idxrow, bank2.ap[0:1, :], 4096.0,
                                                  None, op0=ALU.add), "dve")
            bank2.consumed("dve", t_ir)
            for j in range(4):
                pmisc.wait_reuse(kb, nc.tensor)
                kb.wait(nc.tensor, "dve", t_ir)
                t_t = kb.inc(nc.tensor.transpose(pmisc.ap[:, 0:1],
                                                 scr[0:1, TTW + j * 128:TTW + (j + 1) * 128],
                                                 C["id_f32"][0:1, 0:1]), "pe")
                kb.wait(nc.vector, "pe", t_t)
                t_ix = kb.inc(nc.vector.tensor_copy(
                    idx_all[:, e * 8 + half * 4 + j:e * 8 + half * 4 + j + 1],
                    pmisc.ap[:, 0:1]), "dve")
                pmisc.consumed("dve", t_ix)
    t_idx = kb.cnt["dve"]
    if DEBUG_L0FFN:
        kb.wait(nc.sync, "dve", t_idx)
        kb.dinc(nc.sync.dma_start(out=dbg_idx[:, :], in_=idx_all[:]), "stq")

    w1_tok = [0, 0]
    xtm_pe = [0, 0]
    ytm_tok = [0, 0]
    ytm_stq_war = [t_zs, 0]
    ytm_ldq_war = [0, t_xtm_ldq]
    ytm_pcol_tok = [0, 0]
    t_scat_prev = 0
    t_w2 = 0
    t_yphase = kb.cnt["pe"]
    for e in range(NE):
        kb.wait(nc.gpsimd, "pe", t_yphase)
        t_w2 = kb.dinc(nc.gpsimd.dma_start(
            out=a_w2[:].rearrange("p (h c) -> p h c", h=32),
            in_=w2_d[e].rearrange("(h p) c -> p h c", p=128)), "wq")
        for j in range(NC):
            buf = j % 2
            kb.wait(nc.vector, "pe", xtm_pe[buf])
            t_ms = kb.inc(nc.vector.memset(x_tm[buf][:], 0.0), "dve")
            kb.wait(nc.gpsimd, "dve", max(t_ms, t_idx))
            t_g = kb.dinc(nc.gpsimd.indirect_dma_start(
                out=x_tm[buf][:], out_offset=None, in_=xtm_dram[:, :],
                in_offset=IndirectOffsetOnAxis(ap=idx_all[:, e * 8 + j:e * 8 + j + 1], axis=0),
                bounds_check=bnd_reg, oob_is_err=False), "gq")
            for cc in range(NC):
                slot = pbf[cc % 2]
                slot.wait_reuse(kb, nc.tensor)
                kb.wait(nc.tensor, "gq", t_g)
                t_t = kb.inc(nc.tensor.transpose(slot.ap[:, :],
                                                 x_tm[buf][:, cc * 128:(cc + 1) * 128],
                                                 C["id_bf"]), "pe")
                xtm_pe[buf] = t_t
                kb.wait(nc.vector, "pe", t_t)
                t_ev = kb.inc(nc.vector.tensor_copy(
                    x_fm[:, cc * D + j * 128:cc * D + j * 128 + 128], slot.ap[:, :]), "dve")
                slot.consumed("dve", t_ev)
        t_xfm = kb.cnt["dve"]
        for half in range(2):
            hs = half * TTW
            for hc in range(32):
                buf = hc % 2
                kb.wait(nc.gpsimd, "pe", w1_tok[buf])
                t_w1 = kb.dinc(nc.gpsimd.dma_start(
                    out=w1_sb[buf][:].rearrange("p (c h) -> p c h", c=NC),
                    in_=w1_d[e].rearrange("(c p) h -> p c h", p=128)[:, :, hc * 128:(hc + 1) * 128]),
                    "wq")
                bank = pb[hc % 2]
                bank.wait_reuse(kb, nc.tensor)
                kb.wait(nc.tensor, "wq", t_w1)
                kb.wait(nc.tensor, "dve", t_xfm)
                for cc in range(NC):
                    i = nc.tensor.matmul(bank.ap[:, :],
                                         w1_sb[buf][:, cc * 128:(cc + 1) * 128],
                                         x_fm[:, cc * D + hs:cc * D + hs + TTW],
                                         start=(cc == 0), stop=(cc == NC - 1))
                w1_tok[buf] = t_mm = kb.inc(i, "pe")
                kb.wait(nc.scalar, "pe", t_mm)
                t_gl = kb.inc(nc.scalar.activation(
                    a_h[:, hc * TTW:(hc + 1) * TTW], bank.ap[:, :], AF.Gelu), "act")
                bank.consumed("act", t_gl)
            t_h = kb.cnt["act"]
            kb.wait(nc.tensor, "wq", t_w2)
            for grp in ((0, 1, 2, 3), (4, 5, 6, 7)):
                for cc in grp:
                    bank = pb[cc - grp[0]]
                    bank.wait_reuse(kb, nc.tensor)
                    kb.wait(nc.tensor, "act", t_h)
                    for hc in range(32):
                        i = nc.tensor.matmul(bank.ap[:, :],
                                             a_w2[:, hc * D + cc * 128:hc * D + cc * 128 + 128],
                                             a_h[:, hc * TTW:(hc + 1) * TTW],
                                             start=(hc == 0), stop=(hc == 31))
                    t_mm = kb.inc(i, "pe")
                    kb.wait(nc.vector, "pe", t_mm)
                    t_y = kb.inc(nc.vector.tensor_copy(
                        y_arena[:, cc * TTW:(cc + 1) * TTW], bank.ap[:, :]), "dve")
                    bank.consumed("dve", t_y)
            t_yphase = kb.cnt["pe"]
            t_yall = kb.cnt["dve"]
            for j in range(4):
                buf = j % 2
                icol = idx_all[:, e * 8 + half * 4 + j:e * 8 + half * 4 + j + 1]
                kb.wait(nc.gpsimd, "dve", ytm_pcol_tok[buf])
                t_pg = kb.dinc(nc.gpsimd.indirect_dma_start(
                    out=p_col[buf][:], out_offset=None,
                    in_=pdram.rearrange("e (t one) -> (e t) one", one=1),
                    in_offset=IndirectOffsetOnAxis(ap=icol, axis=0),
                    element_offset=e * T,
                    bounds_check=bnd_reg, oob_is_err=False), "gq")
                kb.wait(nc.vector, "gq", ytm_tok[buf])
                kb.wait(nc.vector, "stq", ytm_stq_war[buf])
                kb.wait(nc.vector, "ldq", ytm_ldq_war[buf])
                ytm_stq_war[buf] = ytm_ldq_war[buf] = 0
                for cc in range(NC):
                    slot = pbf[cc % 2]
                    slot.wait_reuse(kb, nc.tensor)
                    kb.wait(nc.tensor, "dve", t_yall)
                    t_t = kb.inc(nc.tensor.transpose(
                        slot.ap[:, :], y_arena[:, cc * TTW + j * 128:cc * TTW + j * 128 + 128],
                        C["id_bf"]), "pe")
                    kb.wait(nc.vector, "pe", t_t)
                    t_ev = kb.inc(nc.vector.tensor_copy(
                        y_tm[buf][:, cc * 128:(cc + 1) * 128], slot.ap[:, :]), "dve")
                    slot.consumed("dve", t_ev)
                kb.wait(nc.vector, "gq", t_pg)
                t_ysc = kb.inc(nc.vector.tensor_scalar(
                    y_tm[buf][:], y_tm[buf][:], p_col[buf][:, :], None, op0=ALU.mult), "dve")
                ytm_pcol_tok[buf] = t_ysc
                kb.wait(nc.gpsimd, "dve", t_ysc)
                kb.wait(nc.gpsimd, "gq", t_scat_prev)
                t_sc = kb.dinc(nc.gpsimd.indirect_dma_start(
                    out=dense_dram[:, :],
                    out_offset=IndirectOffsetOnAxis(ap=icol, axis=0),
                    in_=y_tm[buf][:], in_offset=None,
                    bounds_check=bnd_reg, oob_is_err=False,
                    compute_op=ALU.add), "gq")
                ytm_tok[buf] = t_sc
        t_scat_prev = kb.cnt["gq"]

    x_src = x_in.rearrange("(c p) t -> p c t", p=128)
    x_dst = x_out.rearrange("(c p) t -> p c t", p=128)
    dn_sb = x_tm[0]
    xck = xf[:, 0:NC * 128]
    t_resid = 0
    for tc in range(32):
        tcs = tc * 128
        kb.wait(nc.sync, "gq", t_scat_prev)
        kb.wait(nc.sync, "dve", t_resid)
        t_xl = kb.dinc(nc.sync.dma_start(
            out=xck.rearrange("p (c t) -> p c t", c=NC), in_=x_src[:, :, tcs:tcs + 128]), "ldq")
        kb.wait(nc.sync, "pe", kb.cnt["pe"]) if tc == 0 else None
        t_dn = kb.dinc(nc.sync.dma_start(out=dn_sb[:], in_=dense_dram[tcs:tcs + 128, :]), "ldq")
        for cc in range(NC):
            slot = pbf[cc % 2]
            slot.wait_reuse(kb, nc.tensor)
            kb.wait(nc.tensor, "ldq", t_dn)
            t_t = kb.inc(nc.tensor.transpose(slot.ap[:, :],
                                             dn_sb[:, cc * 128:(cc + 1) * 128],
                                             C["id_bf"]), "pe")
            kb.wait(nc.vector, "pe", t_t)
            kb.wait(nc.vector, "ldq", t_xl)
            t_cv = kb.inc(nc.vector.tensor_copy(scr[:, 0:128], slot.ap[:, :]), "dve")
            slot.consumed("dve", t_cv)
            kb.wait(nc.vector, "dve", t_cv)
            t_resid = kb.inc(nc.vector.scalar_tensor_tensor(
                out=xck[:, cc * 128:(cc + 1) * 128], in0=scr[:, 0:128], scalar=0.5,
                in1=xck[:, cc * 128:(cc + 1) * 128], op0=ALU.mult, op1=ALU.add), "dve")
        kb.wait(nc.sync, "dve", t_resid)
        kb.dinc(nc.sync.dma_start(out=x_dst[:, :, tcs:tcs + 128],
                                  in_=xck.rearrange("p (c t) -> p c t", c=NC)), "stq")

    kb.wait(nc.sync, "stq")
    return kb.finish()


def l0ffn_weights(inputs):
    return {
        "gate_w": np.ascontiguousarray(inputs["L0_ffn_gate_w"]),
        "w1": np.ascontiguousarray(inputs["L0_ffn_w1"]),
        "w2": np.ascontiguousarray(inputs["L0_ffn_w2"]),
    }


_BUILDER_CACHE = {}


def _get(name):
    if name not in _BUILDER_CACHE:
        _BUILDER_CACHE[name] = {"l1ffn": build_l1ffn, "l0attn": build_l0attn, "l1attn": build_l1attn, "l0ffn": build_l0ffn}[name]()
    return _BUILDER_CACHE[name]


def run_stage(kb, per_core_inputs):
    in_maps = []
    for c in range(8):
        m = dict(kb.consts)
        m.update(per_core_inputs[c])
        in_maps.append(m)
    return run_bass_kernel_spmd(kb.nc, in_maps, core_ids=list(range(8)))


def l1ffn_weights(inputs):
    return {
        "wiT": np.ascontiguousarray(inputs["L1_ffn_wi_w"].T),
        "woT": np.ascontiguousarray(inputs["L1_ffn_wo_w"].T),
        "AsT": np.ascontiguousarray(inputs["L1_ffn_lora_As"].transpose(0, 2, 1)).reshape(NE * D, RANK),
        "BsT": np.ascontiguousarray(inputs["L1_ffn_lora_Bs"].transpose(0, 2, 1)).reshape(NE * RANK, DFF),
        "tki": np.ascontiguousarray(inputs["L1_ffn_top_k_indices"].reshape(1, TOPK)),
        "tkv": np.ascontiguousarray(inputs["L1_ffn_top_k_values"].reshape(1, TOPK)),
    }


_STAGES = (
    ("l0ffn", "l0ffn"),
    ("l0attn", "l0attn"),
    ("l0ffn", "l0ffn2"),
    ("l1ffn", "l1ffn"),
    ("l1attn", "l1attn"),
    ("l1ffn", "l1ffn2"),
)


def kernel(**inputs):
    x = [np.ascontiguousarray(inputs["L0_attn_x"][c].T) for c in range(8)]
    wmaps = {
        "l0ffn": l0ffn_weights(inputs),
        "l0attn": l0attn_weights(inputs),
        "l1ffn": l1ffn_weights(inputs),
        "l1attn": l1attn_weights(inputs),
    }
    for builder, _name in _STAGES:
        kb = _get(builder)
        per_core = []
        for c in range(8):
            m = dict(wmaps[builder])
            m["x_in"] = x[c]
            per_core.append(m)
        res = run_stage(kb, per_core)
        x = [np.ascontiguousarray(res.results[c]["x_out"]) for c in range(8)]
    return np.stack([xc.T for xc in x]).astype(np.float32)


# version 02
# speedup vs baseline: 1.0089x; 1.0089x over previous
import contextlib

import numpy as np

import concourse.bass as bass
import concourse.mybir as mybir
from concourse.bass_core_rust import IndirectOffsetOnAxis
from concourse.bass_utils import run_bass_kernel_spmd

F32 = mybir.dt.float32
BF16 = mybir.dt.bfloat16
I32 = mybir.dt.int32
U8 = mybir.dt.uint8
AF = mybir.ActivationFunctionType
ALU = mybir.AluOpType
AX = mybir.AxisListType

D = 1024
T = 4096
NC = 8
NTT = 8
TTW = 512
NH = 16
HD = 64
NE = 8
CAP = 1024
DFF = 4096
RANK = 16
TOPK = 2
WIN = 256
LN_EPS = 1e-5
RMS_EPS = 1e-8
SIG_EPS = 1e-8
NEG_BIG = -1e30


class KB:

    def __init__(self):
        self.nc = bass.Bass("TRN2", target_bir_lowering=False)
        self.es = contextlib.ExitStack()
        self.sems = {}
        self.cnt = {}
        self.consts = {}

    def sem(self, name):
        if name not in self.sems:
            self.sems[name] = self.es.enter_context(self.nc.semaphore(name))
            self.cnt[name] = 0
        return self.sems[name]

    def inc(self, inst, name, k=1):
        s = self.sem(name)
        self.cnt[name] += k
        inst.then_inc(s, k)
        return self.cnt[name]

    def dinc(self, inst, name):
        return self.inc(inst, name, 16)

    def wait(self, eng, name, val=None):
        if val is None:
            val = self.cnt[name]
        if val <= 0:
            return
        eng.wait_ge(self.sem(name), val)

    def sb(self, name, shape, dt):
        scope = getattr(self, "scopes", None)
        tgt = scope[-1] if scope else self.es
        name = getattr(self, "prefix", "") + name
        return tgt.enter_context(self.nc.sbuf_tensor(name, shape, dt))

    def ps(self, name, shape, dt=F32):
        scope = getattr(self, "scopes", None)
        tgt = scope[-1] if scope else self.es
        name = getattr(self, "prefix", "") + name
        return tgt.enter_context(self.nc.psum_tensor(name, shape, dt))

    def push_scope(self, tag):
        if not hasattr(self, "scopes"):
            self.scopes = []
        self.scopes.append(contextlib.ExitStack())
        self.prefix = tag + "_"

    def pop_scope(self):
        self.scopes.pop().close()
        self.prefix = ""

    def barrier(self):
        nc = self.nc
        for eng in (nc.tensor, nc.vector, nc.scalar, nc.gpsimd, nc.sync):
            for name in list(self.sems):
                self.wait(eng, name)

    def din(self, name, shape, dt=F32):
        return self.nc.dram_tensor(name, shape, dt, kind="ExternalInput").ap()

    def dout(self, name, shape, dt=F32):
        return self.nc.dram_tensor(name, shape, dt, kind="ExternalOutput").ap()

    def dtmp(self, name, shape, dt):
        return self.nc.dram_tensor(name, shape, dt, kind="Internal").ap()

    def const(self, name, arr):
        arr = np.ascontiguousarray(arr)
        self.consts[name] = arr
        dt = {np.dtype(np.float32): F32, np.dtype(np.int32): I32,
              np.dtype(np.uint16): BF16}[arr.dtype]
        return self.nc.dram_tensor(name, list(arr.shape), dt, kind="ExternalInput").ap()

    def cbf16(self, name, arr_f32):
        import ml_dtypes
        a = np.ascontiguousarray(np.asarray(arr_f32, np.float32).astype(ml_dtypes.bfloat16)).view(np.uint16)
        return self.const(name, a)

    def start(self):
        self.block = self.es.enter_context(self.nc.Block())

    def finish(self):
        self.es.close()
        return self


class Bank:

    def __init__(self, ap):
        self.ap = ap
        self.sem = None
        self.tok = 0

    def wait_reuse(self, kb, eng):
        if self.sem is not None:
            kb.wait(eng, self.sem, self.tok)

    def consumed(self, sem, tok):
        self.sem = sem
        self.tok = tok


def host_consts(kb, extra=()):
    spec = {
        "ones_col_bf": (np.ones((128, 1), np.float32), BF16),
        "ones_col_f32": (np.ones((128, 1), np.float32), F32),
        "ones_row_f32": (np.ones((1, 128), np.float32), F32),
        "ones_row_bf": (np.ones((1, 128), np.float32), BF16),
        "id_bf": (np.eye(128, dtype=np.float32), BF16),
        "id_f32": (np.eye(128, dtype=np.float32), F32),
        "iota_col_f32": (np.arange(128, dtype=np.float32).reshape(128, 1), F32),
    }
    for name, arr, dt in extra:
        spec[name] = (arr, dt)
    c = {}
    kb._const_pairs = []
    for name, (arr, dt) in spec.items():
        if dt == BF16:
            dap = kb.cbf16("c_" + name, arr)
        elif dt == I32:
            dap = kb.const("c_" + name, np.asarray(arr, np.int32))
        else:
            dap = kb.const("c_" + name, np.asarray(arr, np.float32))
        tile = kb.sb(name, list(arr.shape), dt)
        kb._const_pairs.append((tile, dap))
        c[name] = tile[:]
    return c


def load_consts(kb):
    nc = kb.nc
    t = 0
    for tile, dap in kb._const_pairs:
        t = kb.dinc(nc.sync.dma_start(out=tile[:], in_=dap), "ldq")
    for eng in (nc.tensor, nc.vector, nc.scalar, nc.gpsimd):
        kb.wait(eng, "ldq", t)


def emit_ln(kb, C, x_dram, w, xln, pb, pbf=None, xrms=None, xtm_dram=None, per_tt_cb=None, f32_stats=False):
    nc = kb.nc
    xf, xb, sq, stat = w["xf"], w["xb"], w["sq"], w["stat"]
    s0 = stat[:, 0 * TTW:1 * TTW]
    s1 = stat[:, 1 * TTW:2 * TTW]
    s2 = stat[:, 2 * TTW:3 * TTW]
    x_src = x_dram.rearrange("(c p) t -> p c t", p=128)
    t_store = 0
    t_ap = 0
    t_pe_prev = 0
    for tt in range(NTT):
        ts = tt * TTW
        kb.wait(nc.sync, "dve", t_ap)
        t_ld = kb.dinc(nc.sync.dma_start(
            out=xf[:].rearrange("p (c t) -> p c t", c=NC),
            in_=x_src[:, :, ts:ts + TTW]), "ldq")
        kb.wait(nc.vector, "ldq", t_ld)
        kb.wait(nc.vector, "pe", t_pe_prev)
        t_cast = kb.inc(nc.vector.tensor_copy(xb[:], xf[:]), "dve")
        for bk in (pb[0], pb[1]):
            bk.wait_reuse(kb, nc.tensor)
        if f32_stats:
            sqb = [w["sqf"][:, 0:TTW], w["sqf"][:, TTW:2 * TTW]]
            kb.wait(nc.tensor, "ldq", t_ld)
            for cc in range(NC):
                nc.tensor.matmul(pb[0].ap[0:1, :], C["ones_col_f32"],
                                 xf[:, cc * TTW:(cc + 1) * TTW],
                                 start=(cc == 0), stop=(cc == NC - 1))
            t_sqa = [0, 0]
            kb.wait(nc.scalar, "ldq", t_ld)
            for cc in range(NC):
                buf = cc % 2
                if t_sqa[buf]:
                    kb.wait(nc.scalar, "pe", t_sqa[buf])
                t_sq = kb.inc(nc.scalar.activation(sqb[buf], xf[:, cc * TTW:(cc + 1) * TTW],
                                                   AF.Square), "act")
                kb.wait(nc.tensor, "act", t_sq)
                i = nc.tensor.matmul(pb[1].ap[0:1, :], C["ones_col_f32"], sqb[buf],
                                     start=(cc == 0), stop=(cc == NC - 1))
                t_sqa[buf] = t_mm = kb.inc(i, "pe")
        else:
            kb.wait(nc.scalar, "dve", t_cast)
            t_sq = kb.inc(nc.scalar.activation(sq[:], xb[:], AF.Square), "act")
            kb.wait(nc.tensor, "dve", t_cast)
            kb.wait(nc.tensor, "act", t_sq)
            for cc in range(NC):
                nc.tensor.matmul(pb[0].ap[0:1, :], C["ones_col_bf"], xb[:, cc * TTW:(cc + 1) * TTW],
                                 start=(cc == 0), stop=(cc == NC - 1))
            for cc in range(NC):
                i = nc.tensor.matmul(pb[1].ap[0:1, :], C["ones_col_bf"], sq[:, cc * TTW:(cc + 1) * TTW],
                                     start=(cc == 0), stop=(cc == NC - 1))
            t_mm = kb.inc(i, "pe")
        kb.wait(nc.vector, "pe", t_mm)
        nc.vector.tensor_scalar(s0, pb[0].ap[0:1, :], 1.0 / D, None, op0=ALU.mult)
        nc.vector.tensor_scalar(s1, pb[1].ap[0:1, :], 1.0 / D, None, op0=ALU.mult)
        nc.vector.tensor_tensor(s2, s0, s0, op=ALU.mult)
        nc.vector.tensor_tensor(s1, s1, s2, op=ALU.subtract)
        t_v = kb.inc(nc.vector.tensor_scalar(s1, s1, LN_EPS, None, op0=ALU.add), "dve")
        pb[0].consumed("dve", t_v)
        pb[1].consumed("dve", t_v)
        kb.wait(nc.scalar, "dve", t_v)
        t_sd = kb.inc(nc.scalar.activation(s2, s1, AF.Sqrt), "act")
        kb.wait(nc.vector, "act", t_sd)
        t_rc = kb.inc(nc.vector.reciprocal(s1, s2), "dve")
        kb.wait(nc.vector, "dve", t_rc)
        t_mr = kb.inc(nc.vector.tensor_tensor(s2, s0, s1, op=ALU.mult), "dve")
        for bk in (pb[2], pb[3]):
            bk.wait_reuse(kb, nc.tensor)
        kb.wait(nc.tensor, "dve", t_mr)
        nc.tensor.matmul(pb[2].ap[:, :], C["ones_row_f32"], s1, start=True, stop=True)
        t_bc = kb.inc(nc.tensor.matmul(pb[3].ap[:, :], C["ones_row_f32"], s2,
                                       start=True, stop=True), "pe")
        kb.wait(nc.vector, "pe", t_bc)
        if per_tt_cb is None:
            for cc in range(NC):
                nc.vector.tensor_tensor(xf[:, cc * TTW:(cc + 1) * TTW],
                                        xf[:, cc * TTW:(cc + 1) * TTW], pb[2].ap[:, :], op=ALU.mult)
                i = nc.vector.tensor_tensor(xln[cc][:, ts:ts + TTW],
                                            xf[:, cc * TTW:(cc + 1) * TTW], pb[3].ap[:, :],
                                            op=ALU.subtract)
            t_ap = kb.inc(i, "dve")
        else:
            for cc in range(NC):
                nc.vector.tensor_tensor(xf[:, cc * TTW:(cc + 1) * TTW],
                                        xf[:, cc * TTW:(cc + 1) * TTW], pb[2].ap[:, :], op=ALU.mult)
                nc.vector.tensor_tensor(xf[:, cc * TTW:(cc + 1) * TTW],
                                        xf[:, cc * TTW:(cc + 1) * TTW], pb[3].ap[:, :],
                                        op=ALU.subtract)
                i = nc.vector.tensor_copy(xln[cc][:, ts:ts + TTW],
                                          xf[:, cc * TTW:(cc + 1) * TTW])
            t_ap = kb.inc(i, "dve")
            per_tt_cb(tt, ts, t_ap)
        pb[2].consumed("dve", t_ap)
        pb[3].consumed("dve", t_ap)

        if xrms is not None:
            kb.wait(nc.scalar, "dve", t_ap)
            last = None
            for cc in range(NC):
                last = nc.scalar.activation(sq[:, cc * TTW:(cc + 1) * TTW],
                                            xln[cc][:, ts:ts + TTW], AF.Square)
            t_sq2 = kb.inc(last, "act")
            pb[0].wait_reuse(kb, nc.tensor)
            kb.wait(nc.tensor, "act", t_sq2)
            for cc in range(NC):
                i = nc.tensor.matmul(pb[0].ap[0:1, :], C["ones_col_bf"],
                                     sq[:, cc * TTW:(cc + 1) * TTW],
                                     start=(cc == 0), stop=(cc == NC - 1))
            t_mm2 = kb.inc(i, "pe")
            kb.wait(nc.vector, "pe", t_mm2)
            t_v2 = kb.inc(nc.vector.tensor_scalar(s1, pb[0].ap[0:1, :], 1.0 / D, RMS_EPS,
                                                  op0=ALU.mult, op1=ALU.add), "dve")
            pb[0].consumed("dve", t_v2)
            kb.wait(nc.scalar, "dve", t_v2)
            t_sd2 = kb.inc(nc.scalar.activation(s2, s1, AF.Sqrt), "act")
            kb.wait(nc.vector, "act", t_sd2)
            t_r2 = kb.inc(nc.vector.reciprocal(s1, s2), "dve")
            pb[2].wait_reuse(kb, nc.tensor)
            kb.wait(nc.tensor, "dve", t_r2)
            t_bc2 = kb.inc(nc.tensor.matmul(pb[2].ap[:, :], C["ones_row_f32"], s1,
                                            start=True, stop=True), "pe")
            kb.wait(nc.vector, "pe", t_bc2)
            nc.vector.tensor_copy(xb[:, 0:TTW], pb[2].ap[:, :])
            for cc in range(NC):
                i = nc.vector.tensor_tensor(xrms[cc][:, ts:ts + TTW],
                                            xln[cc][:, ts:ts + TTW], xb[:, 0:TTW],
                                            op=ALU.mult)
            t_ap = kb.inc(i, "dve")
            pb[2].consumed("dve", t_ap)

        t_pe_prev = kb.cnt["pe"]
        if xtm_dram is not None:
            xtm_sb = w["xtm_sb"]
            for j in range(4):
                js = ts + j * 128
                kb.wait(nc.vector, "ldq", t_store)
                t_done = 0
                for cc in range(NC):
                    slot = pbf[cc % 2]
                    slot.wait_reuse(kb, nc.tensor)
                    kb.wait(nc.tensor, "dve", t_ap)
                    t_t = kb.inc(nc.tensor.transpose(slot.ap[:, 0:128],
                                                     xln[cc][:, js:js + 128],
                                                     C["id_bf"]), "pe")
                    kb.wait(nc.vector, "pe", t_t)
                    t_done = kb.inc(nc.vector.tensor_copy(
                        xtm_sb[:, cc * 128:(cc + 1) * 128], slot.ap[:, 0:128]), "dve")
                    slot.consumed("dve", t_done)
                kb.wait(nc.sync, "dve", t_done)
                t_store = kb.dinc(nc.sync.dma_start(out=xtm_dram[js:js + 128, :],
                                                    in_=xtm_sb[:]), "ldq")
        t_pe_prev = kb.cnt["pe"]


def ln_work_tiles(kb):
    scratch = kb.sb("ln_scr", [128, 3 * TTW], F32)
    return {
        "xf": kb.sb("ln_xf", [128, NC * TTW], F32),
        "xb": kb.sb("ln_xb", [128, NC * TTW], BF16),
        "sq": kb.sb("ln_sq", [128, NC * TTW], BF16),
        "stat": scratch[0:1, :],
        "scr": scratch,
    }


DEBUG_L1FFN = False


def emit_l1ffn(kb, C, x_in, x_out, W):
    nc = kb.nc
    wiT, woT, AsT, BsT, tki, tkv = (W["wiT"], W["woT"], W["AsT"], W["BsT"],
                                    W["tki"], W["tkv"])
    if DEBUG_L1FFN:
        dbg_xln = kb.dout("dbg_xln", [128, T], BF16)
        dbg_asel = kb.dout("dbg_asel", [128, NC * RANK], BF16)
        dbg_bsel = kb.dout("dbg_bsel", [16, DFF], BF16)
        dbg_t1 = kb.dout("dbg_t1", [16, TTW], BF16)
        dbg_int = kb.dout("dbg_int", [128, TTW], BF16)
        dbg_h0 = kb.dout("dbg_h0", [128, TTW], BF16)
        dbg_idxf = kb.dout("dbg_idxf", [128, 1], F32)
        dbg_idxi = kb.dout("dbg_idxi", [128, 1], I32)

    a_xln = kb.sb("a_xln", [128, NC * T], BF16)
    xln = [a_xln[:, c * T:(c + 1) * T] for c in range(NC)]
    a_int = kb.sb("a_int", [128, 32 * TTW], BF16)
    a_h0 = kb.sb("a_h0", [128, 32 * TTW], BF16)
    w = ln_work_tiles(kb)
    wi_sb = [kb.sb(f"wi{i}", [128, NC * 128], BF16) for i in range(2)]
    wo_sb = [kb.sb(f"wo{i}", [128, 32 * 128], BF16) for i in range(2)]
    a_f32 = kb.sb("a_f32", [128, RANK], F32)
    b_f32 = w["xf"][0:16, :]
    a_sel = [kb.sb(f"a_sel{k}", [128, NC * RANK], BF16) for k in range(TOPK)]
    b_sel = [kb.sb(f"b_sel{k}", [16, DFF], BF16) for k in range(TOPK)]
    t1_sb = [kb.sb(f"t1_{k}", [16, TTW], BF16) for k in range(TOPK)]
    tki_i = kb.sb("tki_i", [1, TOPK], I32)
    tki_f = kb.sb("tki_f", [1, TOPK], F32)
    tkv_f = kb.sb("tkv_f", [1, TOPK], F32)
    tkv_col = [kb.sb(f"tkv_col{k}", [128, 1], F32) for k in range(TOPK)]
    idx_f = kb.sb("idx_f", [128, 1], F32)
    idx_i = kb.sb("idx_i", [128, 1], I32)
    x_sb = [w["xf"][:, 0:TTW], w["xf"][:, TTW:2 * TTW]]

    pb = [Bank(kb.ps(f"pb{i}", [128, TTW])) for i in range(6)]
    pmisc = kb.ps("pmisc", [128, 128])

    t_ld = kb.dinc(nc.sync.dma_start(out=tki_i[:], in_=tki[:, :]), "ldq")
    t_ld2 = kb.dinc(nc.sync.dma_start(out=tkv_f[:], in_=tkv[:, :]), "ldq")
    kb.wait(nc.vector, "ldq", t_ld2)
    t_cf = kb.inc(nc.vector.tensor_copy(tki_f[:], tki_i[:]), "dve")
    t_half = kb.inc(nc.vector.tensor_scalar(tkv_f[:], tkv_f[:], 0.5, None, op0=ALU.mult), "dve")
    for k in range(TOPK):
        kb.wait(nc.tensor, "dve")
        t_bc = kb.inc(nc.tensor.matmul(pmisc[:, 0:1], C["ones_row_f32"], tki_f[0:1, k:k + 1],
                                       start=True, stop=True), "pe")
        t_bv = kb.inc(nc.tensor.matmul(pmisc[:, 1:2], C["ones_row_f32"], tkv_f[0:1, k:k + 1],
                                       start=True, stop=True), "pe")
        kb.wait(nc.vector, "pe", t_bv)
        kb.inc(nc.vector.tensor_copy(tkv_col[k][:, :], pmisc[:, 1:2]), "dve")
        for cc in range(NC):
            t_a = kb.inc(nc.vector.tensor_scalar(idx_f[:], pmisc[:, 0:1], float(D), float(128 * cc),
                                                 op0=ALU.mult, op1=ALU.add), "dve")
            kb.wait(nc.vector, "dve", t_a)
            t_b = kb.inc(nc.vector.tensor_tensor(idx_f[:], idx_f[:], C_iota(kb, C), op=ALU.add), "dve")
            kb.wait(nc.vector, "dve", t_b)
            t_ix = kb.inc(nc.vector.tensor_copy(idx_i[:], idx_f[:]), "dve")
            kb.wait(nc.gpsimd, "dve", t_ix)
            t_g = kb.dinc(nc.gpsimd.indirect_dma_start(
                out=a_f32[:, :], out_offset=None, in_=AsT[:, :],
                in_offset=IndirectOffsetOnAxis(ap=idx_i[:, :], axis=0)), "gq")
            kb.wait(nc.vector, "gq", t_g)
            kb.inc(nc.vector.tensor_copy(
                a_sel[k][:, cc * RANK:(cc + 1) * RANK], a_f32[:, :]), "dve")
        t_a = kb.inc(nc.vector.tensor_scalar(idx_f[:], pmisc[:, 0:1], float(RANK), None, op0=ALU.mult), "dve")
        kb.wait(nc.vector, "dve", t_a)
        t_b = kb.inc(nc.vector.tensor_tensor(idx_f[:], idx_f[:], C_iota(kb, C), op=ALU.add), "dve")
        kb.wait(nc.vector, "dve", t_b)
        t_ix = kb.inc(nc.vector.tensor_copy(idx_i[:], idx_f[:]), "dve")
        kb.wait(nc.gpsimd, "dve", t_ix)
        t_g = kb.dinc(nc.gpsimd.indirect_dma_start(
            out=b_f32, out_offset=None, in_=BsT[:, :],
            in_offset=IndirectOffsetOnAxis(ap=idx_i[0:16, :], axis=0)), "gq")
        kb.wait(nc.vector, "gq", t_g)
        kb.inc(nc.vector.tensor_copy(b_sel[k][:, :], b_f32), "dve")
        if DEBUG_L1FFN and k == 1:
            kb.wait(nc.sync, "dve", t_ix)
            kb.dinc(nc.sync.dma_start(out=dbg_idxf[:, :], in_=idx_f[:]), "stq")
            kb.dinc(nc.sync.dma_start(out=dbg_idxi[:, :], in_=idx_i[:]), "stq")
    t_prep = kb.cnt["dve"]

    kb.wait(nc.sync, "dve", t_prep)
    emit_ln(kb, C, x_in, w, xln, pb)
    t_ln = kb.cnt["dve"]
    if DEBUG_L1FFN:
        kb.wait(nc.sync, "dve", max(t_prep, t_ln))
        kb.dinc(nc.sync.dma_start(out=dbg_xln[:, :], in_=a_xln[:, 0:T]), "stq")
        kb.dinc(nc.sync.dma_start(out=dbg_asel[:, :], in_=a_sel[0][:]), "stq")
        kb.dinc(nc.sync.dma_start(out=dbg_bsel[:, :], in_=b_sel[0][:]), "stq")

    wi_src = wiT.rearrange("(c p) h -> p c h", p=128)
    wo_src = woT.rearrange("(h p) c -> p h c", p=128)
    wi_tok = [0, 0]
    wo_tok = [0, 0]
    x_tok = [t_ln, t_ln]
    for tt in range(NTT):
        ts = tt * TTW
        kb.wait(nc.tensor, "dve", max(t_prep, t_ln))
        for k in range(TOPK):
            pb[4].wait_reuse(kb, nc.tensor)
            for cc in range(NC):
                i = nc.tensor.matmul(pb[4].ap[0:16, :],
                                     a_sel[k][:, cc * RANK:(cc + 1) * RANK],
                                     xln[cc][:, ts:ts + TTW],
                                     start=(cc == 0), stop=(cc == NC - 1))
            t_t1 = kb.inc(i, "pe")
            kb.wait(nc.vector, "pe", t_t1)
            t_ev = kb.inc(nc.vector.tensor_copy(t1_sb[k][:, :], pb[4].ap[0:16, :]), "dve")
            pb[4].consumed("dve", t_ev)
            if DEBUG_L1FFN and tt == 0 and k == 0:
                kb.wait(nc.sync, "dve", t_ev)
                kb.dinc(nc.sync.dma_start(out=dbg_t1[:, :], in_=t1_sb[k][:]), "stq")
        t_t1all = t_ev
        for hc in range(32):
            buf = hc % 2
            kb.wait(nc.gpsimd, "pe", wi_tok[buf])
            t_w = kb.dinc(nc.gpsimd.dma_start(
                out=wi_sb[buf][:].rearrange("p (c h) -> p c h", c=NC),
                in_=wi_src[:, :, hc * 128:(hc + 1) * 128]), "wq")
            bank = pb[4 + (hc % 2)]
            bank.wait_reuse(kb, nc.tensor)
            kb.wait(nc.tensor, "wq", t_w)
            for cc in range(NC):
                i = nc.tensor.matmul(bank.ap[:, :],
                                     wi_sb[buf][:, cc * 128:(cc + 1) * 128],
                                     xln[cc][:, ts:ts + TTW],
                                     start=(cc == 0), stop=(cc == NC - 1))
            wi_tok[buf] = t_mm = kb.inc(i, "pe")
            kb.wait(nc.scalar, "pe", t_mm)
            t_e = kb.inc(nc.scalar.activation(
                a_int[:, hc * TTW:(hc + 1) * TTW], bank.ap[:, :], AF.Copy), "act")
            bank.consumed("act", t_e)
            if DEBUG_L1FFN and tt == 0 and hc == 0:
                kb.wait(nc.sync, "act", t_e)
                kb.dinc(nc.sync.dma_start(out=dbg_int[:, :], in_=a_int[:, 0:TTW]), "stq")
        for k in range(TOPK):
            dst = a_h0 if k == 0 else a_int
            for hc in range(32):
                bank = pb[4 + (hc % 2)]
                bank.wait_reuse(kb, nc.tensor)
                kb.wait(nc.tensor, "dve", t_t1all)
                nc.tensor.matmul(bank.ap[:, :], b_sel[k][:, hc * 128:(hc + 1) * 128],
                                 t1_sb[k][:, :], start=True, stop=False)
                i = nc.tensor.matmul(bank.ap[:, :], C["id_bf"],
                                     a_int[:, hc * TTW:(hc + 1) * TTW],
                                     start=False, stop=True)
                t_mm = kb.inc(i, "pe")
                kb.wait(nc.scalar, "pe", t_mm)
                t_e = kb.inc(nc.scalar.activation(
                    dst[:, hc * TTW:(hc + 1) * TTW], bank.ap[:, :], AF.Relu), "act")
                bank.consumed("act", t_e)
                if DEBUG_L1FFN and tt == 0 and hc == 0 and k == 0:
                    kb.wait(nc.sync, "act", t_e)
                    kb.dinc(nc.sync.dma_start(out=dbg_h0[:, :], in_=a_h0[:, 0:TTW]), "stq")
        t_relu = kb.cnt["act"]
        for grp in ((0, 1, 2), (3, 4, 5), (6, 7)):
            for cc in grp:
                buf = cc % 2
                kb.wait(nc.gpsimd, "pe", wo_tok[buf])
                t_w = kb.dinc(nc.gpsimd.dma_start(
                    out=wo_sb[buf][:].rearrange("p (h c) -> p h c", h=32),
                    in_=wo_src[:, :, cc * 128:(cc + 1) * 128]), "wq")
                b0 = pb[2 * (cc - grp[0])]
                b1 = pb[2 * (cc - grp[0]) + 1]
                b0.wait_reuse(kb, nc.tensor)
                b1.wait_reuse(kb, nc.tensor)
                kb.wait(nc.tensor, "wq", t_w)
                kb.wait(nc.tensor, "act", t_relu)
                for hc in range(32):
                    nc.tensor.matmul(b0.ap[:, :], wo_sb[buf][:, hc * 128:(hc + 1) * 128],
                                     a_h0[:, hc * TTW:(hc + 1) * TTW],
                                     start=(hc == 0), stop=(hc == 31))
                for hc in range(32):
                    i = nc.tensor.matmul(b1.ap[:, :], wo_sb[buf][:, hc * 128:(hc + 1) * 128],
                                         a_int[:, hc * TTW:(hc + 1) * TTW],
                                         start=(hc == 0), stop=(hc == 31))
                wo_tok[buf] = t_mm = kb.inc(i, "pe")
                xbuf = cc % 2
                kb.wait(nc.sync, "dve", x_tok[xbuf])
                t_x = kb.dinc(nc.sync.dma_start(
                    out=x_sb[xbuf][:],
                    in_=x_in[cc * 128:(cc + 1) * 128, ts:ts + TTW]), "ldq")
                kb.wait(nc.vector, "ldq", t_x)
                kb.wait(nc.vector, "pe", t_mm)
                nc.vector.scalar_tensor_tensor(out=x_sb[xbuf][:], in0=b0.ap[:, :],
                                               scalar=tkv_col[0][:, :], in1=x_sb[xbuf][:],
                                               op0=ALU.mult, op1=ALU.add)
                i = nc.vector.scalar_tensor_tensor(out=x_sb[xbuf][:], in0=b1.ap[:, :],
                                                   scalar=tkv_col[1][:, :], in1=x_sb[xbuf][:],
                                                   op0=ALU.mult, op1=ALU.add)
                x_tok[xbuf] = t_u = kb.inc(i, "dve")
                b0.consumed("dve", t_u)
                b1.consumed("dve", t_u)
                kb.wait(nc.sync, "dve", t_u)
                kb.dinc(nc.sync.dma_start(
                    out=x_out[cc * 128:(cc + 1) * 128, ts:ts + TTW],
                    in_=x_sb[xbuf][:]), "stq")
    kb.wait(nc.sync, "stq")


def C_iota(kb, C):
    return C["iota_col_f32"]




def _rope_tables():
    inv = 1.0 / (10000.0 ** (np.arange(0, HD, 2, dtype=np.float32) / HD))
    t = np.arange(T, dtype=np.float32)
    freqs = t[:, None] * inv[None, :]
    emb = np.concatenate([freqs, freqs], axis=-1)
    cos = np.cos(emb)
    sin = np.sin(emb)
    return np.tile(cos.T, (2, 1)), np.tile(sin.T, (2, 1))


def _p_rot():
    p = np.zeros((128, 128), np.float32)
    for b in (0, 64):
        for j in range(32):
            p[b + 2 * j + 1, b + 2 * j] = -1.0
            p[b + 2 * j, b + 2 * j + 1] = 1.0
    return p


def emit_l0attn(kb, C, x_in, x_out, W, S):
    nc = kb.nc
    cos_d, sin_d, p_rot = C["cos_d"], C["sin_d"], C["p_rot"]
    qwT, kwT, vwT, owT = W["qwT"], W["kwT"], W["vwT"], W["owT"]
    qb_d, kbias_d, obr_d, vbr_d = W["qb"], W["kbias"], W["ob_row"], W["vb_row"]
    qn_d, kn_d = W["qn"], W["kn"]
    q_dram = S["q_dram"]
    k_dram = S["k_dram"]

    a_xln = kb.sb("a_xln", [128, NC * T], BF16)
    xln = [a_xln[:, c * T:(c + 1) * T] for c in range(NC)]
    VA_W = NH * 65
    v_attn = kb.sb("v_attn", [128, 32 * VA_W], BF16)
    w = ln_work_tiles(kb)
    warena = kb.sb("warena", [128, NC * D], BF16)
    wA = warena[:, 0:NC * D]
    qb_sb = kb.sb("qb_sb", [128, NC], F32)
    kb_sb = kb.sb("kb_sb", [128, NC], F32)
    qn_sb = kb.sb("qn_sb", [128, NC], F32)
    kn_sb = kb.sb("kn_sb", [128, NC], F32)
    obr_sb = kb.sb("obr_sb", [1, D], BF16)
    vbr_sb = kb.sb("vbr_sb", [1, D], BF16)
    qsb = kb.sb("qsb", [128, TTW], BF16)
    rtmp = kb.sb("rtmp", [128, TTW], BF16)
    rtmp2 = kb.sb("rtmp2", [128, TTW], BF16)
    cs_sb = kb.sb("cs_sb", [128, 2 * TTW], BF16)
    k_sb = kb.sb("k_sb", [128, 2048], BF16)
    qq = kb.sb("qq", [128, TTW], BF16)
    w_ring = kb.sb("w_ring", [128, 4 * TTW], BF16)
    pb = [Bank(kb.ps(f"pb{i}", [128, TTW])) for i in range(6)]
    out65 = w["scr"][0:65, 0:TTW]
    zr = w["scr"][0:1, TTW:2 * TTW]
    zr2 = w["scr"][0:1, 2 * TTW:3 * TTW]

    t_small = 0
    for dst, src in ((qb_sb, qb_d), (kb_sb, kbias_d), (qn_sb, qn_d), (kn_sb, kn_d)):
        t_small = kb.dinc(nc.sync.dma_start(out=dst[:], in_=src), "ldq")
    t_smallg = kb.dinc(nc.gpsimd.dma_start(out=obr_sb[:], in_=obr_d), "wq")
    t_smallg = kb.dinc(nc.gpsimd.dma_start(out=vbr_sb[:], in_=vbr_d), "wq")
    va = v_attn[:].rearrange("p (c h e) -> p c h e", c=32, h=NH)
    t_ones = kb.inc(nc.vector.memset(va[:, :, :, 64:65], 1.0), "dve")
    t_wv = kb.dinc(nc.gpsimd.dma_start(
        out=wA.rearrange("p (c j) -> p c j", c=NC),
        in_=vwT.rearrange("(c p) j -> p c j", p=128)), "wq")

    emit_ln(kb, C, x_in, w, xln, pb)
    t_ln = kb.cnt["dve"]

    for tc in range(32):
        tcs = tc * 128
        for half in range(2):
            bank = pb[half]
            bank.wait_reuse(kb, nc.tensor)
            if tc == 0 and half == 0:
                kb.wait(nc.tensor, "wq", t_wv)
                kb.wait(nc.tensor, "dve", max(t_ln, t_ones))
            for cc in range(NC):
                nc.tensor.matmul(bank.ap[:, :],
                                 xln[cc][:, tcs:tcs + 128],
                                 wA[:, cc * D + half * TTW:cc * D + half * TTW + TTW],
                                 start=(cc == 0), stop=False)
            i = nc.tensor.matmul(bank.ap[:, :], C["ones_row_bf"][0:1, 0:128],
                                 vbr_sb[0:1, half * TTW:half * TTW + TTW],
                                 start=False, stop=True)
            t_mm = kb.inc(i, "pe")
            kb.wait(nc.vector, "pe", t_mm)
            dstv = va[:, tc, half * NC:half * NC + NC, 0:64]
            i = nc.vector.tensor_copy(
                dstv, bank.ap[:, :].rearrange("p (h e) -> p h e", h=NC))
            t_ev = kb.inc(i, "dve")
            bank.consumed("dve", t_ev)
    t_vdone = kb.cnt["dve"]

    xf, xb, sq, stat = w["xf"], w["xb"], w["sq"], w["stat"]
    s1 = stat[:, TTW:2 * TTW]
    s2 = stat[:, 2 * TTW:3 * TTW]
    for tt in range(NTT):
        ts = tt * TTW
        last = None
        for cc in range(NC):
            last = nc.scalar.activation(sq[:, cc * TTW:(cc + 1) * TTW],
                                        xln[cc][:, ts:ts + TTW], AF.Square)
        t_sq = kb.inc(last, "act")
        pb[0].wait_reuse(kb, nc.tensor)
        kb.wait(nc.tensor, "act", t_sq)
        for cc in range(NC):
            i = nc.tensor.matmul(pb[0].ap[0:1, :], C["ones_col_bf"],
                                 sq[:, cc * TTW:(cc + 1) * TTW],
                                 start=(cc == 0), stop=(cc == NC - 1))
        t_mm = kb.inc(i, "pe")
        kb.wait(nc.vector, "pe", t_mm)
        t_v = kb.inc(nc.vector.tensor_scalar(s1, pb[0].ap[0:1, :], 1.0 / D, RMS_EPS,
                                             op0=ALU.mult, op1=ALU.add), "dve")
        pb[0].consumed("dve", t_v)
        kb.wait(nc.scalar, "dve", t_v)
        t_sd = kb.inc(nc.scalar.activation(s2, s1, AF.Sqrt), "act")
        kb.wait(nc.vector, "act", t_sd)
        t_rc = kb.inc(nc.vector.reciprocal(s1, s2), "dve")
        pb[2].wait_reuse(kb, nc.tensor)
        kb.wait(nc.tensor, "dve", t_rc)
        t_bc = kb.inc(nc.tensor.matmul(pb[2].ap[:, :], C["ones_row_f32"], s1,
                                       start=True, stop=True), "pe")
        kb.wait(nc.vector, "pe", t_bc)
        nc.vector.tensor_copy(xb[:, 0:TTW], pb[2].ap[:, :])
        for cc in range(NC):
            i = nc.vector.tensor_tensor(xln[cc][:, ts:ts + TTW],
                                        xln[cc][:, ts:ts + TTW], xb[:, 0:TTW],
                                        op=ALU.mult)
        t_rms = kb.inc(i, "dve")
        pb[2].consumed("dve", t_rms)

    wsrc = {"q": qwT, "k": kwT}
    wscale = {"q": qn_sb, "k": kn_sb}
    wbias = {"q": qb_sb, "k": kb_sb}
    wdram = {"q": q_dram, "k": k_dram}
    wtmp = [w_ring[:, 0:D], w_ring[:, D:2 * D]]
    t_st_prev = 0
    for name in ("q", "k"):
        src = wsrc[name].rearrange("(c p) j -> p c j", p=128)
        bias, dram = wbias[name], wdram[name]
        wtmp_tok = [0, 0]
        t_prior_pe = kb.cnt["pe"]
        t_wready = 0
        for cc in range(NC):
            buf = cc % 2
            kb.wait(nc.gpsimd, "dve", wtmp_tok[buf])
            t_w = kb.dinc(nc.gpsimd.dma_start(out=wtmp[buf][:, :], in_=src[:, cc, :]), "wq")
            kb.wait(nc.vector, "wq", t_w)
            if cc == 0:
                kb.wait(nc.vector, "pe", t_prior_pe)
            i = nc.vector.tensor_scalar(wA[:, cc * D:(cc + 1) * D],
                                        wtmp[buf][:, :], wscale[name][:, cc:cc + 1], None,
                                        op0=ALU.mult)
            wtmp_tok[buf] = t_wready = kb.inc(i, "dve")
        for tt in range(NTT):
            ts = tt * TTW
            kb.wait(nc.sync, "dve", t_st_prev if False else 0)
            t_cs = kb.dinc(nc.sync.dma_start(out=cs_sb[:, 0:TTW], in_=cos_d[:, ts:ts + TTW]), "ldq")
            t_cs = kb.dinc(nc.sync.dma_start(out=cs_sb[:, TTW:2 * TTW], in_=sin_d[:, ts:ts + TTW]), "ldq")
            for cc in range(NC):
                bank = pb[cc % 2]
                rbank = pb[2 + cc % 2]
                bank.wait_reuse(kb, nc.tensor)
                if tt == 0 and cc == 0:
                    kb.wait(nc.tensor, "dve", t_wready)
                for c2 in range(NC):
                    i = nc.tensor.matmul(bank.ap[:, :],
                                         wA[:, c2 * D + cc * 128:c2 * D + cc * 128 + 128],
                                         xln[c2][:, ts:ts + TTW],
                                         start=(c2 == 0), stop=(c2 == NC - 1))
                t_mm = kb.inc(i, "pe")
                kb.wait(nc.vector, "pe", t_mm)
                t_q = kb.inc(nc.vector.tensor_scalar(qsb[:], bank.ap[:, :],
                                                     bias[:, cc:cc + 1], None,
                                                     op0=ALU.add), "dve")
                bank.consumed("dve", t_q)
                rbank.wait_reuse(kb, nc.tensor)
                kb.wait(nc.tensor, "dve", t_q)
                t_rot = kb.inc(nc.tensor.matmul(rbank.ap[:, :], p_rot[:], qsb[:],
                                                start=True, stop=True), "pe")
                kb.wait(nc.vector, "pe", t_rot)
                kb.wait(nc.vector, "ldq", t_cs)
                kb.wait(nc.vector, "stq", t_st_prev)
                nc.vector.tensor_copy(rtmp[:], rbank.ap[:, :])
                nc.vector.tensor_tensor(rtmp2[:], qsb[:], cs_sb[:, 0:TTW], op=ALU.mult)
                nc.vector.tensor_tensor(rtmp[:], rtmp[:], cs_sb[:, TTW:2 * TTW], op=ALU.mult)
                t_r = kb.inc(nc.vector.tensor_tensor(rtmp2[:], rtmp2[:], rtmp[:],
                                                     op=ALU.add), "dve")
                rbank.consumed("dve", t_r)
                kb.wait(nc.sync, "dve", t_r)
                t_st_prev = kb.dinc(nc.sync.dma_start(
                    out=dram[cc * 128:(cc + 1) * 128, ts:ts + TTW], in_=rtmp2[:]), "stq")
    t_qk = kb.cnt["stq"]

    wslot_tok = [0, 0, 0, 0]
    t_arena_ok = kb.cnt["pe"]
    t_av = 0
    for h in range(NH):
        kb.wait(nc.sync, "stq", t_qk)
        kb.wait(nc.sync, "pe", t_av)
        t_k = kb.dinc(nc.sync.dma_start(out=k_sb[0:64, :],
                                        in_=k_dram[h * 64:h * 64 + 64, 0:2048]), "ldq")
        t_k = kb.dinc(nc.sync.dma_start(out=k_sb[64:128, :],
                                        in_=k_dram[h * 64:h * 64 + 64, 2048:4096]), "ldq")
        for qt in range(NTT):
            qs = qt * TTW
            kb.wait(nc.sync, "pe", t_av)
            t_qq = kb.dinc(nc.sync.dma_start(out=qq[0:64, :],
                                             in_=q_dram[h * 64:h * 64 + 64, qs:qs + TTW]), "ldq")
            t_qq = kb.dinc(nc.sync.dma_start(out=qq[64:128, :],
                                             in_=q_dram[h * 64:h * 64 + 64, qs:qs + TTW]), "ldq")
            kb.wait(nc.tensor, "ldq", t_qq)
            for pc in range(16):
                c0, c1 = pc, pc + 16
                sA = pb[pc % 2]
                sB = pb[2 + pc % 2]
                sA.wait_reuse(kb, nc.tensor)
                sB.wait_reuse(kb, nc.tensor)
                if pc == 0:
                    pb[4].wait_reuse(kb, nc.tensor)
                    pb[5].wait_reuse(kb, nc.tensor)
                nc.tensor.matmul(sA.ap[:, :], k_sb[0:64, c0 * 128:c0 * 128 + 128],
                                 qq[0:64, :], start=True, stop=True, tile_position=(0, 0))
                i = nc.tensor.matmul(sB.ap[:, :], k_sb[64:128, c0 * 128:c0 * 128 + 128],
                                     qq[64:128, :], start=True, stop=True, tile_position=(64, 0))
                t_sc = kb.inc(i, "pe")
                s0, s1_ = (2 * pc) % 4, (2 * pc + 1) % 4
                kb.wait(nc.scalar, "pe", max(t_sc, wslot_tok[s0], wslot_tok[s1_]))
                nc.scalar.activation(w_ring[:, s0 * TTW:(s0 + 1) * TTW], sA.ap[:, :],
                                     AF.Sigmoid, scale=0.125)
                t_sg = kb.inc(nc.scalar.activation(w_ring[:, s1_ * TTW:(s1_ + 1) * TTW],
                                                   sB.ap[:, :], AF.Sigmoid, scale=0.125), "act")
                sA.consumed("act", t_sg)
                sB.consumed("act", t_sg)
                kb.wait(nc.tensor, "act", t_sg)
                for ci, sl in ((c0, s0), (c1, s1_)):
                    vt = v_attn[:, ci * VA_W + h * 65:ci * VA_W + h * 65 + 65]
                    nc.tensor.matmul(pb[4].ap[0:65, :], vt[0:64, :],
                                     w_ring[0:64, sl * TTW:(sl + 1) * TTW],
                                     start=(pc == 0 and ci == c0), stop=(pc == 15 and ci == c1),
                                     tile_position=(0, 0))
                    i = nc.tensor.matmul(pb[5].ap[0:65, :], vt[64:128, :],
                                         w_ring[64:128, sl * TTW:(sl + 1) * TTW],
                                         start=(pc == 0 and ci == c0), stop=(pc == 15 and ci == c1),
                                         tile_position=(64, 0))
                t_av = kb.inc(i, "pe")
                wslot_tok[s0] = wslot_tok[s1_] = t_av
            kb.wait(nc.vector, "pe", t_av)
            nc.vector.tensor_copy(out65, pb[4].ap[0:65, :])
            t_o = kb.inc(nc.vector.tensor_tensor(out65, out65, pb[5].ap[0:65, :],
                                                 op=ALU.add), "dve")
            pb[4].consumed("dve", t_o)
            pb[5].consumed("dve", t_o)
            kb.wait(nc.vector, "dve", t_o)
            t_z = kb.inc(nc.vector.tensor_scalar(zr, out65[64:65, :], SIG_EPS, None,
                                                 op0=ALU.add), "dve")
            kb.wait(nc.vector, "dve", t_z)
            t_rc = kb.inc(nc.vector.reciprocal(zr2, zr), "dve")
            pb[0].wait_reuse(kb, nc.tensor)
            kb.wait(nc.tensor, "dve", t_rc)
            t_bc = kb.inc(nc.tensor.matmul(pb[0].ap[:, :], C["ones_row_f32"], zr2,
                                           start=True, stop=True), "pe")
            kb.wait(nc.vector, "pe", max(t_bc, t_arena_ok))
            dst = a_xln[(h % 2) * 64:(h % 2) * 64 + 64, (h // 2) * T + qs:(h // 2) * T + qs + TTW]
            t_at = kb.inc(nc.vector.tensor_tensor(dst, out65[0:64, 0:TTW], pb[0].ap[0:64, :],
                                                  op=ALU.mult), "dve")
            pb[0].consumed("dve", t_at)
    t_attn = kb.cnt["dve"]

    kb.wait(nc.gpsimd, "pe", kb.cnt["pe"])
    t_wo = kb.dinc(nc.gpsimd.dma_start(
        out=wA.rearrange("p (c j) -> p c j", c=NC),
        in_=owT.rearrange("(c p) j -> p c j", p=128)), "wq")
    x_sb = [xf[:, 0:TTW], xf[:, TTW:2 * TTW]]
    x_tok = [0, 0]
    for cc in range(NC):
        for tt in range(NTT):
            ts = tt * TTW
            bank = pb[tt % 2]
            bank.wait_reuse(kb, nc.tensor)
            if cc == 0 and tt == 0:
                kb.wait(nc.tensor, "wq", t_wo)
                kb.wait(nc.tensor, "dve", t_attn)
                kb.wait(nc.tensor, "wq", t_smallg)
            for c2 in range(NC):
                nc.tensor.matmul(bank.ap[:, :],
                                 wA[:, c2 * D + cc * 128:c2 * D + cc * 128 + 128],
                                 a_xln[:, c2 * T + ts:c2 * T + ts + TTW],
                                 start=(c2 == 0), stop=False)
            i = nc.tensor.matmul(bank.ap[:, :], obr_sb[0:1, cc * 128:cc * 128 + 128],
                                 C["ones_row512_bf"], start=False, stop=True)
            t_mm = kb.inc(i, "pe")
            xb_i = tt % 2
            kb.wait(nc.sync, "dve", x_tok[xb_i])
            t_x = kb.dinc(nc.sync.dma_start(
                out=x_sb[xb_i], in_=x_in[cc * 128:(cc + 1) * 128, ts:ts + TTW]), "ldq")
            kb.wait(nc.vector, "ldq", t_x)
            kb.wait(nc.vector, "pe", t_mm)
            i = nc.vector.tensor_tensor(x_sb[xb_i], x_sb[xb_i], bank.ap[:, :], op=ALU.add)
            x_tok[xb_i] = t_u = kb.inc(i, "dve")
            bank.consumed("dve", t_u)
            kb.wait(nc.sync, "dve", t_u)
            kb.dinc(nc.sync.dma_start(
                out=x_out[cc * 128:(cc + 1) * 128, ts:ts + TTW], in_=x_sb[xb_i]), "stq")

    kb.wait(nc.sync, "stq")


def emit_l1attn(kb, C, x_in, x_out, W, S, phase_limit=None):
    nc = kb.nc
    Wq, Wk, Wv, Wo = W["Wq"], W["Wk"], W["Wv"], W["Wo"]
    qb_d, kbias_d, obr_d, vbr_d = W["qb"], W["kbias"], W["ob_row"], W["vb_row"]
    q_dram = S["q_dram"]
    k_dram = S["k_dram"]

    a_xln = kb.sb("a_xln", [128, NC * T], BF16)
    xln = [a_xln[:, c * T:(c + 1) * T] for c in range(NC)]
    VA_W = NH * 65
    v_attn = kb.sb("v_attn", [128, 32 * VA_W], BF16)
    w = ln_work_tiles(kb)
    warena = kb.sb("warena", [128, NC * D], BF16)
    wA = warena[:, 0:NC * D]
    qb_sb = kb.sb("qb_sb", [128, NC], F32)
    kb_sb = kb.sb("kb_sb", [128, NC], F32)
    obr_sb = kb.sb("obr_sb", [1, D], BF16)
    vbr_sb = kb.sb("vbr_sb", [1, D], BF16)
    qsb = kb.sb("qsb", [128, TTW], BF16)
    k_sb = kb.sb("k_sb", [64, 8 * 128], BF16)
    kcol0 = kb.sb("kcol0", [64, 1], BF16)
    qq = kb.sb("qq", [64, TTW], BF16)
    w_ring = kb.sb("w_ring", [128, 4 * TTW], BF16)
    w0_sb = kb.sb("w0_sb", [1, TTW], BF16)
    wq0_sb = kb.sb("wq0_sb", [128, 32], BF16)
    qmax = kb.sb("qmax", [1, NTT], F32)
    kmax = kb.sb("kmax", [1, 1], F32)
    mrow = kb.sb("mrow", [1, NTT], F32)
    mbias = kb.sb("mbias", [128, 1], F32)

    pb = [Bank(kb.ps(f"pb{i}", [128, TTW])) for i in range(6)]
    out65 = w["scr"][0:65, 0:TTW]
    zr = w["scr"][0:1, TTW:2 * TTW]
    zr2 = w["scr"][0:1, 2 * TTW:3 * TTW]

    t_small = 0
    for dst, src in ((qb_sb, qb_d), (kb_sb, kbias_d)):
        t_small = kb.dinc(nc.sync.dma_start(out=dst[:], in_=src), "ldq")
    t_smallg = kb.dinc(nc.gpsimd.dma_start(out=obr_sb[:], in_=obr_d), "wq")
    t_smallg = kb.dinc(nc.gpsimd.dma_start(out=vbr_sb[:], in_=vbr_d), "wq")
    va = v_attn[:].rearrange("p (c h e) -> p c h e", c=32, h=NH)
    t_ones = kb.inc(nc.vector.memset(va[:, :, :, 64:65], 1.0), "dve")
    t_wv = kb.dinc(nc.gpsimd.dma_start(
        out=wA.rearrange("p (c j) -> p c j", c=NC),
        in_=Wv.rearrange("(c p) j -> p c j", p=128)), "wq")

    emit_ln(kb, C, x_in, w, xln, pb)
    t_ln = kb.cnt["dve"]
    if phase_limit == "ln":
        kb.wait(nc.sync, "ldq")
        return

    for tc in range(32):
        tcs = tc * 128
        for half in range(2):
            bank = pb[half]
            bank.wait_reuse(kb, nc.tensor)
            if tc == 0 and half == 0:
                kb.wait(nc.tensor, "wq", t_wv)
                kb.wait(nc.tensor, "dve", max(t_ln, t_ones))
            for cc in range(NC):
                nc.tensor.matmul(bank.ap[:, :],
                                 xln[cc][:, tcs:tcs + 128],
                                 wA[:, cc * D + half * TTW:cc * D + half * TTW + TTW],
                                 start=(cc == 0), stop=False)
            i = nc.tensor.matmul(bank.ap[:, :], C["ones_row_bf"][0:1, 0:128],
                                 vbr_sb[0:1, half * TTW:half * TTW + TTW],
                                 start=False, stop=True)
            t_mm = kb.inc(i, "pe")
            kb.wait(nc.vector, "pe", t_mm)
            dstv = va[:, tc, half * NC:half * NC + NC, 0:64]
            i = nc.vector.tensor_copy(
                dstv, bank.ap[:, :].rearrange("p (h e) -> p h e", h=NC))
            t_ev = kb.inc(i, "dve")
            bank.consumed("dve", t_ev)
    if phase_limit == "vproj":
        kb.wait(nc.vector, "dve")
        kb.wait(nc.sync, "ldq")
        return

    wsrc = {"q": Wq, "k": Wk}
    wbias = {"q": qb_sb, "k": kb_sb}
    wdram = {"q": q_dram, "k": k_dram}
    t_st_prev = 0
    for name in ("q", "k"):
        t_prior_pe = kb.cnt["pe"]
        kb.wait(nc.gpsimd, "pe", t_prior_pe)
        t_w = kb.dinc(nc.gpsimd.dma_start(
            out=wA.rearrange("p (c j) -> p c j", c=NC),
            in_=wsrc[name].rearrange("(c p) j -> p c j", p=128)), "wq")
        bias, dram = wbias[name], wdram[name]
        for tt in range(NTT):
            ts = tt * TTW
            for cc in range(NC):
                bank = pb[cc % 2]
                bank.wait_reuse(kb, nc.tensor)
                if tt == 0 and cc == 0:
                    kb.wait(nc.tensor, "wq", t_w)
                for c2 in range(NC):
                    i = nc.tensor.matmul(bank.ap[:, :],
                                         wA[:, c2 * D + cc * 128:c2 * D + cc * 128 + 128],
                                         xln[c2][:, ts:ts + TTW],
                                         start=(c2 == 0), stop=(c2 == NC - 1))
                t_mm = kb.inc(i, "pe")
                kb.wait(nc.vector, "pe", t_mm)
                kb.wait(nc.vector, "stq", t_st_prev)
                t_q = kb.inc(nc.vector.tensor_scalar(qsb[:], bank.ap[:, :],
                                                     bias[:, cc:cc + 1], None,
                                                     op0=ALU.add), "dve")
                bank.consumed("dve", t_q)
                kb.wait(nc.sync, "dve", t_q)
                t_st_prev = kb.dinc(nc.sync.dma_start(
                    out=dram[cc * 128:(cc + 1) * 128, ts:ts + TTW], in_=qsb[:]), "stq")
    t_qk = kb.cnt["stq"]
    if phase_limit == "qkproj":
        kb.wait(nc.sync, "stq")
        return

    wslot_tok = [0, 0, 0, 0]
    t_arena_ok = kb.cnt["pe"]
    t_av = 0
    t_kf_pe = 0
    kfull = warena[0:64, 0:2 * T]
    for h in range(NH):
        hs = h * 64
        kb.wait(nc.sync, "stq", t_qk)
        kb.wait(nc.sync, "pe", max(t_arena_ok, t_kf_pe))
        t_qh = kb.dinc(nc.sync.dma_start(out=kfull[:, 0:T], in_=q_dram[hs:hs + 64, :]), "ldq")
        if phase_limit == "cut1" and h == 0:
            kb.wait(nc.sync, "ldq")
            return
        kb.wait(nc.scalar, "ldq", t_qh)
        t_sqq = kb.inc(nc.scalar.activation(kfull[:, T:2 * T], kfull[:, 0:T], AF.Square), "act")
        if phase_limit == "cut2" and h == 0:
            kb.wait(nc.sync, "act")
            kb.wait(nc.sync, "ldq")
            return
        kb.wait(nc.tensor, "act", t_sqq)
        t_nq = 0
        nrm_row = w["scr"][0:1, 0:TTW]
        for qt in range(NTT):
            pb[0].wait_reuse(kb, nc.tensor)
            i = nc.tensor.matmul(pb[0].ap[0:1, :],
                                 C["ones_col_bf"][0:64, :],
                                 kfull[:, T + qt * TTW:T + (qt + 1) * TTW],
                                 start=True, stop=True)
            t_nq = kb.inc(i, "pe")
            kb.wait(nc.vector, "pe", t_nq)
            t_cp = kb.inc(nc.vector.tensor_copy(nrm_row, pb[0].ap[0:1, :]), "dve")
            pb[0].consumed("dve", t_cp)
            kb.wait(nc.vector, "dve", t_cp)
            t_rm = kb.inc(nc.vector.reduce_max(qmax[0:1, qt:qt + 1], nrm_row,
                                               axis=AX.X), "dve")
        if phase_limit == "cut3" and h == 0:
            kb.wait(nc.sync, "dve")
            kb.wait(nc.sync, "ldq")
            return
        kb.wait(nc.sync, "act", t_sqq)
        t_kh = kb.dinc(nc.sync.dma_start(out=kfull[:, 0:T], in_=k_dram[hs:hs + 64, :]), "ldq")
        kb.wait(nc.scalar, "ldq", t_kh)
        kb.wait(nc.scalar, "pe", t_nq)
        t_sqk = kb.inc(nc.scalar.activation(kfull[:, T:2 * T], kfull[:, 0:T], AF.Square), "act")
        kb.wait(nc.tensor, "act", t_sqk)
        for qt in range(NTT):
            pb[0].wait_reuse(kb, nc.tensor)
            i = nc.tensor.matmul(pb[0].ap[0:1, :], C["ones_col_bf"][0:64, :],
                                 kfull[:, T + qt * TTW:T + (qt + 1) * TTW],
                                 start=True, stop=True)
            t_nk = kb.inc(i, "pe")
            kb.wait(nc.vector, "pe", t_nk)
            t_cp = kb.inc(nc.vector.tensor_copy(nrm_row, pb[0].ap[0:1, :]), "dve")
            pb[0].consumed("dve", t_cp)
            kb.wait(nc.vector, "dve", t_cp)
            t_rm = kb.inc(nc.vector.reduce_max(mrow[0:1, qt:qt + 1], nrm_row,
                                               axis=AX.X), "dve")
        kb.wait(nc.vector, "dve", kb.cnt["dve"])
        t_km = kb.inc(nc.vector.reduce_max(kmax[0:1, :], mrow[0:1, :], axis=AX.X), "dve")
        kb.wait(nc.vector, "dve", t_km)
        t_m1 = kb.inc(nc.vector.tensor_scalar(mrow[0:1, 0:NTT], qmax[0:1, :], kmax[0:1, 0:1],
                                              None, op0=ALU.mult), "dve")
        kb.wait(nc.scalar, "dve", t_m1)
        nc.scalar.activation(mrow[0:1, 0:NTT], mrow[0:1, 0:NTT], AF.Ln)
        t_m2 = kb.inc(nc.scalar.activation(mrow[0:1, 0:NTT], mrow[0:1, 0:NTT], AF.Exp,
                                           scale=0.5), "act")
        kb.wait(nc.vector, "act", t_m2)
        t_m3 = kb.inc(nc.vector.tensor_scalar(mrow[0:1, 0:NTT], mrow[0:1, 0:NTT], -0.125,
                                              None, op0=ALU.mult), "dve")
        if phase_limit == "norms0" and h == 0:
            return
        if phase_limit == "norms" and h == 1:
            return
        t_k0 = kb.dinc(nc.sync.dma_start(out=kcol0[:], in_=k_dram[hs:hs + 64, 0:1], allow_slow_non_contiguous=True), "ldq")
        for qt in range(NTT):
            qs = qt * TTW
            lo = max(0, 4 * qt - 2)
            hi = min(31, 4 * qt + 5)
            nblk = hi - lo + 1
            kb.wait(nc.sync, "pe", t_av)
            t_qq = kb.dinc(nc.sync.dma_start(out=qq[:, :],
                                             in_=q_dram[hs:hs + 64, qs:qs + TTW]), "ldq")
            t_kb = kb.dinc(nc.sync.dma_start(out=k_sb[:, 0:nblk * 128],
                                             in_=k_dram[hs:hs + 64, lo * 128:(hi + 1) * 128]), "ldq")
            pb[3].wait_reuse(kb, nc.tensor)
            kb.wait(nc.tensor, "dve", t_m3)
            t_mb = kb.inc(nc.tensor.matmul(pb[3].ap[:, 0:1], C["ones_row_f32"],
                                           mrow[0:1, qt:qt + 1], start=True, stop=True), "pe")
            kb.wait(nc.vector, "pe", t_mb)
            kb.wait(nc.vector, "act")
            t_mb2 = kb.inc(nc.vector.tensor_copy(mbias[:], pb[3].ap[:, 0:1]), "dve")
            pb[3].consumed("dve", t_mb2)
            kb.wait(nc.tensor, "ldq", t_kb)
            n_contrib = nblk + 1 + (30 if qt == 0 else 0)
            ci = 0
            pb[4].wait_reuse(kb, nc.tensor)
            for bi in range(nblk):
                kbk = lo + bi
                sA = pb[bi % 2]
                sA.wait_reuse(kb, nc.tensor)
                i = nc.tensor.matmul(sA.ap[:, :], k_sb[:, bi * 128:bi * 128 + 128],
                                     qq[:, :], start=True, stop=True)
                t_sc = kb.inc(i, "pe")
                sl = bi % 4
                kb.wait(nc.scalar, "pe", max(t_sc, wslot_tok[sl]))
                kb.wait(nc.scalar, "dve", t_mb2)
                t_sg = kb.inc(nc.scalar.activation(w_ring[:, sl * TTW:(sl + 1) * TTW],
                                                   sA.ap[:, :], AF.Exp, bias=mbias[:, :],
                                                   scale=0.125), "act")
                sA.consumed("act", t_sg)
                dmin = qs - (128 * kbk + 127)
                dmax = qs + 511 - 128 * kbk
                wslice = w_ring[:, sl * TTW:(sl + 1) * TTW]
                masks = []
                if dmax > WIN:
                    masks.append(([[-1, TTW]], WIN - qs + 128 * kbk, 1))
                if dmin < -WIN:
                    masks.append(([[1, TTW]], qs - 128 * kbk + WIN, -1))
                if kbk == 0:
                    masks.append(([[0, TTW]], -1, 1))
                t_pool = None
                if masks:
                    kb.wait(nc.gpsimd, "act", t_sg)
                    for pat, base, cm in masks:
                        if t_pool is not None:
                            kb.wait(nc.gpsimd, "pool", t_pool)
                        t_pool = kb.inc(nc.gpsimd.affine_select(
                            wslice, wslice, pat, ALU.is_ge, 0.0,
                            base=base, channel_multiplier=cm), "pool")
                if t_pool is not None:
                    kb.wait(nc.tensor, "pool", t_pool)
                vt = v_attn[:, kbk * VA_W + h * 65:kbk * VA_W + h * 65 + 65]
                i = nc.tensor.matmul(pb[4].ap[0:65, :], vt[:, :],
                                     w_ring[:, sl * TTW:(sl + 1) * TTW],
                                     start=(ci == 0), stop=(ci == n_contrib - 1))
                t_av = kb.inc(i, "pe")
                wslot_tok[sl] = t_av
                ci += 1
            sA = pb[2]
            sA.wait_reuse(kb, nc.tensor)
            kb.wait(nc.tensor, "ldq", t_k0)
            i = nc.tensor.matmul(sA.ap[0:1, :], kcol0[:, :], qq[:, :], start=True, stop=True)
            t_s0 = kb.inc(i, "pe")
            kb.wait(nc.scalar, "pe", max(t_s0, wslot_tok[0]))
            t_w0 = kb.inc(nc.scalar.activation(w0_sb[:], sA.ap[0:1, :], AF.Exp,
                                               bias=mbias[0:1, :], scale=0.125), "act")
            sA.consumed("act", t_w0)
            kb.wait(nc.tensor, "act", t_w0)
            vt = v_attn[:, 0 * VA_W + h * 65:0 * VA_W + h * 65 + 65]
            i = nc.tensor.matmul(pb[4].ap[0:65, :], vt[0:1, :], w0_sb[:, :],
                                 start=False, stop=(ci == n_contrib - 1))
            t_av = kb.inc(i, "pe")
            ci += 1
            if qt == 0:
                sB = pb[5]
                sB.wait_reuse(kb, nc.tensor)
                for kbk in range(2, 32):
                    i = nc.tensor.matmul(sB.ap[:, kbk:kbk + 1],
                                         kfull[:, kbk * 128:kbk * 128 + 128],
                                         qq[:, 0:1], start=True, stop=True)
                t_q0 = kb.inc(i, "pe")
                kb.wait(nc.scalar, "pe", t_q0)
                t_wq0 = kb.inc(nc.scalar.activation(wq0_sb[:, 2:32], sB.ap[:, 2:32], AF.Exp,
                                                    bias=mbias[:, :], scale=0.125), "act")
                sB.consumed("act", t_wq0)
                kb.wait(nc.vector, "act", t_wq0)
                t_wz = kb.inc(nc.vector.memset(wq0_sb[0:1, 2:3], 0.0), "dve")
                kb.wait(nc.tensor, "dve", t_wz)
                for kbk in range(2, 32):
                    vt = v_attn[:, kbk * VA_W + h * 65:kbk * VA_W + h * 65 + 65]
                    i = nc.tensor.matmul(pb[4].ap[0:65, 0:1], vt[:, :],
                                         wq0_sb[:, kbk:kbk + 1],
                                         start=False, stop=(ci == n_contrib - 1))
                    t_av = kb.inc(i, "pe")
                    ci += 1
                t_kf_pe = t_av
            assert ci == n_contrib
            kb.wait(nc.vector, "pe", t_av)
            t_o = kb.inc(nc.vector.tensor_copy(out65, pb[4].ap[0:65, :]), "dve")
            pb[4].consumed("dve", t_o)
            kb.wait(nc.vector, "dve", t_o)
            t_rc = kb.inc(nc.vector.reciprocal(zr2, out65[64:65, 0:TTW]), "dve")
            pb[0].wait_reuse(kb, nc.tensor)
            kb.wait(nc.tensor, "dve", t_rc)
            t_bc = kb.inc(nc.tensor.matmul(pb[0].ap[:, :], C["ones_row_f32"], zr2,
                                           start=True, stop=True), "pe")
            kb.wait(nc.vector, "pe", max(t_bc, t_arena_ok))
            dst = a_xln[(h % 2) * 64:(h % 2) * 64 + 64, (h // 2) * T + qs:(h // 2) * T + qs + TTW]
            t_at = kb.inc(nc.vector.tensor_tensor(dst, out65[0:64, 0:TTW], pb[0].ap[0:64, :],
                                                  op=ALU.mult), "dve")
            pb[0].consumed("dve", t_at)
    t_attn = kb.cnt["dve"]
    if phase_limit == "attn":
        return

    kb.wait(nc.gpsimd, "pe", kb.cnt["pe"])
    t_wo = kb.dinc(nc.gpsimd.dma_start(
        out=wA.rearrange("p (c j) -> p c j", c=NC),
        in_=Wo.rearrange("(c p) j -> p c j", p=128)), "wq")
    xf = w["xf"]
    x_sb = [xf[:, 0:TTW], xf[:, TTW:2 * TTW]]
    x_tok = [0, 0]
    for cc in range(NC):
        for tt in range(NTT):
            ts = tt * TTW
            bank = pb[tt % 2]
            bank.wait_reuse(kb, nc.tensor)
            if cc == 0 and tt == 0:
                kb.wait(nc.tensor, "wq", t_wo)
                kb.wait(nc.tensor, "dve", t_attn)
                kb.wait(nc.tensor, "wq", t_smallg)
            for c2 in range(NC):
                nc.tensor.matmul(bank.ap[:, :],
                                 wA[:, c2 * D + cc * 128:c2 * D + cc * 128 + 128],
                                 a_xln[:, c2 * T + ts:c2 * T + ts + TTW],
                                 start=(c2 == 0), stop=False)
            i = nc.tensor.matmul(bank.ap[:, :], obr_sb[0:1, cc * 128:cc * 128 + 128],
                                 C["ones_row512_bf"], start=False, stop=True)
            t_mm = kb.inc(i, "pe")
            xb_i = tt % 2
            kb.wait(nc.sync, "dve", x_tok[xb_i])
            t_x = kb.dinc(nc.sync.dma_start(
                out=x_sb[xb_i], in_=x_in[cc * 128:(cc + 1) * 128, ts:ts + TTW]), "ldq")
            kb.wait(nc.vector, "ldq", t_x)
            kb.wait(nc.vector, "pe", t_mm)
            i = nc.vector.tensor_tensor(x_sb[xb_i], x_sb[xb_i], bank.ap[:, :], op=ALU.add)
            x_tok[xb_i] = t_u = kb.inc(i, "dve")
            bank.consumed("dve", t_u)
            kb.wait(nc.sync, "dve", t_u)
            kb.dinc(nc.sync.dma_start(
                out=x_out[cc * 128:(cc + 1) * 128, ts:ts + TTW], in_=x_sb[xb_i]), "stq")
    kb.wait(nc.sync, "stq")


def l1attn_weights(inputs):
    return {
        "Wq": np.ascontiguousarray(inputs["L1_attn_Wq"]),
        "Wk": np.ascontiguousarray(inputs["L1_attn_Wk"]),
        "Wv": np.ascontiguousarray(inputs["L1_attn_Wv"]),
        "Wo": np.ascontiguousarray(inputs["L1_attn_Wo"]),
        "qb": np.ascontiguousarray(inputs["L1_attn_bq"].reshape(NC, 128).T),
        "kbias": np.ascontiguousarray(inputs["L1_attn_bk"].reshape(NC, 128).T),
        "ob_row": np.ascontiguousarray(inputs["L1_attn_bo"].reshape(1, D)),
        "vb_row": np.ascontiguousarray(inputs["L1_attn_bv"].reshape(1, D)),
    }


def l0attn_weights(inputs):
    return {
        "qwT": np.ascontiguousarray(inputs["L0_attn_q_w"].T),
        "kwT": np.ascontiguousarray(inputs["L0_attn_k_w"].T),
        "vwT": np.ascontiguousarray(inputs["L0_attn_v_w"].T),
        "owT": np.ascontiguousarray(inputs["L0_attn_o_w"].T),
        "qb": np.ascontiguousarray(inputs["L0_attn_q_b"].reshape(NC, 128).T),
        "kbias": np.ascontiguousarray(inputs["L0_attn_k_b"].reshape(NC, 128).T),
        "ob_row": np.ascontiguousarray(inputs["L0_attn_o_b"].reshape(1, D)),
        "vb_row": np.ascontiguousarray(inputs["L0_attn_v_b"].reshape(1, D)),
        "qn": np.ascontiguousarray(inputs["L0_attn_qn_scale"].reshape(NC, 128).T),
        "kn": np.ascontiguousarray(inputs["L0_attn_kn_scale"].reshape(NC, 128).T),
    }


def _l0ffn_consts():
    iota512 = np.broadcast_to(np.arange(TTW, dtype=np.float32), (128, TTW))
    idxw = np.zeros((128, 32 * 3), np.float32)
    for tb in range(32):
        idxw[:, 3 * tb] = 128.0 * tb
        idxw[:, 3 * tb + 1] = np.arange(128)
        idxw[:, 3 * tb + 2] = 1.0
    grp16 = np.zeros((128, NE), np.float32)
    for p in range(128):
        grp16[p, p // 16] = 1.0
    sel8 = np.zeros((NE, 128), np.float32)
    for p in range(128):
        sel8[p // 16, p] = 1.0
    oh8 = np.zeros((NE, NE * 128), np.float32)
    for e in range(NE):
        oh8[e, e * 128:(e + 1) * 128] = 1.0
    mrg3 = np.array([[1.0], [1.0], [-4096.0]], np.float32)
    return iota512, idxw, grp16, sel8, oh8, mrg3


DEBUG_L0FFN = False


def emit_l0ffn(kb, C, x_in, x_out, W, S):
    nc = kb.nc
    t_cl = 0
    for name, dap, shape, dt in C["_l0f_spec"]:
        tile = kb.sb("c_" + name, shape, dt)
        t_cl = kb.dinc(nc.sync.dma_start(out=tile[:], in_=dap), "ldq")
        C[name] = tile[:]
    for eng in (nc.tensor, nc.vector, nc.scalar, nc.gpsimd):
        kb.wait(eng, "ldq", t_cl)
    gate_w, w1_d, w2_d = W["gate_w"], W["w1"], W["w2"]
    xtm_dram = S["xtm_dram"]
    pdram = S["pdram"]
    dense_dram = S["dense_dram"]
    if DEBUG_L0FFN:
        dbg_gsb = kb.dout("dbg_gsb", [128, NC * NE], F32)
        dbg_sc0 = kb.dout("dbg_sc0", [8, TTW], F32)
        dbg_probs = kb.dout("dbg_probs", [NE, T], F32)
        dbg_mask = kb.dout("dbg_mask", [NE, T], BF16)
        dbg_idx = kb.dout("dbg_idx", [128, NE * 8], I32)
        dbg_lo = kb.dout("dbg_lo", [8, 8], F32)
        dbg_pm = kb.dout("dbg_pm", [128, 256], F32)
        dbg_bis = kb.dout("dbg_bis", [8, 80], F32)
        dbg_bis_sb = kb.sb("dbg_bis_sb", [8, 80], F32)

    a_w2 = kb.sb("a_w2", [128, 32 * D], BF16)
    xln = [a_w2[:, c * T:(c + 1) * T] for c in range(NC)]
    a_h = kb.sb("a_h", [128, 32 * TTW], BF16)
    xb_t = a_h[:, 0:NC * TTW]
    sq_t = a_h[:, NC * TTW:2 * NC * TTW]
    scr = kb.sb("ln_scr", [128, 3 * TTW], F32)
    sqf = kb.sb("ln_sqf", [128, 2 * TTW], F32)
    xf = kb.sb("ln_xf", [128, NC * TTW], F32)
    w = {"xf": xf, "xb": xb_t, "sq": sq_t, "stat": scr[0:1, :], "scr": scr, "sqf": sqf}
    probs_t = kb.sb("probs_t", [8, T], F32)
    probs = probs_t[:]
    pfxB_t = kb.sb("pfxB_t", [8, T], F32)
    pfxA = probs_t[:]
    pfxB = pfxB_t[:]

    gate_sb = kb.sb("gate_sb", [128, NC * NE], F32)
    ones8_f = kb.sb("ones8_f", [128, 1], F32)
    pm = kb.sb("pm", [128, 256], F32)
    tmpc = kb.sb("tmpc", [128, 256], BF16)
    cnt_col = kb.sb("cnt_col", [128, 1], F32)
    bis = kb.sb("bis", [8, 8], F32)
    b_lo = kb.sb("b_lo", [8, 1], F32)
    b_hi = kb.sb("b_hi", [8, 1], F32)
    b_mid = kb.sb("b_mid", [8, 1], F32)
    b_cnt = kb.sb("b_cnt", [8, 1], F32)
    b_mf = kb.sb("b_mf", [8, 1], F32)
    b_tA = kb.sb("b_tA", [8, 1], F32)
    b_tB = kb.sb("b_tB", [8, 1], F32)
    b_cap = kb.sb("b_cap", [8, 1], F32)
    thr128 = kb.sb("thr128", [128, 1], F32)
    mask_bf = a_h[0:8, 0:T]
    pfxT = kb.sb("pfxT", [128, 32 * 8], F32)
    pfxT2 = kb.sb("pfxT2", [128, 32 * 8], F32)
    maskT = kb.sb("maskT", [128, 32 * 8], F32)
    ptile = kb.sb("ptile", [128, TTW], BF16)
    idx3 = scr[0:3, 0:TTW]
    idxrow = scr[0:1, TTW:2 * TTW]
    idx_all = kb.sb("idx_all", [128, NE * 8], I32)
    x_tm = [kb.sb(f"x_tm{i}", [128, D], BF16) for i in range(2)]
    x_fm = kb.sb("x_fm", [128, NC * D], BF16)
    w1_sb = [kb.sb(f"w1_{i}", [128, NC * 128], BF16) for i in range(2)]
    y_arena = kb.sb("y_arena", [128, NC * TTW], BF16)
    y_tm = [kb.sb(f"y_tm{i}", [128, D], BF16) for i in range(2)]
    w["xtm_sb"] = y_tm[1][:]
    p_col = [kb.sb(f"p_col{i}", [128, 1], F32) for i in range(2)]

    pb = [Bank(kb.ps(f"pb{i}", [128, TTW])) for i in range(5)]
    pbf = [Bank(kb.ps(f"pbf{i}", [128, 128], BF16)) for i in range(2)]
    pmisc = Bank(kb.ps("pmisc", [128, 128]))
    bnd_reg = kb.bnd_reg

    t_z = kb.inc(nc.vector.memset(y_tm[0][:], 0.0), "dve")
    kb.wait(nc.sync, "dve", t_z)
    t_zs = 0
    for tc in range(32):
        t_zs = kb.dinc(nc.sync.dma_start(out=dense_dram[tc * 128:(tc + 1) * 128, :],
                                         in_=y_tm[0][:]), "stq")
    t_gw = kb.dinc(nc.sync.dma_start(
        out=gate_sb[:].rearrange("p (c e) -> p c e", c=NC),
        in_=gate_w.rearrange("(c p) e -> p c e", p=128)), "ldq")
    t_o8 = kb.inc(nc.vector.memset(ones8_f[:], 1.0), "dve")

    def gate_cb(tt, ts, t_ap):
        bank = pb[4]
        bank.wait_reuse(kb, nc.tensor)
        if tt == 0:
            kb.wait(nc.tensor, "ldq", t_gw)
            kb.wait(nc.tensor, "dve", t_o8)
        kb.wait(nc.tensor, "dve", t_ap)
        for cc in range(NC):
            i = nc.tensor.matmul(bank.ap[0:8, :], gate_sb[:, cc * NE:(cc + 1) * NE],
                                 xf[:, cc * TTW:(cc + 1) * TTW],
                                 start=(cc == 0), stop=(cc == NC - 1))
        t_mm = kb.inc(i, "pe")
        if DEBUG_L0FFN and tt == 0:
            kb.wait(nc.vector, "pe", t_mm)
            t_d = kb.inc(nc.vector.tensor_copy(scr[0:8, 2 * TTW:3 * TTW], bank.ap[0:8, :]), "dve")
            kb.wait(nc.sync, "dve", t_d)
            kb.dinc(nc.sync.dma_start(out=dbg_sc0[:, :], in_=scr[0:8, 2 * TTW:3 * TTW]), "stq")
            kb.dinc(nc.sync.dma_start(out=dbg_gsb[:, :], in_=gate_sb[:]), "stq")
        kb.wait(nc.vector, "pe", t_mm)
        t_sce = kb.inc(nc.vector.tensor_copy(sqf[0:8, 0:TTW], bank.ap[0:8, :]), "dve")
        bank.consumed("dve", t_sce)
        kb.wait(nc.scalar, "dve", t_sce)
        t_e = kb.inc(nc.scalar.activation(probs[:, ts:ts + TTW], sqf[0:8, 0:TTW],
                                          AF.Exp), "act")
        kb.wait(nc.tensor, "act", t_e)
        t_s = kb.inc(nc.tensor.matmul(bank.ap[0:1, :], ones8_f[0:8, :],
                                      probs[:, ts:ts + TTW], start=True, stop=True), "pe")
        kb.wait(nc.vector, "pe", t_s)
        t_c = kb.inc(nc.vector.tensor_copy(scr[0:1, 0:TTW], bank.ap[0:1, :]), "dve")
        kb.wait(nc.vector, "dve", t_c)
        t_r = kb.inc(nc.vector.reciprocal(scr[0:1, TTW:2 * TTW], scr[0:1, 0:TTW]), "dve")
        kb.wait(nc.tensor, "dve", t_r)
        t_b = kb.inc(nc.tensor.matmul(bank.ap[0:8, :], C["ones_row_f32"][0:1, 0:8],
                                      scr[0:1, TTW:2 * TTW], start=True, stop=True), "pe")
        kb.wait(nc.vector, "pe", t_b)
        t_p = kb.inc(nc.vector.tensor_tensor(probs[:, ts:ts + TTW], probs[:, ts:ts + TTW],
                                             bank.ap[0:8, :], op=ALU.mult), "dve")
        bank.consumed("dve", t_p)

    emit_ln(kb, C, x_in, w, xln, pb, pbf=pbf, xtm_dram=xtm_dram, per_tt_cb=gate_cb, f32_stats=True)
    t_probs = kb.cnt["dve"]
    t_xtm_ldq = kb.cnt["ldq"]

    kb.wait(nc.sync, "dve", t_probs)
    t_pd = kb.dinc(nc.sync.dma_start(out=pdram[:, :], in_=probs), "stq")
    kb.wait(nc.sync, "stq", t_pd)
    t_pm = kb.dinc(nc.sync.dma_start(
        out=pm[:], in_=pdram.rearrange("e (g i) -> (e g) i", g=16)), "ldq")
    lo = b_lo[:]
    hi = b_hi[:]
    mid = b_mid[:]
    cnt8 = b_cnt[:]
    capcol = b_cap[:]
    nc.vector.memset(lo[:, :] if False else lo, 0.0)
    nc.vector.memset(hi, 1.0)
    nc.vector.memset(capcol, float(CAP) - 0.5)
    t_mid = kb.inc(nc.vector.memset(mid, 0.5), "dve")
    for it in range(40):
        pmisc.wait_reuse(kb, nc.tensor)
        kb.wait(nc.tensor, "dve", t_mid)
        t_bc = kb.inc(nc.tensor.matmul(pmisc.ap[:, 0:1], C["sel8"], mid,
                                       start=True, stop=True), "pe")
        kb.wait(nc.vector, "pe", t_bc)
        t_th = kb.inc(nc.vector.tensor_copy(thr128[:], pmisc.ap[:, 0:1]), "dve")
        pmisc.consumed("dve", t_th)
        kb.wait(nc.vector, "dve", t_th)
        kb.wait(nc.vector, "ldq", t_pm)
        t_cm = kb.inc(nc.vector.tensor_scalar(tmpc[:], pm[:], thr128[:, :], None,
                                              op0=ALU.is_gt), "dve")
        kb.wait(nc.vector, "dve", t_cm)
        t_ct = kb.inc(nc.vector.reduce_sum(cnt_col[:], tmpc[:], axis=AX.X), "dve")
        pmisc.wait_reuse(kb, nc.tensor)
        kb.wait(nc.tensor, "dve", t_ct)
        t_c8 = kb.inc(nc.tensor.matmul(pmisc.ap[0:8, 0:1], C["grp16"], cnt_col[:],
                                       start=True, stop=True), "pe")
        kb.wait(nc.vector, "pe", t_c8)
        t_cc = kb.inc(nc.vector.tensor_copy(cnt8, pmisc.ap[0:8, 0:1]), "dve")
        pmisc.consumed("dve", t_cc)
        if DEBUG_L0FFN and it < 13:
            nc.vector.tensor_copy(dbg_bis_sb[:, 6 * it:6 * it + 1], mid)
            nc.vector.tensor_copy(dbg_bis_sb[:, 6 * it + 1:6 * it + 2], cnt8)
        kb.wait(nc.vector, "dve", t_cc)
        mf = b_mf[:]
        tA = b_tA[:]
        tB = b_tB[:]
        t_m8 = kb.inc(nc.vector.tensor_tensor(mf, cnt8, capcol, op=ALU.is_ge), "dve")
        kb.wait(nc.vector, "dve", t_m8)
        nc.vector.tensor_tensor(tA, mid, lo, op=ALU.subtract)
        t_s1 = kb.inc(nc.vector.tensor_tensor(tB, hi, mid, op=ALU.subtract), "dve")
        kb.wait(nc.vector, "dve", t_s1)
        nc.vector.tensor_tensor(tA, tA, mf, op=ALU.mult)
        t_s2 = kb.inc(nc.vector.tensor_tensor(tB, tB, mf, op=ALU.mult), "dve")
        kb.wait(nc.vector, "dve", t_s2)
        nc.vector.tensor_tensor(lo, lo, tA, op=ALU.add)
        t_s3 = kb.inc(nc.vector.tensor_tensor(hi, mid, tB, op=ALU.add), "dve")
        kb.wait(nc.vector, "dve", t_s3)
        if DEBUG_L0FFN and it < 13:
            nc.vector.tensor_copy(dbg_bis_sb[:, 6 * it + 2:6 * it + 3], mf)
            nc.vector.tensor_copy(dbg_bis_sb[:, 6 * it + 3:6 * it + 4], lo)
            nc.vector.tensor_copy(dbg_bis_sb[:, 6 * it + 4:6 * it + 5], hi)
            nc.vector.tensor_copy(dbg_bis_sb[:, 6 * it + 5:6 * it + 6], tA)
        nc.vector.tensor_scalar(tA, lo, 0.5, None, op0=ALU.mult)
        nc.vector.tensor_scalar(tB, hi, 0.5, None, op0=ALU.mult)
        nc.vector.tensor_copy(bis[:, 0:1], mf)
        t_mid = kb.inc(nc.vector.tensor_tensor(mid, tA, tB, op=ALU.add), "dve")
    kb.wait(nc.vector, "dve", t_mid)
    t_msk = kb.inc(nc.vector.tensor_scalar(mask_bf, probs, lo, None, op0=ALU.is_gt), "dve")

    if DEBUG_L0FFN:
        kb.wait(nc.sync, "dve", t_msk)
        kb.dinc(nc.sync.dma_start(out=dbg_probs[:, :], in_=probs), "stq")
        kb.dinc(nc.sync.dma_start(out=dbg_mask[:, :], in_=mask_bf), "stq")
        kb.dinc(nc.sync.dma_start(out=dbg_lo[:, :], in_=bis[:]), "stq")
        kb.dinc(nc.sync.dma_start(out=dbg_pm[:, :], in_=pm[:]), "stq")
        t_dbg = kb.dinc(nc.sync.dma_start(out=dbg_bis[:, :], in_=dbg_bis_sb[:]), "stq")
        kb.wait(nc.vector, "stq", t_dbg)
    kb.wait(nc.vector, "dve", t_msk)
    kb.wait(nc.vector, "stq", t_pd)
    t_pf = kb.inc(nc.vector.tensor_copy(pfxA[:, :], mask_bf), "dve")
    cur, nxt = pfxA, pfxB
    sh = 1
    while sh < T:
        kb.wait(nc.vector, "dve", t_pf)
        nc.vector.tensor_copy(nxt[:, 0:sh], cur[:, 0:sh])
        t_pf = kb.inc(nc.vector.tensor_tensor(nxt[:, sh:T], cur[:, sh:T],
                                              cur[:, 0:T - sh], op=ALU.add), "dve")
        cur, nxt = nxt, cur
        sh *= 2
    kb.wait(nc.vector, "dve", t_pf)
    t_pf = kb.inc(nc.vector.tensor_tensor(nxt[:, :], cur[:, :], mask_bf,
                                          op=ALU.subtract), "dve")
    expfx = nxt
    for tb in range(32):
        for srcv, dstv in ((expfx, pfxT), (mask_bf, maskT)):
            pmisc.wait_reuse(kb, nc.tensor)
            kb.wait(nc.tensor, "dve", t_pf)
            if srcv is mask_bf:
                pbf[0].wait_reuse(kb, nc.tensor)
                t_t = kb.inc(nc.tensor.transpose(pbf[0].ap[:, 0:8],
                                                 srcv[:, tb * 128:(tb + 1) * 128],
                                                 C["id_bf"][0:8, 0:8]), "pe")
                kb.wait(nc.vector, "pe", t_t)
                t_ev = kb.inc(nc.vector.tensor_copy(dstv[:, tb * 8:(tb + 1) * 8],
                                                    pbf[0].ap[:, 0:8]), "dve")
                pbf[0].consumed("dve", t_ev)
            else:
                t_t = kb.inc(nc.tensor.transpose(pmisc.ap[:, 0:8],
                                                 srcv[:, tb * 128:(tb + 1) * 128],
                                                 C["id_f32"][0:8, 0:8]), "pe")
                kb.wait(nc.vector, "pe", t_t)
                t_ev = kb.inc(nc.vector.tensor_copy(dstv[:, tb * 8:(tb + 1) * 8],
                                                    pmisc.ap[:, 0:8]), "dve")
                pmisc.consumed("dve", t_ev)
    t_tr = kb.cnt["dve"]
    kb.wait(nc.vector, "dve", t_tr)
    t_p2 = kb.inc(nc.vector.tensor_scalar(pfxT2[:], pfxT[:], -512.0, None, op0=ALU.add), "dve")

    for e in range(NE):
        for half in range(2):
            bank = pb[4]
            bank.wait_reuse(kb, nc.tensor)
            pfx_src = pfxT if half == 0 else pfxT2
            t_pt_last = 0
            for tb in range(32):
                kb.wait(nc.vector, "dve", max(t_p2, t_tr))
                if tb >= 1:
                    kb.wait(nc.vector, "pe", t_pt_last)
                t_pt = kb.inc(nc.vector.tensor_scalar(
                    ptile[:], C["iota512"], pfx_src[:, tb * 8 + e:tb * 8 + e + 1],
                    maskT[:, tb * 8 + e:tb * 8 + e + 1],
                    op0=ALU.is_equal, op1=ALU.mult), "dve")
                kb.wait(nc.tensor, "dve", t_pt)
                i = nc.tensor.matmul(bank.ap[0:3, :], C["idxw"][:, tb * 3:tb * 3 + 3],
                                     ptile[:], start=(tb == 0), stop=(tb == 31))
                t_pt_last = kb.inc(i, "pe")
            kb.wait(nc.vector, "pe", t_pt_last)
            t_i3 = kb.inc(nc.vector.tensor_copy(idx3, bank.ap[0:3, :]), "dve")
            bank.consumed("dve", t_i3)
            bank2 = pb[3]
            bank2.wait_reuse(kb, nc.tensor)
            kb.wait(nc.tensor, "dve", t_i3)
            t_mg = kb.inc(nc.tensor.matmul(bank2.ap[0:1, :], C["mrg3"], idx3,
                                           start=True, stop=True), "pe")
            kb.wait(nc.vector, "pe", t_mg)
            t_ir = kb.inc(nc.vector.tensor_scalar(idxrow, bank2.ap[0:1, :], 4096.0,
                                                  None, op0=ALU.add), "dve")
            bank2.consumed("dve", t_ir)
            for j in range(4):
                pmisc.wait_reuse(kb, nc.tensor)
                kb.wait(nc.tensor, "dve", t_ir)
                t_t = kb.inc(nc.tensor.transpose(pmisc.ap[:, 0:1],
                                                 scr[0:1, TTW + j * 128:TTW + (j + 1) * 128],
                                                 C["id_f32"][0:1, 0:1]), "pe")
                kb.wait(nc.vector, "pe", t_t)
                t_ix = kb.inc(nc.vector.tensor_copy(
                    idx_all[:, e * 8 + half * 4 + j:e * 8 + half * 4 + j + 1],
                    pmisc.ap[:, 0:1]), "dve")
                pmisc.consumed("dve", t_ix)
    t_idx = kb.cnt["dve"]
    if DEBUG_L0FFN:
        kb.wait(nc.sync, "dve", t_idx)
        kb.dinc(nc.sync.dma_start(out=dbg_idx[:, :], in_=idx_all[:]), "stq")

    w1_tok = [0, 0]
    xtm_pe = [0, 0]
    ytm_tok = [0, 0]
    ytm_stq_war = [t_zs, 0]
    ytm_ldq_war = [0, t_xtm_ldq]
    ytm_pcol_tok = [0, 0]
    t_scat_prev = 0
    t_w2 = 0
    t_yphase = kb.cnt["pe"]
    for e in range(NE):
        kb.wait(nc.gpsimd, "pe", t_yphase)
        t_w2 = kb.dinc(nc.gpsimd.dma_start(
            out=a_w2[:].rearrange("p (h c) -> p h c", h=32),
            in_=w2_d[e].rearrange("(h p) c -> p h c", p=128)), "wq")
        for j in range(NC):
            buf = j % 2
            kb.wait(nc.vector, "pe", xtm_pe[buf])
            t_ms = kb.inc(nc.vector.memset(x_tm[buf][:], 0.0), "dve")
            kb.wait(nc.gpsimd, "dve", max(t_ms, t_idx))
            t_g = kb.dinc(nc.gpsimd.indirect_dma_start(
                out=x_tm[buf][:], out_offset=None, in_=xtm_dram[:, :],
                in_offset=IndirectOffsetOnAxis(ap=idx_all[:, e * 8 + j:e * 8 + j + 1], axis=0),
                bounds_check=bnd_reg, oob_is_err=False), "gq")
            for cc in range(NC):
                slot = pbf[cc % 2]
                slot.wait_reuse(kb, nc.tensor)
                kb.wait(nc.tensor, "gq", t_g)
                t_t = kb.inc(nc.tensor.transpose(slot.ap[:, :],
                                                 x_tm[buf][:, cc * 128:(cc + 1) * 128],
                                                 C["id_bf"]), "pe")
                xtm_pe[buf] = t_t
                kb.wait(nc.vector, "pe", t_t)
                t_ev = kb.inc(nc.vector.tensor_copy(
                    x_fm[:, cc * D + j * 128:cc * D + j * 128 + 128], slot.ap[:, :]), "dve")
                slot.consumed("dve", t_ev)
        t_xfm = kb.cnt["dve"]
        for half in range(2):
            hs = half * TTW
            for hc in range(32):
                buf = hc % 2
                kb.wait(nc.gpsimd, "pe", w1_tok[buf])
                t_w1 = kb.dinc(nc.gpsimd.dma_start(
                    out=w1_sb[buf][:].rearrange("p (c h) -> p c h", c=NC),
                    in_=w1_d[e].rearrange("(c p) h -> p c h", p=128)[:, :, hc * 128:(hc + 1) * 128]),
                    "wq")
                bank = pb[hc % 2]
                bank.wait_reuse(kb, nc.tensor)
                kb.wait(nc.tensor, "wq", t_w1)
                kb.wait(nc.tensor, "dve", t_xfm)
                for cc in range(NC):
                    i = nc.tensor.matmul(bank.ap[:, :],
                                         w1_sb[buf][:, cc * 128:(cc + 1) * 128],
                                         x_fm[:, cc * D + hs:cc * D + hs + TTW],
                                         start=(cc == 0), stop=(cc == NC - 1))
                w1_tok[buf] = t_mm = kb.inc(i, "pe")
                kb.wait(nc.scalar, "pe", t_mm)
                t_gl = kb.inc(nc.scalar.activation(
                    a_h[:, hc * TTW:(hc + 1) * TTW], bank.ap[:, :], AF.Gelu), "act")
                bank.consumed("act", t_gl)
            t_h = kb.cnt["act"]
            kb.wait(nc.tensor, "wq", t_w2)
            for grp in ((0, 1, 2, 3), (4, 5, 6, 7)):
                for cc in grp:
                    bank = pb[cc - grp[0]]
                    bank.wait_reuse(kb, nc.tensor)
                    kb.wait(nc.tensor, "act", t_h)
                    for hc in range(32):
                        i = nc.tensor.matmul(bank.ap[:, :],
                                             a_w2[:, hc * D + cc * 128:hc * D + cc * 128 + 128],
                                             a_h[:, hc * TTW:(hc + 1) * TTW],
                                             start=(hc == 0), stop=(hc == 31))
                    t_mm = kb.inc(i, "pe")
                    kb.wait(nc.vector, "pe", t_mm)
                    t_y = kb.inc(nc.vector.tensor_copy(
                        y_arena[:, cc * TTW:(cc + 1) * TTW], bank.ap[:, :]), "dve")
                    bank.consumed("dve", t_y)
            t_yphase = kb.cnt["pe"]
            t_yall = kb.cnt["dve"]
            for j in range(4):
                buf = j % 2
                icol = idx_all[:, e * 8 + half * 4 + j:e * 8 + half * 4 + j + 1]
                kb.wait(nc.gpsimd, "dve", ytm_pcol_tok[buf])
                t_pg = kb.dinc(nc.gpsimd.indirect_dma_start(
                    out=p_col[buf][:], out_offset=None,
                    in_=pdram.rearrange("e (t one) -> (e t) one", one=1),
                    in_offset=IndirectOffsetOnAxis(ap=icol, axis=0),
                    element_offset=e * T,
                    bounds_check=bnd_reg, oob_is_err=False), "gq")
                kb.wait(nc.vector, "gq", ytm_tok[buf])
                kb.wait(nc.vector, "stq", ytm_stq_war[buf])
                kb.wait(nc.vector, "ldq", ytm_ldq_war[buf])
                ytm_stq_war[buf] = ytm_ldq_war[buf] = 0
                for cc in range(NC):
                    slot = pbf[cc % 2]
                    slot.wait_reuse(kb, nc.tensor)
                    kb.wait(nc.tensor, "dve", t_yall)
                    t_t = kb.inc(nc.tensor.transpose(
                        slot.ap[:, :], y_arena[:, cc * TTW + j * 128:cc * TTW + j * 128 + 128],
                        C["id_bf"]), "pe")
                    kb.wait(nc.vector, "pe", t_t)
                    t_ev = kb.inc(nc.vector.tensor_copy(
                        y_tm[buf][:, cc * 128:(cc + 1) * 128], slot.ap[:, :]), "dve")
                    slot.consumed("dve", t_ev)
                kb.wait(nc.vector, "gq", t_pg)
                t_ysc = kb.inc(nc.vector.tensor_scalar(
                    y_tm[buf][:], y_tm[buf][:], p_col[buf][:, :], None, op0=ALU.mult), "dve")
                ytm_pcol_tok[buf] = t_ysc
                kb.wait(nc.gpsimd, "dve", t_ysc)
                kb.wait(nc.gpsimd, "gq", t_scat_prev)
                t_sc = kb.dinc(nc.gpsimd.indirect_dma_start(
                    out=dense_dram[:, :],
                    out_offset=IndirectOffsetOnAxis(ap=icol, axis=0),
                    in_=y_tm[buf][:], in_offset=None,
                    bounds_check=bnd_reg, oob_is_err=False,
                    compute_op=ALU.add), "gq")
                ytm_tok[buf] = t_sc
        t_scat_prev = kb.cnt["gq"]

    x_src = x_in.rearrange("(c p) t -> p c t", p=128)
    x_dst = x_out.rearrange("(c p) t -> p c t", p=128)
    dn_sb = x_tm[0]
    xck = xf[:, 0:NC * 128]
    t_resid = 0
    for tc in range(32):
        tcs = tc * 128
        kb.wait(nc.sync, "gq", t_scat_prev)
        kb.wait(nc.sync, "dve", t_resid)
        t_xl = kb.dinc(nc.sync.dma_start(
            out=xck.rearrange("p (c t) -> p c t", c=NC), in_=x_src[:, :, tcs:tcs + 128]), "ldq")
        kb.wait(nc.sync, "pe", kb.cnt["pe"]) if tc == 0 else None
        t_dn = kb.dinc(nc.sync.dma_start(out=dn_sb[:], in_=dense_dram[tcs:tcs + 128, :]), "ldq")
        for cc in range(NC):
            slot = pbf[cc % 2]
            slot.wait_reuse(kb, nc.tensor)
            kb.wait(nc.tensor, "ldq", t_dn)
            t_t = kb.inc(nc.tensor.transpose(slot.ap[:, :],
                                             dn_sb[:, cc * 128:(cc + 1) * 128],
                                             C["id_bf"]), "pe")
            kb.wait(nc.vector, "pe", t_t)
            kb.wait(nc.vector, "ldq", t_xl)
            t_cv = kb.inc(nc.vector.tensor_copy(scr[:, 0:128], slot.ap[:, :]), "dve")
            slot.consumed("dve", t_cv)
            kb.wait(nc.vector, "dve", t_cv)
            t_resid = kb.inc(nc.vector.scalar_tensor_tensor(
                out=xck[:, cc * 128:(cc + 1) * 128], in0=scr[:, 0:128], scalar=0.5,
                in1=xck[:, cc * 128:(cc + 1) * 128], op0=ALU.mult, op1=ALU.add), "dve")
        kb.wait(nc.sync, "dve", t_resid)
        kb.dinc(nc.sync.dma_start(out=x_dst[:, :, tcs:tcs + 128],
                                  in_=xck.rearrange("p (c t) -> p c t", c=NC)), "stq")
    kb.wait(nc.sync, "stq")


def l0ffn_weights(inputs):
    return {
        "gate_w": np.ascontiguousarray(inputs["L0_ffn_gate_w"]),
        "w1": np.ascontiguousarray(inputs["L0_ffn_w1"]),
        "w2": np.ascontiguousarray(inputs["L0_ffn_w2"]),
    }


_BUILDER_CACHE = {}


def build_fused():
    kb = KB()
    nc = kb.nc
    iota512, idxw, grp16, sel8, oh8, mrg3 = _l0ffn_consts()
    cos_t, sin_t = _rope_tables()
    C = host_consts(kb, extra=(
        ("ones_row512_bf", np.ones((1, TTW), np.float32), BF16),
        ("p_rot", _p_rot(), BF16),
    ))
    C["cos_d"] = kb.cbf16("c_cos_fm", cos_t)
    C["sin_d"] = kb.cbf16("c_sin_fm", sin_t)
    C["_l0f_spec"] = [
        ("iota512", kb.const("c_iota512", iota512), [128, TTW], F32),
        ("idxw", kb.cbf16("c_idxw", idxw), [128, 96], BF16),
        ("grp16", kb.const("c_grp16", grp16), [128, NE], F32),
        ("sel8", kb.const("c_sel8", sel8), [NE, 128], F32),
        ("mrg3", kb.const("c_mrg3", mrg3), [3, 1], F32),
    ]
    x_in = kb.din("x_in", [D, T], F32)
    x_out = kb.dout("x_out", [D, T], F32)
    xsA = kb.dtmp("xsA", [D, T], F32)
    xsB = kb.dtmp("xsB", [D, T], F32)
    S = {
        "q_dram": kb.dtmp("q_dram", [D, T], BF16),
        "k_dram": kb.dtmp("k_dram", [D, T], BF16),
        "xtm_dram": kb.dtmp("xtm_dram", [T, D], BF16),
        "pdram": kb.dtmp("pdram", [NE, T], F32),
        "dense_dram": kb.dtmp("dense_dram", [T, D], BF16),
    }
    W0f = {"gate_w": kb.din("l0f_gate_w", [D, NE], F32),
           "w1": kb.din("l0f_w1", [NE, D, DFF], F32),
           "w2": kb.din("l0f_w2", [NE, DFF, D], F32)}
    W0a = {"qwT": kb.din("l0a_qwT", [D, D], F32),
           "kwT": kb.din("l0a_kwT", [D, D], F32),
           "vwT": kb.din("l0a_vwT", [D, D], F32),
           "owT": kb.din("l0a_owT", [D, D], F32),
           "qb": kb.din("l0a_qb", [128, NC], F32),
           "kbias": kb.din("l0a_kbias", [128, NC], F32),
           "ob_row": kb.din("l0a_ob_row", [1, D], F32),
           "vb_row": kb.din("l0a_vb_row", [1, D], F32),
           "qn": kb.din("l0a_qn", [128, NC], F32),
           "kn": kb.din("l0a_kn", [128, NC], F32)}
    W1f = {"wiT": kb.din("l1f_wiT", [D, DFF], F32),
           "woT": kb.din("l1f_woT", [DFF, D], F32),
           "AsT": kb.din("l1f_AsT", [NE * D, RANK], F32),
           "BsT": kb.din("l1f_BsT", [NE * RANK, DFF], F32),
           "tki": kb.din("l1f_tki", [1, TOPK], I32),
           "tkv": kb.din("l1f_tkv", [1, TOPK], F32)}
    W1a = {"Wq": kb.din("l1a_Wq", [D, D], F32),
           "Wk": kb.din("l1a_Wk", [D, D], F32),
           "Wv": kb.din("l1a_Wv", [D, D], F32),
           "Wo": kb.din("l1a_Wo", [D, D], F32),
           "qb": kb.din("l1a_qb", [128, NC], F32),
           "kbias": kb.din("l1a_kbias", [128, NC], F32),
           "ob_row": kb.din("l1a_ob_row", [1, D], F32),
           "vb_row": kb.din("l1a_vb_row", [1, D], F32)}

    kb.start()
    load_consts(kb)
    kb.bnd_reg = nc.gpsimd.to_reg(T - 1)

    stages = (
        ("s1", emit_l0ffn, (C, x_in, xsA, W0f, S)),
        ("s2", emit_l0attn, (C, xsA, xsB, W0a, S)),
        ("s3", emit_l0ffn, (C, xsB, xsA, W0f, S)),
        ("s4", emit_l1ffn, (C, xsA, xsB, W1f)),
        ("s5", emit_l1attn, (C, xsB, xsA, W1a, S)),
        ("s6", emit_l1ffn, (C, xsA, x_out, W1f)),
    )
    for tag, fn, args in stages:
        kb.push_scope(tag)
        fn(kb, *args)
        kb.pop_scope()
        kb.barrier()
    kb.wait(nc.sync, "stq")
    return kb.finish()


def _get(name):
    if name not in _BUILDER_CACHE:
        _BUILDER_CACHE[name] = {"fused": build_fused}[name]()
    return _BUILDER_CACHE[name]


def run_stage(kb, per_core_inputs):
    in_maps = []
    for c in range(8):
        m = dict(kb.consts)
        m.update(per_core_inputs[c])
        in_maps.append(m)
    return run_bass_kernel_spmd(kb.nc, in_maps, core_ids=list(range(8)))


def l1ffn_weights(inputs):
    return {
        "wiT": np.ascontiguousarray(inputs["L1_ffn_wi_w"].T),
        "woT": np.ascontiguousarray(inputs["L1_ffn_wo_w"].T),
        "AsT": np.ascontiguousarray(inputs["L1_ffn_lora_As"].transpose(0, 2, 1)).reshape(NE * D, RANK),
        "BsT": np.ascontiguousarray(inputs["L1_ffn_lora_Bs"].transpose(0, 2, 1)).reshape(NE * RANK, DFF),
        "tki": np.ascontiguousarray(inputs["L1_ffn_top_k_indices"].reshape(1, TOPK)),
        "tkv": np.ascontiguousarray(inputs["L1_ffn_top_k_values"].reshape(1, TOPK)),
    }


def kernel(**inputs):
    kb = _get("fused")
    w0f = l0ffn_weights(inputs)
    w0a = l0attn_weights(inputs)
    w1f = l1ffn_weights(inputs)
    w1a = l1attn_weights(inputs)
    wmap = {}
    for pre, wm in (("l0f_", w0f), ("l0a_", w0a), ("l1f_", w1f), ("l1a_", w1a)):
        for k, v in wm.items():
            wmap[pre + k] = v
    per_core = []
    for c in range(8):
        m = dict(wmap)
        m["x_in"] = np.ascontiguousarray(inputs["L0_attn_x"][c].T)
        per_core.append(m)
    res = run_stage(kb, per_core)
    return np.stack([np.ascontiguousarray(res.results[c]["x_out"]).T
                     for c in range(8)]).astype(np.float32)


# version 03
# speedup vs baseline: 1.1570x; 1.1468x over previous
import contextlib

import numpy as np

import concourse.bass as bass
import concourse.mybir as mybir
from concourse.bass_core_rust import IndirectOffsetOnAxis
from concourse.bass_utils import run_bass_kernel_spmd

F32 = mybir.dt.float32
BF16 = mybir.dt.bfloat16
I32 = mybir.dt.int32
U8 = mybir.dt.uint8
AF = mybir.ActivationFunctionType
ALU = mybir.AluOpType
AX = mybir.AxisListType

D = 1024
T = 4096
NC = 8
NTT = 8
TTW = 512
NH = 16
HD = 64
NE = 8
CAP = 1024
DFF = 4096
RANK = 16
TOPK = 2
WIN = 256
LN_EPS = 1e-5
RMS_EPS = 1e-8
SIG_EPS = 1e-8
NEG_BIG = -1e30


class KB:

    def __init__(self):
        self.nc = bass.Bass("TRN2", target_bir_lowering=False)
        self.es = contextlib.ExitStack()
        self.sems = {}
        self.cnt = {}
        self.consts = {}

    def sem(self, name):
        if name not in self.sems:
            self.sems[name] = self.es.enter_context(self.nc.semaphore(name))
            self.cnt[name] = 0
        return self.sems[name]

    def inc(self, inst, name, k=1):
        s = self.sem(name)
        self.cnt[name] += k
        inst.then_inc(s, k)
        return self.cnt[name]

    def dinc(self, inst, name):
        return self.inc(inst, name, 16)

    def wait(self, eng, name, val=None):
        if val is None:
            val = self.cnt[name]
        if val <= 0:
            return
        eng.wait_ge(self.sem(name), val)

    def sb(self, name, shape, dt):
        scope = getattr(self, "scopes", None)
        tgt = scope[-1] if scope else self.es
        name = getattr(self, "prefix", "") + name
        return tgt.enter_context(self.nc.sbuf_tensor(name, shape, dt))

    def ps(self, name, shape, dt=F32):
        scope = getattr(self, "scopes", None)
        tgt = scope[-1] if scope else self.es
        name = getattr(self, "prefix", "") + name
        return tgt.enter_context(self.nc.psum_tensor(name, shape, dt))

    def push_scope(self, tag):
        if not hasattr(self, "scopes"):
            self.scopes = []
        self.scopes.append(contextlib.ExitStack())
        self.prefix = tag + "_"

    def pop_scope(self):
        self.scopes.pop().close()
        self.prefix = ""

    def barrier(self):
        nc = self.nc
        for eng in (nc.tensor, nc.vector, nc.scalar, nc.gpsimd, nc.sync):
            for name in list(self.sems):
                self.wait(eng, name)

    def din(self, name, shape, dt=F32):
        return self.nc.dram_tensor(name, shape, dt, kind="ExternalInput").ap()

    def dout(self, name, shape, dt=F32):
        return self.nc.dram_tensor(name, shape, dt, kind="ExternalOutput").ap()

    def dtmp(self, name, shape, dt):
        return self.nc.dram_tensor(name, shape, dt, kind="Internal").ap()

    def const(self, name, arr):
        arr = np.ascontiguousarray(arr)
        self.consts[name] = arr
        dt = {np.dtype(np.float32): F32, np.dtype(np.int32): I32,
              np.dtype(np.uint16): BF16}[arr.dtype]
        return self.nc.dram_tensor(name, list(arr.shape), dt, kind="ExternalInput").ap()

    def cbf16(self, name, arr_f32):
        import ml_dtypes
        a = np.ascontiguousarray(np.asarray(arr_f32, np.float32).astype(ml_dtypes.bfloat16)).view(np.uint16)
        return self.const(name, a)

    def start(self):
        self.block = self.es.enter_context(self.nc.Block())

    def finish(self):
        self.es.close()
        return self


class Bank:

    def __init__(self, ap):
        self.ap = ap
        self.sem = None
        self.tok = 0

    def wait_reuse(self, kb, eng):
        if self.sem is not None:
            kb.wait(eng, self.sem, self.tok)

    def consumed(self, sem, tok):
        self.sem = sem
        self.tok = tok


def host_consts(kb, extra=()):
    spec = {
        "ones_col_bf": (np.ones((128, 1), np.float32), BF16),
        "ones_col_f32": (np.ones((128, 1), np.float32), F32),
        "ones_row_f32": (np.ones((1, 128), np.float32), F32),
        "ones_row_bf": (np.ones((1, 128), np.float32), BF16),
        "id_bf": (np.eye(128, dtype=np.float32), BF16),
        "id_f32": (np.eye(128, dtype=np.float32), F32),
        "iota_col_f32": (np.arange(128, dtype=np.float32).reshape(128, 1), F32),
    }
    for name, arr, dt in extra:
        spec[name] = (arr, dt)
    c = {}
    kb._const_pairs = []
    for name, (arr, dt) in spec.items():
        if dt == BF16:
            dap = kb.cbf16("c_" + name, arr)
        elif dt == I32:
            dap = kb.const("c_" + name, np.asarray(arr, np.int32))
        else:
            dap = kb.const("c_" + name, np.asarray(arr, np.float32))
        tile = kb.sb(name, list(arr.shape), dt)
        kb._const_pairs.append((tile, dap))
        c[name] = tile[:]
    return c


def load_consts(kb):
    nc = kb.nc
    t = 0
    for tile, dap in kb._const_pairs:
        t = kb.dinc(nc.sync.dma_start(out=tile[:], in_=dap), "ldq")
    for eng in (nc.tensor, nc.vector, nc.scalar, nc.gpsimd):
        kb.wait(eng, "ldq", t)


def emit_ln(kb, C, x_dram, w, xln, pb, pbf=None, xrms=None, xtm_dram=None, per_tt_cb=None, f32_stats=False):
    nc = kb.nc
    xf, xb, sq, stat = w["xf"], w["xb"], w["sq"], w["stat"]
    s0 = stat[:, 0 * TTW:1 * TTW]
    s1 = stat[:, 1 * TTW:2 * TTW]
    s2 = stat[:, 2 * TTW:3 * TTW]
    x_src = x_dram.rearrange("(c p) t -> p c t", p=128)
    t_store = 0
    t_ap = 0
    t_pe_prev = 0
    for tt in range(NTT):
        ts = tt * TTW
        kb.wait(nc.sync, "dve", t_ap)
        t_ld = kb.dinc(nc.sync.dma_start(
            out=xf[:].rearrange("p (c t) -> p c t", c=NC),
            in_=x_src[:, :, ts:ts + TTW]), "ldq")
        kb.wait(nc.vector, "ldq", t_ld)
        kb.wait(nc.vector, "pe", t_pe_prev)
        t_cast = kb.inc(nc.vector.tensor_copy(xb[:], xf[:]), "dve")
        for bk in (pb[0], pb[1]):
            bk.wait_reuse(kb, nc.tensor)
        if f32_stats:
            sqb = [w["sqf"][:, 0:TTW], w["sqf"][:, TTW:2 * TTW]]
            kb.wait(nc.tensor, "ldq", t_ld)
            for cc in range(NC):
                nc.tensor.matmul(pb[0].ap[0:1, :], C["ones_col_f32"],
                                 xf[:, cc * TTW:(cc + 1) * TTW],
                                 start=(cc == 0), stop=(cc == NC - 1))
            t_sqa = [0, 0]
            kb.wait(nc.scalar, "ldq", t_ld)
            for cc in range(NC):
                buf = cc % 2
                if t_sqa[buf]:
                    kb.wait(nc.scalar, "pe", t_sqa[buf])
                t_sq = kb.inc(nc.scalar.activation(sqb[buf], xf[:, cc * TTW:(cc + 1) * TTW],
                                                   AF.Square), "act")
                kb.wait(nc.tensor, "act", t_sq)
                i = nc.tensor.matmul(pb[1].ap[0:1, :], C["ones_col_f32"], sqb[buf],
                                     start=(cc == 0), stop=(cc == NC - 1))
                t_sqa[buf] = t_mm = kb.inc(i, "pe")
        else:
            kb.wait(nc.scalar, "dve", t_cast)
            t_sq = kb.inc(nc.scalar.activation(sq[:], xb[:], AF.Square), "act")
            kb.wait(nc.tensor, "dve", t_cast)
            kb.wait(nc.tensor, "act", t_sq)
            for cc in range(NC):
                nc.tensor.matmul(pb[0].ap[0:1, :], C["ones_col_bf"], xb[:, cc * TTW:(cc + 1) * TTW],
                                 start=(cc == 0), stop=(cc == NC - 1))
            for cc in range(NC):
                i = nc.tensor.matmul(pb[1].ap[0:1, :], C["ones_col_bf"], sq[:, cc * TTW:(cc + 1) * TTW],
                                     start=(cc == 0), stop=(cc == NC - 1))
            t_mm = kb.inc(i, "pe")
        kb.wait(nc.vector, "pe", t_mm)
        nc.vector.tensor_scalar(s0, pb[0].ap[0:1, :], 1.0 / D, None, op0=ALU.mult)
        nc.vector.tensor_scalar(s1, pb[1].ap[0:1, :], 1.0 / D, None, op0=ALU.mult)
        nc.vector.tensor_tensor(s2, s0, s0, op=ALU.mult)
        nc.vector.tensor_tensor(s1, s1, s2, op=ALU.subtract)
        t_v = kb.inc(nc.vector.tensor_scalar(s1, s1, LN_EPS, None, op0=ALU.add), "dve")
        pb[0].consumed("dve", t_v)
        pb[1].consumed("dve", t_v)
        kb.wait(nc.scalar, "dve", t_v)
        t_sd = kb.inc(nc.scalar.activation(s2, s1, AF.Sqrt), "act")
        kb.wait(nc.vector, "act", t_sd)
        t_rc = kb.inc(nc.vector.reciprocal(s1, s2), "dve")
        kb.wait(nc.vector, "dve", t_rc)
        t_mr = kb.inc(nc.vector.tensor_tensor(s2, s0, s1, op=ALU.mult), "dve")
        for bk in (pb[2], pb[3]):
            bk.wait_reuse(kb, nc.tensor)
        kb.wait(nc.tensor, "dve", t_mr)
        nc.tensor.matmul(pb[2].ap[:, :], C["ones_row_f32"], s1, start=True, stop=True)
        t_bc = kb.inc(nc.tensor.matmul(pb[3].ap[:, :], C["ones_row_f32"], s2,
                                       start=True, stop=True), "pe")
        kb.wait(nc.vector, "pe", t_bc)
        if per_tt_cb is None:
            for cc in range(NC):
                nc.vector.tensor_tensor(xf[:, cc * TTW:(cc + 1) * TTW],
                                        xf[:, cc * TTW:(cc + 1) * TTW], pb[2].ap[:, :], op=ALU.mult)
                i = nc.vector.tensor_tensor(xln[cc][:, ts:ts + TTW],
                                            xf[:, cc * TTW:(cc + 1) * TTW], pb[3].ap[:, :],
                                            op=ALU.subtract)
            t_ap = kb.inc(i, "dve")
        else:
            for cc in range(NC):
                nc.vector.tensor_tensor(xf[:, cc * TTW:(cc + 1) * TTW],
                                        xf[:, cc * TTW:(cc + 1) * TTW], pb[2].ap[:, :], op=ALU.mult)
                nc.vector.tensor_tensor(xf[:, cc * TTW:(cc + 1) * TTW],
                                        xf[:, cc * TTW:(cc + 1) * TTW], pb[3].ap[:, :],
                                        op=ALU.subtract)
                i = nc.vector.tensor_copy(xln[cc][:, ts:ts + TTW],
                                          xf[:, cc * TTW:(cc + 1) * TTW])
            t_ap = kb.inc(i, "dve")
            per_tt_cb(tt, ts, t_ap)
        pb[2].consumed("dve", t_ap)
        pb[3].consumed("dve", t_ap)

        if xrms is not None:
            kb.wait(nc.scalar, "dve", t_ap)
            last = None
            for cc in range(NC):
                last = nc.scalar.activation(sq[:, cc * TTW:(cc + 1) * TTW],
                                            xln[cc][:, ts:ts + TTW], AF.Square)
            t_sq2 = kb.inc(last, "act")
            pb[0].wait_reuse(kb, nc.tensor)
            kb.wait(nc.tensor, "act", t_sq2)
            for cc in range(NC):
                i = nc.tensor.matmul(pb[0].ap[0:1, :], C["ones_col_bf"],
                                     sq[:, cc * TTW:(cc + 1) * TTW],
                                     start=(cc == 0), stop=(cc == NC - 1))
            t_mm2 = kb.inc(i, "pe")
            kb.wait(nc.vector, "pe", t_mm2)
            t_v2 = kb.inc(nc.vector.tensor_scalar(s1, pb[0].ap[0:1, :], 1.0 / D, RMS_EPS,
                                                  op0=ALU.mult, op1=ALU.add), "dve")
            pb[0].consumed("dve", t_v2)
            kb.wait(nc.scalar, "dve", t_v2)
            t_sd2 = kb.inc(nc.scalar.activation(s2, s1, AF.Sqrt), "act")
            kb.wait(nc.vector, "act", t_sd2)
            t_r2 = kb.inc(nc.vector.reciprocal(s1, s2), "dve")
            pb[2].wait_reuse(kb, nc.tensor)
            kb.wait(nc.tensor, "dve", t_r2)
            t_bc2 = kb.inc(nc.tensor.matmul(pb[2].ap[:, :], C["ones_row_f32"], s1,
                                            start=True, stop=True), "pe")
            kb.wait(nc.vector, "pe", t_bc2)
            nc.vector.tensor_copy(xb[:, 0:TTW], pb[2].ap[:, :])
            for cc in range(NC):
                i = nc.vector.tensor_tensor(xrms[cc][:, ts:ts + TTW],
                                            xln[cc][:, ts:ts + TTW], xb[:, 0:TTW],
                                            op=ALU.mult)
            t_ap = kb.inc(i, "dve")
            pb[2].consumed("dve", t_ap)

        t_pe_prev = kb.cnt["pe"]
        if xtm_dram is not None:
            xtm_sb = w["xtm_sb"]
            for j in range(4):
                js = ts + j * 128
                kb.wait(nc.vector, "ldq", t_store)
                t_done = 0
                for cc in range(NC):
                    slot = pbf[cc % 2]
                    slot.wait_reuse(kb, nc.tensor)
                    kb.wait(nc.tensor, "dve", t_ap)
                    t_t = kb.inc(nc.tensor.transpose(slot.ap[:, 0:128],
                                                     xln[cc][:, js:js + 128],
                                                     C["id_bf"]), "pe")
                    kb.wait(nc.vector, "pe", t_t)
                    t_done = kb.inc(nc.vector.tensor_copy(
                        xtm_sb[:, cc * 128:(cc + 1) * 128], slot.ap[:, 0:128]), "dve")
                    slot.consumed("dve", t_done)
                kb.wait(nc.sync, "dve", t_done)
                t_store = kb.dinc(nc.sync.dma_start(out=xtm_dram[js:js + 128, :],
                                                    in_=xtm_sb[:]), "ldq")
        t_pe_prev = kb.cnt["pe"]


def ln_work_tiles(kb):
    scratch = kb.sb("ln_scr", [128, 3 * TTW], F32)
    return {
        "xf": kb.sb("ln_xf", [128, NC * TTW], F32),
        "xb": kb.sb("ln_xb", [128, NC * TTW], BF16),
        "sq": kb.sb("ln_sq", [128, NC * TTW], BF16),
        "stat": scratch[0:1, :],
        "scr": scratch,
    }


DEBUG_L1FFN = False


def emit_l1ffn(kb, C, x_in, x_out, W):
    nc = kb.nc
    wiT, woT, AsT, BsT, tki, tkv = (W["wiT"], W["woT"], W["AsT"], W["BsT"],
                                    W["tki"], W["tkv"])
    if DEBUG_L1FFN:
        dbg_xln = kb.dout("dbg_xln", [128, T], BF16)
        dbg_asel = kb.dout("dbg_asel", [128, NC * RANK], BF16)
        dbg_bsel = kb.dout("dbg_bsel", [16, DFF], BF16)
        dbg_t1 = kb.dout("dbg_t1", [16, TTW], BF16)
        dbg_int = kb.dout("dbg_int", [128, TTW], BF16)
        dbg_h0 = kb.dout("dbg_h0", [128, TTW], BF16)
        dbg_idxf = kb.dout("dbg_idxf", [128, 1], F32)
        dbg_idxi = kb.dout("dbg_idxi", [128, 1], I32)

    a_xln = kb.sb("a_xln", [128, NC * T], BF16)
    xln = [a_xln[:, c * T:(c + 1) * T] for c in range(NC)]
    a_int = kb.sb("a_int", [128, 32 * TTW], BF16)
    a_h0 = kb.sb("a_h0", [128, 32 * TTW], BF16)
    w = ln_work_tiles(kb)
    wi_sb = [kb.sb(f"wi{i}", [128, NC * 128], BF16) for i in range(2)]
    wo_sb = [kb.sb(f"wo{i}", [128, 32 * 128], BF16) for i in range(2)]
    a_f32 = kb.sb("a_f32", [128, RANK], F32)
    b_f32 = w["xf"][0:16, :]
    a_sel = [kb.sb(f"a_sel{k}", [128, NC * RANK], BF16) for k in range(TOPK)]
    b_sel = [kb.sb(f"b_sel{k}", [16, DFF], BF16) for k in range(TOPK)]
    t1_sb = [kb.sb(f"t1_{k}", [16, TTW], BF16) for k in range(TOPK)]
    tki_i = kb.sb("tki_i", [1, TOPK], I32)
    tki_f = kb.sb("tki_f", [1, TOPK], F32)
    tkv_f = kb.sb("tkv_f", [1, TOPK], F32)
    tkv_col = [kb.sb(f"tkv_col{k}", [128, 1], F32) for k in range(TOPK)]
    idx_f = kb.sb("idx_f", [128, 1], F32)
    idx_i = kb.sb("idx_i", [128, 1], I32)
    x_sb = [w["xf"][:, 0:TTW], w["xf"][:, TTW:2 * TTW]]

    pb = [Bank(kb.ps(f"pb{i}", [128, TTW])) for i in range(6)]
    pmisc = kb.ps("pmisc", [128, 128])

    t_ld = kb.dinc(nc.sync.dma_start(out=tki_i[:], in_=tki[:, :]), "ldq")
    t_ld2 = kb.dinc(nc.sync.dma_start(out=tkv_f[:], in_=tkv[:, :]), "ldq")
    kb.wait(nc.vector, "ldq", t_ld2)
    t_cf = kb.inc(nc.vector.tensor_copy(tki_f[:], tki_i[:]), "dve")
    t_half = kb.inc(nc.vector.tensor_scalar(tkv_f[:], tkv_f[:], 0.5, None, op0=ALU.mult), "dve")
    for k in range(TOPK):
        kb.wait(nc.tensor, "dve")
        t_bc = kb.inc(nc.tensor.matmul(pmisc[:, 0:1], C["ones_row_f32"], tki_f[0:1, k:k + 1],
                                       start=True, stop=True), "pe")
        t_bv = kb.inc(nc.tensor.matmul(pmisc[:, 1:2], C["ones_row_f32"], tkv_f[0:1, k:k + 1],
                                       start=True, stop=True), "pe")
        kb.wait(nc.vector, "pe", t_bv)
        kb.inc(nc.vector.tensor_copy(tkv_col[k][:, :], pmisc[:, 1:2]), "dve")
        for cc in range(NC):
            t_a = kb.inc(nc.vector.tensor_scalar(idx_f[:], pmisc[:, 0:1], float(D), float(128 * cc),
                                                 op0=ALU.mult, op1=ALU.add), "dve")
            kb.wait(nc.vector, "dve", t_a)
            t_b = kb.inc(nc.vector.tensor_tensor(idx_f[:], idx_f[:], C_iota(kb, C), op=ALU.add), "dve")
            kb.wait(nc.vector, "dve", t_b)
            t_ix = kb.inc(nc.vector.tensor_copy(idx_i[:], idx_f[:]), "dve")
            kb.wait(nc.gpsimd, "dve", t_ix)
            t_g = kb.dinc(nc.gpsimd.indirect_dma_start(
                out=a_f32[:, :], out_offset=None, in_=AsT[:, :],
                in_offset=IndirectOffsetOnAxis(ap=idx_i[:, :], axis=0)), "gq")
            kb.wait(nc.vector, "gq", t_g)
            kb.inc(nc.vector.tensor_copy(
                a_sel[k][:, cc * RANK:(cc + 1) * RANK], a_f32[:, :]), "dve")
        t_a = kb.inc(nc.vector.tensor_scalar(idx_f[:], pmisc[:, 0:1], float(RANK), None, op0=ALU.mult), "dve")
        kb.wait(nc.vector, "dve", t_a)
        t_b = kb.inc(nc.vector.tensor_tensor(idx_f[:], idx_f[:], C_iota(kb, C), op=ALU.add), "dve")
        kb.wait(nc.vector, "dve", t_b)
        t_ix = kb.inc(nc.vector.tensor_copy(idx_i[:], idx_f[:]), "dve")
        kb.wait(nc.gpsimd, "dve", t_ix)
        t_g = kb.dinc(nc.gpsimd.indirect_dma_start(
            out=b_f32, out_offset=None, in_=BsT[:, :],
            in_offset=IndirectOffsetOnAxis(ap=idx_i[0:16, :], axis=0)), "gq")
        kb.wait(nc.vector, "gq", t_g)
        kb.inc(nc.vector.tensor_copy(b_sel[k][:, :], b_f32), "dve")
        if DEBUG_L1FFN and k == 1:
            kb.wait(nc.sync, "dve", t_ix)
            kb.dinc(nc.sync.dma_start(out=dbg_idxf[:, :], in_=idx_f[:]), "stq")
            kb.dinc(nc.sync.dma_start(out=dbg_idxi[:, :], in_=idx_i[:]), "stq")
    t_prep = kb.cnt["dve"]

    kb.wait(nc.sync, "dve", t_prep)
    emit_ln(kb, C, x_in, w, xln, pb)
    t_ln = kb.cnt["dve"]
    if DEBUG_L1FFN:
        kb.wait(nc.sync, "dve", max(t_prep, t_ln))
        kb.dinc(nc.sync.dma_start(out=dbg_xln[:, :], in_=a_xln[:, 0:T]), "stq")
        kb.dinc(nc.sync.dma_start(out=dbg_asel[:, :], in_=a_sel[0][:]), "stq")
        kb.dinc(nc.sync.dma_start(out=dbg_bsel[:, :], in_=b_sel[0][:]), "stq")

    wi_src = wiT.rearrange("(c p) h -> p c h", p=128)
    wo_src = woT.rearrange("(h p) c -> p h c", p=128)
    wi_tok = [0, 0]
    wo_tok = [0, 0]
    x_tok = [t_ln, t_ln]
    for tt in range(NTT):
        ts = tt * TTW
        kb.wait(nc.tensor, "dve", max(t_prep, t_ln))
        for k in range(TOPK):
            pb[4].wait_reuse(kb, nc.tensor)
            for cc in range(NC):
                i = nc.tensor.matmul(pb[4].ap[0:16, :],
                                     a_sel[k][:, cc * RANK:(cc + 1) * RANK],
                                     xln[cc][:, ts:ts + TTW],
                                     start=(cc == 0), stop=(cc == NC - 1))
            t_t1 = kb.inc(i, "pe")
            kb.wait(nc.vector, "pe", t_t1)
            t_ev = kb.inc(nc.vector.tensor_copy(t1_sb[k][:, :], pb[4].ap[0:16, :]), "dve")
            pb[4].consumed("dve", t_ev)
            if DEBUG_L1FFN and tt == 0 and k == 0:
                kb.wait(nc.sync, "dve", t_ev)
                kb.dinc(nc.sync.dma_start(out=dbg_t1[:, :], in_=t1_sb[k][:]), "stq")
        t_t1all = t_ev
        for hc in range(32):
            buf = hc % 2
            kb.wait(nc.gpsimd, "pe", wi_tok[buf])
            t_w = kb.dinc(nc.gpsimd.dma_start(
                out=wi_sb[buf][:].rearrange("p (c h) -> p c h", c=NC),
                in_=wi_src[:, :, hc * 128:(hc + 1) * 128]), "wq")
            bank = pb[4 + (hc % 2)]
            bank.wait_reuse(kb, nc.tensor)
            kb.wait(nc.tensor, "wq", t_w)
            for cc in range(NC):
                i = nc.tensor.matmul(bank.ap[:, :],
                                     wi_sb[buf][:, cc * 128:(cc + 1) * 128],
                                     xln[cc][:, ts:ts + TTW],
                                     start=(cc == 0), stop=(cc == NC - 1))
            wi_tok[buf] = t_mm = kb.inc(i, "pe")
            kb.wait(nc.scalar, "pe", t_mm)
            t_e = kb.inc(nc.scalar.activation(
                a_int[:, hc * TTW:(hc + 1) * TTW], bank.ap[:, :], AF.Copy), "act")
            bank.consumed("act", t_e)
            if DEBUG_L1FFN and tt == 0 and hc == 0:
                kb.wait(nc.sync, "act", t_e)
                kb.dinc(nc.sync.dma_start(out=dbg_int[:, :], in_=a_int[:, 0:TTW]), "stq")
        for k in range(TOPK):
            dst = a_h0 if k == 0 else a_int
            for hc in range(32):
                bank = pb[4 + (hc % 2)]
                bank.wait_reuse(kb, nc.tensor)
                kb.wait(nc.tensor, "dve", t_t1all)
                nc.tensor.matmul(bank.ap[:, :], b_sel[k][:, hc * 128:(hc + 1) * 128],
                                 t1_sb[k][:, :], start=True, stop=False)
                i = nc.tensor.matmul(bank.ap[:, :], C["id_bf"],
                                     a_int[:, hc * TTW:(hc + 1) * TTW],
                                     start=False, stop=True)
                t_mm = kb.inc(i, "pe")
                kb.wait(nc.scalar, "pe", t_mm)
                t_e = kb.inc(nc.scalar.activation(
                    dst[:, hc * TTW:(hc + 1) * TTW], bank.ap[:, :], AF.Relu), "act")
                bank.consumed("act", t_e)
                if DEBUG_L1FFN and tt == 0 and hc == 0 and k == 0:
                    kb.wait(nc.sync, "act", t_e)
                    kb.dinc(nc.sync.dma_start(out=dbg_h0[:, :], in_=a_h0[:, 0:TTW]), "stq")
        t_relu = kb.cnt["act"]
        for grp in ((0, 1, 2), (3, 4, 5), (6, 7)):
            for cc in grp:
                buf = cc % 2
                kb.wait(nc.gpsimd, "pe", wo_tok[buf])
                t_w = kb.dinc(nc.gpsimd.dma_start(
                    out=wo_sb[buf][:].rearrange("p (h c) -> p h c", h=32),
                    in_=wo_src[:, :, cc * 128:(cc + 1) * 128]), "wq")
                b0 = pb[2 * (cc - grp[0])]
                b1 = pb[2 * (cc - grp[0]) + 1]
                b0.wait_reuse(kb, nc.tensor)
                b1.wait_reuse(kb, nc.tensor)
                kb.wait(nc.tensor, "wq", t_w)
                kb.wait(nc.tensor, "act", t_relu)
                for hc in range(32):
                    nc.tensor.matmul(b0.ap[:, :], wo_sb[buf][:, hc * 128:(hc + 1) * 128],
                                     a_h0[:, hc * TTW:(hc + 1) * TTW],
                                     start=(hc == 0), stop=(hc == 31))
                for hc in range(32):
                    i = nc.tensor.matmul(b1.ap[:, :], wo_sb[buf][:, hc * 128:(hc + 1) * 128],
                                         a_int[:, hc * TTW:(hc + 1) * TTW],
                                         start=(hc == 0), stop=(hc == 31))
                wo_tok[buf] = t_mm = kb.inc(i, "pe")
                xbuf = cc % 2
                kb.wait(nc.sync, "dve", x_tok[xbuf])
                t_x = kb.dinc(nc.sync.dma_start(
                    out=x_sb[xbuf][:],
                    in_=x_in[cc * 128:(cc + 1) * 128, ts:ts + TTW]), "ldq")
                kb.wait(nc.vector, "ldq", t_x)
                kb.wait(nc.vector, "pe", t_mm)
                nc.vector.scalar_tensor_tensor(out=x_sb[xbuf][:], in0=b0.ap[:, :],
                                               scalar=tkv_col[0][:, :], in1=x_sb[xbuf][:],
                                               op0=ALU.mult, op1=ALU.add)
                i = nc.vector.scalar_tensor_tensor(out=x_sb[xbuf][:], in0=b1.ap[:, :],
                                                   scalar=tkv_col[1][:, :], in1=x_sb[xbuf][:],
                                                   op0=ALU.mult, op1=ALU.add)
                x_tok[xbuf] = t_u = kb.inc(i, "dve")
                b0.consumed("dve", t_u)
                b1.consumed("dve", t_u)
                kb.wait(nc.sync, "dve", t_u)
                kb.dinc(nc.sync.dma_start(
                    out=x_out[cc * 128:(cc + 1) * 128, ts:ts + TTW],
                    in_=x_sb[xbuf][:]), "stq")
    kb.wait(nc.sync, "stq")


def C_iota(kb, C):
    return C["iota_col_f32"]




def _rope_tables():
    inv = 1.0 / (10000.0 ** (np.arange(0, HD, 2, dtype=np.float32) / HD))
    t = np.arange(T, dtype=np.float32)
    freqs = t[:, None] * inv[None, :]
    emb = np.concatenate([freqs, freqs], axis=-1)
    cos = np.cos(emb)
    sin = np.sin(emb)
    return np.tile(cos.T, (2, 1)), np.tile(sin.T, (2, 1))


def _p_rot():
    p = np.zeros((128, 128), np.float32)
    for b in (0, 64):
        for j in range(32):
            p[b + 2 * j + 1, b + 2 * j] = -1.0
            p[b + 2 * j, b + 2 * j + 1] = 1.0
    return p


def emit_l0attn(kb, C, x_in, x_out, W, S):
    nc = kb.nc
    cos_d, sin_d, p_rot = C["cos_d"], C["sin_d"], C["p_rot"]
    qwT, kwT, vwT, owT = W["qwT"], W["kwT"], W["vwT"], W["owT"]
    qb_d, kbias_d, obr_d, vbr_d = W["qb"], W["kbias"], W["ob_row"], W["vb_row"]
    qn_d, kn_d = W["qn"], W["kn"]
    q_dram = S["q_dram"]
    k_dram = S["k_dram"]

    a_xln = kb.sb("a_xln", [128, NC * T], BF16)
    xln = [a_xln[:, c * T:(c + 1) * T] for c in range(NC)]
    VA_W = NH * 65
    v_attn = kb.sb("v_attn", [128, 32 * VA_W], BF16)
    w = ln_work_tiles(kb)
    warena = kb.sb("warena", [128, NC * D], BF16)
    wA = warena[:, 0:NC * D]
    qb_sb = kb.sb("qb_sb", [128, NC], F32)
    kb_sb = kb.sb("kb_sb", [128, NC], F32)
    qn_sb = kb.sb("qn_sb", [128, NC], F32)
    kn_sb = kb.sb("kn_sb", [128, NC], F32)
    obr_sb = kb.sb("obr_sb", [1, D], BF16)
    vbr_sb = kb.sb("vbr_sb", [1, D], BF16)
    qsb = kb.sb("qsb", [128, TTW], BF16)
    rtmp = kb.sb("rtmp", [128, TTW], BF16)
    rtmp2 = kb.sb("rtmp2", [128, TTW], BF16)
    cs_sb = kb.sb("cs_sb", [128, 2 * TTW], BF16)
    k_sb = kb.sb("k_sb", [128, 2048], BF16)
    qq = kb.sb("qq", [128, TTW], BF16)
    w_ring = kb.sb("w_ring", [128, 4 * TTW], BF16)
    pquad = kb.ps("pquad", [128, 4 * TTW])
    pb45 = [kb.ps("pb4", [128, TTW]), kb.ps("pb5", [128, TTW])]
    pb = [Bank(pquad[:, i * TTW:(i + 1) * TTW]) for i in range(4)] + \
         [Bank(pb45[0][:, :]), Bank(pb45[1][:, :])]
    pq = [Bank(pquad[:, 0:2 * TTW]), Bank(pquad[:, 2 * TTW:4 * TTW])]
    out65 = w["scr"][0:65, 0:TTW]
    zr = w["scr"][0:1, TTW:2 * TTW]
    zr2 = w["scr"][0:1, 2 * TTW:3 * TTW]

    t_small = 0
    for dst, src in ((qb_sb, qb_d), (kb_sb, kbias_d), (qn_sb, qn_d), (kn_sb, kn_d)):
        t_small = kb.dinc(nc.sync.dma_start(out=dst[:], in_=src), "ldq")
    t_smallg = kb.dinc(nc.gpsimd.dma_start(out=obr_sb[:], in_=obr_d), "wq")
    t_smallg = kb.dinc(nc.gpsimd.dma_start(out=vbr_sb[:], in_=vbr_d), "wq")
    va = v_attn[:].rearrange("p (c h e) -> p c h e", c=32, h=NH)
    t_ones = kb.inc(nc.vector.memset(va[:, :, :, 64:65], 1.0), "dve")
    t_wv = kb.dinc(nc.gpsimd.dma_start(
        out=wA.rearrange("p (c j) -> p c j", c=NC),
        in_=vwT.rearrange("(c p) j -> p c j", p=128)), "wq")

    emit_ln(kb, C, x_in, w, xln, pb)
    t_ln = kb.cnt["dve"]

    for tc in range(32):
        tcs = tc * 128
        for half in range(2):
            bank = pb[half]
            bank.wait_reuse(kb, nc.tensor)
            if tc == 0 and half == 0:
                kb.wait(nc.tensor, "wq", t_wv)
                kb.wait(nc.tensor, "dve", max(t_ln, t_ones))
            for cc in range(NC):
                nc.tensor.matmul(bank.ap[:, :],
                                 xln[cc][:, tcs:tcs + 128],
                                 wA[:, cc * D + half * TTW:cc * D + half * TTW + TTW],
                                 start=(cc == 0), stop=False)
            i = nc.tensor.matmul(bank.ap[:, :], C["ones_row_bf"][0:1, 0:128],
                                 vbr_sb[0:1, half * TTW:half * TTW + TTW],
                                 start=False, stop=True)
            t_mm = kb.inc(i, "pe")
            kb.wait(nc.vector, "pe", t_mm)
            dstv = va[:, tc, half * NC:half * NC + NC, 0:64]
            i = nc.vector.tensor_copy(
                dstv, bank.ap[:, :].rearrange("p (h e) -> p h e", h=NC))
            t_ev = kb.inc(i, "dve")
            bank.consumed("dve", t_ev)
    t_vdone = kb.cnt["dve"]

    xf, xb, sq, stat = w["xf"], w["xb"], w["sq"], w["stat"]
    s1 = stat[:, TTW:2 * TTW]
    s2 = stat[:, 2 * TTW:3 * TTW]
    for tt in range(NTT):
        ts = tt * TTW
        last = None
        for cc in range(NC):
            last = nc.scalar.activation(sq[:, cc * TTW:(cc + 1) * TTW],
                                        xln[cc][:, ts:ts + TTW], AF.Square)
        t_sq = kb.inc(last, "act")
        pb[0].wait_reuse(kb, nc.tensor)
        kb.wait(nc.tensor, "act", t_sq)
        for cc in range(NC):
            i = nc.tensor.matmul(pb[0].ap[0:1, :], C["ones_col_bf"],
                                 sq[:, cc * TTW:(cc + 1) * TTW],
                                 start=(cc == 0), stop=(cc == NC - 1))
        t_mm = kb.inc(i, "pe")
        kb.wait(nc.vector, "pe", t_mm)
        t_v = kb.inc(nc.vector.tensor_scalar(s1, pb[0].ap[0:1, :], 1.0 / D, RMS_EPS,
                                             op0=ALU.mult, op1=ALU.add), "dve")
        pb[0].consumed("dve", t_v)
        kb.wait(nc.scalar, "dve", t_v)
        t_sd = kb.inc(nc.scalar.activation(s2, s1, AF.Sqrt), "act")
        kb.wait(nc.vector, "act", t_sd)
        t_rc = kb.inc(nc.vector.reciprocal(s1, s2), "dve")
        pb[2].wait_reuse(kb, nc.tensor)
        kb.wait(nc.tensor, "dve", t_rc)
        t_bc = kb.inc(nc.tensor.matmul(pb[2].ap[:, :], C["ones_row_f32"], s1,
                                       start=True, stop=True), "pe")
        kb.wait(nc.vector, "pe", t_bc)
        nc.vector.tensor_copy(xb[:, 0:TTW], pb[2].ap[:, :])
        for cc in range(NC):
            i = nc.vector.tensor_tensor(xln[cc][:, ts:ts + TTW],
                                        xln[cc][:, ts:ts + TTW], xb[:, 0:TTW],
                                        op=ALU.mult)
        t_rms = kb.inc(i, "dve")
        pb[2].consumed("dve", t_rms)

    wsrc = {"q": qwT, "k": kwT}
    wscale = {"q": qn_sb, "k": kn_sb}
    wbias = {"q": qb_sb, "k": kb_sb}
    wdram = {"q": q_dram, "k": k_dram}
    wtmp = [w_ring[:, 0:D], w_ring[:, D:2 * D]]
    t_st_prev = 0
    for name in ("q", "k"):
        src = wsrc[name].rearrange("(c p) j -> p c j", p=128)
        bias, dram = wbias[name], wdram[name]
        wtmp_tok = [0, 0]
        t_prior_pe = kb.cnt["pe"]
        t_wready = 0
        for cc in range(NC):
            buf = cc % 2
            kb.wait(nc.gpsimd, "dve", wtmp_tok[buf])
            t_w = kb.dinc(nc.gpsimd.dma_start(out=wtmp[buf][:, :], in_=src[:, cc, :]), "wq")
            kb.wait(nc.vector, "wq", t_w)
            if cc == 0:
                kb.wait(nc.vector, "pe", t_prior_pe)
            i = nc.vector.tensor_scalar(wA[:, cc * D:(cc + 1) * D],
                                        wtmp[buf][:, :], wscale[name][:, cc:cc + 1], None,
                                        op0=ALU.mult)
            wtmp_tok[buf] = t_wready = kb.inc(i, "dve")
        for tt in range(NTT):
            ts = tt * TTW
            kb.wait(nc.sync, "dve", t_st_prev if False else 0)
            t_cs = kb.dinc(nc.sync.dma_start(out=cs_sb[:, 0:TTW], in_=cos_d[:, ts:ts + TTW]), "ldq")
            t_cs = kb.dinc(nc.sync.dma_start(out=cs_sb[:, TTW:2 * TTW], in_=sin_d[:, ts:ts + TTW]), "ldq")
            for cc in range(NC):
                bank = pb[cc % 2]
                rbank = pb[2 + cc % 2]
                bank.wait_reuse(kb, nc.tensor)
                if tt == 0 and cc == 0:
                    kb.wait(nc.tensor, "dve", t_wready)
                for c2 in range(NC):
                    i = nc.tensor.matmul(bank.ap[:, :],
                                         wA[:, c2 * D + cc * 128:c2 * D + cc * 128 + 128],
                                         xln[c2][:, ts:ts + TTW],
                                         start=(c2 == 0), stop=(c2 == NC - 1))
                t_mm = kb.inc(i, "pe")
                kb.wait(nc.vector, "pe", t_mm)
                t_q = kb.inc(nc.vector.tensor_scalar(qsb[:], bank.ap[:, :],
                                                     bias[:, cc:cc + 1], None,
                                                     op0=ALU.add), "dve")
                bank.consumed("dve", t_q)
                rbank.wait_reuse(kb, nc.tensor)
                kb.wait(nc.tensor, "dve", t_q)
                t_rot = kb.inc(nc.tensor.matmul(rbank.ap[:, :], p_rot[:], qsb[:],
                                                start=True, stop=True), "pe")
                kb.wait(nc.vector, "pe", t_rot)
                kb.wait(nc.vector, "ldq", t_cs)
                kb.wait(nc.vector, "stq", t_st_prev)
                nc.vector.tensor_copy(rtmp[:], rbank.ap[:, :])
                nc.vector.tensor_tensor(rtmp2[:], qsb[:], cs_sb[:, 0:TTW], op=ALU.mult)
                nc.vector.tensor_tensor(rtmp[:], rtmp[:], cs_sb[:, TTW:2 * TTW], op=ALU.mult)
                t_r = kb.inc(nc.vector.tensor_tensor(rtmp2[:], rtmp2[:], rtmp[:],
                                                     op=ALU.add), "dve")
                rbank.consumed("dve", t_r)
                kb.wait(nc.sync, "dve", t_r)
                t_st_prev = kb.dinc(nc.sync.dma_start(
                    out=dram[cc * 128:(cc + 1) * 128, ts:ts + TTW], in_=rtmp2[:]), "stq")
    t_qk = kb.cnt["stq"]

    wpair_tok = [0, 0]
    t_arena_ok = kb.cnt["pe"]
    t_av = 0
    kb.wait(nc.tensor, "dve")
    kb.wait(nc.tensor, "act")
    for h in range(NH):
        kb.wait(nc.sync, "stq", t_qk)
        kb.wait(nc.sync, "pe", t_av)
        t_k = kb.dinc(nc.sync.dma_start(out=k_sb[0:64, :],
                                        in_=k_dram[h * 64:h * 64 + 64, 0:2048]), "ldq")
        t_k = kb.dinc(nc.sync.dma_start(out=k_sb[64:128, :],
                                        in_=k_dram[h * 64:h * 64 + 64, 2048:4096]), "ldq")
        for qt in range(NTT):
            qs = qt * TTW
            kb.wait(nc.sync, "pe", t_av)
            t_qq = kb.dinc(nc.sync.dma_start(out=qq[0:64, :],
                                             in_=q_dram[h * 64:h * 64 + 64, qs:qs + TTW]), "ldq")
            t_qq = kb.dinc(nc.sync.dma_start(out=qq[64:128, :],
                                             in_=q_dram[h * 64:h * 64 + 64, qs:qs + TTW]), "ldq")
            kb.wait(nc.tensor, "ldq", t_qq)
            t_sg_of = {}
            for pc in range(17):
                if pc < 16:
                    c0 = pc
                    q_ = pq[pc % 2]
                    q_.wait_reuse(kb, nc.tensor)
                    if pc == 0:
                        pb[4].wait_reuse(kb, nc.tensor)
                        pb[5].wait_reuse(kb, nc.tensor)
                    nc.tensor.matmul(q_.ap[:, 0:TTW], k_sb[0:64, c0 * 128:c0 * 128 + 128],
                                     qq[0:64, :], start=True, stop=True, tile_position=(0, 0))
                    i = nc.tensor.matmul(q_.ap[:, TTW:2 * TTW],
                                         k_sb[64:128, c0 * 128:c0 * 128 + 128],
                                         qq[64:128, :], start=True, stop=True,
                                         tile_position=(64, 0))
                    t_sc = kb.inc(i, "pe")
                    sp = pc % 2
                    kb.wait(nc.scalar, "pe", max(t_sc, wpair_tok[sp]))
                    t_sg = kb.inc(nc.scalar.activation(
                        w_ring[:, sp * 2 * TTW:(sp + 1) * 2 * TTW], q_.ap[:, :],
                        AF.Sigmoid, scale=0.125), "act")
                    q_.consumed("act", t_sg)
                    t_sg_of[pc] = t_sg
                if pc >= 1:
                    pd = pc - 1
                    sp = pd % 2
                    kb.wait(nc.tensor, "act", t_sg_of[pd])
                    for ci, off in ((pd, sp * 2 * TTW), (pd + 16, sp * 2 * TTW + TTW)):
                        vt = v_attn[:, ci * VA_W + h * 65:ci * VA_W + h * 65 + 65]
                        nc.tensor.matmul(pb[4].ap[0:65, :], vt[0:64, :],
                                         w_ring[0:64, off:off + TTW],
                                         start=(pd == 0 and ci == pd), stop=(pd == 15 and ci == pd + 16),
                                         tile_position=(0, 0))
                        i = nc.tensor.matmul(pb[5].ap[0:65, :], vt[64:128, :],
                                             w_ring[64:128, off:off + TTW],
                                             start=(pd == 0 and ci == pd), stop=(pd == 15 and ci == pd + 16),
                                             tile_position=(64, 0))
                    t_av = kb.inc(i, "pe")
                    wpair_tok[sp] = t_av
            kb.wait(nc.vector, "pe", t_av)
            nc.vector.tensor_copy(out65, pb[4].ap[0:65, :])
            t_o = kb.inc(nc.vector.tensor_tensor(out65, out65, pb[5].ap[0:65, :],
                                                 op=ALU.add), "dve")
            pb[5].consumed("dve", t_o)
            kb.wait(nc.vector, "dve", t_o)
            t_z = kb.inc(nc.vector.tensor_scalar(zr, out65[64:65, :], SIG_EPS, None,
                                                 op0=ALU.add), "dve")
            kb.wait(nc.vector, "dve", t_z)
            t_rc = kb.inc(nc.vector.reciprocal(zr2, zr), "dve")
            kb.wait(nc.tensor, "dve", t_rc)
            t_bc = kb.inc(nc.tensor.matmul(pb[4].ap[:, :], C["ones_row_f32"], zr2,
                                           start=True, stop=True), "pe")
            kb.wait(nc.vector, "pe", max(t_bc, t_arena_ok))
            dst = a_xln[(h % 2) * 64:(h % 2) * 64 + 64, (h // 2) * T + qs:(h // 2) * T + qs + TTW]
            t_at = kb.inc(nc.vector.tensor_tensor(dst, out65[0:64, 0:TTW], pb[4].ap[0:64, :],
                                                  op=ALU.mult), "dve")
            pb[4].consumed("dve", t_at)
    t_attn = kb.cnt["dve"]

    kb.wait(nc.gpsimd, "pe", kb.cnt["pe"])
    t_wo = kb.dinc(nc.gpsimd.dma_start(
        out=wA.rearrange("p (c j) -> p c j", c=NC),
        in_=owT.rearrange("(c p) j -> p c j", p=128)), "wq")
    x_sb = [xf[:, 0:TTW], xf[:, TTW:2 * TTW]]
    x_tok = [0, 0]
    for cc in range(NC):
        for tt in range(NTT):
            ts = tt * TTW
            bank = pb[tt % 2]
            bank.wait_reuse(kb, nc.tensor)
            if cc == 0 and tt == 0:
                kb.wait(nc.tensor, "wq", t_wo)
                kb.wait(nc.tensor, "dve", t_attn)
                kb.wait(nc.tensor, "wq", t_smallg)
            for c2 in range(NC):
                nc.tensor.matmul(bank.ap[:, :],
                                 wA[:, c2 * D + cc * 128:c2 * D + cc * 128 + 128],
                                 a_xln[:, c2 * T + ts:c2 * T + ts + TTW],
                                 start=(c2 == 0), stop=False)
            i = nc.tensor.matmul(bank.ap[:, :], obr_sb[0:1, cc * 128:cc * 128 + 128],
                                 C["ones_row512_bf"], start=False, stop=True)
            t_mm = kb.inc(i, "pe")
            xb_i = tt % 2
            kb.wait(nc.sync, "dve", x_tok[xb_i])
            t_x = kb.dinc(nc.sync.dma_start(
                out=x_sb[xb_i], in_=x_in[cc * 128:(cc + 1) * 128, ts:ts + TTW]), "ldq")
            kb.wait(nc.vector, "ldq", t_x)
            kb.wait(nc.vector, "pe", t_mm)
            i = nc.vector.tensor_tensor(x_sb[xb_i], x_sb[xb_i], bank.ap[:, :], op=ALU.add)
            x_tok[xb_i] = t_u = kb.inc(i, "dve")
            bank.consumed("dve", t_u)
            kb.wait(nc.sync, "dve", t_u)
            kb.dinc(nc.sync.dma_start(
                out=x_out[cc * 128:(cc + 1) * 128, ts:ts + TTW], in_=x_sb[xb_i]), "stq")

    kb.wait(nc.sync, "stq")


def emit_l1attn(kb, C, x_in, x_out, W, S, phase_limit=None):
    nc = kb.nc
    Wq, Wk, Wv, Wo = W["Wq"], W["Wk"], W["Wv"], W["Wo"]
    qb_d, kbias_d, obr_d, vbr_d = W["qb"], W["kbias"], W["ob_row"], W["vb_row"]
    q_dram = S["q_dram"]
    k_dram = S["k_dram"]

    a_xln = kb.sb("a_xln", [128, NC * T], BF16)
    xln = [a_xln[:, c * T:(c + 1) * T] for c in range(NC)]
    VA_W = NH * 65
    v_attn = kb.sb("v_attn", [128, 32 * VA_W], BF16)
    w = ln_work_tiles(kb)
    warena = kb.sb("warena", [128, NC * D], BF16)
    wA = warena[:, 0:NC * D]
    qb_sb = kb.sb("qb_sb", [128, NC], F32)
    kb_sb = kb.sb("kb_sb", [128, NC], F32)
    obr_sb = kb.sb("obr_sb", [1, D], BF16)
    vbr_sb = kb.sb("vbr_sb", [1, D], BF16)
    qsb = kb.sb("qsb", [128, TTW], BF16)
    k_sb = kb.sb("k_sb", [64, 8 * 128], BF16)
    kcol0 = kb.sb("kcol0", [64, 1], BF16)
    qq = kb.sb("qq", [64, TTW], BF16)
    w_ring = kb.sb("w_ring", [128, 4 * TTW], BF16)
    w0_sb = kb.sb("w0_sb", [1, TTW], BF16)
    wq0_sb = kb.sb("wq0_sb", [128, 32], BF16)
    qmax = kb.sb("qmax", [1, NTT], F32)
    kmax = kb.sb("kmax", [1, 1], F32)
    mrow = kb.sb("mrow", [1, NTT], F32)
    mbias = kb.sb("mbias", [128, 1], F32)

    pb = [Bank(kb.ps(f"pb{i}", [128, TTW])) for i in range(6)]
    out65 = w["scr"][0:65, 0:TTW]
    zr = w["scr"][0:1, TTW:2 * TTW]
    zr2 = w["scr"][0:1, 2 * TTW:3 * TTW]

    t_small = 0
    for dst, src in ((qb_sb, qb_d), (kb_sb, kbias_d)):
        t_small = kb.dinc(nc.sync.dma_start(out=dst[:], in_=src), "ldq")
    t_smallg = kb.dinc(nc.gpsimd.dma_start(out=obr_sb[:], in_=obr_d), "wq")
    t_smallg = kb.dinc(nc.gpsimd.dma_start(out=vbr_sb[:], in_=vbr_d), "wq")
    va = v_attn[:].rearrange("p (c h e) -> p c h e", c=32, h=NH)
    t_ones = kb.inc(nc.vector.memset(va[:, :, :, 64:65], 1.0), "dve")
    t_wv = kb.dinc(nc.gpsimd.dma_start(
        out=wA.rearrange("p (c j) -> p c j", c=NC),
        in_=Wv.rearrange("(c p) j -> p c j", p=128)), "wq")

    emit_ln(kb, C, x_in, w, xln, pb)
    t_ln = kb.cnt["dve"]
    if phase_limit == "ln":
        kb.wait(nc.sync, "ldq")
        return

    for tc in range(32):
        tcs = tc * 128
        for half in range(2):
            bank = pb[half]
            bank.wait_reuse(kb, nc.tensor)
            if tc == 0 and half == 0:
                kb.wait(nc.tensor, "wq", t_wv)
                kb.wait(nc.tensor, "dve", max(t_ln, t_ones))
            for cc in range(NC):
                nc.tensor.matmul(bank.ap[:, :],
                                 xln[cc][:, tcs:tcs + 128],
                                 wA[:, cc * D + half * TTW:cc * D + half * TTW + TTW],
                                 start=(cc == 0), stop=False)
            i = nc.tensor.matmul(bank.ap[:, :], C["ones_row_bf"][0:1, 0:128],
                                 vbr_sb[0:1, half * TTW:half * TTW + TTW],
                                 start=False, stop=True)
            t_mm = kb.inc(i, "pe")
            kb.wait(nc.vector, "pe", t_mm)
            dstv = va[:, tc, half * NC:half * NC + NC, 0:64]
            i = nc.vector.tensor_copy(
                dstv, bank.ap[:, :].rearrange("p (h e) -> p h e", h=NC))
            t_ev = kb.inc(i, "dve")
            bank.consumed("dve", t_ev)
    if phase_limit == "vproj":
        kb.wait(nc.vector, "dve")
        kb.wait(nc.sync, "ldq")
        return

    wsrc = {"q": Wq, "k": Wk}
    wbias = {"q": qb_sb, "k": kb_sb}
    wdram = {"q": q_dram, "k": k_dram}
    t_st_prev = 0
    for name in ("q", "k"):
        t_prior_pe = kb.cnt["pe"]
        kb.wait(nc.gpsimd, "pe", t_prior_pe)
        t_w = kb.dinc(nc.gpsimd.dma_start(
            out=wA.rearrange("p (c j) -> p c j", c=NC),
            in_=wsrc[name].rearrange("(c p) j -> p c j", p=128)), "wq")
        bias, dram = wbias[name], wdram[name]
        for tt in range(NTT):
            ts = tt * TTW
            for cc in range(NC):
                bank = pb[cc % 2]
                bank.wait_reuse(kb, nc.tensor)
                if tt == 0 and cc == 0:
                    kb.wait(nc.tensor, "wq", t_w)
                for c2 in range(NC):
                    i = nc.tensor.matmul(bank.ap[:, :],
                                         wA[:, c2 * D + cc * 128:c2 * D + cc * 128 + 128],
                                         xln[c2][:, ts:ts + TTW],
                                         start=(c2 == 0), stop=(c2 == NC - 1))
                t_mm = kb.inc(i, "pe")
                kb.wait(nc.vector, "pe", t_mm)
                kb.wait(nc.vector, "stq", t_st_prev)
                t_q = kb.inc(nc.vector.tensor_scalar(qsb[:], bank.ap[:, :],
                                                     bias[:, cc:cc + 1], None,
                                                     op0=ALU.add), "dve")
                bank.consumed("dve", t_q)
                kb.wait(nc.sync, "dve", t_q)
                t_st_prev = kb.dinc(nc.sync.dma_start(
                    out=dram[cc * 128:(cc + 1) * 128, ts:ts + TTW], in_=qsb[:]), "stq")
    t_qk = kb.cnt["stq"]
    if phase_limit == "qkproj":
        kb.wait(nc.sync, "stq")
        return

    wslot_tok = [0, 0, 0, 0]
    t_arena_ok = kb.cnt["pe"]
    t_av = 0
    t_kf_pe = 0
    kfull = warena[0:64, 0:2 * T]
    for h in range(NH):
        hs = h * 64
        kb.wait(nc.sync, "stq", t_qk)
        kb.wait(nc.sync, "pe", max(t_arena_ok, t_kf_pe))
        t_qh = kb.dinc(nc.sync.dma_start(out=kfull[:, 0:T], in_=q_dram[hs:hs + 64, :]), "ldq")
        if phase_limit == "cut1" and h == 0:
            kb.wait(nc.sync, "ldq")
            return
        kb.wait(nc.scalar, "ldq", t_qh)
        t_sqq = kb.inc(nc.scalar.activation(kfull[:, T:2 * T], kfull[:, 0:T], AF.Square), "act")
        if phase_limit == "cut2" and h == 0:
            kb.wait(nc.sync, "act")
            kb.wait(nc.sync, "ldq")
            return
        kb.wait(nc.tensor, "act", t_sqq)
        t_nq = 0
        nrm_row = w["scr"][0:1, 0:TTW]
        for qt in range(NTT):
            pb[0].wait_reuse(kb, nc.tensor)
            i = nc.tensor.matmul(pb[0].ap[0:1, :],
                                 C["ones_col_bf"][0:64, :],
                                 kfull[:, T + qt * TTW:T + (qt + 1) * TTW],
                                 start=True, stop=True)
            t_nq = kb.inc(i, "pe")
            kb.wait(nc.vector, "pe", t_nq)
            t_cp = kb.inc(nc.vector.tensor_copy(nrm_row, pb[0].ap[0:1, :]), "dve")
            pb[0].consumed("dve", t_cp)
            kb.wait(nc.vector, "dve", t_cp)
            t_rm = kb.inc(nc.vector.reduce_max(qmax[0:1, qt:qt + 1], nrm_row,
                                               axis=AX.X), "dve")
        if phase_limit == "cut3" and h == 0:
            kb.wait(nc.sync, "dve")
            kb.wait(nc.sync, "ldq")
            return
        kb.wait(nc.sync, "act", t_sqq)
        t_kh = kb.dinc(nc.sync.dma_start(out=kfull[:, 0:T], in_=k_dram[hs:hs + 64, :]), "ldq")
        kb.wait(nc.scalar, "ldq", t_kh)
        kb.wait(nc.scalar, "pe", t_nq)
        t_sqk = kb.inc(nc.scalar.activation(kfull[:, T:2 * T], kfull[:, 0:T], AF.Square), "act")
        kb.wait(nc.tensor, "act", t_sqk)
        for qt in range(NTT):
            pb[0].wait_reuse(kb, nc.tensor)
            i = nc.tensor.matmul(pb[0].ap[0:1, :], C["ones_col_bf"][0:64, :],
                                 kfull[:, T + qt * TTW:T + (qt + 1) * TTW],
                                 start=True, stop=True)
            t_nk = kb.inc(i, "pe")
            kb.wait(nc.vector, "pe", t_nk)
            t_cp = kb.inc(nc.vector.tensor_copy(nrm_row, pb[0].ap[0:1, :]), "dve")
            pb[0].consumed("dve", t_cp)
            kb.wait(nc.vector, "dve", t_cp)
            t_rm = kb.inc(nc.vector.reduce_max(mrow[0:1, qt:qt + 1], nrm_row,
                                               axis=AX.X), "dve")
        kb.wait(nc.vector, "dve", kb.cnt["dve"])
        t_km = kb.inc(nc.vector.reduce_max(kmax[0:1, :], mrow[0:1, :], axis=AX.X), "dve")
        kb.wait(nc.vector, "dve", t_km)
        t_m1 = kb.inc(nc.vector.tensor_scalar(mrow[0:1, 0:NTT], qmax[0:1, :], kmax[0:1, 0:1],
                                              None, op0=ALU.mult), "dve")
        kb.wait(nc.scalar, "dve", t_m1)
        nc.scalar.activation(mrow[0:1, 0:NTT], mrow[0:1, 0:NTT], AF.Ln)
        t_m2 = kb.inc(nc.scalar.activation(mrow[0:1, 0:NTT], mrow[0:1, 0:NTT], AF.Exp,
                                           scale=0.5), "act")
        kb.wait(nc.vector, "act", t_m2)
        t_m3 = kb.inc(nc.vector.tensor_scalar(mrow[0:1, 0:NTT], mrow[0:1, 0:NTT], -0.125,
                                              None, op0=ALU.mult), "dve")
        if phase_limit == "norms0" and h == 0:
            return
        if phase_limit == "norms" and h == 1:
            return
        t_k0 = kb.dinc(nc.sync.dma_start(out=kcol0[:], in_=k_dram[hs:hs + 64, 0:1], allow_slow_non_contiguous=True), "ldq")
        for qt in range(NTT):
            qs = qt * TTW
            lo = max(0, 4 * qt - 2)
            hi = min(31, 4 * qt + 5)
            nblk = hi - lo + 1
            kb.wait(nc.sync, "pe", t_av)
            t_qq = kb.dinc(nc.sync.dma_start(out=qq[:, :],
                                             in_=q_dram[hs:hs + 64, qs:qs + TTW]), "ldq")
            t_kb = kb.dinc(nc.sync.dma_start(out=k_sb[:, 0:nblk * 128],
                                             in_=k_dram[hs:hs + 64, lo * 128:(hi + 1) * 128]), "ldq")
            pb[3].wait_reuse(kb, nc.tensor)
            kb.wait(nc.tensor, "dve", t_m3)
            t_mb = kb.inc(nc.tensor.matmul(pb[3].ap[:, 0:1], C["ones_row_f32"],
                                           mrow[0:1, qt:qt + 1], start=True, stop=True), "pe")
            kb.wait(nc.vector, "pe", t_mb)
            kb.wait(nc.vector, "act")
            t_mb2 = kb.inc(nc.vector.tensor_copy(mbias[:], pb[3].ap[:, 0:1]), "dve")
            pb[3].consumed("dve", t_mb2)
            kb.wait(nc.tensor, "ldq", t_kb)
            n_contrib = nblk + 1 + (30 if qt == 0 else 0)
            ci = 0
            pb[4].wait_reuse(kb, nc.tensor)
            blk_ready = {}
            for bi in range(nblk + 1):
                if bi < nblk:
                    kbk = lo + bi
                    sA = pb[bi % 2]
                    sA.wait_reuse(kb, nc.tensor)
                    i = nc.tensor.matmul(sA.ap[:, :], k_sb[:, bi * 128:bi * 128 + 128],
                                         qq[:, :], start=True, stop=True)
                    t_sc = kb.inc(i, "pe")
                    sl = bi % 4
                    kb.wait(nc.scalar, "pe", max(t_sc, wslot_tok[sl]))
                    kb.wait(nc.scalar, "dve", t_mb2)
                    t_sg = kb.inc(nc.scalar.activation(w_ring[:, sl * TTW:(sl + 1) * TTW],
                                                       sA.ap[:, :], AF.Exp, bias=mbias[:, :],
                                                       scale=0.125), "act")
                    sA.consumed("act", t_sg)
                    dmin = qs - (128 * kbk + 127)
                    dmax = qs + 511 - 128 * kbk
                    wslice = w_ring[:, sl * TTW:(sl + 1) * TTW]
                    masks = []
                    if dmax > WIN:
                        masks.append(([[-1, TTW]], WIN - qs + 128 * kbk, 1))
                    if dmin < -WIN:
                        masks.append(([[1, TTW]], qs - 128 * kbk + WIN, -1))
                    if kbk == 0:
                        masks.append(([[0, TTW]], -1, 1))
                    t_pool = None
                    if masks:
                        kb.wait(nc.gpsimd, "act", t_sg)
                        for pat, base, cm in masks:
                            if t_pool is not None:
                                kb.wait(nc.gpsimd, "pool", t_pool)
                            t_pool = kb.inc(nc.gpsimd.affine_select(
                                wslice, wslice, pat, ALU.is_ge, 0.0,
                                base=base, channel_multiplier=cm), "pool")
                    blk_ready[bi] = ("pool", t_pool) if t_pool is not None else ("act", t_sg)
                if bi >= 1:
                    bd = bi - 1
                    sem, tok = blk_ready[bd]
                    kb.wait(nc.tensor, sem, tok)
                    sl = bd % 4
                    kbk = lo + bd
                    vt = v_attn[:, kbk * VA_W + h * 65:kbk * VA_W + h * 65 + 65]
                    i = nc.tensor.matmul(pb[4].ap[0:65, :], vt[:, :],
                                         w_ring[:, sl * TTW:(sl + 1) * TTW],
                                         start=(ci == 0), stop=(ci == n_contrib - 1))
                    t_av = kb.inc(i, "pe")
                    wslot_tok[sl] = t_av
                    ci += 1
            sA = pb[2]
            sA.wait_reuse(kb, nc.tensor)
            kb.wait(nc.tensor, "ldq", t_k0)
            i = nc.tensor.matmul(sA.ap[0:1, :], kcol0[:, :], qq[:, :], start=True, stop=True)
            t_s0 = kb.inc(i, "pe")
            kb.wait(nc.scalar, "pe", max(t_s0, wslot_tok[0]))
            t_w0 = kb.inc(nc.scalar.activation(w0_sb[:], sA.ap[0:1, :], AF.Exp,
                                               bias=mbias[0:1, :], scale=0.125), "act")
            sA.consumed("act", t_w0)
            kb.wait(nc.tensor, "act", t_w0)
            vt = v_attn[:, 0 * VA_W + h * 65:0 * VA_W + h * 65 + 65]
            i = nc.tensor.matmul(pb[4].ap[0:65, :], vt[0:1, :], w0_sb[:, :],
                                 start=False, stop=(ci == n_contrib - 1))
            t_av = kb.inc(i, "pe")
            ci += 1
            if qt == 0:
                sB = pb[5]
                sB.wait_reuse(kb, nc.tensor)
                for kbk in range(2, 32):
                    i = nc.tensor.matmul(sB.ap[:, kbk:kbk + 1],
                                         kfull[:, kbk * 128:kbk * 128 + 128],
                                         qq[:, 0:1], start=True, stop=True)
                t_q0 = kb.inc(i, "pe")
                kb.wait(nc.scalar, "pe", t_q0)
                t_wq0 = kb.inc(nc.scalar.activation(wq0_sb[:, 2:32], sB.ap[:, 2:32], AF.Exp,
                                                    bias=mbias[:, :], scale=0.125), "act")
                sB.consumed("act", t_wq0)
                kb.wait(nc.vector, "act", t_wq0)
                t_wz = kb.inc(nc.vector.memset(wq0_sb[0:1, 2:3], 0.0), "dve")
                kb.wait(nc.tensor, "dve", t_wz)
                for kbk in range(2, 32):
                    vt = v_attn[:, kbk * VA_W + h * 65:kbk * VA_W + h * 65 + 65]
                    i = nc.tensor.matmul(pb[4].ap[0:65, 0:1], vt[:, :],
                                         wq0_sb[:, kbk:kbk + 1],
                                         start=False, stop=(ci == n_contrib - 1))
                    t_av = kb.inc(i, "pe")
                    ci += 1
                t_kf_pe = t_av
            assert ci == n_contrib
            kb.wait(nc.vector, "pe", t_av)
            t_o = kb.inc(nc.vector.tensor_copy(out65, pb[4].ap[0:65, :]), "dve")
            pb[4].consumed("dve", t_o)
            kb.wait(nc.vector, "dve", t_o)
            t_rc = kb.inc(nc.vector.reciprocal(zr2, out65[64:65, 0:TTW]), "dve")
            pb[0].wait_reuse(kb, nc.tensor)
            kb.wait(nc.tensor, "dve", t_rc)
            t_bc = kb.inc(nc.tensor.matmul(pb[0].ap[:, :], C["ones_row_f32"], zr2,
                                           start=True, stop=True), "pe")
            kb.wait(nc.vector, "pe", max(t_bc, t_arena_ok))
            dst = a_xln[(h % 2) * 64:(h % 2) * 64 + 64, (h // 2) * T + qs:(h // 2) * T + qs + TTW]
            t_at = kb.inc(nc.vector.tensor_tensor(dst, out65[0:64, 0:TTW], pb[0].ap[0:64, :],
                                                  op=ALU.mult), "dve")
            pb[0].consumed("dve", t_at)
    t_attn = kb.cnt["dve"]
    if phase_limit == "attn":
        return

    kb.wait(nc.gpsimd, "pe", kb.cnt["pe"])
    t_wo = kb.dinc(nc.gpsimd.dma_start(
        out=wA.rearrange("p (c j) -> p c j", c=NC),
        in_=Wo.rearrange("(c p) j -> p c j", p=128)), "wq")
    xf = w["xf"]
    x_sb = [xf[:, 0:TTW], xf[:, TTW:2 * TTW]]
    x_tok = [0, 0]
    for cc in range(NC):
        for tt in range(NTT):
            ts = tt * TTW
            bank = pb[tt % 2]
            bank.wait_reuse(kb, nc.tensor)
            if cc == 0 and tt == 0:
                kb.wait(nc.tensor, "wq", t_wo)
                kb.wait(nc.tensor, "dve", t_attn)
                kb.wait(nc.tensor, "wq", t_smallg)
            for c2 in range(NC):
                nc.tensor.matmul(bank.ap[:, :],
                                 wA[:, c2 * D + cc * 128:c2 * D + cc * 128 + 128],
                                 a_xln[:, c2 * T + ts:c2 * T + ts + TTW],
                                 start=(c2 == 0), stop=False)
            i = nc.tensor.matmul(bank.ap[:, :], obr_sb[0:1, cc * 128:cc * 128 + 128],
                                 C["ones_row512_bf"], start=False, stop=True)
            t_mm = kb.inc(i, "pe")
            xb_i = tt % 2
            kb.wait(nc.sync, "dve", x_tok[xb_i])
            t_x = kb.dinc(nc.sync.dma_start(
                out=x_sb[xb_i], in_=x_in[cc * 128:(cc + 1) * 128, ts:ts + TTW]), "ldq")
            kb.wait(nc.vector, "ldq", t_x)
            kb.wait(nc.vector, "pe", t_mm)
            i = nc.vector.tensor_tensor(x_sb[xb_i], x_sb[xb_i], bank.ap[:, :], op=ALU.add)
            x_tok[xb_i] = t_u = kb.inc(i, "dve")
            bank.consumed("dve", t_u)
            kb.wait(nc.sync, "dve", t_u)
            kb.dinc(nc.sync.dma_start(
                out=x_out[cc * 128:(cc + 1) * 128, ts:ts + TTW], in_=x_sb[xb_i]), "stq")
    kb.wait(nc.sync, "stq")


def l1attn_weights(inputs):
    return {
        "Wq": np.ascontiguousarray(inputs["L1_attn_Wq"]),
        "Wk": np.ascontiguousarray(inputs["L1_attn_Wk"]),
        "Wv": np.ascontiguousarray(inputs["L1_attn_Wv"]),
        "Wo": np.ascontiguousarray(inputs["L1_attn_Wo"]),
        "qb": np.ascontiguousarray(inputs["L1_attn_bq"].reshape(NC, 128).T),
        "kbias": np.ascontiguousarray(inputs["L1_attn_bk"].reshape(NC, 128).T),
        "ob_row": np.ascontiguousarray(inputs["L1_attn_bo"].reshape(1, D)),
        "vb_row": np.ascontiguousarray(inputs["L1_attn_bv"].reshape(1, D)),
    }


def l0attn_weights(inputs):
    return {
        "qwT": np.ascontiguousarray(inputs["L0_attn_q_w"].T),
        "kwT": np.ascontiguousarray(inputs["L0_attn_k_w"].T),
        "vwT": np.ascontiguousarray(inputs["L0_attn_v_w"].T),
        "owT": np.ascontiguousarray(inputs["L0_attn_o_w"].T),
        "qb": np.ascontiguousarray(inputs["L0_attn_q_b"].reshape(NC, 128).T),
        "kbias": np.ascontiguousarray(inputs["L0_attn_k_b"].reshape(NC, 128).T),
        "ob_row": np.ascontiguousarray(inputs["L0_attn_o_b"].reshape(1, D)),
        "vb_row": np.ascontiguousarray(inputs["L0_attn_v_b"].reshape(1, D)),
        "qn": np.ascontiguousarray(inputs["L0_attn_qn_scale"].reshape(NC, 128).T),
        "kn": np.ascontiguousarray(inputs["L0_attn_kn_scale"].reshape(NC, 128).T),
    }


def _l0ffn_consts():
    iota512 = np.broadcast_to(np.arange(TTW, dtype=np.float32), (128, TTW))
    idxw = np.zeros((128, 32 * 3), np.float32)
    for tb in range(32):
        idxw[:, 3 * tb] = 128.0 * tb
        idxw[:, 3 * tb + 1] = np.arange(128)
        idxw[:, 3 * tb + 2] = 1.0
    grp16 = np.zeros((128, NE), np.float32)
    for p in range(128):
        grp16[p, p // 16] = 1.0
    sel8 = np.zeros((NE, 128), np.float32)
    for p in range(128):
        sel8[p // 16, p] = 1.0
    oh8 = np.zeros((NE, NE * 128), np.float32)
    for e in range(NE):
        oh8[e, e * 128:(e + 1) * 128] = 1.0
    mrg3 = np.array([[1.0], [1.0], [-4096.0]], np.float32)
    return iota512, idxw, grp16, sel8, oh8, mrg3


DEBUG_L0FFN = False


def emit_l0ffn(kb, C, x_in, x_out, W, S):
    nc = kb.nc
    t_cl = 0
    for name, dap, shape, dt in C["_l0f_spec"]:
        tile = kb.sb("c_" + name, shape, dt)
        t_cl = kb.dinc(nc.sync.dma_start(out=tile[:], in_=dap), "ldq")
        C[name] = tile[:]
    for eng in (nc.tensor, nc.vector, nc.scalar, nc.gpsimd):
        kb.wait(eng, "ldq", t_cl)
    gate_w, w1_d, w2_d = W["gate_w"], W["w1"], W["w2"]
    xtm_dram = S["xtm_dram"]
    pdram = S["pdram"]
    dense_dram = S["dense_dram"]
    if DEBUG_L0FFN:
        dbg_gsb = kb.dout("dbg_gsb", [128, NC * NE], F32)
        dbg_sc0 = kb.dout("dbg_sc0", [8, TTW], F32)
        dbg_probs = kb.dout("dbg_probs", [NE, T], F32)
        dbg_mask = kb.dout("dbg_mask", [NE, T], BF16)
        dbg_idx = kb.dout("dbg_idx", [128, NE * 8], I32)
        dbg_lo = kb.dout("dbg_lo", [8, 8], F32)
        dbg_pm = kb.dout("dbg_pm", [128, 256], F32)
        dbg_bis = kb.dout("dbg_bis", [8, 80], F32)
        dbg_bis_sb = kb.sb("dbg_bis_sb", [8, 80], F32)

    a_w2 = kb.sb("a_w2", [128, 32 * D], BF16)
    xln = [a_w2[:, c * T:(c + 1) * T] for c in range(NC)]
    a_h = kb.sb("a_h", [128, 32 * TTW], BF16)
    xb_t = a_h[:, 0:NC * TTW]
    sq_t = a_h[:, NC * TTW:2 * NC * TTW]
    scr = kb.sb("ln_scr", [128, 3 * TTW], F32)
    sqf = kb.sb("ln_sqf", [128, 2 * TTW], F32)
    xf = kb.sb("ln_xf", [128, NC * TTW], F32)
    w = {"xf": xf, "xb": xb_t, "sq": sq_t, "stat": scr[0:1, :], "scr": scr, "sqf": sqf}
    probs_t = kb.sb("probs_t", [8, T], F32)
    probs = probs_t[:]
    pfxB_t = kb.sb("pfxB_t", [8, T], F32)
    pfxA = probs_t[:]
    pfxB = pfxB_t[:]

    gate_sb = kb.sb("gate_sb", [128, NC * NE], F32)
    ones8_f = kb.sb("ones8_f", [128, 1], F32)
    pm = kb.sb("pm", [128, 256], F32)
    tmpc = kb.sb("tmpc", [128, 256], BF16)
    cnt_col = kb.sb("cnt_col", [128, 1], F32)
    bis = kb.sb("bis", [8, 8], F32)
    b_lo = kb.sb("b_lo", [8, 1], F32)
    b_hi = kb.sb("b_hi", [8, 1], F32)
    b_mid = kb.sb("b_mid", [8, 1], F32)
    b_cnt = kb.sb("b_cnt", [8, 1], F32)
    b_mf = kb.sb("b_mf", [8, 1], F32)
    b_tA = kb.sb("b_tA", [8, 1], F32)
    b_tB = kb.sb("b_tB", [8, 1], F32)
    b_cap = kb.sb("b_cap", [8, 1], F32)
    thr128 = kb.sb("thr128", [128, 1], F32)
    mask_bf = a_h[0:8, 0:T]
    pfxT = kb.sb("pfxT", [128, 32 * 8], F32)
    pfxT2 = kb.sb("pfxT2", [128, 32 * 8], F32)
    maskT = kb.sb("maskT", [128, 32 * 8], F32)
    ptile = kb.sb("ptile", [128, TTW], BF16)
    idx3 = scr[0:3, 0:TTW]
    idxrow = scr[0:1, TTW:2 * TTW]
    idx_all = kb.sb("idx_all", [128, NE * 8], I32)
    x_tm = [kb.sb(f"x_tm{i}", [128, D], BF16) for i in range(2)]
    x_fm = kb.sb("x_fm", [128, NC * D], BF16)
    w1_sb = [kb.sb(f"w1_{i}", [128, NC * 128], BF16) for i in range(2)]
    y_arena = kb.sb("y_arena", [128, NC * TTW], BF16)
    y_tm = [kb.sb(f"y_tm{i}", [128, D], BF16) for i in range(2)]
    w["xtm_sb"] = y_tm[1][:]
    p_col = [kb.sb(f"p_col{i}", [128, 1], F32) for i in range(2)]

    pb = [Bank(kb.ps(f"pb{i}", [128, TTW])) for i in range(5)]
    pbf = [Bank(kb.ps(f"pbf{i}", [128, 128], BF16)) for i in range(2)]
    pmisc = Bank(kb.ps("pmisc", [128, 128]))
    bnd_reg = kb.bnd_reg

    t_z = kb.inc(nc.vector.memset(y_tm[0][:], 0.0), "dve")
    kb.wait(nc.sync, "dve", t_z)
    t_zs = 0
    for tc in range(32):
        t_zs = kb.dinc(nc.sync.dma_start(out=dense_dram[tc * 128:(tc + 1) * 128, :],
                                         in_=y_tm[0][:]), "stq")
    t_gw = kb.dinc(nc.sync.dma_start(
        out=gate_sb[:].rearrange("p (c e) -> p c e", c=NC),
        in_=gate_w.rearrange("(c p) e -> p c e", p=128)), "ldq")
    t_o8 = kb.inc(nc.vector.memset(ones8_f[:], 1.0), "dve")

    def gate_cb(tt, ts, t_ap):
        bank = pb[4]
        bank.wait_reuse(kb, nc.tensor)
        if tt == 0:
            kb.wait(nc.tensor, "ldq", t_gw)
            kb.wait(nc.tensor, "dve", t_o8)
        kb.wait(nc.tensor, "dve", t_ap)
        for cc in range(NC):
            i = nc.tensor.matmul(bank.ap[0:8, :], gate_sb[:, cc * NE:(cc + 1) * NE],
                                 xf[:, cc * TTW:(cc + 1) * TTW],
                                 start=(cc == 0), stop=(cc == NC - 1))
        t_mm = kb.inc(i, "pe")
        if DEBUG_L0FFN and tt == 0:
            kb.wait(nc.vector, "pe", t_mm)
            t_d = kb.inc(nc.vector.tensor_copy(scr[0:8, 2 * TTW:3 * TTW], bank.ap[0:8, :]), "dve")
            kb.wait(nc.sync, "dve", t_d)
            kb.dinc(nc.sync.dma_start(out=dbg_sc0[:, :], in_=scr[0:8, 2 * TTW:3 * TTW]), "stq")
            kb.dinc(nc.sync.dma_start(out=dbg_gsb[:, :], in_=gate_sb[:]), "stq")
        kb.wait(nc.vector, "pe", t_mm)
        t_sce = kb.inc(nc.vector.tensor_copy(sqf[0:8, 0:TTW], bank.ap[0:8, :]), "dve")
        bank.consumed("dve", t_sce)
        kb.wait(nc.scalar, "dve", t_sce)
        t_e = kb.inc(nc.scalar.activation(probs[:, ts:ts + TTW], sqf[0:8, 0:TTW],
                                          AF.Exp), "act")
        kb.wait(nc.tensor, "act", t_e)
        t_s = kb.inc(nc.tensor.matmul(bank.ap[0:1, :], ones8_f[0:8, :],
                                      probs[:, ts:ts + TTW], start=True, stop=True), "pe")
        kb.wait(nc.vector, "pe", t_s)
        t_c = kb.inc(nc.vector.tensor_copy(scr[0:1, 0:TTW], bank.ap[0:1, :]), "dve")
        kb.wait(nc.vector, "dve", t_c)
        t_r = kb.inc(nc.vector.reciprocal(scr[0:1, TTW:2 * TTW], scr[0:1, 0:TTW]), "dve")
        kb.wait(nc.tensor, "dve", t_r)
        t_b = kb.inc(nc.tensor.matmul(bank.ap[0:8, :], C["ones_row_f32"][0:1, 0:8],
                                      scr[0:1, TTW:2 * TTW], start=True, stop=True), "pe")
        kb.wait(nc.vector, "pe", t_b)
        t_p = kb.inc(nc.vector.tensor_tensor(probs[:, ts:ts + TTW], probs[:, ts:ts + TTW],
                                             bank.ap[0:8, :], op=ALU.mult), "dve")
        bank.consumed("dve", t_p)

    emit_ln(kb, C, x_in, w, xln, pb, pbf=pbf, xtm_dram=xtm_dram, per_tt_cb=gate_cb, f32_stats=True)
    t_probs = kb.cnt["dve"]
    t_xtm_ldq = kb.cnt["ldq"]

    kb.wait(nc.sync, "dve", t_probs)
    t_pd = kb.dinc(nc.sync.dma_start(out=pdram[:, :], in_=probs), "stq")
    kb.wait(nc.sync, "stq", t_pd)
    t_pm = kb.dinc(nc.sync.dma_start(
        out=pm[:], in_=pdram.rearrange("e (g i) -> (e g) i", g=16)), "ldq")
    lo = b_lo[:]
    hi = b_hi[:]
    mid = b_mid[:]
    cnt8 = b_cnt[:]
    capcol = b_cap[:]
    nc.vector.memset(lo[:, :] if False else lo, 0.0)
    nc.vector.memset(hi, 1.0)
    nc.vector.memset(capcol, float(CAP) - 0.5)
    t_mid = kb.inc(nc.vector.memset(mid, 0.5), "dve")
    for it in range(40):
        pmisc.wait_reuse(kb, nc.tensor)
        kb.wait(nc.tensor, "dve", t_mid)
        t_bc = kb.inc(nc.tensor.matmul(pmisc.ap[:, 0:1], C["sel8"], mid,
                                       start=True, stop=True), "pe")
        kb.wait(nc.vector, "pe", t_bc)
        t_th = kb.inc(nc.vector.tensor_copy(thr128[:], pmisc.ap[:, 0:1]), "dve")
        pmisc.consumed("dve", t_th)
        kb.wait(nc.vector, "dve", t_th)
        kb.wait(nc.vector, "ldq", t_pm)
        t_cm = kb.inc(nc.vector.tensor_scalar(tmpc[:], pm[:], thr128[:, :], None,
                                              op0=ALU.is_gt), "dve")
        kb.wait(nc.vector, "dve", t_cm)
        t_ct = kb.inc(nc.vector.reduce_sum(cnt_col[:], tmpc[:], axis=AX.X), "dve")
        pmisc.wait_reuse(kb, nc.tensor)
        kb.wait(nc.tensor, "dve", t_ct)
        t_c8 = kb.inc(nc.tensor.matmul(pmisc.ap[0:8, 0:1], C["grp16"], cnt_col[:],
                                       start=True, stop=True), "pe")
        kb.wait(nc.vector, "pe", t_c8)
        t_cc = kb.inc(nc.vector.tensor_copy(cnt8, pmisc.ap[0:8, 0:1]), "dve")
        pmisc.consumed("dve", t_cc)
        if DEBUG_L0FFN and it < 13:
            nc.vector.tensor_copy(dbg_bis_sb[:, 6 * it:6 * it + 1], mid)
            nc.vector.tensor_copy(dbg_bis_sb[:, 6 * it + 1:6 * it + 2], cnt8)
        kb.wait(nc.vector, "dve", t_cc)
        mf = b_mf[:]
        tA = b_tA[:]
        tB = b_tB[:]
        t_m8 = kb.inc(nc.vector.tensor_tensor(mf, cnt8, capcol, op=ALU.is_ge), "dve")
        kb.wait(nc.vector, "dve", t_m8)
        nc.vector.tensor_tensor(tA, mid, lo, op=ALU.subtract)
        t_s1 = kb.inc(nc.vector.tensor_tensor(tB, hi, mid, op=ALU.subtract), "dve")
        kb.wait(nc.vector, "dve", t_s1)
        nc.vector.tensor_tensor(tA, tA, mf, op=ALU.mult)
        t_s2 = kb.inc(nc.vector.tensor_tensor(tB, tB, mf, op=ALU.mult), "dve")
        kb.wait(nc.vector, "dve", t_s2)
        nc.vector.tensor_tensor(lo, lo, tA, op=ALU.add)
        t_s3 = kb.inc(nc.vector.tensor_tensor(hi, mid, tB, op=ALU.add), "dve")
        kb.wait(nc.vector, "dve", t_s3)
        if DEBUG_L0FFN and it < 13:
            nc.vector.tensor_copy(dbg_bis_sb[:, 6 * it + 2:6 * it + 3], mf)
            nc.vector.tensor_copy(dbg_bis_sb[:, 6 * it + 3:6 * it + 4], lo)
            nc.vector.tensor_copy(dbg_bis_sb[:, 6 * it + 4:6 * it + 5], hi)
            nc.vector.tensor_copy(dbg_bis_sb[:, 6 * it + 5:6 * it + 6], tA)
        nc.vector.tensor_scalar(tA, lo, 0.5, None, op0=ALU.mult)
        nc.vector.tensor_scalar(tB, hi, 0.5, None, op0=ALU.mult)
        nc.vector.tensor_copy(bis[:, 0:1], mf)
        t_mid = kb.inc(nc.vector.tensor_tensor(mid, tA, tB, op=ALU.add), "dve")
    kb.wait(nc.vector, "dve", t_mid)
    t_msk = kb.inc(nc.vector.tensor_scalar(mask_bf, probs, lo, None, op0=ALU.is_gt), "dve")

    if DEBUG_L0FFN:
        kb.wait(nc.sync, "dve", t_msk)
        kb.dinc(nc.sync.dma_start(out=dbg_probs[:, :], in_=probs), "stq")
        kb.dinc(nc.sync.dma_start(out=dbg_mask[:, :], in_=mask_bf), "stq")
        kb.dinc(nc.sync.dma_start(out=dbg_lo[:, :], in_=bis[:]), "stq")
        kb.dinc(nc.sync.dma_start(out=dbg_pm[:, :], in_=pm[:]), "stq")
        t_dbg = kb.dinc(nc.sync.dma_start(out=dbg_bis[:, :], in_=dbg_bis_sb[:]), "stq")
        kb.wait(nc.vector, "stq", t_dbg)
    kb.wait(nc.vector, "dve", t_msk)
    kb.wait(nc.vector, "stq", t_pd)
    t_pf = kb.inc(nc.vector.tensor_copy(pfxA[:, :], mask_bf), "dve")
    cur, nxt = pfxA, pfxB
    sh = 1
    while sh < T:
        kb.wait(nc.vector, "dve", t_pf)
        nc.vector.tensor_copy(nxt[:, 0:sh], cur[:, 0:sh])
        t_pf = kb.inc(nc.vector.tensor_tensor(nxt[:, sh:T], cur[:, sh:T],
                                              cur[:, 0:T - sh], op=ALU.add), "dve")
        cur, nxt = nxt, cur
        sh *= 2
    kb.wait(nc.vector, "dve", t_pf)
    t_pf = kb.inc(nc.vector.tensor_tensor(nxt[:, :], cur[:, :], mask_bf,
                                          op=ALU.subtract), "dve")
    expfx = nxt
    for tb in range(32):
        for srcv, dstv in ((expfx, pfxT), (mask_bf, maskT)):
            pmisc.wait_reuse(kb, nc.tensor)
            kb.wait(nc.tensor, "dve", t_pf)
            if srcv is mask_bf:
                pbf[0].wait_reuse(kb, nc.tensor)
                t_t = kb.inc(nc.tensor.transpose(pbf[0].ap[:, 0:8],
                                                 srcv[:, tb * 128:(tb + 1) * 128],
                                                 C["id_bf"][0:8, 0:8]), "pe")
                kb.wait(nc.vector, "pe", t_t)
                t_ev = kb.inc(nc.vector.tensor_copy(dstv[:, tb * 8:(tb + 1) * 8],
                                                    pbf[0].ap[:, 0:8]), "dve")
                pbf[0].consumed("dve", t_ev)
            else:
                t_t = kb.inc(nc.tensor.transpose(pmisc.ap[:, 0:8],
                                                 srcv[:, tb * 128:(tb + 1) * 128],
                                                 C["id_f32"][0:8, 0:8]), "pe")
                kb.wait(nc.vector, "pe", t_t)
                t_ev = kb.inc(nc.vector.tensor_copy(dstv[:, tb * 8:(tb + 1) * 8],
                                                    pmisc.ap[:, 0:8]), "dve")
                pmisc.consumed("dve", t_ev)
    t_tr = kb.cnt["dve"]
    kb.wait(nc.vector, "dve", t_tr)
    t_p2 = kb.inc(nc.vector.tensor_scalar(pfxT2[:], pfxT[:], -512.0, None, op0=ALU.add), "dve")

    for e in range(NE):
        for half in range(2):
            bank = pb[4]
            bank.wait_reuse(kb, nc.tensor)
            pfx_src = pfxT if half == 0 else pfxT2
            t_pt_last = 0
            for tb in range(32):
                kb.wait(nc.vector, "dve", max(t_p2, t_tr))
                if tb >= 1:
                    kb.wait(nc.vector, "pe", t_pt_last)
                t_pt = kb.inc(nc.vector.tensor_scalar(
                    ptile[:], C["iota512"], pfx_src[:, tb * 8 + e:tb * 8 + e + 1],
                    maskT[:, tb * 8 + e:tb * 8 + e + 1],
                    op0=ALU.is_equal, op1=ALU.mult), "dve")
                kb.wait(nc.tensor, "dve", t_pt)
                i = nc.tensor.matmul(bank.ap[0:3, :], C["idxw"][:, tb * 3:tb * 3 + 3],
                                     ptile[:], start=(tb == 0), stop=(tb == 31))
                t_pt_last = kb.inc(i, "pe")
            kb.wait(nc.vector, "pe", t_pt_last)
            t_i3 = kb.inc(nc.vector.tensor_copy(idx3, bank.ap[0:3, :]), "dve")
            bank.consumed("dve", t_i3)
            bank2 = pb[3]
            bank2.wait_reuse(kb, nc.tensor)
            kb.wait(nc.tensor, "dve", t_i3)
            t_mg = kb.inc(nc.tensor.matmul(bank2.ap[0:1, :], C["mrg3"], idx3,
                                           start=True, stop=True), "pe")
            kb.wait(nc.vector, "pe", t_mg)
            t_ir = kb.inc(nc.vector.tensor_scalar(idxrow, bank2.ap[0:1, :], 4096.0,
                                                  None, op0=ALU.add), "dve")
            bank2.consumed("dve", t_ir)
            for j in range(4):
                pmisc.wait_reuse(kb, nc.tensor)
                kb.wait(nc.tensor, "dve", t_ir)
                t_t = kb.inc(nc.tensor.transpose(pmisc.ap[:, 0:1],
                                                 scr[0:1, TTW + j * 128:TTW + (j + 1) * 128],
                                                 C["id_f32"][0:1, 0:1]), "pe")
                kb.wait(nc.vector, "pe", t_t)
                t_ix = kb.inc(nc.vector.tensor_copy(
                    idx_all[:, e * 8 + half * 4 + j:e * 8 + half * 4 + j + 1],
                    pmisc.ap[:, 0:1]), "dve")
                pmisc.consumed("dve", t_ix)
    t_idx = kb.cnt["dve"]
    if DEBUG_L0FFN:
        kb.wait(nc.sync, "dve", t_idx)
        kb.dinc(nc.sync.dma_start(out=dbg_idx[:, :], in_=idx_all[:]), "stq")

    w1_tok = [0, 0]
    xtm_pe = [0, 0]
    ytm_tok = [0, 0]
    ytm_stq_war = [t_zs, 0]
    ytm_ldq_war = [0, t_xtm_ldq]
    ytm_pcol_tok = [0, 0]
    t_scat_prev = 0
    t_w2 = 0
    t_yphase = kb.cnt["pe"]
    for e in range(NE):
        kb.wait(nc.gpsimd, "pe", t_yphase)
        t_w2 = kb.dinc(nc.gpsimd.dma_start(
            out=a_w2[:].rearrange("p (h c) -> p h c", h=32),
            in_=w2_d[e].rearrange("(h p) c -> p h c", p=128)), "wq")
        for j in range(NC):
            buf = j % 2
            kb.wait(nc.vector, "pe", xtm_pe[buf])
            t_ms = kb.inc(nc.vector.memset(x_tm[buf][:], 0.0), "dve")
            kb.wait(nc.gpsimd, "dve", max(t_ms, t_idx))
            t_g = kb.dinc(nc.gpsimd.indirect_dma_start(
                out=x_tm[buf][:], out_offset=None, in_=xtm_dram[:, :],
                in_offset=IndirectOffsetOnAxis(ap=idx_all[:, e * 8 + j:e * 8 + j + 1], axis=0),
                bounds_check=bnd_reg, oob_is_err=False), "gq")
            for cc in range(NC):
                slot = pbf[cc % 2]
                slot.wait_reuse(kb, nc.tensor)
                kb.wait(nc.tensor, "gq", t_g)
                t_t = kb.inc(nc.tensor.transpose(slot.ap[:, :],
                                                 x_tm[buf][:, cc * 128:(cc + 1) * 128],
                                                 C["id_bf"]), "pe")
                xtm_pe[buf] = t_t
                kb.wait(nc.vector, "pe", t_t)
                t_ev = kb.inc(nc.vector.tensor_copy(
                    x_fm[:, cc * D + j * 128:cc * D + j * 128 + 128], slot.ap[:, :]), "dve")
                slot.consumed("dve", t_ev)
        t_xfm = kb.cnt["dve"]
        for half in range(2):
            hs = half * TTW
            for hc in range(32):
                buf = hc % 2
                kb.wait(nc.gpsimd, "pe", w1_tok[buf])
                t_w1 = kb.dinc(nc.gpsimd.dma_start(
                    out=w1_sb[buf][:].rearrange("p (c h) -> p c h", c=NC),
                    in_=w1_d[e].rearrange("(c p) h -> p c h", p=128)[:, :, hc * 128:(hc + 1) * 128]),
                    "wq")
                bank = pb[hc % 2]
                bank.wait_reuse(kb, nc.tensor)
                kb.wait(nc.tensor, "wq", t_w1)
                kb.wait(nc.tensor, "dve", t_xfm)
                for cc in range(NC):
                    i = nc.tensor.matmul(bank.ap[:, :],
                                         w1_sb[buf][:, cc * 128:(cc + 1) * 128],
                                         x_fm[:, cc * D + hs:cc * D + hs + TTW],
                                         start=(cc == 0), stop=(cc == NC - 1))
                w1_tok[buf] = t_mm = kb.inc(i, "pe")
                kb.wait(nc.scalar, "pe", t_mm)
                t_gl = kb.inc(nc.scalar.activation(
                    a_h[:, hc * TTW:(hc + 1) * TTW], bank.ap[:, :], AF.Gelu), "act")
                bank.consumed("act", t_gl)
            t_h = kb.cnt["act"]
            kb.wait(nc.tensor, "wq", t_w2)
            for grp in ((0, 1, 2, 3), (4, 5, 6, 7)):
                for cc in grp:
                    bank = pb[cc - grp[0]]
                    bank.wait_reuse(kb, nc.tensor)
                    kb.wait(nc.tensor, "act", t_h)
                    for hc in range(32):
                        i = nc.tensor.matmul(bank.ap[:, :],
                                             a_w2[:, hc * D + cc * 128:hc * D + cc * 128 + 128],
                                             a_h[:, hc * TTW:(hc + 1) * TTW],
                                             start=(hc == 0), stop=(hc == 31))
                    t_mm = kb.inc(i, "pe")
                    kb.wait(nc.vector, "pe", t_mm)
                    t_y = kb.inc(nc.vector.tensor_copy(
                        y_arena[:, cc * TTW:(cc + 1) * TTW], bank.ap[:, :]), "dve")
                    bank.consumed("dve", t_y)
            t_yphase = kb.cnt["pe"]
            t_yall = kb.cnt["dve"]
            for j in range(4):
                buf = j % 2
                icol = idx_all[:, e * 8 + half * 4 + j:e * 8 + half * 4 + j + 1]
                kb.wait(nc.gpsimd, "dve", ytm_pcol_tok[buf])
                t_pg = kb.dinc(nc.gpsimd.indirect_dma_start(
                    out=p_col[buf][:], out_offset=None,
                    in_=pdram.rearrange("e (t one) -> (e t) one", one=1),
                    in_offset=IndirectOffsetOnAxis(ap=icol, axis=0),
                    element_offset=e * T,
                    bounds_check=bnd_reg, oob_is_err=False), "gq")
                kb.wait(nc.vector, "gq", ytm_tok[buf])
                kb.wait(nc.vector, "stq", ytm_stq_war[buf])
                kb.wait(nc.vector, "ldq", ytm_ldq_war[buf])
                ytm_stq_war[buf] = ytm_ldq_war[buf] = 0
                for cc in range(NC):
                    slot = pbf[cc % 2]
                    slot.wait_reuse(kb, nc.tensor)
                    kb.wait(nc.tensor, "dve", t_yall)
                    t_t = kb.inc(nc.tensor.transpose(
                        slot.ap[:, :], y_arena[:, cc * TTW + j * 128:cc * TTW + j * 128 + 128],
                        C["id_bf"]), "pe")
                    kb.wait(nc.vector, "pe", t_t)
                    t_ev = kb.inc(nc.vector.tensor_copy(
                        y_tm[buf][:, cc * 128:(cc + 1) * 128], slot.ap[:, :]), "dve")
                    slot.consumed("dve", t_ev)
                kb.wait(nc.vector, "gq", t_pg)
                t_ysc = kb.inc(nc.vector.tensor_scalar(
                    y_tm[buf][:], y_tm[buf][:], p_col[buf][:, :], None, op0=ALU.mult), "dve")
                ytm_pcol_tok[buf] = t_ysc
                kb.wait(nc.gpsimd, "dve", t_ysc)
                kb.wait(nc.gpsimd, "gq", t_scat_prev)
                t_sc = kb.dinc(nc.gpsimd.indirect_dma_start(
                    out=dense_dram[:, :],
                    out_offset=IndirectOffsetOnAxis(ap=icol, axis=0),
                    in_=y_tm[buf][:], in_offset=None,
                    bounds_check=bnd_reg, oob_is_err=False,
                    compute_op=ALU.add), "gq")
                ytm_tok[buf] = t_sc
        t_scat_prev = kb.cnt["gq"]

    x_src = x_in.rearrange("(c p) t -> p c t", p=128)
    x_dst = x_out.rearrange("(c p) t -> p c t", p=128)
    dn_sb = x_tm[0]
    xck = xf[:, 0:NC * 128]
    t_resid = 0
    for tc in range(32):
        tcs = tc * 128
        kb.wait(nc.sync, "gq", t_scat_prev)
        kb.wait(nc.sync, "dve", t_resid)
        t_xl = kb.dinc(nc.sync.dma_start(
            out=xck.rearrange("p (c t) -> p c t", c=NC), in_=x_src[:, :, tcs:tcs + 128]), "ldq")
        kb.wait(nc.sync, "pe", kb.cnt["pe"]) if tc == 0 else None
        t_dn = kb.dinc(nc.sync.dma_start(out=dn_sb[:], in_=dense_dram[tcs:tcs + 128, :]), "ldq")
        for cc in range(NC):
            slot = pbf[cc % 2]
            slot.wait_reuse(kb, nc.tensor)
            kb.wait(nc.tensor, "ldq", t_dn)
            t_t = kb.inc(nc.tensor.transpose(slot.ap[:, :],
                                             dn_sb[:, cc * 128:(cc + 1) * 128],
                                             C["id_bf"]), "pe")
            kb.wait(nc.vector, "pe", t_t)
            kb.wait(nc.vector, "ldq", t_xl)
            t_cv = kb.inc(nc.vector.tensor_copy(scr[:, 0:128], slot.ap[:, :]), "dve")
            slot.consumed("dve", t_cv)
            kb.wait(nc.vector, "dve", t_cv)
            t_resid = kb.inc(nc.vector.scalar_tensor_tensor(
                out=xck[:, cc * 128:(cc + 1) * 128], in0=scr[:, 0:128], scalar=0.5,
                in1=xck[:, cc * 128:(cc + 1) * 128], op0=ALU.mult, op1=ALU.add), "dve")
        kb.wait(nc.sync, "dve", t_resid)
        kb.dinc(nc.sync.dma_start(out=x_dst[:, :, tcs:tcs + 128],
                                  in_=xck.rearrange("p (c t) -> p c t", c=NC)), "stq")
    kb.wait(nc.sync, "stq")


def l0ffn_weights(inputs):
    return {
        "gate_w": np.ascontiguousarray(inputs["L0_ffn_gate_w"]),
        "w1": np.ascontiguousarray(inputs["L0_ffn_w1"]),
        "w2": np.ascontiguousarray(inputs["L0_ffn_w2"]),
    }


_BUILDER_CACHE = {}


def build_fused():
    kb = KB()
    nc = kb.nc
    iota512, idxw, grp16, sel8, oh8, mrg3 = _l0ffn_consts()
    cos_t, sin_t = _rope_tables()
    C = host_consts(kb, extra=(
        ("ones_row512_bf", np.ones((1, TTW), np.float32), BF16),
        ("p_rot", _p_rot(), BF16),
    ))
    C["cos_d"] = kb.cbf16("c_cos_fm", cos_t)
    C["sin_d"] = kb.cbf16("c_sin_fm", sin_t)
    C["_l0f_spec"] = [
        ("iota512", kb.const("c_iota512", iota512), [128, TTW], F32),
        ("idxw", kb.cbf16("c_idxw", idxw), [128, 96], BF16),
        ("grp16", kb.const("c_grp16", grp16), [128, NE], F32),
        ("sel8", kb.const("c_sel8", sel8), [NE, 128], F32),
        ("mrg3", kb.const("c_mrg3", mrg3), [3, 1], F32),
    ]
    x_in = kb.din("x_in", [D, T], F32)
    x_out = kb.dout("x_out", [D, T], F32)
    xsA = kb.dtmp("xsA", [D, T], F32)
    xsB = kb.dtmp("xsB", [D, T], F32)
    S = {
        "q_dram": kb.dtmp("q_dram", [D, T], BF16),
        "k_dram": kb.dtmp("k_dram", [D, T], BF16),
        "xtm_dram": kb.dtmp("xtm_dram", [T, D], BF16),
        "pdram": kb.dtmp("pdram", [NE, T], F32),
        "dense_dram": kb.dtmp("dense_dram", [T, D], BF16),
    }
    W0f = {"gate_w": kb.din("l0f_gate_w", [D, NE], F32),
           "w1": kb.din("l0f_w1", [NE, D, DFF], F32),
           "w2": kb.din("l0f_w2", [NE, DFF, D], F32)}
    W0a = {"qwT": kb.din("l0a_qwT", [D, D], F32),
           "kwT": kb.din("l0a_kwT", [D, D], F32),
           "vwT": kb.din("l0a_vwT", [D, D], F32),
           "owT": kb.din("l0a_owT", [D, D], F32),
           "qb": kb.din("l0a_qb", [128, NC], F32),
           "kbias": kb.din("l0a_kbias", [128, NC], F32),
           "ob_row": kb.din("l0a_ob_row", [1, D], F32),
           "vb_row": kb.din("l0a_vb_row", [1, D], F32),
           "qn": kb.din("l0a_qn", [128, NC], F32),
           "kn": kb.din("l0a_kn", [128, NC], F32)}
    W1f = {"wiT": kb.din("l1f_wiT", [D, DFF], F32),
           "woT": kb.din("l1f_woT", [DFF, D], F32),
           "AsT": kb.din("l1f_AsT", [NE * D, RANK], F32),
           "BsT": kb.din("l1f_BsT", [NE * RANK, DFF], F32),
           "tki": kb.din("l1f_tki", [1, TOPK], I32),
           "tkv": kb.din("l1f_tkv", [1, TOPK], F32)}
    W1a = {"Wq": kb.din("l1a_Wq", [D, D], F32),
           "Wk": kb.din("l1a_Wk", [D, D], F32),
           "Wv": kb.din("l1a_Wv", [D, D], F32),
           "Wo": kb.din("l1a_Wo", [D, D], F32),
           "qb": kb.din("l1a_qb", [128, NC], F32),
           "kbias": kb.din("l1a_kbias", [128, NC], F32),
           "ob_row": kb.din("l1a_ob_row", [1, D], F32),
           "vb_row": kb.din("l1a_vb_row", [1, D], F32)}

    kb.start()
    load_consts(kb)
    kb.bnd_reg = nc.gpsimd.to_reg(T - 1)

    stages = (
        ("s1", emit_l0ffn, (C, x_in, xsA, W0f, S)),
        ("s2", emit_l0attn, (C, xsA, xsB, W0a, S)),
        ("s3", emit_l0ffn, (C, xsB, xsA, W0f, S)),
        ("s4", emit_l1ffn, (C, xsA, xsB, W1f)),
        ("s5", emit_l1attn, (C, xsB, xsA, W1a, S)),
        ("s6", emit_l1ffn, (C, xsA, x_out, W1f)),
    )
    for tag, fn, args in stages:
        kb.push_scope(tag)
        fn(kb, *args)
        kb.pop_scope()
        kb.barrier()
    kb.wait(nc.sync, "stq")
    return kb.finish()


def _get(name):
    if name not in _BUILDER_CACHE:
        _BUILDER_CACHE[name] = {"fused": build_fused}[name]()
    return _BUILDER_CACHE[name]


def run_stage(kb, per_core_inputs):
    in_maps = []
    for c in range(8):
        m = dict(kb.consts)
        m.update(per_core_inputs[c])
        in_maps.append(m)
    return run_bass_kernel_spmd(kb.nc, in_maps, core_ids=list(range(8)))


def l1ffn_weights(inputs):
    return {
        "wiT": np.ascontiguousarray(inputs["L1_ffn_wi_w"].T),
        "woT": np.ascontiguousarray(inputs["L1_ffn_wo_w"].T),
        "AsT": np.ascontiguousarray(inputs["L1_ffn_lora_As"].transpose(0, 2, 1)).reshape(NE * D, RANK),
        "BsT": np.ascontiguousarray(inputs["L1_ffn_lora_Bs"].transpose(0, 2, 1)).reshape(NE * RANK, DFF),
        "tki": np.ascontiguousarray(inputs["L1_ffn_top_k_indices"].reshape(1, TOPK)),
        "tkv": np.ascontiguousarray(inputs["L1_ffn_top_k_values"].reshape(1, TOPK)),
    }


def kernel(**inputs):
    kb = _get("fused")
    w0f = l0ffn_weights(inputs)
    w0a = l0attn_weights(inputs)
    w1f = l1ffn_weights(inputs)
    w1a = l1attn_weights(inputs)
    wmap = {}
    for pre, wm in (("l0f_", w0f), ("l0a_", w0a), ("l1f_", w1f), ("l1a_", w1a)):
        for k, v in wm.items():
            wmap[pre + k] = v
    per_core = []
    for c in range(8):
        m = dict(wmap)
        m["x_in"] = np.ascontiguousarray(inputs["L0_attn_x"][c].T)
        per_core.append(m)
    res = run_stage(kb, per_core)
    return np.stack([np.ascontiguousarray(res.results[c]["x_out"]).T
                     for c in range(8)]).astype(np.float32)


# version 04
# speedup vs baseline: 1.2862x; 1.1117x over previous
import contextlib

import numpy as np

import concourse.bass as bass
import concourse.mybir as mybir
from concourse.bass_core_rust import IndirectOffsetOnAxis
from concourse.bass_utils import run_bass_kernel_spmd

F32 = mybir.dt.float32
BF16 = mybir.dt.bfloat16
I32 = mybir.dt.int32
U8 = mybir.dt.uint8
AF = mybir.ActivationFunctionType
ALU = mybir.AluOpType
AX = mybir.AxisListType

D = 1024
T = 4096
NC = 8
NTT = 8
TTW = 512
NH = 16
HD = 64
NE = 8
CAP = 1024
DFF = 4096
RANK = 16
TOPK = 2
WIN = 256
LN_EPS = 1e-5
RMS_EPS = 1e-8
SIG_EPS = 1e-8
NEG_BIG = -1e30


class KB:

    def __init__(self):
        self.nc = bass.Bass("TRN2", target_bir_lowering=False)
        self.es = contextlib.ExitStack()
        self.sems = {}
        self.cnt = {}
        self.consts = {}

    def sem(self, name):
        if name not in self.sems:
            self.sems[name] = self.es.enter_context(self.nc.semaphore(name))
            self.cnt[name] = 0
        return self.sems[name]

    def inc(self, inst, name, k=1):
        s = self.sem(name)
        self.cnt[name] += k
        inst.then_inc(s, k)
        return self.cnt[name]

    def dinc(self, inst, name):
        return self.inc(inst, name, 16)

    def wait(self, eng, name, val=None):
        if val is None:
            val = self.cnt[name]
        if val <= 0:
            return
        eng.wait_ge(self.sem(name), val)

    def sb(self, name, shape, dt):
        scope = getattr(self, "scopes", None)
        tgt = scope[-1] if scope else self.es
        name = getattr(self, "prefix", "") + name
        return tgt.enter_context(self.nc.sbuf_tensor(name, shape, dt))

    def ps(self, name, shape, dt=F32):
        scope = getattr(self, "scopes", None)
        tgt = scope[-1] if scope else self.es
        name = getattr(self, "prefix", "") + name
        return tgt.enter_context(self.nc.psum_tensor(name, shape, dt))

    def push_scope(self, tag):
        if not hasattr(self, "scopes"):
            self.scopes = []
        self.scopes.append(contextlib.ExitStack())
        self.prefix = tag + "_"

    def pop_scope(self):
        self.scopes.pop().close()
        self.prefix = ""

    def barrier(self):
        nc = self.nc
        for eng in (nc.tensor, nc.vector, nc.scalar, nc.gpsimd, nc.sync):
            for name in list(self.sems):
                self.wait(eng, name)

    def din(self, name, shape, dt=F32):
        return self.nc.dram_tensor(name, shape, dt, kind="ExternalInput").ap()

    def dout(self, name, shape, dt=F32):
        return self.nc.dram_tensor(name, shape, dt, kind="ExternalOutput").ap()

    def dtmp(self, name, shape, dt):
        return self.nc.dram_tensor(name, shape, dt, kind="Internal").ap()

    def const(self, name, arr):
        arr = np.ascontiguousarray(arr)
        self.consts[name] = arr
        dt = {np.dtype(np.float32): F32, np.dtype(np.int32): I32,
              np.dtype(np.uint16): BF16}[arr.dtype]
        return self.nc.dram_tensor(name, list(arr.shape), dt, kind="ExternalInput").ap()

    def cbf16(self, name, arr_f32):
        import ml_dtypes
        a = np.ascontiguousarray(np.asarray(arr_f32, np.float32).astype(ml_dtypes.bfloat16)).view(np.uint16)
        return self.const(name, a)

    def start(self):
        self.block = self.es.enter_context(self.nc.Block())

    def finish(self):
        self.es.close()
        return self


class Bank:

    def __init__(self, ap):
        self.ap = ap
        self.sem = None
        self.tok = 0

    def wait_reuse(self, kb, eng):
        if self.sem is not None:
            kb.wait(eng, self.sem, self.tok)

    def consumed(self, sem, tok):
        self.sem = sem
        self.tok = tok


def host_consts(kb, extra=()):
    spec = {
        "ones_col_bf": (np.ones((128, 1), np.float32), BF16),
        "ones_col_f32": (np.ones((128, 1), np.float32), F32),
        "ones_row_f32": (np.ones((1, 128), np.float32), F32),
        "ones_row_bf": (np.ones((1, 128), np.float32), BF16),
        "id_bf": (np.eye(128, dtype=np.float32), BF16),
        "id_f32": (np.eye(128, dtype=np.float32), F32),
        "iota_col_f32": (np.arange(128, dtype=np.float32).reshape(128, 1), F32),
    }
    for name, arr, dt in extra:
        spec[name] = (arr, dt)
    c = {}
    kb._const_pairs = []
    for name, (arr, dt) in spec.items():
        if dt == BF16:
            dap = kb.cbf16("c_" + name, arr)
        elif dt == I32:
            dap = kb.const("c_" + name, np.asarray(arr, np.int32))
        else:
            dap = kb.const("c_" + name, np.asarray(arr, np.float32))
        tile = kb.sb(name, list(arr.shape), dt)
        kb._const_pairs.append((tile, dap))
        c[name] = tile[:]
    return c


def load_consts(kb):
    nc = kb.nc
    t = 0
    for tile, dap in kb._const_pairs:
        t = kb.dinc(nc.sync.dma_start(out=tile[:], in_=dap), "ldq")
    for eng in (nc.tensor, nc.vector, nc.scalar, nc.gpsimd):
        kb.wait(eng, "ldq", t)


def emit_ln(kb, C, x_dram, w, xln, pb, pbf=None, xrms=None, xtm_dram=None, per_tt_cb=None, f32_stats=False):
    nc = kb.nc
    xf, xb, sq, stat = w["xf"], w["xb"], w["sq"], w["stat"]
    s0 = stat[:, 0 * TTW:1 * TTW]
    s1 = stat[:, 1 * TTW:2 * TTW]
    s2 = stat[:, 2 * TTW:3 * TTW]
    x_src = x_dram.rearrange("(c p) t -> p c t", p=128)
    t_store = 0
    t_ap = 0
    t_pe_prev = 0
    for tt in range(NTT):
        ts = tt * TTW
        kb.wait(nc.sync, "dve", t_ap)
        t_ld = kb.dinc(nc.sync.dma_start(
            out=xf[:].rearrange("p (c t) -> p c t", c=NC),
            in_=x_src[:, :, ts:ts + TTW]), "ldq")
        kb.wait(nc.vector, "ldq", t_ld)
        kb.wait(nc.vector, "pe", t_pe_prev)
        t_cast = kb.inc(nc.vector.tensor_copy(xb[:], xf[:]), "dve")
        for bk in (pb[0], pb[1]):
            bk.wait_reuse(kb, nc.tensor)
        if f32_stats:
            sqb = [w["sqf"][:, 0:TTW], w["sqf"][:, TTW:2 * TTW]]
            kb.wait(nc.tensor, "ldq", t_ld)
            for cc in range(NC):
                nc.tensor.matmul(pb[0].ap[0:1, :], C["ones_col_f32"],
                                 xf[:, cc * TTW:(cc + 1) * TTW],
                                 start=(cc == 0), stop=(cc == NC - 1))
            t_sqa = [0, 0]
            kb.wait(nc.scalar, "ldq", t_ld)
            for cc in range(NC):
                buf = cc % 2
                if t_sqa[buf]:
                    kb.wait(nc.scalar, "pe", t_sqa[buf])
                t_sq = kb.inc(nc.scalar.activation(sqb[buf], xf[:, cc * TTW:(cc + 1) * TTW],
                                                   AF.Square), "act")
                kb.wait(nc.tensor, "act", t_sq)
                i = nc.tensor.matmul(pb[1].ap[0:1, :], C["ones_col_f32"], sqb[buf],
                                     start=(cc == 0), stop=(cc == NC - 1))
                t_sqa[buf] = t_mm = kb.inc(i, "pe")
        else:
            kb.wait(nc.scalar, "dve", t_cast)
            t_sq = kb.inc(nc.scalar.activation(sq[:], xb[:], AF.Square), "act")
            kb.wait(nc.tensor, "dve", t_cast)
            kb.wait(nc.tensor, "act", t_sq)
            for cc in range(NC):
                nc.tensor.matmul(pb[0].ap[0:1, :], C["ones_col_bf"], xb[:, cc * TTW:(cc + 1) * TTW],
                                 start=(cc == 0), stop=(cc == NC - 1))
            for cc in range(NC):
                i = nc.tensor.matmul(pb[1].ap[0:1, :], C["ones_col_bf"], sq[:, cc * TTW:(cc + 1) * TTW],
                                     start=(cc == 0), stop=(cc == NC - 1))
            t_mm = kb.inc(i, "pe")
        kb.wait(nc.vector, "pe", t_mm)
        nc.vector.tensor_scalar(s0, pb[0].ap[0:1, :], 1.0 / D, None, op0=ALU.mult)
        nc.vector.tensor_scalar(s1, pb[1].ap[0:1, :], 1.0 / D, None, op0=ALU.mult)
        nc.vector.tensor_tensor(s2, s0, s0, op=ALU.mult)
        nc.vector.tensor_tensor(s1, s1, s2, op=ALU.subtract)
        t_v = kb.inc(nc.vector.tensor_scalar(s1, s1, LN_EPS, None, op0=ALU.add), "dve")
        pb[0].consumed("dve", t_v)
        pb[1].consumed("dve", t_v)
        kb.wait(nc.scalar, "dve", t_v)
        t_sd = kb.inc(nc.scalar.activation(s2, s1, AF.Sqrt), "act")
        kb.wait(nc.vector, "act", t_sd)
        t_rc = kb.inc(nc.vector.reciprocal(s1, s2), "dve")
        kb.wait(nc.vector, "dve", t_rc)
        t_mr = kb.inc(nc.vector.tensor_tensor(s2, s0, s1, op=ALU.mult), "dve")
        for bk in (pb[2], pb[3]):
            bk.wait_reuse(kb, nc.tensor)
        kb.wait(nc.tensor, "dve", t_mr)
        nc.tensor.matmul(pb[2].ap[:, :], C["ones_row_f32"], s1, start=True, stop=True)
        t_bc = kb.inc(nc.tensor.matmul(pb[3].ap[:, :], C["ones_row_f32"], s2,
                                       start=True, stop=True), "pe")
        kb.wait(nc.vector, "pe", t_bc)
        if per_tt_cb is None:
            for cc in range(NC):
                nc.vector.tensor_tensor(xf[:, cc * TTW:(cc + 1) * TTW],
                                        xf[:, cc * TTW:(cc + 1) * TTW], pb[2].ap[:, :], op=ALU.mult)
                i = nc.vector.tensor_tensor(xln[cc][:, ts:ts + TTW],
                                            xf[:, cc * TTW:(cc + 1) * TTW], pb[3].ap[:, :],
                                            op=ALU.subtract)
            t_ap = kb.inc(i, "dve")
        else:
            for cc in range(NC):
                nc.vector.tensor_tensor(xf[:, cc * TTW:(cc + 1) * TTW],
                                        xf[:, cc * TTW:(cc + 1) * TTW], pb[2].ap[:, :], op=ALU.mult)
                nc.vector.tensor_tensor(xf[:, cc * TTW:(cc + 1) * TTW],
                                        xf[:, cc * TTW:(cc + 1) * TTW], pb[3].ap[:, :],
                                        op=ALU.subtract)
                i = nc.vector.tensor_copy(xln[cc][:, ts:ts + TTW],
                                          xf[:, cc * TTW:(cc + 1) * TTW])
            t_ap = kb.inc(i, "dve")
            per_tt_cb(tt, ts, t_ap)
        pb[2].consumed("dve", t_ap)
        pb[3].consumed("dve", t_ap)

        if xrms is not None:
            kb.wait(nc.scalar, "dve", t_ap)
            last = None
            for cc in range(NC):
                last = nc.scalar.activation(sq[:, cc * TTW:(cc + 1) * TTW],
                                            xln[cc][:, ts:ts + TTW], AF.Square)
            t_sq2 = kb.inc(last, "act")
            pb[0].wait_reuse(kb, nc.tensor)
            kb.wait(nc.tensor, "act", t_sq2)
            for cc in range(NC):
                i = nc.tensor.matmul(pb[0].ap[0:1, :], C["ones_col_bf"],
                                     sq[:, cc * TTW:(cc + 1) * TTW],
                                     start=(cc == 0), stop=(cc == NC - 1))
            t_mm2 = kb.inc(i, "pe")
            kb.wait(nc.vector, "pe", t_mm2)
            t_v2 = kb.inc(nc.vector.tensor_scalar(s1, pb[0].ap[0:1, :], 1.0 / D, RMS_EPS,
                                                  op0=ALU.mult, op1=ALU.add), "dve")
            pb[0].consumed("dve", t_v2)
            kb.wait(nc.scalar, "dve", t_v2)
            t_sd2 = kb.inc(nc.scalar.activation(s2, s1, AF.Sqrt), "act")
            kb.wait(nc.vector, "act", t_sd2)
            t_r2 = kb.inc(nc.vector.reciprocal(s1, s2), "dve")
            pb[2].wait_reuse(kb, nc.tensor)
            kb.wait(nc.tensor, "dve", t_r2)
            t_bc2 = kb.inc(nc.tensor.matmul(pb[2].ap[:, :], C["ones_row_f32"], s1,
                                            start=True, stop=True), "pe")
            kb.wait(nc.vector, "pe", t_bc2)
            nc.vector.tensor_copy(xb[:, 0:TTW], pb[2].ap[:, :])
            for cc in range(NC):
                i = nc.vector.tensor_tensor(xrms[cc][:, ts:ts + TTW],
                                            xln[cc][:, ts:ts + TTW], xb[:, 0:TTW],
                                            op=ALU.mult)
            t_ap = kb.inc(i, "dve")
            pb[2].consumed("dve", t_ap)

        t_pe_prev = kb.cnt["pe"]
        if xtm_dram is not None:
            xtm_sb = w["xtm_sb"]
            for j in range(4):
                js = ts + j * 128
                kb.wait(nc.vector, "ldq", t_store)
                t_done = 0
                for cc in range(NC):
                    slot = pbf[cc % 2]
                    slot.wait_reuse(kb, nc.tensor)
                    kb.wait(nc.tensor, "dve", t_ap)
                    t_t = kb.inc(nc.tensor.transpose(slot.ap[:, 0:128],
                                                     xln[cc][:, js:js + 128],
                                                     C["id_bf"]), "pe")
                    kb.wait(nc.vector, "pe", t_t)
                    t_done = kb.inc(nc.vector.tensor_copy(
                        xtm_sb[:, cc * 128:(cc + 1) * 128], slot.ap[:, 0:128]), "dve")
                    slot.consumed("dve", t_done)
                kb.wait(nc.sync, "dve", t_done)
                t_store = kb.dinc(nc.sync.dma_start(out=xtm_dram[js:js + 128, :],
                                                    in_=xtm_sb[:]), "ldq")
        t_pe_prev = kb.cnt["pe"]


def ln_work_tiles(kb):
    scratch = kb.sb("ln_scr", [128, 3 * TTW], F32)
    return {
        "xf": kb.sb("ln_xf", [128, NC * TTW], F32),
        "xb": kb.sb("ln_xb", [128, NC * TTW], BF16),
        "sq": kb.sb("ln_sq", [128, NC * TTW], BF16),
        "stat": scratch[0:1, :],
        "scr": scratch,
    }


DEBUG_L1FFN = False


def emit_l1ffn(kb, C, x_in, x_out, W):
    nc = kb.nc
    wiT, woT, AsT, BsT, tki, tkv = (W["wiT"], W["woT"], W["AsT"], W["BsT"],
                                    W["tki"], W["tkv"])
    if DEBUG_L1FFN:
        dbg_xln = kb.dout("dbg_xln", [128, T], BF16)
        dbg_asel = kb.dout("dbg_asel", [128, NC * RANK], BF16)
        dbg_bsel = kb.dout("dbg_bsel", [16, DFF], BF16)
        dbg_t1 = kb.dout("dbg_t1", [16, TTW], BF16)
        dbg_int = kb.dout("dbg_int", [128, TTW], BF16)
        dbg_h0 = kb.dout("dbg_h0", [128, TTW], BF16)
        dbg_idxf = kb.dout("dbg_idxf", [128, 1], F32)
        dbg_idxi = kb.dout("dbg_idxi", [128, 1], I32)

    a_xln = kb.sb("a_xln", [128, NC * T], BF16)
    xln = [a_xln[:, c * T:(c + 1) * T] for c in range(NC)]
    a_int = kb.sb("a_int", [128, 32 * TTW], BF16)
    a_h0 = kb.sb("a_h0", [128, 32 * TTW], BF16)
    w = ln_work_tiles(kb)
    wi_sb = [kb.sb(f"wi{i}", [128, NC * 128], BF16) for i in range(2)]
    wo_sb = [kb.sb(f"wo{i}", [128, 32 * 128], BF16) for i in range(2)]
    a_f32 = kb.sb("a_f32", [128, RANK], F32)
    b_f32 = w["xf"][0:16, :]
    a_sel = [kb.sb(f"a_sel{k}", [128, NC * RANK], BF16) for k in range(TOPK)]
    b_sel = [kb.sb(f"b_sel{k}", [16, DFF], BF16) for k in range(TOPK)]
    t1_sb = [kb.sb(f"t1_{k}", [16, TTW], BF16) for k in range(TOPK)]
    tki_i = kb.sb("tki_i", [1, TOPK], I32)
    tki_f = kb.sb("tki_f", [1, TOPK], F32)
    tkv_f = kb.sb("tkv_f", [1, TOPK], F32)
    tkv_col = [kb.sb(f"tkv_col{k}", [128, 1], F32) for k in range(TOPK)]
    idx_f = kb.sb("idx_f", [128, 1], F32)
    idx_i = kb.sb("idx_i", [128, 1], I32)
    x_sb = [w["xf"][:, 0:TTW], w["xf"][:, TTW:2 * TTW]]

    pb = [Bank(kb.ps(f"pb{i}", [128, TTW])) for i in range(6)]
    pmisc = kb.ps("pmisc", [128, 128])

    t_ld = kb.dinc(nc.sync.dma_start(out=tki_i[:], in_=tki[:, :]), "ldq")
    t_ld2 = kb.dinc(nc.sync.dma_start(out=tkv_f[:], in_=tkv[:, :]), "ldq")
    kb.wait(nc.vector, "ldq", t_ld2)
    t_cf = kb.inc(nc.vector.tensor_copy(tki_f[:], tki_i[:]), "dve")
    t_half = kb.inc(nc.vector.tensor_scalar(tkv_f[:], tkv_f[:], 0.5, None, op0=ALU.mult), "dve")
    for k in range(TOPK):
        kb.wait(nc.tensor, "dve")
        t_bc = kb.inc(nc.tensor.matmul(pmisc[:, 0:1], C["ones_row_f32"], tki_f[0:1, k:k + 1],
                                       start=True, stop=True), "pe")
        t_bv = kb.inc(nc.tensor.matmul(pmisc[:, 1:2], C["ones_row_f32"], tkv_f[0:1, k:k + 1],
                                       start=True, stop=True), "pe")
        kb.wait(nc.vector, "pe", t_bv)
        kb.inc(nc.vector.tensor_copy(tkv_col[k][:, :], pmisc[:, 1:2]), "dve")
        for cc in range(NC):
            t_a = kb.inc(nc.vector.tensor_scalar(idx_f[:], pmisc[:, 0:1], float(D), float(128 * cc),
                                                 op0=ALU.mult, op1=ALU.add), "dve")
            kb.wait(nc.vector, "dve", t_a)
            t_b = kb.inc(nc.vector.tensor_tensor(idx_f[:], idx_f[:], C_iota(kb, C), op=ALU.add), "dve")
            kb.wait(nc.vector, "dve", t_b)
            t_ix = kb.inc(nc.vector.tensor_copy(idx_i[:], idx_f[:]), "dve")
            kb.wait(nc.gpsimd, "dve", t_ix)
            t_g = kb.dinc(nc.gpsimd.indirect_dma_start(
                out=a_f32[:, :], out_offset=None, in_=AsT[:, :],
                in_offset=IndirectOffsetOnAxis(ap=idx_i[:, :], axis=0)), "gq")
            kb.wait(nc.vector, "gq", t_g)
            kb.inc(nc.vector.tensor_copy(
                a_sel[k][:, cc * RANK:(cc + 1) * RANK], a_f32[:, :]), "dve")
        t_a = kb.inc(nc.vector.tensor_scalar(idx_f[:], pmisc[:, 0:1], float(RANK), None, op0=ALU.mult), "dve")
        kb.wait(nc.vector, "dve", t_a)
        t_b = kb.inc(nc.vector.tensor_tensor(idx_f[:], idx_f[:], C_iota(kb, C), op=ALU.add), "dve")
        kb.wait(nc.vector, "dve", t_b)
        t_ix = kb.inc(nc.vector.tensor_copy(idx_i[:], idx_f[:]), "dve")
        kb.wait(nc.gpsimd, "dve", t_ix)
        t_g = kb.dinc(nc.gpsimd.indirect_dma_start(
            out=b_f32, out_offset=None, in_=BsT[:, :],
            in_offset=IndirectOffsetOnAxis(ap=idx_i[0:16, :], axis=0)), "gq")
        kb.wait(nc.vector, "gq", t_g)
        kb.inc(nc.vector.tensor_copy(b_sel[k][:, :], b_f32), "dve")
        if DEBUG_L1FFN and k == 1:
            kb.wait(nc.sync, "dve", t_ix)
            kb.dinc(nc.sync.dma_start(out=dbg_idxf[:, :], in_=idx_f[:]), "stq")
            kb.dinc(nc.sync.dma_start(out=dbg_idxi[:, :], in_=idx_i[:]), "stq")
    t_prep = kb.cnt["dve"]

    kb.wait(nc.sync, "dve", t_prep)
    emit_ln(kb, C, x_in, w, xln, pb)
    t_ln = kb.cnt["dve"]
    if DEBUG_L1FFN:
        kb.wait(nc.sync, "dve", max(t_prep, t_ln))
        kb.dinc(nc.sync.dma_start(out=dbg_xln[:, :], in_=a_xln[:, 0:T]), "stq")
        kb.dinc(nc.sync.dma_start(out=dbg_asel[:, :], in_=a_sel[0][:]), "stq")
        kb.dinc(nc.sync.dma_start(out=dbg_bsel[:, :], in_=b_sel[0][:]), "stq")

    wi_src = wiT.rearrange("(c p) h -> p c h", p=128)
    wo_src = woT.rearrange("(h p) c -> p h c", p=128)
    wi_tok = [0, 0]
    wo_tok = [0, 0]
    x_tok = [t_ln, t_ln]
    for tt in range(NTT):
        ts = tt * TTW
        kb.wait(nc.tensor, "dve", max(t_prep, t_ln))
        for k in range(TOPK):
            pb[4].wait_reuse(kb, nc.tensor)
            for cc in range(NC):
                i = nc.tensor.matmul(pb[4].ap[0:16, :],
                                     a_sel[k][:, cc * RANK:(cc + 1) * RANK],
                                     xln[cc][:, ts:ts + TTW],
                                     start=(cc == 0), stop=(cc == NC - 1))
            t_t1 = kb.inc(i, "pe")
            kb.wait(nc.vector, "pe", t_t1)
            t_ev = kb.inc(nc.vector.tensor_copy(t1_sb[k][:, :], pb[4].ap[0:16, :]), "dve")
            pb[4].consumed("dve", t_ev)
            if DEBUG_L1FFN and tt == 0 and k == 0:
                kb.wait(nc.sync, "dve", t_ev)
                kb.dinc(nc.sync.dma_start(out=dbg_t1[:, :], in_=t1_sb[k][:]), "stq")
        t_t1all = t_ev
        for hc in range(32):
            buf = hc % 2
            kb.wait(nc.gpsimd, "pe", wi_tok[buf])
            t_w = kb.dinc(nc.gpsimd.dma_start(
                out=wi_sb[buf][:].rearrange("p (c h) -> p c h", c=NC),
                in_=wi_src[:, :, hc * 128:(hc + 1) * 128]), "wq")
            bank = pb[4 + (hc % 2)]
            bank.wait_reuse(kb, nc.tensor)
            kb.wait(nc.tensor, "wq", t_w)
            for cc in range(NC):
                i = nc.tensor.matmul(bank.ap[:, :],
                                     wi_sb[buf][:, cc * 128:(cc + 1) * 128],
                                     xln[cc][:, ts:ts + TTW],
                                     start=(cc == 0), stop=(cc == NC - 1))
            wi_tok[buf] = t_mm = kb.inc(i, "pe")
            kb.wait(nc.scalar, "pe", t_mm)
            t_e = kb.inc(nc.scalar.activation(
                a_int[:, hc * TTW:(hc + 1) * TTW], bank.ap[:, :], AF.Copy), "act")
            bank.consumed("act", t_e)
            if DEBUG_L1FFN and tt == 0 and hc == 0:
                kb.wait(nc.sync, "act", t_e)
                kb.dinc(nc.sync.dma_start(out=dbg_int[:, :], in_=a_int[:, 0:TTW]), "stq")
        for k in range(TOPK):
            dst = a_h0 if k == 0 else a_int
            for hc in range(32):
                bank = pb[4 + (hc % 2)]
                bank.wait_reuse(kb, nc.tensor)
                kb.wait(nc.tensor, "dve", t_t1all)
                nc.tensor.matmul(bank.ap[:, :], b_sel[k][:, hc * 128:(hc + 1) * 128],
                                 t1_sb[k][:, :], start=True, stop=False)
                i = nc.tensor.matmul(bank.ap[:, :], C["id_bf"],
                                     a_int[:, hc * TTW:(hc + 1) * TTW],
                                     start=False, stop=True)
                t_mm = kb.inc(i, "pe")
                kb.wait(nc.scalar, "pe", t_mm)
                t_e = kb.inc(nc.scalar.activation(
                    dst[:, hc * TTW:(hc + 1) * TTW], bank.ap[:, :], AF.Relu), "act")
                bank.consumed("act", t_e)
                if DEBUG_L1FFN and tt == 0 and hc == 0 and k == 0:
                    kb.wait(nc.sync, "act", t_e)
                    kb.dinc(nc.sync.dma_start(out=dbg_h0[:, :], in_=a_h0[:, 0:TTW]), "stq")
        t_relu = kb.cnt["act"]
        for grp in ((0, 1, 2), (3, 4, 5), (6, 7)):
            for cc in grp:
                buf = cc % 2
                kb.wait(nc.gpsimd, "pe", wo_tok[buf])
                t_w = kb.dinc(nc.gpsimd.dma_start(
                    out=wo_sb[buf][:].rearrange("p (h c) -> p h c", h=32),
                    in_=wo_src[:, :, cc * 128:(cc + 1) * 128]), "wq")
                b0 = pb[2 * (cc - grp[0])]
                b1 = pb[2 * (cc - grp[0]) + 1]
                b0.wait_reuse(kb, nc.tensor)
                b1.wait_reuse(kb, nc.tensor)
                kb.wait(nc.tensor, "wq", t_w)
                kb.wait(nc.tensor, "act", t_relu)
                for hc in range(32):
                    nc.tensor.matmul(b0.ap[:, :], wo_sb[buf][:, hc * 128:(hc + 1) * 128],
                                     a_h0[:, hc * TTW:(hc + 1) * TTW],
                                     start=(hc == 0), stop=(hc == 31))
                for hc in range(32):
                    i = nc.tensor.matmul(b1.ap[:, :], wo_sb[buf][:, hc * 128:(hc + 1) * 128],
                                         a_int[:, hc * TTW:(hc + 1) * TTW],
                                         start=(hc == 0), stop=(hc == 31))
                wo_tok[buf] = t_mm = kb.inc(i, "pe")
                xbuf = cc % 2
                kb.wait(nc.sync, "dve", x_tok[xbuf])
                t_x = kb.dinc(nc.sync.dma_start(
                    out=x_sb[xbuf][:],
                    in_=x_in[cc * 128:(cc + 1) * 128, ts:ts + TTW]), "ldq")
                kb.wait(nc.vector, "ldq", t_x)
                kb.wait(nc.vector, "pe", t_mm)
                nc.vector.scalar_tensor_tensor(out=x_sb[xbuf][:], in0=b0.ap[:, :],
                                               scalar=tkv_col[0][:, :], in1=x_sb[xbuf][:],
                                               op0=ALU.mult, op1=ALU.add)
                i = nc.vector.scalar_tensor_tensor(out=x_sb[xbuf][:], in0=b1.ap[:, :],
                                                   scalar=tkv_col[1][:, :], in1=x_sb[xbuf][:],
                                                   op0=ALU.mult, op1=ALU.add)
                x_tok[xbuf] = t_u = kb.inc(i, "dve")
                b0.consumed("dve", t_u)
                b1.consumed("dve", t_u)
                kb.wait(nc.sync, "dve", t_u)
                kb.dinc(nc.sync.dma_start(
                    out=x_out[cc * 128:(cc + 1) * 128, ts:ts + TTW],
                    in_=x_sb[xbuf][:]), "stq")
    kb.wait(nc.sync, "stq")


def C_iota(kb, C):
    return C["iota_col_f32"]




def _rope_tables():
    inv = 1.0 / (10000.0 ** (np.arange(0, HD, 2, dtype=np.float32) / HD))
    t = np.arange(T, dtype=np.float32)
    freqs = t[:, None] * inv[None, :]
    emb = np.concatenate([freqs, freqs], axis=-1)
    cos = np.cos(emb)
    sin = np.sin(emb)
    return np.tile(cos.T, (2, 1)), np.tile(sin.T, (2, 1))


def _p_rot():
    p = np.zeros((128, 128), np.float32)
    for b in (0, 64):
        for j in range(32):
            p[b + 2 * j + 1, b + 2 * j] = -1.0
            p[b + 2 * j, b + 2 * j + 1] = 1.0
    return p


def emit_l0attn(kb, C, x_in, x_out, W, S):
    nc = kb.nc
    cos_d, sin_d, p_rot = C["cos_d"], C["sin_d"], C["p_rot"]
    qwT, kwT, vwT, owT = W["qwT"], W["kwT"], W["vwT"], W["owT"]
    qb_d, kbias_d, obr_d, vbr_d = W["qb"], W["kbias"], W["ob_row"], W["vb_row"]
    qn_d, kn_d = W["qn"], W["kn"]
    q_dram = S["q_dram"]
    k_dram = S["k_dram"]

    a_xln = kb.sb("a_xln", [128, NC * T], BF16)
    xln = [a_xln[:, c * T:(c + 1) * T] for c in range(NC)]
    VA_W = NH * 65
    v_attn = kb.sb("v_attn", [128, 32 * VA_W], BF16)
    w = ln_work_tiles(kb)
    warena = kb.sb("warena", [128, NC * D], BF16)
    wA = warena[:, 0:NC * D]
    qb_sb = kb.sb("qb_sb", [128, NC], F32)
    kb_sb = kb.sb("kb_sb", [128, NC], F32)
    qn_sb = kb.sb("qn_sb", [128, NC], F32)
    kn_sb = kb.sb("kn_sb", [128, NC], F32)
    obr_sb = kb.sb("obr_sb", [1, D], BF16)
    vbr_sb = kb.sb("vbr_sb", [1, D], BF16)
    qsb = kb.sb("qsb", [128, TTW], BF16)
    rtmp = kb.sb("rtmp", [128, TTW], BF16)
    rtmp2 = kb.sb("rtmp2", [128, TTW], BF16)
    cs_sb = kb.sb("cs_sb", [128, 2 * TTW], BF16)
    k_sb = kb.sb("k_sb", [128, 2048], BF16)
    qq = kb.sb("qq", [128, TTW], BF16)
    w_ring = kb.sb("w_ring", [128, 4 * TTW], BF16)
    pquad = kb.ps("pquad", [128, 4 * TTW])
    pb45 = [kb.ps("pb4", [128, TTW]), kb.ps("pb5", [128, TTW])]
    pb = [Bank(pquad[:, i * TTW:(i + 1) * TTW]) for i in range(4)] + \
         [Bank(pb45[0][:, :]), Bank(pb45[1][:, :])]
    pq = [Bank(pquad[:, 0:2 * TTW]), Bank(pquad[:, 2 * TTW:4 * TTW])]
    out65 = w["scr"][0:65, 0:TTW]
    zr = w["scr"][0:1, TTW:2 * TTW]
    zr2 = w["scr"][0:1, 2 * TTW:3 * TTW]

    t_small = 0
    for dst, src in ((qb_sb, qb_d), (kb_sb, kbias_d), (qn_sb, qn_d), (kn_sb, kn_d)):
        t_small = kb.dinc(nc.sync.dma_start(out=dst[:], in_=src), "ldq")
    t_smallg = kb.dinc(nc.gpsimd.dma_start(out=obr_sb[:], in_=obr_d), "wq")
    t_smallg = kb.dinc(nc.gpsimd.dma_start(out=vbr_sb[:], in_=vbr_d), "wq")
    va = v_attn[:].rearrange("p (c h e) -> p c h e", c=32, h=NH)
    t_ones = kb.inc(nc.vector.memset(va[:, :, :, 64:65], 1.0), "dve")
    t_wv = kb.dinc(nc.gpsimd.dma_start(
        out=wA.rearrange("p (c j) -> p c j", c=NC),
        in_=vwT.rearrange("(c p) j -> p c j", p=128)), "wq")

    emit_ln(kb, C, x_in, w, xln, pb)
    t_ln = kb.cnt["dve"]

    for tc in range(32):
        tcs = tc * 128
        for half in range(2):
            bank = pb[half]
            bank.wait_reuse(kb, nc.tensor)
            if tc == 0 and half == 0:
                kb.wait(nc.tensor, "wq", t_wv)
                kb.wait(nc.tensor, "dve", max(t_ln, t_ones))
            for cc in range(NC):
                nc.tensor.matmul(bank.ap[:, :],
                                 xln[cc][:, tcs:tcs + 128],
                                 wA[:, cc * D + half * TTW:cc * D + half * TTW + TTW],
                                 start=(cc == 0), stop=False)
            i = nc.tensor.matmul(bank.ap[:, :], C["ones_row_bf"][0:1, 0:128],
                                 vbr_sb[0:1, half * TTW:half * TTW + TTW],
                                 start=False, stop=True)
            t_mm = kb.inc(i, "pe")
            kb.wait(nc.vector, "pe", t_mm)
            dstv = va[:, tc, half * NC:half * NC + NC, 0:64]
            i = nc.vector.tensor_copy(
                dstv, bank.ap[:, :].rearrange("p (h e) -> p h e", h=NC))
            t_ev = kb.inc(i, "dve")
            bank.consumed("dve", t_ev)
    t_vdone = kb.cnt["dve"]

    xf, xb, sq, stat = w["xf"], w["xb"], w["sq"], w["stat"]
    s1 = stat[:, TTW:2 * TTW]
    s2 = stat[:, 2 * TTW:3 * TTW]
    for tt in range(NTT):
        ts = tt * TTW
        last = None
        for cc in range(NC):
            last = nc.scalar.activation(sq[:, cc * TTW:(cc + 1) * TTW],
                                        xln[cc][:, ts:ts + TTW], AF.Square)
        t_sq = kb.inc(last, "act")
        pb[0].wait_reuse(kb, nc.tensor)
        kb.wait(nc.tensor, "act", t_sq)
        for cc in range(NC):
            i = nc.tensor.matmul(pb[0].ap[0:1, :], C["ones_col_bf"],
                                 sq[:, cc * TTW:(cc + 1) * TTW],
                                 start=(cc == 0), stop=(cc == NC - 1))
        t_mm = kb.inc(i, "pe")
        kb.wait(nc.vector, "pe", t_mm)
        t_v = kb.inc(nc.vector.tensor_scalar(s1, pb[0].ap[0:1, :], 1.0 / D, RMS_EPS,
                                             op0=ALU.mult, op1=ALU.add), "dve")
        pb[0].consumed("dve", t_v)
        kb.wait(nc.scalar, "dve", t_v)
        t_sd = kb.inc(nc.scalar.activation(s2, s1, AF.Sqrt), "act")
        kb.wait(nc.vector, "act", t_sd)
        t_rc = kb.inc(nc.vector.reciprocal(s1, s2), "dve")
        pb[2].wait_reuse(kb, nc.tensor)
        kb.wait(nc.tensor, "dve", t_rc)
        t_bc = kb.inc(nc.tensor.matmul(pb[2].ap[:, :], C["ones_row_f32"], s1,
                                       start=True, stop=True), "pe")
        kb.wait(nc.vector, "pe", t_bc)
        nc.vector.tensor_copy(xb[:, 0:TTW], pb[2].ap[:, :])
        for cc in range(NC):
            i = nc.vector.tensor_tensor(xln[cc][:, ts:ts + TTW],
                                        xln[cc][:, ts:ts + TTW], xb[:, 0:TTW],
                                        op=ALU.mult)
        t_rms = kb.inc(i, "dve")
        pb[2].consumed("dve", t_rms)

    wsrc = {"q": qwT, "k": kwT}
    wscale = {"q": qn_sb, "k": kn_sb}
    wbias = {"q": qb_sb, "k": kb_sb}
    wdram = {"q": q_dram, "k": k_dram}
    wtmp = [w_ring[:, 0:D], w_ring[:, D:2 * D]]
    t_st_prev = 0
    for name in ("q", "k"):
        src = wsrc[name].rearrange("(c p) j -> p c j", p=128)
        bias, dram = wbias[name], wdram[name]
        wtmp_tok = [0, 0]
        t_prior_pe = kb.cnt["pe"]
        t_wready = 0
        for cc in range(NC):
            buf = cc % 2
            kb.wait(nc.gpsimd, "dve", wtmp_tok[buf])
            t_w = kb.dinc(nc.gpsimd.dma_start(out=wtmp[buf][:, :], in_=src[:, cc, :]), "wq")
            kb.wait(nc.vector, "wq", t_w)
            if cc == 0:
                kb.wait(nc.vector, "pe", t_prior_pe)
            i = nc.vector.tensor_scalar(wA[:, cc * D:(cc + 1) * D],
                                        wtmp[buf][:, :], wscale[name][:, cc:cc + 1], None,
                                        op0=ALU.mult)
            wtmp_tok[buf] = t_wready = kb.inc(i, "dve")
        for tt in range(NTT):
            ts = tt * TTW
            kb.wait(nc.sync, "dve", t_st_prev if False else 0)
            t_cs = kb.dinc(nc.sync.dma_start(out=cs_sb[:, 0:TTW], in_=cos_d[:, ts:ts + TTW]), "ldq")
            t_cs = kb.dinc(nc.sync.dma_start(out=cs_sb[:, TTW:2 * TTW], in_=sin_d[:, ts:ts + TTW]), "ldq")
            for cc in range(NC):
                bank = pb[cc % 2]
                rbank = pb[2 + cc % 2]
                bank.wait_reuse(kb, nc.tensor)
                if tt == 0 and cc == 0:
                    kb.wait(nc.tensor, "dve", t_wready)
                for c2 in range(NC):
                    i = nc.tensor.matmul(bank.ap[:, :],
                                         wA[:, c2 * D + cc * 128:c2 * D + cc * 128 + 128],
                                         xln[c2][:, ts:ts + TTW],
                                         start=(c2 == 0), stop=(c2 == NC - 1))
                t_mm = kb.inc(i, "pe")
                kb.wait(nc.vector, "pe", t_mm)
                t_q = kb.inc(nc.vector.tensor_scalar(qsb[:], bank.ap[:, :],
                                                     bias[:, cc:cc + 1], None,
                                                     op0=ALU.add), "dve")
                bank.consumed("dve", t_q)
                rbank.wait_reuse(kb, nc.tensor)
                kb.wait(nc.tensor, "dve", t_q)
                t_rot = kb.inc(nc.tensor.matmul(rbank.ap[:, :], p_rot[:], qsb[:],
                                                start=True, stop=True), "pe")
                kb.wait(nc.vector, "pe", t_rot)
                kb.wait(nc.vector, "ldq", t_cs)
                kb.wait(nc.vector, "stq", t_st_prev)
                nc.vector.tensor_copy(rtmp[:], rbank.ap[:, :])
                nc.vector.tensor_tensor(rtmp2[:], qsb[:], cs_sb[:, 0:TTW], op=ALU.mult)
                nc.vector.tensor_tensor(rtmp[:], rtmp[:], cs_sb[:, TTW:2 * TTW], op=ALU.mult)
                t_r = kb.inc(nc.vector.tensor_tensor(rtmp2[:], rtmp2[:], rtmp[:],
                                                     op=ALU.add), "dve")
                rbank.consumed("dve", t_r)
                kb.wait(nc.sync, "dve", t_r)
                t_st_prev = kb.dinc(nc.sync.dma_start(
                    out=dram[cc * 128:(cc + 1) * 128, ts:ts + TTW], in_=rtmp2[:]), "stq")
    t_qk = kb.cnt["stq"]

    wpair_tok = [0, 0]
    t_arena_ok = kb.cnt["pe"]
    t_av = 0
    kb.wait(nc.tensor, "dve")
    kb.wait(nc.tensor, "act")
    for h in range(NH):
        kb.wait(nc.sync, "stq", t_qk)
        kb.wait(nc.sync, "pe", t_av)
        t_k = kb.dinc(nc.sync.dma_start(out=k_sb[0:64, :],
                                        in_=k_dram[h * 64:h * 64 + 64, 0:2048]), "ldq")
        t_k = kb.dinc(nc.sync.dma_start(out=k_sb[64:128, :],
                                        in_=k_dram[h * 64:h * 64 + 64, 2048:4096]), "ldq")
        for qt in range(NTT):
            qs = qt * TTW
            kb.wait(nc.sync, "pe", t_av)
            t_qq = kb.dinc(nc.sync.dma_start(out=qq[0:64, :],
                                             in_=q_dram[h * 64:h * 64 + 64, qs:qs + TTW]), "ldq")
            t_qq = kb.dinc(nc.sync.dma_start(out=qq[64:128, :],
                                             in_=q_dram[h * 64:h * 64 + 64, qs:qs + TTW]), "ldq")
            kb.wait(nc.tensor, "ldq", t_qq)
            t_sg_of = {}
            for pc in range(17):
                if pc < 16:
                    c0 = pc
                    q_ = pq[pc % 2]
                    q_.wait_reuse(kb, nc.tensor)
                    if pc == 0:
                        pb[4].wait_reuse(kb, nc.tensor)
                        pb[5].wait_reuse(kb, nc.tensor)
                    nc.tensor.matmul(q_.ap[:, 0:TTW], k_sb[0:64, c0 * 128:c0 * 128 + 128],
                                     qq[0:64, :], start=True, stop=True, tile_position=(0, 0))
                    i = nc.tensor.matmul(q_.ap[:, TTW:2 * TTW],
                                         k_sb[64:128, c0 * 128:c0 * 128 + 128],
                                         qq[64:128, :], start=True, stop=True,
                                         tile_position=(64, 0))
                    t_sc = kb.inc(i, "pe")
                    sp = pc % 2
                    kb.wait(nc.scalar, "pe", max(t_sc, wpair_tok[sp]))
                    t_sg = kb.inc(nc.scalar.activation(
                        w_ring[:, sp * 2 * TTW:(sp + 1) * 2 * TTW], q_.ap[:, :],
                        AF.Sigmoid, scale=0.125), "act")
                    q_.consumed("act", t_sg)
                    t_sg_of[pc] = t_sg
                if pc >= 1:
                    pd = pc - 1
                    sp = pd % 2
                    kb.wait(nc.tensor, "act", t_sg_of[pd])
                    for ci, off in ((pd, sp * 2 * TTW), (pd + 16, sp * 2 * TTW + TTW)):
                        vt = v_attn[:, ci * VA_W + h * 65:ci * VA_W + h * 65 + 65]
                        nc.tensor.matmul(pb[4].ap[0:65, :], vt[0:64, :],
                                         w_ring[0:64, off:off + TTW],
                                         start=(pd == 0 and ci == pd), stop=(pd == 15 and ci == pd + 16),
                                         tile_position=(0, 0))
                        i = nc.tensor.matmul(pb[5].ap[0:65, :], vt[64:128, :],
                                             w_ring[64:128, off:off + TTW],
                                             start=(pd == 0 and ci == pd), stop=(pd == 15 and ci == pd + 16),
                                             tile_position=(64, 0))
                    t_av = kb.inc(i, "pe")
                    wpair_tok[sp] = t_av
            kb.wait(nc.vector, "pe", t_av)
            nc.vector.tensor_copy(out65, pb[4].ap[0:65, :])
            t_o = kb.inc(nc.vector.tensor_tensor(out65, out65, pb[5].ap[0:65, :],
                                                 op=ALU.add), "dve")
            pb[5].consumed("dve", t_o)
            kb.wait(nc.vector, "dve", t_o)
            t_z = kb.inc(nc.vector.tensor_scalar(zr, out65[64:65, :], SIG_EPS, None,
                                                 op0=ALU.add), "dve")
            kb.wait(nc.vector, "dve", t_z)
            t_rc = kb.inc(nc.vector.reciprocal(zr2, zr), "dve")
            kb.wait(nc.tensor, "dve", t_rc)
            t_bc = kb.inc(nc.tensor.matmul(pb[4].ap[:, :], C["ones_row_f32"], zr2,
                                           start=True, stop=True), "pe")
            kb.wait(nc.vector, "pe", max(t_bc, t_arena_ok))
            dst = a_xln[(h % 2) * 64:(h % 2) * 64 + 64, (h // 2) * T + qs:(h // 2) * T + qs + TTW]
            t_at = kb.inc(nc.vector.tensor_tensor(dst, out65[0:64, 0:TTW], pb[4].ap[0:64, :],
                                                  op=ALU.mult), "dve")
            pb[4].consumed("dve", t_at)
    t_attn = kb.cnt["dve"]

    kb.wait(nc.gpsimd, "pe", kb.cnt["pe"])
    t_wo = kb.dinc(nc.gpsimd.dma_start(
        out=wA.rearrange("p (c j) -> p c j", c=NC),
        in_=owT.rearrange("(c p) j -> p c j", p=128)), "wq")
    x_sb = [xf[:, 0:TTW], xf[:, TTW:2 * TTW]]
    x_tok = [0, 0]
    for cc in range(NC):
        for tt in range(NTT):
            ts = tt * TTW
            bank = pb[tt % 2]
            bank.wait_reuse(kb, nc.tensor)
            if cc == 0 and tt == 0:
                kb.wait(nc.tensor, "wq", t_wo)
                kb.wait(nc.tensor, "dve", t_attn)
                kb.wait(nc.tensor, "wq", t_smallg)
            for c2 in range(NC):
                nc.tensor.matmul(bank.ap[:, :],
                                 wA[:, c2 * D + cc * 128:c2 * D + cc * 128 + 128],
                                 a_xln[:, c2 * T + ts:c2 * T + ts + TTW],
                                 start=(c2 == 0), stop=False)
            i = nc.tensor.matmul(bank.ap[:, :], obr_sb[0:1, cc * 128:cc * 128 + 128],
                                 C["ones_row512_bf"], start=False, stop=True)
            t_mm = kb.inc(i, "pe")
            xb_i = tt % 2
            kb.wait(nc.sync, "dve", x_tok[xb_i])
            t_x = kb.dinc(nc.sync.dma_start(
                out=x_sb[xb_i], in_=x_in[cc * 128:(cc + 1) * 128, ts:ts + TTW]), "ldq")
            kb.wait(nc.vector, "ldq", t_x)
            kb.wait(nc.vector, "pe", t_mm)
            i = nc.vector.tensor_tensor(x_sb[xb_i], x_sb[xb_i], bank.ap[:, :], op=ALU.add)
            x_tok[xb_i] = t_u = kb.inc(i, "dve")
            bank.consumed("dve", t_u)
            kb.wait(nc.sync, "dve", t_u)
            kb.dinc(nc.sync.dma_start(
                out=x_out[cc * 128:(cc + 1) * 128, ts:ts + TTW], in_=x_sb[xb_i]), "stq")

    kb.wait(nc.sync, "stq")


def emit_l1attn(kb, C, x_in, x_out, W, S, phase_limit=None):
    nc = kb.nc
    Wq, Wk, Wv, Wo = W["Wq"], W["Wk"], W["Wv"], W["Wo"]
    qb_d, kbias_d, obr_d, vbr_d = W["qb"], W["kbias"], W["ob_row"], W["vb_row"]
    q_dram = S["q_dram"]
    k_dram = S["k_dram"]

    a_xln = kb.sb("a_xln", [128, NC * T], BF16)
    xln = [a_xln[:, c * T:(c + 1) * T] for c in range(NC)]
    VA_W = NH * 65
    v_attn = kb.sb("v_attn", [128, 32 * VA_W], BF16)
    w = ln_work_tiles(kb)
    warena = kb.sb("warena", [128, NC * D], BF16)
    wA = warena[:, 0:NC * D]
    qb_sb = kb.sb("qb_sb", [128, NC], F32)
    kb_sb = kb.sb("kb_sb", [128, NC], F32)
    obr_sb = kb.sb("obr_sb", [1, D], BF16)
    vbr_sb = kb.sb("vbr_sb", [1, D], BF16)
    qsb = kb.sb("qsb", [128, TTW], BF16)
    k_sb = kb.sb("k_sb", [64, 8 * 128], BF16)
    kcol0 = kb.sb("kcol0", [64, 1], BF16)
    qq = kb.sb("qq", [64, TTW], BF16)
    w_ring = kb.sb("w_ring", [128, 4 * TTW], BF16)
    w0_sb = kb.sb("w0_sb", [1, TTW], BF16)
    wq0_sb = kb.sb("wq0_sb", [128, 32], BF16)
    qmax = kb.sb("qmax", [1, NTT], F32)
    kmax = kb.sb("kmax", [1, 1], F32)
    mrow = kb.sb("mrow", [1, NTT], F32)
    mbias = kb.sb("mbias", [128, 1], F32)

    pb = [Bank(kb.ps(f"pb{i}", [128, TTW])) for i in range(6)]
    out65 = w["scr"][0:65, 0:TTW]
    zr = w["scr"][0:1, TTW:2 * TTW]
    zr2 = w["scr"][0:1, 2 * TTW:3 * TTW]

    t_small = 0
    for dst, src in ((qb_sb, qb_d), (kb_sb, kbias_d)):
        t_small = kb.dinc(nc.sync.dma_start(out=dst[:], in_=src), "ldq")
    t_smallg = kb.dinc(nc.gpsimd.dma_start(out=obr_sb[:], in_=obr_d), "wq")
    t_smallg = kb.dinc(nc.gpsimd.dma_start(out=vbr_sb[:], in_=vbr_d), "wq")
    va = v_attn[:].rearrange("p (c h e) -> p c h e", c=32, h=NH)
    t_ones = kb.inc(nc.vector.memset(va[:, :, :, 64:65], 1.0), "dve")
    t_wv = kb.dinc(nc.gpsimd.dma_start(
        out=wA.rearrange("p (c j) -> p c j", c=NC),
        in_=Wv.rearrange("(c p) j -> p c j", p=128)), "wq")

    emit_ln(kb, C, x_in, w, xln, pb)
    t_ln = kb.cnt["dve"]
    if phase_limit == "ln":
        kb.wait(nc.sync, "ldq")
        return

    for tc in range(32):
        tcs = tc * 128
        for half in range(2):
            bank = pb[half]
            bank.wait_reuse(kb, nc.tensor)
            if tc == 0 and half == 0:
                kb.wait(nc.tensor, "wq", t_wv)
                kb.wait(nc.tensor, "dve", max(t_ln, t_ones))
            for cc in range(NC):
                nc.tensor.matmul(bank.ap[:, :],
                                 xln[cc][:, tcs:tcs + 128],
                                 wA[:, cc * D + half * TTW:cc * D + half * TTW + TTW],
                                 start=(cc == 0), stop=False)
            i = nc.tensor.matmul(bank.ap[:, :], C["ones_row_bf"][0:1, 0:128],
                                 vbr_sb[0:1, half * TTW:half * TTW + TTW],
                                 start=False, stop=True)
            t_mm = kb.inc(i, "pe")
            kb.wait(nc.vector, "pe", t_mm)
            dstv = va[:, tc, half * NC:half * NC + NC, 0:64]
            i = nc.vector.tensor_copy(
                dstv, bank.ap[:, :].rearrange("p (h e) -> p h e", h=NC))
            t_ev = kb.inc(i, "dve")
            bank.consumed("dve", t_ev)
    if phase_limit == "vproj":
        kb.wait(nc.vector, "dve")
        kb.wait(nc.sync, "ldq")
        return

    wsrc = {"q": Wq, "k": Wk}
    wbias = {"q": qb_sb, "k": kb_sb}
    wdram = {"q": q_dram, "k": k_dram}
    t_st_prev = 0
    for name in ("q", "k"):
        t_prior_pe = kb.cnt["pe"]
        kb.wait(nc.gpsimd, "pe", t_prior_pe)
        t_w = kb.dinc(nc.gpsimd.dma_start(
            out=wA.rearrange("p (c j) -> p c j", c=NC),
            in_=wsrc[name].rearrange("(c p) j -> p c j", p=128)), "wq")
        bias, dram = wbias[name], wdram[name]
        for tt in range(NTT):
            ts = tt * TTW
            for cc in range(NC):
                bank = pb[cc % 2]
                bank.wait_reuse(kb, nc.tensor)
                if tt == 0 and cc == 0:
                    kb.wait(nc.tensor, "wq", t_w)
                for c2 in range(NC):
                    i = nc.tensor.matmul(bank.ap[:, :],
                                         wA[:, c2 * D + cc * 128:c2 * D + cc * 128 + 128],
                                         xln[c2][:, ts:ts + TTW],
                                         start=(c2 == 0), stop=(c2 == NC - 1))
                t_mm = kb.inc(i, "pe")
                kb.wait(nc.vector, "pe", t_mm)
                kb.wait(nc.vector, "stq", t_st_prev)
                t_q = kb.inc(nc.vector.tensor_scalar(qsb[:], bank.ap[:, :],
                                                     bias[:, cc:cc + 1], None,
                                                     op0=ALU.add), "dve")
                bank.consumed("dve", t_q)
                kb.wait(nc.sync, "dve", t_q)
                t_st_prev = kb.dinc(nc.sync.dma_start(
                    out=dram[cc * 128:(cc + 1) * 128, ts:ts + TTW], in_=qsb[:]), "stq")
    t_qk = kb.cnt["stq"]
    if phase_limit == "qkproj":
        kb.wait(nc.sync, "stq")
        return

    wslot_tok = [0, 0, 0, 0]
    t_arena_ok = kb.cnt["pe"]
    t_av = 0
    t_kf_pe = 0
    kfull = warena[0:64, 0:2 * T]
    for h in range(NH):
        hs = h * 64
        kb.wait(nc.sync, "stq", t_qk)
        kb.wait(nc.sync, "pe", max(t_arena_ok, t_kf_pe))
        t_qh = kb.dinc(nc.sync.dma_start(out=kfull[:, 0:T], in_=q_dram[hs:hs + 64, :]), "ldq")
        if phase_limit == "cut1" and h == 0:
            kb.wait(nc.sync, "ldq")
            return
        kb.wait(nc.scalar, "ldq", t_qh)
        t_sqq = kb.inc(nc.scalar.activation(kfull[:, T:2 * T], kfull[:, 0:T], AF.Square), "act")
        if phase_limit == "cut2" and h == 0:
            kb.wait(nc.sync, "act")
            kb.wait(nc.sync, "ldq")
            return
        kb.wait(nc.tensor, "act", t_sqq)
        t_nq = 0
        nrm_row = w["scr"][0:1, 0:TTW]
        for qt in range(NTT):
            pb[0].wait_reuse(kb, nc.tensor)
            i = nc.tensor.matmul(pb[0].ap[0:1, :],
                                 C["ones_col_bf"][0:64, :],
                                 kfull[:, T + qt * TTW:T + (qt + 1) * TTW],
                                 start=True, stop=True)
            t_nq = kb.inc(i, "pe")
            kb.wait(nc.vector, "pe", t_nq)
            t_cp = kb.inc(nc.vector.tensor_copy(nrm_row, pb[0].ap[0:1, :]), "dve")
            pb[0].consumed("dve", t_cp)
            kb.wait(nc.vector, "dve", t_cp)
            t_rm = kb.inc(nc.vector.reduce_max(qmax[0:1, qt:qt + 1], nrm_row,
                                               axis=AX.X), "dve")
        if phase_limit == "cut3" and h == 0:
            kb.wait(nc.sync, "dve")
            kb.wait(nc.sync, "ldq")
            return
        kb.wait(nc.sync, "act", t_sqq)
        t_kh = kb.dinc(nc.sync.dma_start(out=kfull[:, 0:T], in_=k_dram[hs:hs + 64, :]), "ldq")
        kb.wait(nc.scalar, "ldq", t_kh)
        kb.wait(nc.scalar, "pe", t_nq)
        t_sqk = kb.inc(nc.scalar.activation(kfull[:, T:2 * T], kfull[:, 0:T], AF.Square), "act")
        kb.wait(nc.tensor, "act", t_sqk)
        for qt in range(NTT):
            pb[0].wait_reuse(kb, nc.tensor)
            i = nc.tensor.matmul(pb[0].ap[0:1, :], C["ones_col_bf"][0:64, :],
                                 kfull[:, T + qt * TTW:T + (qt + 1) * TTW],
                                 start=True, stop=True)
            t_nk = kb.inc(i, "pe")
            kb.wait(nc.vector, "pe", t_nk)
            t_cp = kb.inc(nc.vector.tensor_copy(nrm_row, pb[0].ap[0:1, :]), "dve")
            pb[0].consumed("dve", t_cp)
            kb.wait(nc.vector, "dve", t_cp)
            t_rm = kb.inc(nc.vector.reduce_max(mrow[0:1, qt:qt + 1], nrm_row,
                                               axis=AX.X), "dve")
        kb.wait(nc.vector, "dve", kb.cnt["dve"])
        t_km = kb.inc(nc.vector.reduce_max(kmax[0:1, :], mrow[0:1, :], axis=AX.X), "dve")
        kb.wait(nc.vector, "dve", t_km)
        t_m1 = kb.inc(nc.vector.tensor_scalar(mrow[0:1, 0:NTT], qmax[0:1, :], kmax[0:1, 0:1],
                                              None, op0=ALU.mult), "dve")
        kb.wait(nc.scalar, "dve", t_m1)
        nc.scalar.activation(mrow[0:1, 0:NTT], mrow[0:1, 0:NTT], AF.Ln)
        t_m2 = kb.inc(nc.scalar.activation(mrow[0:1, 0:NTT], mrow[0:1, 0:NTT], AF.Exp,
                                           scale=0.5), "act")
        kb.wait(nc.vector, "act", t_m2)
        t_m3 = kb.inc(nc.vector.tensor_scalar(mrow[0:1, 0:NTT], mrow[0:1, 0:NTT], -0.125,
                                              None, op0=ALU.mult), "dve")
        if phase_limit == "norms0" and h == 0:
            return
        if phase_limit == "norms" and h == 1:
            return
        t_k0 = kb.dinc(nc.sync.dma_start(out=kcol0[:], in_=k_dram[hs:hs + 64, 0:1], allow_slow_non_contiguous=True), "ldq")
        for qt in range(NTT):
            qs = qt * TTW
            lo = max(0, 4 * qt - 2)
            hi = min(31, 4 * qt + 5)
            nblk = hi - lo + 1
            kb.wait(nc.sync, "pe", t_av)
            t_qq = kb.dinc(nc.sync.dma_start(out=qq[:, :],
                                             in_=q_dram[hs:hs + 64, qs:qs + TTW]), "ldq")
            t_kb = kb.dinc(nc.sync.dma_start(out=k_sb[:, 0:nblk * 128],
                                             in_=k_dram[hs:hs + 64, lo * 128:(hi + 1) * 128]), "ldq")
            pb[3].wait_reuse(kb, nc.tensor)
            kb.wait(nc.tensor, "dve", t_m3)
            t_mb = kb.inc(nc.tensor.matmul(pb[3].ap[:, 0:1], C["ones_row_f32"],
                                           mrow[0:1, qt:qt + 1], start=True, stop=True), "pe")
            kb.wait(nc.vector, "pe", t_mb)
            kb.wait(nc.vector, "act")
            t_mb2 = kb.inc(nc.vector.tensor_copy(mbias[:], pb[3].ap[:, 0:1]), "dve")
            pb[3].consumed("dve", t_mb2)
            kb.wait(nc.tensor, "ldq", t_kb)
            n_contrib = nblk + 1 + (30 if qt == 0 else 0)
            ci = 0
            pb[4].wait_reuse(kb, nc.tensor)
            blk_ready = {}
            for bi in range(nblk + 1):
                if bi < nblk:
                    kbk = lo + bi
                    sA = pb[bi % 2]
                    sA.wait_reuse(kb, nc.tensor)
                    i = nc.tensor.matmul(sA.ap[:, :], k_sb[:, bi * 128:bi * 128 + 128],
                                         qq[:, :], start=True, stop=True)
                    t_sc = kb.inc(i, "pe")
                    sl = bi % 4
                    kb.wait(nc.scalar, "pe", max(t_sc, wslot_tok[sl]))
                    kb.wait(nc.scalar, "dve", t_mb2)
                    t_sg = kb.inc(nc.scalar.activation(w_ring[:, sl * TTW:(sl + 1) * TTW],
                                                       sA.ap[:, :], AF.Exp, bias=mbias[:, :],
                                                       scale=0.125), "act")
                    sA.consumed("act", t_sg)
                    dmin = qs - (128 * kbk + 127)
                    dmax = qs + 511 - 128 * kbk
                    wslice = w_ring[:, sl * TTW:(sl + 1) * TTW]
                    masks = []
                    if dmax > WIN:
                        masks.append(([[-1, TTW]], WIN - qs + 128 * kbk, 1))
                    if dmin < -WIN:
                        masks.append(([[1, TTW]], qs - 128 * kbk + WIN, -1))
                    if kbk == 0:
                        masks.append(([[0, TTW]], -1, 1))
                    t_pool = None
                    if masks:
                        kb.wait(nc.gpsimd, "act", t_sg)
                        for pat, base, cm in masks:
                            if t_pool is not None:
                                kb.wait(nc.gpsimd, "pool", t_pool)
                            t_pool = kb.inc(nc.gpsimd.affine_select(
                                wslice, wslice, pat, ALU.is_ge, 0.0,
                                base=base, channel_multiplier=cm), "pool")
                    blk_ready[bi] = ("pool", t_pool) if t_pool is not None else ("act", t_sg)
                if bi >= 1:
                    bd = bi - 1
                    sem, tok = blk_ready[bd]
                    kb.wait(nc.tensor, sem, tok)
                    sl = bd % 4
                    kbk = lo + bd
                    vt = v_attn[:, kbk * VA_W + h * 65:kbk * VA_W + h * 65 + 65]
                    i = nc.tensor.matmul(pb[4].ap[0:65, :], vt[:, :],
                                         w_ring[:, sl * TTW:(sl + 1) * TTW],
                                         start=(ci == 0), stop=(ci == n_contrib - 1))
                    t_av = kb.inc(i, "pe")
                    wslot_tok[sl] = t_av
                    ci += 1
            sA = pb[2]
            sA.wait_reuse(kb, nc.tensor)
            kb.wait(nc.tensor, "ldq", t_k0)
            i = nc.tensor.matmul(sA.ap[0:1, :], kcol0[:, :], qq[:, :], start=True, stop=True)
            t_s0 = kb.inc(i, "pe")
            kb.wait(nc.scalar, "pe", max(t_s0, wslot_tok[0]))
            t_w0 = kb.inc(nc.scalar.activation(w0_sb[:], sA.ap[0:1, :], AF.Exp,
                                               bias=mbias[0:1, :], scale=0.125), "act")
            sA.consumed("act", t_w0)
            kb.wait(nc.tensor, "act", t_w0)
            vt = v_attn[:, 0 * VA_W + h * 65:0 * VA_W + h * 65 + 65]
            i = nc.tensor.matmul(pb[4].ap[0:65, :], vt[0:1, :], w0_sb[:, :],
                                 start=False, stop=(ci == n_contrib - 1))
            t_av = kb.inc(i, "pe")
            ci += 1
            if qt == 0:
                sB = pb[5]
                sB.wait_reuse(kb, nc.tensor)
                for kbk in range(2, 32):
                    i = nc.tensor.matmul(sB.ap[:, kbk:kbk + 1],
                                         kfull[:, kbk * 128:kbk * 128 + 128],
                                         qq[:, 0:1], start=True, stop=True)
                t_q0 = kb.inc(i, "pe")
                kb.wait(nc.scalar, "pe", t_q0)
                t_wq0 = kb.inc(nc.scalar.activation(wq0_sb[:, 2:32], sB.ap[:, 2:32], AF.Exp,
                                                    bias=mbias[:, :], scale=0.125), "act")
                sB.consumed("act", t_wq0)
                kb.wait(nc.vector, "act", t_wq0)
                t_wz = kb.inc(nc.vector.memset(wq0_sb[0:1, 2:3], 0.0), "dve")
                kb.wait(nc.tensor, "dve", t_wz)
                for kbk in range(2, 32):
                    vt = v_attn[:, kbk * VA_W + h * 65:kbk * VA_W + h * 65 + 65]
                    i = nc.tensor.matmul(pb[4].ap[0:65, 0:1], vt[:, :],
                                         wq0_sb[:, kbk:kbk + 1],
                                         start=False, stop=(ci == n_contrib - 1))
                    t_av = kb.inc(i, "pe")
                    ci += 1
                t_kf_pe = t_av
            assert ci == n_contrib
            kb.wait(nc.vector, "pe", t_av)
            t_o = kb.inc(nc.vector.tensor_copy(out65, pb[4].ap[0:65, :]), "dve")
            pb[4].consumed("dve", t_o)
            kb.wait(nc.vector, "dve", t_o)
            t_rc = kb.inc(nc.vector.reciprocal(zr2, out65[64:65, 0:TTW]), "dve")
            pb[0].wait_reuse(kb, nc.tensor)
            kb.wait(nc.tensor, "dve", t_rc)
            t_bc = kb.inc(nc.tensor.matmul(pb[0].ap[:, :], C["ones_row_f32"], zr2,
                                           start=True, stop=True), "pe")
            kb.wait(nc.vector, "pe", max(t_bc, t_arena_ok))
            dst = a_xln[(h % 2) * 64:(h % 2) * 64 + 64, (h // 2) * T + qs:(h // 2) * T + qs + TTW]
            t_at = kb.inc(nc.vector.tensor_tensor(dst, out65[0:64, 0:TTW], pb[0].ap[0:64, :],
                                                  op=ALU.mult), "dve")
            pb[0].consumed("dve", t_at)
    t_attn = kb.cnt["dve"]
    if phase_limit == "attn":
        return

    kb.wait(nc.gpsimd, "pe", kb.cnt["pe"])
    t_wo = kb.dinc(nc.gpsimd.dma_start(
        out=wA.rearrange("p (c j) -> p c j", c=NC),
        in_=Wo.rearrange("(c p) j -> p c j", p=128)), "wq")
    xf = w["xf"]
    x_sb = [xf[:, 0:TTW], xf[:, TTW:2 * TTW]]
    x_tok = [0, 0]
    for cc in range(NC):
        for tt in range(NTT):
            ts = tt * TTW
            bank = pb[tt % 2]
            bank.wait_reuse(kb, nc.tensor)
            if cc == 0 and tt == 0:
                kb.wait(nc.tensor, "wq", t_wo)
                kb.wait(nc.tensor, "dve", t_attn)
                kb.wait(nc.tensor, "wq", t_smallg)
            for c2 in range(NC):
                nc.tensor.matmul(bank.ap[:, :],
                                 wA[:, c2 * D + cc * 128:c2 * D + cc * 128 + 128],
                                 a_xln[:, c2 * T + ts:c2 * T + ts + TTW],
                                 start=(c2 == 0), stop=False)
            i = nc.tensor.matmul(bank.ap[:, :], obr_sb[0:1, cc * 128:cc * 128 + 128],
                                 C["ones_row512_bf"], start=False, stop=True)
            t_mm = kb.inc(i, "pe")
            xb_i = tt % 2
            kb.wait(nc.sync, "dve", x_tok[xb_i])
            t_x = kb.dinc(nc.sync.dma_start(
                out=x_sb[xb_i], in_=x_in[cc * 128:(cc + 1) * 128, ts:ts + TTW]), "ldq")
            kb.wait(nc.vector, "ldq", t_x)
            kb.wait(nc.vector, "pe", t_mm)
            i = nc.vector.tensor_tensor(x_sb[xb_i], x_sb[xb_i], bank.ap[:, :], op=ALU.add)
            x_tok[xb_i] = t_u = kb.inc(i, "dve")
            bank.consumed("dve", t_u)
            kb.wait(nc.sync, "dve", t_u)
            kb.dinc(nc.sync.dma_start(
                out=x_out[cc * 128:(cc + 1) * 128, ts:ts + TTW], in_=x_sb[xb_i]), "stq")
    kb.wait(nc.sync, "stq")


def l1attn_weights(inputs):
    return {
        "Wq": np.ascontiguousarray(inputs["L1_attn_Wq"]),
        "Wk": np.ascontiguousarray(inputs["L1_attn_Wk"]),
        "Wv": np.ascontiguousarray(inputs["L1_attn_Wv"]),
        "Wo": np.ascontiguousarray(inputs["L1_attn_Wo"]),
        "qb": np.ascontiguousarray(inputs["L1_attn_bq"].reshape(NC, 128).T),
        "kbias": np.ascontiguousarray(inputs["L1_attn_bk"].reshape(NC, 128).T),
        "ob_row": np.ascontiguousarray(inputs["L1_attn_bo"].reshape(1, D)),
        "vb_row": np.ascontiguousarray(inputs["L1_attn_bv"].reshape(1, D)),
    }


def l0attn_weights(inputs):
    return {
        "qwT": np.ascontiguousarray(inputs["L0_attn_q_w"].T),
        "kwT": np.ascontiguousarray(inputs["L0_attn_k_w"].T),
        "vwT": np.ascontiguousarray(inputs["L0_attn_v_w"].T),
        "owT": np.ascontiguousarray(inputs["L0_attn_o_w"].T),
        "qb": np.ascontiguousarray(inputs["L0_attn_q_b"].reshape(NC, 128).T),
        "kbias": np.ascontiguousarray(inputs["L0_attn_k_b"].reshape(NC, 128).T),
        "ob_row": np.ascontiguousarray(inputs["L0_attn_o_b"].reshape(1, D)),
        "vb_row": np.ascontiguousarray(inputs["L0_attn_v_b"].reshape(1, D)),
        "qn": np.ascontiguousarray(inputs["L0_attn_qn_scale"].reshape(NC, 128).T),
        "kn": np.ascontiguousarray(inputs["L0_attn_kn_scale"].reshape(NC, 128).T),
    }


def _l0ffn_consts():
    iota512 = np.broadcast_to(np.arange(TTW, dtype=np.float32), (128, TTW))
    idxw = np.zeros((128, 32 * 3), np.float32)
    for tb in range(32):
        idxw[:, 3 * tb] = 128.0 * tb
        idxw[:, 3 * tb + 1] = np.arange(128)
        idxw[:, 3 * tb + 2] = 1.0
    grp16 = np.zeros((128, NE), np.float32)
    for p in range(128):
        grp16[p, p // 16] = 1.0
    sel8 = np.zeros((NE, 128), np.float32)
    for p in range(128):
        sel8[p // 16, p] = 1.0
    oh8 = np.zeros((NE, NE * 128), np.float32)
    for e in range(NE):
        oh8[e, e * 128:(e + 1) * 128] = 1.0
    mrg3 = np.array([[1.0], [1.0], [-4096.0]], np.float32)
    return iota512, idxw, grp16, sel8, oh8, mrg3


DEBUG_L0FFN = False


def emit_l0ffn(kb, C, x_in, x_out, W, S):
    nc = kb.nc
    t_cl = 0
    for name, dap, shape, dt in C["_l0f_spec"]:
        tile = kb.sb("c_" + name, shape, dt)
        t_cl = kb.dinc(nc.sync.dma_start(out=tile[:], in_=dap), "ldq")
        C[name] = tile[:]
    for eng in (nc.tensor, nc.vector, nc.scalar, nc.gpsimd):
        kb.wait(eng, "ldq", t_cl)
    gate_w, w1_d, w2_d = W["gate_w"], W["w1"], W["w2"]
    xtm_dram = S["xtm_dram"]
    pdram = S["pdram"]
    dense_dram = S["dense_dram"]
    if DEBUG_L0FFN:
        dbg_gsb = kb.dout("dbg_gsb", [128, NC * NE], F32)
        dbg_sc0 = kb.dout("dbg_sc0", [8, TTW], F32)
        dbg_probs = kb.dout("dbg_probs", [NE, T], F32)
        dbg_mask = kb.dout("dbg_mask", [NE, T], BF16)
        dbg_idx = kb.dout("dbg_idx", [128, NE * 8], I32)
        dbg_lo = kb.dout("dbg_lo", [8, 8], F32)
        dbg_pm = kb.dout("dbg_pm", [128, 256], F32)
        dbg_bis = kb.dout("dbg_bis", [8, 80], F32)
        dbg_bis_sb = kb.sb("dbg_bis_sb", [8, 80], F32)

    a_w2 = kb.sb("a_w2", [128, 32 * D], BF16)
    xln = [a_w2[:, c * T:(c + 1) * T] for c in range(NC)]
    a_h = kb.sb("a_h", [128, 32 * TTW], BF16)
    xb_t = a_h[:, 0:NC * TTW]
    sq_t = a_h[:, NC * TTW:2 * NC * TTW]
    scr = kb.sb("ln_scr", [128, 3 * TTW], F32)
    sqf = kb.sb("ln_sqf", [128, 2 * TTW], F32)
    xf = kb.sb("ln_xf", [128, NC * TTW], F32)
    w = {"xf": xf, "xb": xb_t, "sq": sq_t, "stat": scr[0:1, :], "scr": scr, "sqf": sqf}
    probs_t = kb.sb("probs_t", [8, T], F32)
    probs = probs_t[:]
    pfxB_t = kb.sb("pfxB_t", [8, T], F32)
    pfxA = probs_t[:]
    pfxB = pfxB_t[:]

    gate_sb = kb.sb("gate_sb", [128, NC * NE], F32)
    ones8_f = kb.sb("ones8_f", [128, 1], F32)
    pm = kb.sb("pm", [128, 256], F32)
    tmpc = kb.sb("tmpc", [128, 256], BF16)
    cnt_col = kb.sb("cnt_col", [128, 1], F32)
    bis = kb.sb("bis", [8, 8], F32)
    b_lo = kb.sb("b_lo", [8, 1], F32)
    b_hi = kb.sb("b_hi", [8, 1], F32)
    b_mid = kb.sb("b_mid", [8, 1], F32)
    b_cnt = kb.sb("b_cnt", [8, 1], F32)
    b_mf = kb.sb("b_mf", [8, 1], F32)
    b_tA = kb.sb("b_tA", [8, 1], F32)
    b_tB = kb.sb("b_tB", [8, 1], F32)
    b_cap = kb.sb("b_cap", [8, 1], F32)
    thr128 = kb.sb("thr128", [128, 1], F32)
    mask_bf = a_h[0:8, 0:T]
    pfxT = kb.sb("pfxT", [128, 32 * 8], F32)
    pfxT2 = kb.sb("pfxT2", [128, 32 * 8], F32)
    maskT = kb.sb("maskT", [128, 32 * 8], F32)
    ptile2 = [kb.sb(f"ptile{i}", [128, TTW], BF16) for i in range(2)]
    ptile = ptile2[0]
    idx3 = scr[0:3, 0:TTW]
    idxrow = scr[0:1, TTW:2 * TTW]
    idx_all = kb.sb("idx_all", [128, NE * 8], I32)
    x_tm = [kb.sb(f"x_tm{i}", [128, D], BF16) for i in range(2)]
    x_fm = kb.sb("x_fm", [128, NC * D], BF16)
    w1_sb = [kb.sb(f"w1_{i}", [128, NC * 128], BF16) for i in range(4)]
    y_arena = kb.sb("y_arena", [128, NC * TTW], BF16)
    y_tm = [kb.sb(f"y_tm{i}", [128, D], BF16) for i in range(2)]
    w["xtm_sb"] = y_tm[1][:]
    p_col = [kb.sb(f"p_col{i}", [128, 1], F32) for i in range(2)]

    pb = [Bank(kb.ps(f"pb{i}", [128, TTW])) for i in range(5)]
    pbf = [Bank(kb.ps(f"pbf{i}", [128, 128], BF16)) for i in range(2)]
    pmisc = Bank(kb.ps("pmisc", [128, 128]))
    bnd_reg = kb.bnd_reg

    t_z = kb.inc(nc.vector.memset(y_tm[0][:], 0.0), "dve")
    kb.wait(nc.sync, "dve", t_z)
    t_zs = 0
    for tc in range(32):
        t_zs = kb.dinc(nc.sync.dma_start(out=dense_dram[tc * 128:(tc + 1) * 128, :],
                                         in_=y_tm[0][:]), "stq")
    t_gw = kb.dinc(nc.sync.dma_start(
        out=gate_sb[:].rearrange("p (c e) -> p c e", c=NC),
        in_=gate_w.rearrange("(c p) e -> p c e", p=128)), "ldq")
    t_o8 = kb.inc(nc.vector.memset(ones8_f[:], 1.0), "dve")

    def gate_cb(tt, ts, t_ap):
        bank = pb[4]
        bank.wait_reuse(kb, nc.tensor)
        if tt == 0:
            kb.wait(nc.tensor, "ldq", t_gw)
            kb.wait(nc.tensor, "dve", t_o8)
        kb.wait(nc.tensor, "dve", t_ap)
        for cc in range(NC):
            i = nc.tensor.matmul(bank.ap[0:8, :], gate_sb[:, cc * NE:(cc + 1) * NE],
                                 xf[:, cc * TTW:(cc + 1) * TTW],
                                 start=(cc == 0), stop=(cc == NC - 1))
        t_mm = kb.inc(i, "pe")
        if DEBUG_L0FFN and tt == 0:
            kb.wait(nc.vector, "pe", t_mm)
            t_d = kb.inc(nc.vector.tensor_copy(scr[0:8, 2 * TTW:3 * TTW], bank.ap[0:8, :]), "dve")
            kb.wait(nc.sync, "dve", t_d)
            kb.dinc(nc.sync.dma_start(out=dbg_sc0[:, :], in_=scr[0:8, 2 * TTW:3 * TTW]), "stq")
            kb.dinc(nc.sync.dma_start(out=dbg_gsb[:, :], in_=gate_sb[:]), "stq")
        kb.wait(nc.vector, "pe", t_mm)
        t_sce = kb.inc(nc.vector.tensor_copy(sqf[0:8, 0:TTW], bank.ap[0:8, :]), "dve")
        bank.consumed("dve", t_sce)
        kb.wait(nc.scalar, "dve", t_sce)
        t_e = kb.inc(nc.scalar.activation(probs[:, ts:ts + TTW], sqf[0:8, 0:TTW],
                                          AF.Exp), "act")
        kb.wait(nc.tensor, "act", t_e)
        t_s = kb.inc(nc.tensor.matmul(bank.ap[0:1, :], ones8_f[0:8, :],
                                      probs[:, ts:ts + TTW], start=True, stop=True), "pe")
        kb.wait(nc.vector, "pe", t_s)
        t_c = kb.inc(nc.vector.tensor_copy(scr[0:1, 0:TTW], bank.ap[0:1, :]), "dve")
        kb.wait(nc.vector, "dve", t_c)
        t_r = kb.inc(nc.vector.reciprocal(scr[0:1, TTW:2 * TTW], scr[0:1, 0:TTW]), "dve")
        kb.wait(nc.tensor, "dve", t_r)
        t_b = kb.inc(nc.tensor.matmul(bank.ap[0:8, :], C["ones_row_f32"][0:1, 0:8],
                                      scr[0:1, TTW:2 * TTW], start=True, stop=True), "pe")
        kb.wait(nc.vector, "pe", t_b)
        t_p = kb.inc(nc.vector.tensor_tensor(probs[:, ts:ts + TTW], probs[:, ts:ts + TTW],
                                             bank.ap[0:8, :], op=ALU.mult), "dve")
        bank.consumed("dve", t_p)

    emit_ln(kb, C, x_in, w, xln, pb, pbf=pbf, xtm_dram=xtm_dram, per_tt_cb=gate_cb, f32_stats=True)
    t_probs = kb.cnt["dve"]
    t_xtm_ldq = kb.cnt["ldq"]

    kb.wait(nc.sync, "dve", t_probs)
    t_pd = kb.dinc(nc.sync.dma_start(out=pdram[:, :], in_=probs), "stq")
    kb.wait(nc.sync, "stq", t_pd)
    t_pm = kb.dinc(nc.sync.dma_start(
        out=pm[:], in_=pdram.rearrange("e (g i) -> (e g) i", g=16)), "ldq")
    lo = b_lo[:]
    hi = b_hi[:]
    mid = b_mid[:]
    cnt8 = b_cnt[:]
    capcol = b_cap[:]
    nc.vector.memset(lo[:, :] if False else lo, 0.0)
    nc.vector.memset(hi, 1.0)
    nc.vector.memset(capcol, float(CAP) - 0.5)
    t_mid = kb.inc(nc.vector.memset(mid, 0.5), "dve")
    for it in range(40):
        pmisc.wait_reuse(kb, nc.tensor)
        kb.wait(nc.tensor, "dve", t_mid)
        t_bc = kb.inc(nc.tensor.matmul(pmisc.ap[:, 0:1], C["sel8"], mid,
                                       start=True, stop=True), "pe")
        kb.wait(nc.vector, "pe", t_bc)
        t_th = kb.inc(nc.vector.tensor_copy(thr128[:], pmisc.ap[:, 0:1]), "dve")
        pmisc.consumed("dve", t_th)
        kb.wait(nc.vector, "dve", t_th)
        kb.wait(nc.vector, "ldq", t_pm)
        t_cm = kb.inc(nc.vector.tensor_scalar(tmpc[:], pm[:], thr128[:, :], None,
                                              op0=ALU.is_gt), "dve")
        kb.wait(nc.vector, "dve", t_cm)
        t_ct = kb.inc(nc.vector.reduce_sum(cnt_col[:], tmpc[:], axis=AX.X), "dve")
        pmisc.wait_reuse(kb, nc.tensor)
        kb.wait(nc.tensor, "dve", t_ct)
        t_c8 = kb.inc(nc.tensor.matmul(pmisc.ap[0:8, 0:1], C["grp16"], cnt_col[:],
                                       start=True, stop=True), "pe")
        kb.wait(nc.vector, "pe", t_c8)
        t_cc = kb.inc(nc.vector.tensor_copy(cnt8, pmisc.ap[0:8, 0:1]), "dve")
        pmisc.consumed("dve", t_cc)
        if DEBUG_L0FFN and it < 13:
            nc.vector.tensor_copy(dbg_bis_sb[:, 6 * it:6 * it + 1], mid)
            nc.vector.tensor_copy(dbg_bis_sb[:, 6 * it + 1:6 * it + 2], cnt8)
        kb.wait(nc.vector, "dve", t_cc)
        mf = b_mf[:]
        tA = b_tA[:]
        tB = b_tB[:]
        t_m8 = kb.inc(nc.vector.tensor_tensor(mf, cnt8, capcol, op=ALU.is_ge), "dve")
        kb.wait(nc.vector, "dve", t_m8)
        nc.vector.tensor_tensor(tA, mid, lo, op=ALU.subtract)
        t_s1 = kb.inc(nc.vector.tensor_tensor(tB, hi, mid, op=ALU.subtract), "dve")
        kb.wait(nc.vector, "dve", t_s1)
        nc.vector.tensor_tensor(tA, tA, mf, op=ALU.mult)
        t_s2 = kb.inc(nc.vector.tensor_tensor(tB, tB, mf, op=ALU.mult), "dve")
        kb.wait(nc.vector, "dve", t_s2)
        nc.vector.tensor_tensor(lo, lo, tA, op=ALU.add)
        t_s3 = kb.inc(nc.vector.tensor_tensor(hi, mid, tB, op=ALU.add), "dve")
        kb.wait(nc.vector, "dve", t_s3)
        if DEBUG_L0FFN and it < 13:
            nc.vector.tensor_copy(dbg_bis_sb[:, 6 * it + 2:6 * it + 3], mf)
            nc.vector.tensor_copy(dbg_bis_sb[:, 6 * it + 3:6 * it + 4], lo)
            nc.vector.tensor_copy(dbg_bis_sb[:, 6 * it + 4:6 * it + 5], hi)
            nc.vector.tensor_copy(dbg_bis_sb[:, 6 * it + 5:6 * it + 6], tA)
        nc.vector.tensor_scalar(tA, lo, 0.5, None, op0=ALU.mult)
        nc.vector.tensor_scalar(tB, hi, 0.5, None, op0=ALU.mult)
        nc.vector.tensor_copy(bis[:, 0:1], mf)
        t_mid = kb.inc(nc.vector.tensor_tensor(mid, tA, tB, op=ALU.add), "dve")
    kb.wait(nc.vector, "dve", t_mid)
    t_msk = kb.inc(nc.vector.tensor_scalar(mask_bf, probs, lo, None, op0=ALU.is_gt), "dve")

    if DEBUG_L0FFN:
        kb.wait(nc.sync, "dve", t_msk)
        kb.dinc(nc.sync.dma_start(out=dbg_probs[:, :], in_=probs), "stq")
        kb.dinc(nc.sync.dma_start(out=dbg_mask[:, :], in_=mask_bf), "stq")
        kb.dinc(nc.sync.dma_start(out=dbg_lo[:, :], in_=bis[:]), "stq")
        kb.dinc(nc.sync.dma_start(out=dbg_pm[:, :], in_=pm[:]), "stq")
        t_dbg = kb.dinc(nc.sync.dma_start(out=dbg_bis[:, :], in_=dbg_bis_sb[:]), "stq")
        kb.wait(nc.vector, "stq", t_dbg)
    kb.wait(nc.vector, "dve", t_msk)
    kb.wait(nc.vector, "stq", t_pd)
    t_pf = kb.inc(nc.vector.tensor_copy(pfxA[:, :], mask_bf), "dve")
    cur, nxt = pfxA, pfxB
    sh = 1
    while sh < T:
        kb.wait(nc.vector, "dve", t_pf)
        nc.vector.tensor_copy(nxt[:, 0:sh], cur[:, 0:sh])
        t_pf = kb.inc(nc.vector.tensor_tensor(nxt[:, sh:T], cur[:, sh:T],
                                              cur[:, 0:T - sh], op=ALU.add), "dve")
        cur, nxt = nxt, cur
        sh *= 2
    kb.wait(nc.vector, "dve", t_pf)
    t_pf = kb.inc(nc.vector.tensor_tensor(nxt[:, :], cur[:, :], mask_bf,
                                          op=ALU.subtract), "dve")
    expfx = nxt
    for tb in range(32):
        for srcv, dstv in ((expfx, pfxT), (mask_bf, maskT)):
            pmisc.wait_reuse(kb, nc.tensor)
            kb.wait(nc.tensor, "dve", t_pf)
            if srcv is mask_bf:
                pbf[0].wait_reuse(kb, nc.tensor)
                t_t = kb.inc(nc.tensor.transpose(pbf[0].ap[:, 0:8],
                                                 srcv[:, tb * 128:(tb + 1) * 128],
                                                 C["id_bf"][0:8, 0:8]), "pe")
                kb.wait(nc.vector, "pe", t_t)
                t_ev = kb.inc(nc.vector.tensor_copy(dstv[:, tb * 8:(tb + 1) * 8],
                                                    pbf[0].ap[:, 0:8]), "dve")
                pbf[0].consumed("dve", t_ev)
            else:
                t_t = kb.inc(nc.tensor.transpose(pmisc.ap[:, 0:8],
                                                 srcv[:, tb * 128:(tb + 1) * 128],
                                                 C["id_f32"][0:8, 0:8]), "pe")
                kb.wait(nc.vector, "pe", t_t)
                t_ev = kb.inc(nc.vector.tensor_copy(dstv[:, tb * 8:(tb + 1) * 8],
                                                    pmisc.ap[:, 0:8]), "dve")
                pmisc.consumed("dve", t_ev)
    t_tr = kb.cnt["dve"]
    kb.wait(nc.vector, "dve", t_tr)
    t_p2 = kb.inc(nc.vector.tensor_scalar(pfxT2[:], pfxT[:], -512.0, None, op0=ALU.add), "dve")

    pt_pe = [0, 0]
    for e in range(NE):
        for half in range(2):
            bank = pb[4]
            bank.wait_reuse(kb, nc.tensor)
            pfx_src = pfxT if half == 0 else pfxT2
            t_pt_last = 0
            t_pt = {}
            for tb in range(33):
                if tb < 32:
                    buf = tb % 2
                    kb.wait(nc.vector, "dve", max(t_p2, t_tr))
                    kb.wait(nc.vector, "pe", pt_pe[buf])
                    t_pt[tb] = kb.inc(nc.vector.tensor_scalar(
                        ptile2[buf][:], C["iota512"], pfx_src[:, tb * 8 + e:tb * 8 + e + 1],
                        maskT[:, tb * 8 + e:tb * 8 + e + 1],
                        op0=ALU.is_equal, op1=ALU.mult), "dve")
                if tb >= 1:
                    td = tb - 1
                    kb.wait(nc.tensor, "dve", t_pt[td])
                    i = nc.tensor.matmul(bank.ap[0:3, :], C["idxw"][:, td * 3:td * 3 + 3],
                                         ptile2[td % 2][:], start=(td == 0), stop=(td == 31))
                    t_pt_last = kb.inc(i, "pe")
                    pt_pe[td % 2] = t_pt_last
            kb.wait(nc.vector, "pe", t_pt_last)
            t_i3 = kb.inc(nc.vector.tensor_copy(idx3, bank.ap[0:3, :]), "dve")
            bank.consumed("dve", t_i3)
            bank2 = pb[3]
            bank2.wait_reuse(kb, nc.tensor)
            kb.wait(nc.tensor, "dve", t_i3)
            t_mg = kb.inc(nc.tensor.matmul(bank2.ap[0:1, :], C["mrg3"], idx3,
                                           start=True, stop=True), "pe")
            kb.wait(nc.vector, "pe", t_mg)
            t_ir = kb.inc(nc.vector.tensor_scalar(idxrow, bank2.ap[0:1, :], 4096.0,
                                                  None, op0=ALU.add), "dve")
            bank2.consumed("dve", t_ir)
            for j in range(4):
                pmisc.wait_reuse(kb, nc.tensor)
                kb.wait(nc.tensor, "dve", t_ir)
                t_t = kb.inc(nc.tensor.transpose(pmisc.ap[:, 0:1],
                                                 scr[0:1, TTW + j * 128:TTW + (j + 1) * 128],
                                                 C["id_f32"][0:1, 0:1]), "pe")
                kb.wait(nc.vector, "pe", t_t)
                t_ix = kb.inc(nc.vector.tensor_copy(
                    idx_all[:, e * 8 + half * 4 + j:e * 8 + half * 4 + j + 1],
                    pmisc.ap[:, 0:1]), "dve")
                pmisc.consumed("dve", t_ix)
    t_idx = kb.cnt["dve"]
    if DEBUG_L0FFN:
        kb.wait(nc.sync, "dve", t_idx)
        kb.dinc(nc.sync.dma_start(out=dbg_idx[:, :], in_=idx_all[:]), "stq")

    w1_tok = [0, 0, 0, 0]
    xtm_pe = [0, 0]
    ytm_tok = [0, 0]
    ytm_stq_war = [t_zs, 0]
    ytm_ldq_war = [0, t_xtm_ldq]
    ytm_pcol_tok = [0, 0]
    t_scat_prev = 0
    t_w2 = 0
    t_yphase = kb.cnt["pe"]
    for e in range(NE):
        kb.wait(nc.gpsimd, "pe", t_yphase)
        t_w2 = kb.dinc(nc.gpsimd.dma_start(
            out=a_w2[:].rearrange("p (h c) -> p h c", h=32),
            in_=w2_d[e].rearrange("(h p) c -> p h c", p=128)), "wq")
        for j in range(NC):
            buf = j % 2
            kb.wait(nc.vector, "pe", xtm_pe[buf])
            t_ms = kb.inc(nc.vector.memset(x_tm[buf][:], 0.0), "dve")
            kb.wait(nc.gpsimd, "dve", max(t_ms, t_idx))
            t_g = kb.dinc(nc.gpsimd.indirect_dma_start(
                out=x_tm[buf][:], out_offset=None, in_=xtm_dram[:, :],
                in_offset=IndirectOffsetOnAxis(ap=idx_all[:, e * 8 + j:e * 8 + j + 1], axis=0),
                bounds_check=bnd_reg, oob_is_err=False), "gq")
            for cc in range(NC):
                slot = pbf[cc % 2]
                slot.wait_reuse(kb, nc.tensor)
                kb.wait(nc.tensor, "gq", t_g)
                t_t = kb.inc(nc.tensor.transpose(slot.ap[:, :],
                                                 x_tm[buf][:, cc * 128:(cc + 1) * 128],
                                                 C["id_bf"]), "pe")
                xtm_pe[buf] = t_t
                kb.wait(nc.vector, "pe", t_t)
                t_ev = kb.inc(nc.vector.tensor_copy(
                    x_fm[:, cc * D + j * 128:cc * D + j * 128 + 128], slot.ap[:, :]), "dve")
                slot.consumed("dve", t_ev)
        t_xfm = kb.cnt["dve"]
        for half in range(2):
            hs = half * TTW
            for hc in range(32):
                buf = hc % 4
                kb.wait(nc.gpsimd, "pe", w1_tok[buf])
                t_w1 = kb.dinc(nc.gpsimd.dma_start(
                    out=w1_sb[buf][:].rearrange("p (c h) -> p c h", c=NC),
                    in_=w1_d[e].rearrange("(c p) h -> p c h", p=128)[:, :, hc * 128:(hc + 1) * 128]),
                    "wq")
                bank = pb[hc % 2]
                bank.wait_reuse(kb, nc.tensor)
                kb.wait(nc.tensor, "wq", t_w1)
                kb.wait(nc.tensor, "dve", t_xfm)
                for cc in range(NC):
                    i = nc.tensor.matmul(bank.ap[:, :],
                                         w1_sb[buf][:, cc * 128:(cc + 1) * 128],
                                         x_fm[:, cc * D + hs:cc * D + hs + TTW],
                                         start=(cc == 0), stop=(cc == NC - 1))
                w1_tok[buf] = t_mm = kb.inc(i, "pe")
                kb.wait(nc.scalar, "pe", t_mm)
                t_gl = kb.inc(nc.scalar.activation(
                    a_h[:, hc * TTW:(hc + 1) * TTW], bank.ap[:, :], AF.Gelu), "act")
                bank.consumed("act", t_gl)
            t_h = kb.cnt["act"]
            kb.wait(nc.tensor, "wq", t_w2)
            for grp in ((0, 1, 2, 3), (4, 5, 6, 7)):
                for cc in grp:
                    bank = pb[cc - grp[0]]
                    bank.wait_reuse(kb, nc.tensor)
                    kb.wait(nc.tensor, "act", t_h)
                    for hc in range(32):
                        i = nc.tensor.matmul(bank.ap[:, :],
                                             a_w2[:, hc * D + cc * 128:hc * D + cc * 128 + 128],
                                             a_h[:, hc * TTW:(hc + 1) * TTW],
                                             start=(hc == 0), stop=(hc == 31))
                    t_mm = kb.inc(i, "pe")
                    kb.wait(nc.vector, "pe", t_mm)
                    t_y = kb.inc(nc.vector.tensor_copy(
                        y_arena[:, cc * TTW:(cc + 1) * TTW], bank.ap[:, :]), "dve")
                    bank.consumed("dve", t_y)
            t_yphase = kb.cnt["pe"]
            t_yall = kb.cnt["dve"]
            for j in range(4):
                buf = j % 2
                icol = idx_all[:, e * 8 + half * 4 + j:e * 8 + half * 4 + j + 1]
                kb.wait(nc.gpsimd, "dve", ytm_pcol_tok[buf])
                t_pg = kb.dinc(nc.gpsimd.indirect_dma_start(
                    out=p_col[buf][:], out_offset=None,
                    in_=pdram.rearrange("e (t one) -> (e t) one", one=1),
                    in_offset=IndirectOffsetOnAxis(ap=icol, axis=0),
                    element_offset=e * T,
                    bounds_check=bnd_reg, oob_is_err=False), "gq")
                kb.wait(nc.vector, "gq", ytm_tok[buf])
                kb.wait(nc.vector, "stq", ytm_stq_war[buf])
                kb.wait(nc.vector, "ldq", ytm_ldq_war[buf])
                ytm_stq_war[buf] = ytm_ldq_war[buf] = 0
                for cc in range(NC):
                    slot = pbf[cc % 2]
                    slot.wait_reuse(kb, nc.tensor)
                    kb.wait(nc.tensor, "dve", t_yall)
                    t_t = kb.inc(nc.tensor.transpose(
                        slot.ap[:, :], y_arena[:, cc * TTW + j * 128:cc * TTW + j * 128 + 128],
                        C["id_bf"]), "pe")
                    kb.wait(nc.vector, "pe", t_t)
                    t_ev = kb.inc(nc.vector.tensor_copy(
                        y_tm[buf][:, cc * 128:(cc + 1) * 128], slot.ap[:, :]), "dve")
                    slot.consumed("dve", t_ev)
                kb.wait(nc.vector, "gq", t_pg)
                t_ysc = kb.inc(nc.vector.tensor_scalar(
                    y_tm[buf][:], y_tm[buf][:], p_col[buf][:, :], None, op0=ALU.mult), "dve")
                ytm_pcol_tok[buf] = t_ysc
                kb.wait(nc.gpsimd, "dve", t_ysc)
                kb.wait(nc.gpsimd, "gq", t_scat_prev)
                t_sc = kb.dinc(nc.gpsimd.indirect_dma_start(
                    out=dense_dram[:, :],
                    out_offset=IndirectOffsetOnAxis(ap=icol, axis=0),
                    in_=y_tm[buf][:], in_offset=None,
                    bounds_check=bnd_reg, oob_is_err=False,
                    compute_op=ALU.add), "gq")
                ytm_tok[buf] = t_sc
        t_scat_prev = kb.cnt["gq"]

    x_src = x_in.rearrange("(c p) t -> p c t", p=128)
    x_dst = x_out.rearrange("(c p) t -> p c t", p=128)
    dn_sb = x_tm[0]
    xck = xf[:, 0:NC * 128]
    t_resid = 0
    for tc in range(32):
        tcs = tc * 128
        kb.wait(nc.sync, "gq", t_scat_prev)
        kb.wait(nc.sync, "dve", t_resid)
        t_xl = kb.dinc(nc.sync.dma_start(
            out=xck.rearrange("p (c t) -> p c t", c=NC), in_=x_src[:, :, tcs:tcs + 128]), "ldq")
        kb.wait(nc.sync, "pe", kb.cnt["pe"]) if tc == 0 else None
        t_dn = kb.dinc(nc.sync.dma_start(out=dn_sb[:], in_=dense_dram[tcs:tcs + 128, :]), "ldq")
        for cc in range(NC):
            slot = pbf[cc % 2]
            slot.wait_reuse(kb, nc.tensor)
            kb.wait(nc.tensor, "ldq", t_dn)
            t_t = kb.inc(nc.tensor.transpose(slot.ap[:, :],
                                             dn_sb[:, cc * 128:(cc + 1) * 128],
                                             C["id_bf"]), "pe")
            kb.wait(nc.vector, "pe", t_t)
            kb.wait(nc.vector, "ldq", t_xl)
            t_cv = kb.inc(nc.vector.tensor_copy(scr[:, 0:128], slot.ap[:, :]), "dve")
            slot.consumed("dve", t_cv)
            kb.wait(nc.vector, "dve", t_cv)
            t_resid = kb.inc(nc.vector.scalar_tensor_tensor(
                out=xck[:, cc * 128:(cc + 1) * 128], in0=scr[:, 0:128], scalar=0.5,
                in1=xck[:, cc * 128:(cc + 1) * 128], op0=ALU.mult, op1=ALU.add), "dve")
        kb.wait(nc.sync, "dve", t_resid)
        kb.dinc(nc.sync.dma_start(out=x_dst[:, :, tcs:tcs + 128],
                                  in_=xck.rearrange("p (c t) -> p c t", c=NC)), "stq")
    kb.wait(nc.sync, "stq")


def l0ffn_weights(inputs):
    return {
        "gate_w": np.ascontiguousarray(inputs["L0_ffn_gate_w"]),
        "w1": np.ascontiguousarray(inputs["L0_ffn_w1"]),
        "w2": np.ascontiguousarray(inputs["L0_ffn_w2"]),
    }


_BUILDER_CACHE = {}


def build_fused():
    kb = KB()
    nc = kb.nc
    iota512, idxw, grp16, sel8, oh8, mrg3 = _l0ffn_consts()
    cos_t, sin_t = _rope_tables()
    C = host_consts(kb, extra=(
        ("ones_row512_bf", np.ones((1, TTW), np.float32), BF16),
        ("p_rot", _p_rot(), BF16),
    ))
    C["cos_d"] = kb.cbf16("c_cos_fm", cos_t)
    C["sin_d"] = kb.cbf16("c_sin_fm", sin_t)
    C["_l0f_spec"] = [
        ("iota512", kb.const("c_iota512", iota512), [128, TTW], F32),
        ("idxw", kb.cbf16("c_idxw", idxw), [128, 96], BF16),
        ("grp16", kb.const("c_grp16", grp16), [128, NE], F32),
        ("sel8", kb.const("c_sel8", sel8), [NE, 128], F32),
        ("mrg3", kb.const("c_mrg3", mrg3), [3, 1], F32),
    ]
    x_in = kb.din("x_in", [D, T], F32)
    x_out = kb.dout("x_out", [D, T], F32)
    xsA = kb.dtmp("xsA", [D, T], F32)
    xsB = kb.dtmp("xsB", [D, T], F32)
    S = {
        "q_dram": kb.dtmp("q_dram", [D, T], BF16),
        "k_dram": kb.dtmp("k_dram", [D, T], BF16),
        "xtm_dram": kb.dtmp("xtm_dram", [T, D], BF16),
        "pdram": kb.dtmp("pdram", [NE, T], F32),
        "dense_dram": kb.dtmp("dense_dram", [T, D], BF16),
    }
    W0f = {"gate_w": kb.din("l0f_gate_w", [D, NE], F32),
           "w1": kb.din("l0f_w1", [NE, D, DFF], F32),
           "w2": kb.din("l0f_w2", [NE, DFF, D], F32)}
    W0a = {"qwT": kb.din("l0a_qwT", [D, D], F32),
           "kwT": kb.din("l0a_kwT", [D, D], F32),
           "vwT": kb.din("l0a_vwT", [D, D], F32),
           "owT": kb.din("l0a_owT", [D, D], F32),
           "qb": kb.din("l0a_qb", [128, NC], F32),
           "kbias": kb.din("l0a_kbias", [128, NC], F32),
           "ob_row": kb.din("l0a_ob_row", [1, D], F32),
           "vb_row": kb.din("l0a_vb_row", [1, D], F32),
           "qn": kb.din("l0a_qn", [128, NC], F32),
           "kn": kb.din("l0a_kn", [128, NC], F32)}
    W1f = {"wiT": kb.din("l1f_wiT", [D, DFF], F32),
           "woT": kb.din("l1f_woT", [DFF, D], F32),
           "AsT": kb.din("l1f_AsT", [NE * D, RANK], F32),
           "BsT": kb.din("l1f_BsT", [NE * RANK, DFF], F32),
           "tki": kb.din("l1f_tki", [1, TOPK], I32),
           "tkv": kb.din("l1f_tkv", [1, TOPK], F32)}
    W1a = {"Wq": kb.din("l1a_Wq", [D, D], F32),
           "Wk": kb.din("l1a_Wk", [D, D], F32),
           "Wv": kb.din("l1a_Wv", [D, D], F32),
           "Wo": kb.din("l1a_Wo", [D, D], F32),
           "qb": kb.din("l1a_qb", [128, NC], F32),
           "kbias": kb.din("l1a_kbias", [128, NC], F32),
           "ob_row": kb.din("l1a_ob_row", [1, D], F32),
           "vb_row": kb.din("l1a_vb_row", [1, D], F32)}

    kb.start()
    load_consts(kb)
    kb.bnd_reg = nc.gpsimd.to_reg(T - 1)

    stages = (
        ("s1", emit_l0ffn, (C, x_in, xsA, W0f, S)),
        ("s2", emit_l0attn, (C, xsA, xsB, W0a, S)),
        ("s3", emit_l0ffn, (C, xsB, xsA, W0f, S)),
        ("s4", emit_l1ffn, (C, xsA, xsB, W1f)),
        ("s5", emit_l1attn, (C, xsB, xsA, W1a, S)),
        ("s6", emit_l1ffn, (C, xsA, x_out, W1f)),
    )
    for tag, fn, args in stages:
        kb.push_scope(tag)
        fn(kb, *args)
        kb.pop_scope()
        kb.barrier()
    kb.wait(nc.sync, "stq")
    return kb.finish()


def _get(name):
    if name not in _BUILDER_CACHE:
        _BUILDER_CACHE[name] = {"fused": build_fused}[name]()
    return _BUILDER_CACHE[name]


def run_stage(kb, per_core_inputs):
    in_maps = []
    for c in range(8):
        m = dict(kb.consts)
        m.update(per_core_inputs[c])
        in_maps.append(m)
    return run_bass_kernel_spmd(kb.nc, in_maps, core_ids=list(range(8)))


def l1ffn_weights(inputs):
    return {
        "wiT": np.ascontiguousarray(inputs["L1_ffn_wi_w"].T),
        "woT": np.ascontiguousarray(inputs["L1_ffn_wo_w"].T),
        "AsT": np.ascontiguousarray(inputs["L1_ffn_lora_As"].transpose(0, 2, 1)).reshape(NE * D, RANK),
        "BsT": np.ascontiguousarray(inputs["L1_ffn_lora_Bs"].transpose(0, 2, 1)).reshape(NE * RANK, DFF),
        "tki": np.ascontiguousarray(inputs["L1_ffn_top_k_indices"].reshape(1, TOPK)),
        "tkv": np.ascontiguousarray(inputs["L1_ffn_top_k_values"].reshape(1, TOPK)),
    }


def kernel(**inputs):
    kb = _get("fused")
    w0f = l0ffn_weights(inputs)
    w0a = l0attn_weights(inputs)
    w1f = l1ffn_weights(inputs)
    w1a = l1attn_weights(inputs)
    wmap = {}
    for pre, wm in (("l0f_", w0f), ("l0a_", w0a), ("l1f_", w1f), ("l1a_", w1a)):
        for k, v in wm.items():
            wmap[pre + k] = v
    per_core = []
    for c in range(8):
        m = dict(wmap)
        m["x_in"] = np.ascontiguousarray(inputs["L0_attn_x"][c].T)
        per_core.append(m)
    res = run_stage(kb, per_core)
    return np.stack([np.ascontiguousarray(res.results[c]["x_out"]).T
                     for c in range(8)]).astype(np.float32)
